# version 01
# baseline (speedup 1.0000x reference)
import os
from contextlib import ExitStack

import numpy as np
import ml_dtypes
import concourse.bass as bass
import concourse.mybir as mybir
from concourse.bass_utils import run_bass_kernel_spmd

F32 = mybir.dt.float32
BF16 = mybir.dt.bfloat16
I32 = mybir.dt.int32
AF = mybir.ActivationFunctionType
ALU = mybir.AluOpType
AX = mybir.AxisListType

NCORE = 8
B, S, D = 4, 2048, 4096
SLOC = S // NCORE
T = B * SLOC
NCH = D // 128
NTT = T // 128
OSL = D // NCORE
FT = 512
NFT = T // FT
SCALE_A = float(1.0 / np.sqrt(128.0))
GSCALE = float(1.0 / np.sqrt(2048.0))
EPS = 1e-5

ENGINES = ("sync", "scalar", "vector", "tensor", "gpsimd")
SEMS = ["L", "GD", "PE", "DV", "AC", "PO", "CC"]


class Script:
    def __init__(self, nc, stack):
        self.nc = nc
        self.stack = stack
        self.steps = {e: [] for e in ENGINES}
        self.sem = {}
        self.cnt = {}

    def new_sem(self, name, stack=None):
        self.sem[name] = (stack or self.stack).enter_context(self.nc.semaphore(name))
        self.cnt[name] = 0

    def slot(self, name):
        name = "D" + name
        if name not in self.sem:
            self.new_sem(name)
        return name

    AUTO = {"vector": "DV", "scalar": "AC", "gpsimd": "PO"}

    def do(self, eng, fn, inc=None, inc_n=1):
        if inc is None and eng in self.AUTO:
            inc = self.AUTO[eng]
            inc_n = 1
        if inc is not None:
            self.cnt[inc] += inc_n
        sem = self.sem.get(inc)

        def step(e, fn=fn, sem=sem, n=inc_n if inc else 0):
            ins = fn(e)
            if sem is not None:
                ins.then_inc(sem, n)
        self.steps[eng].append(step)

    def chain(self, eng, back=0):
        sem = self.AUTO[eng]
        self.wait(eng, sem, self.cnt[sem] - back)

    def dma(self, eng, fn, slot):
        self.do(eng, fn, self.slot(slot), 16)

    def wslot(self, eng, name):
        self.wait(eng, self.slot(name))

    def wait(self, eng, sem, v=None):
        v = self.cnt[sem] if v is None else v
        if v <= 0:
            return
        s = self.sem[sem]
        self.steps[eng].append(lambda e, s=s, v=v: e.wait_ge(s, v))

    def barrier(self):
        snap = dict(self.cnt)
        for en in ENGINES:
            for s, v in snap.items():
                self.wait(en, s, v)

    def replay(self):
        with self.nc.Block() as block:
            for en in ENGINES:
                def runner(eng, en=en):
                    for st in self.steps[en]:
                        st(eng)
                getattr(block, en)(runner)


def _ln_rows(nc, P, env, n, sums_ps, sumsq_ps, third):
    rows = env["rows"]
    m = rows[:, 0:n]
    r = rows[:, n:2 * n]
    t3 = rows[:, 2 * n:3 * n]
    P.do("vector", lambda e: e.tensor_scalar_mul(m, sums_ps, 1.0 / D))
    P.do("vector", lambda e: e.tensor_scalar_mul(r, sumsq_ps, 1.0 / D))
    P.chain("vector")
    P.do("vector", lambda e: e.tensor_tensor(t3, m, m, ALU.mult))
    P.chain("vector")
    P.do("vector", lambda e: e.tensor_scalar_add(t3, t3, -EPS))
    P.chain("vector")
    P.do("vector", lambda e: e.tensor_tensor(r, r, t3, ALU.subtract))
    P.wait("scalar", "DV")
    P.do("scalar", lambda e: e.activation(r, r, AF.Sqrt))
    P.wait("vector", "AC")
    P.do("vector", lambda e: e.reciprocal(r, r))
    P.chain("vector")
    if third == "negm":
        P.do("vector", lambda e: e.tensor_scalar_mul(t3, m, -1.0))
    else:
        P.do("vector", lambda e: e.tensor_tensor(t3, m, r, ALU.mult))
    return dict(m=m, r=r, t3=t3)


def build_program():
    nc = bass.Bass("TRN2", target_bir_lowering=False)
    stop_phase = int(os.environ.get("KMK_STOP_PHASE", "99"))

    x_tok = nc.dram_tensor("x_tok", [T, D], F32, kind="ExternalInput").ap()
    w_slice = nc.dram_tensor("w_slice", [OSL, D], F32, kind="ExternalInput").ap()
    wq_slice = nc.dram_tensor("wq_slice", [OSL, D], F32, kind="ExternalInput").ap()
    wk_in = nc.dram_tensor("wk_in", [D, 128], F32, kind="ExternalInput").ap()
    wv_in = nc.dram_tensor("wv_in", [D, 128], F32, kind="ExternalInput").ap()
    bq_in = nc.dram_tensor("bq_in", [128, 32], F32, kind="ExternalInput").ap()
    bk_in = nc.dram_tensor("bk_in", [128, 1], F32, kind="ExternalInput").ap()
    bv_in = nc.dram_tensor("bv_in", [128, 1], F32, kind="ExternalInput").ap()
    bb_in = nc.dram_tensor("bb_in", [1, D], F32, kind="ExternalInput").ap()
    rw_in = nc.dram_tensor("rw_in", [15, D], F32, kind="ExternalInput").ap()
    rb_in = nc.dram_tensor("rb_in", [1, 15], F32, kind="ExternalInput").ap()
    a_in = nc.dram_tensor("a_in", [128, D], F32, kind="ExternalInput").ap()
    b_in = nc.dram_tensor("b_in", [D, 16], F32, kind="ExternalInput").ap()
    emap_in = nc.dram_tensor("emap_in", [15, 1], I32, kind="ExternalInput").ap()
    y_tok = nc.dram_tensor("y_tok", [T, D], F32, kind="ExternalOutput").ap()
    dbg = nc.dram_tensor("dbg", [128, 18432], F32, kind="ExternalOutput").ap() \
        if os.environ.get("KMK_DBG") else None

    wt_part = nc.dram_tensor("wt_part", [D, OSL], BF16)
    wqe_part = nc.dram_tensor("wqe_part", [OSL, 128], BF16)
    wsum_part = nc.dram_tensor("wsum_part", [OSL, 1], F32)
    wqs_part = nc.dram_tensor("wqs_part", [1, 128], F32)
    wt_full = nc.dram_tensor("wt_full", [NCORE * D, OSL], BF16)
    wqe_full = nc.dram_tensor("wqe_full", [D, 128], BF16)
    wsum_full = nc.dram_tensor("wsum_full", [D, 1], F32)
    wqs_full = nc.dram_tensor("wqs_full", [1, 128], F32)
    wk_bf = nc.dram_tensor("wk_bf", [D, 128], BF16)
    wv_bf = nc.dram_tensor("wv_bf", [D, 128], BF16)
    a_bf = nc.dram_tensor("a_bf", [128, D], BF16)
    b2_part = nc.dram_tensor("b2_part", [128, 512], BF16)
    b2_full = nc.dram_tensor("b2_full", [8 * 128, 512], BF16)
    g_part = nc.dram_tensor("g_part", [D, 16], F32)
    g_full = nc.dram_tensor("g_full", [D, 16], F32)
    rwt_dram = nc.dram_tensor("rwt_dram", [128, 15 * NCH], BF16)

    root = ExitStack()
    with root:
        P = Script(nc, root)
        for s in SEMS:
            P.new_sem(s, root)

        xfm = root.enter_context(nc.sbuf_tensor("xfm", [128, NCH * T], F32))
        idF = root.enter_context(nc.sbuf_tensor("idF", [128, 128], F32))
        idB = root.enter_context(nc.sbuf_tensor("idB", [128, 128], BF16))
        ones_col_b = root.enter_context(nc.sbuf_tensor("ones_col_b", [128, 1], BF16))
        ones_row_b = root.enter_context(nc.sbuf_tensor("ones_row_b", [1, 128], BF16))

        rb_bc = root.enter_context(nc.sbuf_tensor("rb_bc", [128, 15], F32))
        onehotT = root.enter_context(nc.sbuf_tensor("onehotT", [15, 8], BF16))
        eexp = root.enter_context(nc.sbuf_tensor("eexp", [8, 128], BF16))
        bb_fm = root.enter_context(nc.sbuf_tensor("bb_fm", [128, 32], F32))
        ones_f1 = root.enter_context(nc.sbuf_tensor("ones_f1", [1, 1], F32))
        bqe = root.enter_context(nc.sbuf_tensor("bqe", [128, 1], F32))
        bkc = root.enter_context(nc.sbuf_tensor("bkc", [128, 1], F32))
        bvc = root.enter_context(nc.sbuf_tensor("bvc", [128, 1], F32))
        wsums = root.enter_context(nc.sbuf_tensor("wsums", [1, 3 * 128], BF16))
        m0rep = root.enter_context(nc.sbuf_tensor("m0rep", [4, 128], BF16))
        rows = root.enter_context(nc.sbuf_tensor("rows", [1, 3072], F32))
        rows_b = root.enter_context(nc.sbuf_tensor("rows_b", [1, 2112], BF16))

        def xv(dc, lo=0, n=T):
            return xfm[:, dc * T + lo: dc * T + lo + n]

        env = dict(rows=rows, rows_b=rows_b, xv=xv, xfm=xfm, idF=idF, idB=idB,
                   ones_col_b=ones_col_b, ones_row_b=ones_row_b,
                   rb_bc=rb_bc, onehotT=onehotT, eexp=eexp, bb_fm=bb_fm, ones_f1=ones_f1,
                   rwt_dram=rwt_dram, b2_part=b2_part,
                   bqe=bqe, bkc=bkc, bvc=bvc, wsums=wsums, m0rep=m0rep,
                   wt_full=wt_full, wqe_full=wqe_full, wsum_full=wsum_full,
                   wqs_full=wqs_full, wk_bf=wk_bf, wv_bf=wv_bf, a_bf=a_bf,
                   b2_full=b2_full, g_part=g_part, g_full=g_full)

        env["dbg"] = dbg
        emit_phase0(nc, P, env, x_tok, w_slice, wq_slice, wk_in, wv_in, bq_in,
                    bk_in, bv_in, bb_in, rw_in, rb_in, a_in, b_in, emap_in,
                    wt_part, wqe_part, wsum_part, wqs_part)
        P.barrier()
        if stop_phase >= 1:
            emit_attn_a(nc, P, env)
            P.barrier()
        if stop_phase >= 2:
            emit_ffn(nc, P, env, 0)
            P.barrier()
        if stop_phase >= 3:
            emit_attn_b(nc, P, env)
            P.barrier()
        if stop_phase >= 4:
            emit_ffn(nc, P, env, 1)
            P.barrier()
        emit_phase5(nc, P, env, y_tok)
        P.replay()
    return nc


def emit_phase0(nc, P, env, x_tok, w_slice, wq_slice, wk_in, wv_in, bq_in,
                bk_in, bv_in, bb_in, rw_in, rb_in, a_in, b_in, emap_in,
                wt_part, wqe_part, wsum_part, wqs_part):
    xv = env["xv"]
    idF, idB = env["idF"], env["idB"]
    ones_col_b, ones_row_b = env["ones_col_b"], env["ones_row_b"]
    rows_b = env["rows_b"]

    p = ExitStack()
    pa = ExitStack()
    xstage = [pa.enter_context(nc.sbuf_tensor(f"p0_xs{i}", [128, D], F32)) for i in range(2)]
    bqt = pa.enter_context(nc.sbuf_tensor("p0_bqt", [128, 32], F32))
    bb_rowt = pa.enter_context(nc.sbuf_tensor("p0_bbrow", [1, D], F32))
    emap_f = pa.enter_context(nc.sbuf_tensor("p0_em", [15, 1], I32))
    emap_g = pa.enter_context(nc.sbuf_tensor("p0_emg", [15, 1], F32))
    psT = p.enter_context(nc.psum_tensor("p0_psT", [128, 2048], F32))
    psTb = p.enter_context(nc.psum_tensor("p0_psTb", [128, 2048], BF16))
    psK1 = p.enter_context(nc.psum_tensor("p0_psK1", [128, 512], F32))
    psWS = p.enter_context(nc.psum_tensor("p0_psWS", [1, 512], F32))

    P.do("vector", lambda e: e.memset(idF[:], 1.0), inc="DV")
    P.do("vector", lambda e: e.memset(idB[:], 1.0), inc="DV")
    P.wait("gpsimd", "DV")
    P.do("gpsimd", lambda e: e.affine_select(out=idF[:], in_=idF[:], pattern=[[1, 128]],
                                             compare_op=ALU.is_equal, fill=0.0, base=0,
                                             channel_multiplier=-1), inc="PO")
    P.do("gpsimd", lambda e: e.affine_select(out=idB[:], in_=idB[:], pattern=[[1, 128]],
                                             compare_op=ALU.is_equal, fill=0.0, base=0,
                                             channel_multiplier=-1), inc="PO")
    P.wait("tensor", "PO")
    P.do("vector", lambda e: e.memset(ones_col_b[:], 1.0))
    P.do("vector", lambda e: e.memset(ones_row_b[:], 1.0))
    P.do("vector", lambda e: e.memset(env["eexp"][:], 0.0))
    P.do("vector", lambda e: e.memset(env["m0rep"][:], 0.0))
    P.do("vector", lambda e: e.memset(rows_b[:, 2048:2064], 2.0))
    P.do("vector", lambda e: e.memset(rows_b[:, 2064:2096], 1.0), inc="DV")
    P.wait("gpsimd", "DV")
    for ee in range(8):
        P.dma("gpsimd", lambda e, ee=ee: e.dma_start(
            out=env["eexp"][ee:ee + 1, 16 * ee:16 * ee + 16], in_=rows_b[0:1, 2048:2064]), "cst")
    for dd in range(4):
        P.dma("gpsimd", lambda e, dd=dd: e.dma_start(
            out=env["m0rep"][dd:dd + 1, 32 * dd:32 * dd + 32], in_=rows_b[0:1, 2064:2096]), "cst")

    P.dma("sync", lambda e: e.dma_start(out=bqt[:], in_=bq_in), "sml")
    P.dma("sync", lambda e: e.dma_start(out=env["bkc"][:], in_=bk_in), "sml")
    P.dma("sync", lambda e: e.dma_start(out=env["bvc"][:], in_=bv_in), "sml")
    P.dma("sync", lambda e: e.dma_start(out=bb_rowt[:], in_=bb_in), "sml")
    P.dma("sync", lambda e: e.dma_start(out=env["rb_bc"][0:1, :], in_=rb_in), "sml")
    P.dma("sync", lambda e: e.dma_start(out=emap_f[:], in_=emap_in), "sml")
    P.wslot("vector", "sml")
    P.do("vector", lambda e: e.memset(env["ones_f1"][:], 1.0))
    P.do("vector", lambda e: e.reduce_sum(env["bqe"][:], bqt[:], axis=AX.X))
    P.chain("vector")
    P.do("vector", lambda e: e.tensor_scalar_mul(env["bqe"][:], env["bqe"][:], SCALE_A))
    P.do("vector", lambda e: e.tensor_copy(emap_g[:], emap_f[:]))
    P.chain("vector")
    for ee in range(8):
        P.do("vector", lambda e, ee=ee: e.tensor_scalar(env["onehotT"][:, ee:ee + 1], emap_g[:],
                                                        float(ee), None, ALU.is_equal))
    P.do("vector", lambda e: e.tensor_copy(rows_b[0:1, 2096:2111], env["rb_bc"][0:1, 0:15]), inc="DV")
    P.wait("tensor", "DV")
    P.do("tensor", lambda e: e.matmul(psK1[0:128, 0:15], ones_row_b[0:1, 0:128],
                                      rows_b[0:1, 2096:2111], start=True, stop=True), inc="PE")
    P.wait("vector", "PE")
    P.do("vector", lambda e: e.tensor_copy(env["rb_bc"][:], psK1[:, 0:15]), inc="DV")
    P.wait("tensor", "DV")
    for oc in range(32):
        P.do("tensor", lambda e, oc=oc: e.matmul(
            psK1[:, 128 + oc:128 + oc + 1], bb_rowt[0:1, oc * 128:(oc + 1) * 128],
            env["ones_f1"][:], start=True, stop=True), inc="PE")
    P.wait("vector", "PE")
    P.do("vector", lambda e: e.tensor_copy(env["bb_fm"][:], psK1[:, 128:160]), inc="DV")

    dv_tile_done = []
    for tc in range(NTT):
        st = xstage[tc % 2]
        if tc >= 2:
            P.wait("sync", "DV", dv_tile_done[tc - 2])
        P.dma("sync", lambda e, tc=tc, st=st: e.dma_start(
            out=st[:], in_=x_tok[tc * 128:(tc + 1) * 128, :]), f"xs{tc % 2}")
        P.wslot("tensor", f"xs{tc % 2}")
        for dc in range(NCH):
            if tc * NCH + dc >= 4:
                P.wait("tensor", "DV", P.cnt["DV"] - 3)
            P.do("tensor", lambda e, dc=dc, st=st: e.transpose(
                psT[:, (dc % 4) * 512:(dc % 4) * 512 + 128],
                st[:, dc * 128:(dc + 1) * 128], idF[:]), inc="PE")
            P.wait("vector", "PE")
            P.do("vector", lambda e, tc=tc, dc=dc: e.tensor_copy(
                xv(dc, tc * 128, 128), psT[:, (dc % 4) * 512:(dc % 4) * 512 + 128]), inc="DV")
        dv_tile_done.append(P.cnt["DV"])
        P.wait("tensor", "DV")

    pa.close()
    wqout = p.enter_context(nc.sbuf_tensor("p0_wqo", [128, 128], F32))
    wqout_b = p.enter_context(nc.sbuf_tensor("p0_wqob", [128, 128], BF16))
    smt = p.enter_context(nc.sbuf_tensor("p0_smt", [128, 16], F32))
    castA = p.enter_context(nc.sbuf_tensor("p0_ca", [128, D], BF16))
    rw_sb = p.enter_context(nc.sbuf_tensor("p0_rw", [15, D], BF16))
    pb = ExitStack()
    wtileW = [pb.enter_context(nc.sbuf_tensor(f"p0_w{i}", [128, D], BF16)) for i in range(2)]
    wstage = pb.enter_context(nc.sbuf_tensor("p0_wst", [128, 256], BF16))
    wqtile = pb.enter_context(nc.sbuf_tensor("p0_wq", [128, 2048], F32))

    for i in range(4):
        for jh in range(2):
            P.wait("sync", "DV")
            P.dma("sync", lambda e, i=i, jh=jh: e.dma_start(
                out=wqtile[:], in_=wq_slice[i * 128:(i + 1) * 128,
                                            jh * 2048:(jh + 1) * 2048]), "wqt")
            P.wslot("vector", "wqt")
            P.do("vector", lambda e, jh=jh: e.reduce_sum(
                wqout[:, jh * 64:(jh + 1) * 64].unsqueeze(2),
                wqtile[:].rearrange("p (d h) -> p d h", h=32), axis=AX.X), inc="DV")
        P.chain("vector")
        P.do("vector", lambda e: e.tensor_scalar_mul(wqout[:], wqout[:], SCALE_A))
        P.chain("vector")
        P.wslot("vector", "gst")
        P.do("vector", lambda e: e.tensor_copy(wqout_b[:], wqout[:]))
        P.wait("gpsimd", "DV")
        P.dma("gpsimd", lambda e, i=i: e.dma_start(out=wqe_part.ap()[i * 128:(i + 1) * 128, :],
                                                   in_=wqout_b[:]), "gst")
    P.wslot("gpsimd", "gst")
    P.dma("gpsimd", lambda e: e.dma_start(out=castA[:, 0:512],
                                          in_=wqe_part.ap().rearrange("(a p) d -> p a d", p=128)),
          "gc")
    P.wslot("tensor", "gc")
    for g4 in range(4):
        P.do("tensor", lambda e, g4=g4: e.matmul(
            psWS[0:1, 0:128], ones_col_b[:], castA[:, g4 * 128:(g4 + 1) * 128],
            start=(g4 == 0), stop=(g4 == 3)), inc="PE")
    P.wait("vector", "PE")
    P.do("vector", lambda e: e.tensor_copy(wqout[0:1, 0:128], psWS[0:1, 0:128]), inc="DV")
    P.wait("gpsimd", "DV")
    P.dma("gpsimd", lambda e: e.dma_start(out=wqs_part.ap(), in_=wqout[0:1, 0:128]), "gc")

    for (src, dst, coloff) in ((wk_in, env["wk_bf"], 1), (wv_in, env["wv_bf"], 2)):
        P.wslot("gpsimd", "gc")
        P.wait("gpsimd", "DV")
        P.dma("gpsimd", lambda e, src=src: e.dma_start(
            out=castA[:], in_=src.rearrange("(a p) d -> p a d", p=128)), "gc")
        P.wslot("gpsimd", "gc")
        P.dma("gpsimd", lambda e, dst=dst: e.dma_start(
            out=dst.ap().rearrange("(a p) d -> p a d", p=128), in_=castA[:]), "gc")
        P.wslot("vector", "gc")
        P.do("vector", lambda e: e.reduce_sum(
            wqout[:, 0:128].unsqueeze(2),
            castA[:].rearrange("p (a d) -> p d a", a=32), axis=AX.X))
        P.chain("vector")
        P.do("vector", lambda e: e.tensor_copy(wqout_b[:], wqout[:, 0:128]))
        P.wait("tensor", "DV")
        P.do("tensor", lambda e: e.matmul(psWS[0:1, 128:256], ones_col_b[:], wqout_b[:],
                                          start=True, stop=True), inc="PE")
        P.wait("vector", "PE")
        P.do("vector", lambda e, coloff=coloff: e.tensor_copy(
            env["wsums"][0:1, coloff * 128:(coloff + 1) * 128], psWS[0:1, 128:256]), inc="DV")

    for oh in range(2):
        P.wait("gpsimd", "PE")
        for ot in range(2):
            P.dma("gpsimd", lambda e, oh=oh, ot=ot: e.dma_start(
                out=wtileW[ot][:], in_=w_slice[(oh * 2 + ot) * 128:(oh * 2 + ot + 1) * 128, :]),
                "wtl")
        P.wslot("vector", "wtl")
        for ot in range(2):
            P.do("vector", lambda e, oh=oh, ot=ot: e.reduce_sum(
                smt[:, oh * 2 + ot:oh * 2 + ot + 1], wtileW[ot][:], axis=AX.X), inc="DV")
        P.wslot("tensor", "wtl")
        P.wait("tensor", "DV")
        for dc in range(NCH):
            if dc >= 1:
                P.wslot("tensor", "wst")
            for ot in range(2):
                P.do("tensor", lambda e, dc=dc, ot=ot: e.transpose(
                    psTb[:, ot * 128:(ot + 1) * 128],
                    wtileW[ot][:, dc * 128:(dc + 1) * 128], idB[:]), inc="PE")
            P.wait("vector", "PE")
            P.wslot("vector", "wst")
            P.do("vector", lambda e: e.tensor_copy(wstage[:], psTb[:, 0:256]))
            P.wait("sync", "DV")
            P.dma("sync", lambda e, dc=dc, oh=oh: e.dma_start(
                out=wt_part.ap()[dc * 128:(dc + 1) * 128, oh * 256:(oh + 1) * 256],
                in_=wstage[:]), "wst")
    P.wait("gpsimd", "DV")
    P.dma("gpsimd", lambda e: e.dma_start(
        out=wsum_part.ap().rearrange("(a p) b -> p a b", p=128), in_=smt[:, 0:4],
        allow_slow_non_contiguous=True), "gc")

    pb.close()
    P.wslot("gpsimd", "gc")
    P.wait("gpsimd", "DV")
    P.dma("gpsimd", lambda e: e.dma_start(out=castA[:], in_=a_in), "gc")
    P.wslot("gpsimd", "gc")
    P.dma("gpsimd", lambda e: e.dma_start(out=env["a_bf"].ap(), in_=castA[:]), "gc")
    P.wslot("gpsimd", "gc")
    P.dma("gpsimd", lambda e: e.dma_start(out=castA[:, 0:512],
                                          in_=b_in.rearrange("(a p) r -> p a r", p=128)), "gc")
    P.dma("gpsimd", lambda e: e.dma_start(out=rw_sb[:], in_=rw_in), "gc")
    P.wslot("tensor", "gc")
    rwt_loc = p.enter_context(nc.sbuf_tensor("p0_rwt", [128, 15 * NCH], BF16))
    for dc in range(NCH):
        P.do("tensor", lambda e, dc=dc: e.transpose(
            psTb[:, (dc % 2) * 1024:(dc % 2) * 1024 + 15],
            rw_sb[:, dc * 128:(dc + 1) * 128], idB[0:15, 0:15]), inc="PE")
        P.wait("vector", "PE")
        P.do("vector", lambda e, dc=dc: e.tensor_copy(
            rwt_loc[:, dc * 15:(dc + 1) * 15],
            psTb[:, (dc % 2) * 1024:(dc % 2) * 1024 + 15]), inc="DV")
        if dc >= 1:
            P.wait("tensor", "DV", P.cnt["DV"] - 1)
    P.wait("gpsimd", "DV")
    P.dma("gpsimd", lambda e: e.dma_start(out=env["rwt_dram"].ap(), in_=rwt_loc[:]), "gc")
    b2stg = p.enter_context(nc.sbuf_tensor("p0_b2s", [16, 1024], BF16))
    for ocl in range(4):
        if ocl >= 1:
            P.wslot("tensor", "gc")
        for ee in range(8):
            a_idx = ee * 4 + ocl
            P.do("tensor", lambda e, a_idx=a_idx: e.transpose(
                psTb[0:16, 0:128], castA[:, a_idx * 16:(a_idx + 1) * 16], idB[:]), inc="PE")
            P.wait("vector", "PE")
            P.do("vector", lambda e, ee=ee: e.tensor_copy(
                b2stg[:, ee * 128:(ee + 1) * 128], psTb[0:16, 0:128]), inc="DV")
            P.wait("tensor", "DV")
        P.wait("gpsimd", "DV")
        P.dma("gpsimd", lambda e, ocl=ocl: e.dma_start(
            out=bass.AP(tensor=env["b2_part"], offset=ocl * 128,
                        ap=[[512, 16], [16 * 512, 8], [1, 128]]),
            in_=b2stg[:].rearrange("p (a c) -> p a c", a=8)), "gc")

    for sl in ("gc", "gst", "wst", "wtl"):
        P.wslot("gpsimd", sl)
    for (pi, po) in ((wt_part, env["wt_full"]), (wqe_part, env["wqe_full"]),
                     (wsum_part, env["wsum_full"]), (env["b2_part"], env["b2_full"])):
        P.do("gpsimd", lambda e, pi=pi, po=po: e.collective_compute(
            "AllGather", ALU.bypass, replica_groups=[list(range(NCORE))],
            ins=[pi.ap()], outs=[po.ap()]), inc="CC")
    P.do("gpsimd", lambda e: e.collective_compute(
        "AllReduce", ALU.add, replica_groups=[list(range(NCORE))],
        ins=[wqs_part.ap()], outs=[env["wqs_full"].ap()]), inc="CC")
    P.wait("sync", "CC")
    P.dma("sync", lambda e: e.dma_start(out=env["rows"][0:1, 0:128],
                                        in_=env["wqs_full"].ap()), "wqs")
    P.wslot("vector", "wqs")
    P.do("vector", lambda e: e.tensor_scalar_mul(env["rows"][0:1, 0:128],
                                                 env["rows"][0:1, 0:128], SCALE_A))
    P.chain("vector")
    P.do("vector", lambda e: e.tensor_copy(env["wsums"][0:1, 0:128],
                                           env["rows"][0:1, 0:128]))
    p.close()


def emit_attn_a(nc, P, env):
    xv = env["xv"]
    idF, idB = env["idF"], env["idB"]
    ones_col_b, ones_row_b = env["ones_col_b"], env["ones_row_b"]
    rows, rows_b = env["rows"], env["rows_b"]

    p = ExitStack()
    xb = [p.enter_context(nc.sbuf_tensor(f"aa_xb{i}", [128, T], BF16)) for i in range(2)]
    sqb = [p.enter_context(nc.sbuf_tensor(f"aa_sq{i}", [128, T], BF16)) for i in range(2)]
    wq_t = [p.enter_context(nc.sbuf_tensor(f"aa_wq{i}", [128, 512], BF16)) for i in range(2)]
    wk_t = [p.enter_context(nc.sbuf_tensor(f"aa_wk{i}", [128, 512], BF16)) for i in range(2)]
    wv_t = [p.enter_context(nc.sbuf_tensor(f"aa_wv{i}", [128, 512], BF16)) for i in range(2)]
    afm = p.enter_context(nc.sbuf_tensor("aa_afm", [128, T], F32))
    kfm = p.enter_context(nc.sbuf_tensor("aa_kfm", [128, T], F32))
    vfm = p.enter_context(nc.sbuf_tensor("aa_vfm", [128, T], F32))
    rbf = p.enter_context(nc.sbuf_tensor("aa_rbf", [128, T], F32))

    ps1 = ExitStack()
    psSx = ps1.enter_context(nc.psum_tensor("aa_psSx", [1, 1024], F32))
    psSq = ps1.enter_context(nc.psum_tensor("aa_psSq", [1, 1024], F32))
    for dc in range(NCH):
        bb, sq = xb[dc % 2], sqb[dc % 2]
        if dc >= 2:
            P.wait("scalar", "PE")
        P.do("scalar", lambda e, dc=dc, bb=bb: e.activation(bb[:], xv(dc), AF.Identity))
        P.chain("scalar")
        P.do("scalar", lambda e, bb=bb, sq=sq: e.activation(sq[:], bb[:], AF.Square))
        P.wait("tensor", "AC")
        for th in range(2):
            P.do("tensor", lambda e, dc=dc, th=th, bb=bb: e.matmul(
                psSx[0:1, th * 512:(th + 1) * 512], ones_col_b[:],
                bb[:, th * 512:(th + 1) * 512],
                start=(dc == 0), stop=(dc == NCH - 1)), inc="PE")
            P.do("tensor", lambda e, dc=dc, th=th, sq=sq: e.matmul(
                psSq[0:1, th * 512:(th + 1) * 512], ones_col_b[:],
                sq[:, th * 512:(th + 1) * 512],
                start=(dc == 0), stop=(dc == NCH - 1)), inc="PE")
    P.wait("vector", "PE")
    rr = _ln_rows(nc, P, env, T, psSx[0:1, 0:T], psSq[0:1, 0:T], "negm")
    P.chain("vector")
    P.do("vector", lambda e: e.tensor_copy(rows_b[0:1, 0:T], rr["t3"]))
    ps1.close()

    ps2 = ExitStack()
    psR = ps2.enter_context(nc.psum_tensor("aa_psR", [128, 1024], F32))
    P.do("vector", lambda e: e.tensor_copy(rows_b[0:1, T:2 * T], rr["r"]), inc="DV")
    P.wait("tensor", "DV")
    for th in range(2):
        P.do("tensor", lambda e, th=th: e.matmul(
            psR[:, th * 512:(th + 1) * 512], ones_row_b[0:1, 0:128],
            rows_b[0:1, T + th * 512:T + (th + 1) * 512], start=True, stop=True), inc="PE")
    P.wait("vector", "PE")
    P.do("vector", lambda e: e.tensor_copy(rbf[:], psR[:, 0:T]), inc="DV")
    ps2.close()
    P.wait("tensor", "DV")

    ps3 = ExitStack()
    psQ = ps3.enter_context(nc.psum_tensor("aa_psQ", [128, 1024], F32))
    psK = ps3.enter_context(nc.psum_tensor("aa_psK", [128, 1024], F32))
    psV = ps3.enter_context(nc.psum_tensor("aa_psV", [128, 1024], F32))
    psWSq = ps3.enter_context(nc.psum_tensor("aa_psWSq", [1, 512], F32))
    for g4 in range(8):
        wq, wk, wv = wq_t[g4 % 2], wk_t[g4 % 2], wv_t[g4 % 2]
        if g4 >= 2:
            P.wait("sync", "PE")
            P.wait("scalar", "PE")
        P.dma("sync", lambda e, g4=g4, wq=wq: e.dma_start(
            out=wq[:], in_=env["wqe_full"].ap().rearrange("(g p) d -> p g d", p=128)[
                :, g4 * 4:(g4 + 1) * 4, :]), f"aw{g4 % 2}")
        P.dma("sync", lambda e, g4=g4, wk=wk: e.dma_start(
            out=wk[:], in_=env["wk_bf"].ap().rearrange("(g p) d -> p g d", p=128)[
                :, g4 * 4:(g4 + 1) * 4, :]), f"aw{g4 % 2}")
        P.dma("sync", lambda e, g4=g4, wv=wv: e.dma_start(
            out=wv[:], in_=env["wv_bf"].ap().rearrange("(g p) d -> p g d", p=128)[
                :, g4 * 4:(g4 + 1) * 4, :]), f"aw{g4 % 2}")
        for i4 in range(4):
            dc = g4 * 4 + i4
            bb = xb[dc % 2]
            if dc >= 2:
                P.wait("scalar", "PE")
            P.do("scalar", lambda e, dc=dc, bb=bb: e.activation(bb[:], xv(dc), AF.Identity),
                 inc="AC")
            P.wait("tensor", "AC")
            P.wslot("tensor", f"aw{g4 % 2}")
            first, last = dc == 0, False
            for th in range(2):
                sl = slice(th * 512, (th + 1) * 512)
                P.do("tensor", lambda e, dc=dc, th=th, bb=bb, wq=wq, i4=i4: e.matmul(
                    psQ[:, th * 512:(th + 1) * 512], wq[:, i4 * 128:(i4 + 1) * 128],
                    bb[:, th * 512:(th + 1) * 512], start=(dc == 0), stop=False), inc="PE")
                P.do("tensor", lambda e, dc=dc, th=th, bb=bb, wk=wk, i4=i4: e.matmul(
                    psK[:, th * 512:(th + 1) * 512], wk[:, i4 * 128:(i4 + 1) * 128],
                    bb[:, th * 512:(th + 1) * 512], start=(dc == 0), stop=False), inc="PE")
                P.do("tensor", lambda e, dc=dc, th=th, bb=bb, wv=wv, i4=i4: e.matmul(
                    psV[:, th * 512:(th + 1) * 512], wv[:, i4 * 128:(i4 + 1) * 128],
                    bb[:, th * 512:(th + 1) * 512], start=(dc == 0), stop=False), inc="PE")
    for (psX, co) in ((psQ, 0), (psK, 1), (psV, 2)):
        for th in range(2):
            P.do("tensor", lambda e, psX=psX, co=co, th=th: e.matmul(
                psX[:, th * 512:(th + 1) * 512], env["wsums"][0:1, co * 128:(co + 1) * 128],
                rows_b[0:1, th * 512:(th + 1) * 512], start=False, stop=True), inc="PE")
    P.wait("vector", "PE")
    for (psX, dst, bias) in ((psQ, afm, env["bqe"]), (psK, kfm, env["bkc"]), (psV, vfm, env["bvc"])):
        P.do("vector", lambda e, psX=psX, dst=dst: e.tensor_tensor(dst[:], psX[:, 0:T], rbf[:],
                                                                   ALU.mult))
        P.chain("vector")
        P.do("vector", lambda e, dst=dst, bias=bias: e.tensor_scalar(
            dst[:], dst[:], bias[:], None, ALU.add))
    ps3.close()

    aT, kT, vT = afm, kfm, vfm
    ps4 = ExitStack()
    psTr = ps4.enter_context(nc.psum_tensor("aa_psTr", [128, 1024], F32))
    P.wait("tensor", "DV")
    first_tp = True
    for (src, dst) in ((afm, aT), (kfm, kT), (vfm, vT)):
        for tc in range(NTT):
            if not first_tp:
                P.wait("tensor", "DV", P.cnt["DV"] - 1)
            first_tp = False
            P.do("tensor", lambda e, src=src, tc=tc: e.transpose(
                psTr[:, (tc % 2) * 512:(tc % 2) * 512 + 128],
                src[:, tc * 128:(tc + 1) * 128], idF[:]), inc="PE")
            P.wait("vector", "PE")
            P.do("vector", lambda e, dst=dst, tc=tc: e.tensor_copy(
                dst[:, tc * 128:(tc + 1) * 128],
                psTr[:, (tc % 2) * 512:(tc % 2) * 512 + 128]), inc="DV")
    ps4.close()

    numT = p.enter_context(nc.sbuf_tensor("aa_numT", [128, T], F32))
    zT = p.enter_context(nc.sbuf_tensor("aa_zT", [128, T], F32))
    sc = [p.enter_context(nc.sbuf_tensor("aa_sc0", [128, 2048], F32))] * 2
    ex = [p.enter_context(nc.sbuf_tensor(f"aa_ex{i}", [128, 2048], BF16)) for i in range(2)]
    ev = [t[:].bitcast(BF16) for t in sc]
    DBLK = 16
    P.wait("vector", "DV")
    for ti in range(NTT):
        for bl in range(128 // DBLK):
            i = (ti * 8 + bl) % 2
            scb, exb = sc[i], ex[i]
            evb = ev[i]
            a_b = aT[:, ti * 128 + bl * DBLK: ti * 128 + (bl + 1) * DBLK].unsqueeze(2).broadcast_to(
                [128, DBLK, 128])
            k_b = kT[:, ti * 128:(ti + 1) * 128].unsqueeze(1).broadcast_to([128, DBLK, 128])
            v_b = vT[:, ti * 128:(ti + 1) * 128].unsqueeze(1).broadcast_to([128, DBLK, 128])
            sc3 = scb[:].rearrange("p (d e) -> p d e", e=128)
            ex3 = exb[:].rearrange("p (d e) -> p d e", e=128)
            ev3 = evb[:, 0:2048].rearrange("p (d e) -> p d e", e=128)
            if ti * 8 + bl >= 2:
                P.wait("vector", "AC")
            P.chain("vector")
            P.do("vector", lambda e, sc3=sc3, a_b=a_b, k_b=k_b: e.tensor_tensor(
                sc3, a_b, k_b, ALU.mult))
            P.wait("scalar", "DV")
            P.do("scalar", lambda e, exb=exb, scb=scb: e.activation(exb[:], scb[:], AF.Exp))
            P.wait("vector", "AC")
            P.do("vector", lambda e, ti=ti, bl=bl, ex3=ex3: e.reduce_sum(
                zT[:, ti * 128 + bl * DBLK: ti * 128 + (bl + 1) * DBLK].unsqueeze(2),
                ex3, axis=AX.X))
            P.do("vector", lambda e, ev3=ev3, ex3=ex3, v_b=v_b: e.tensor_tensor(
                ev3, ex3, v_b, ALU.mult))
            P.chain("vector")
            P.do("vector", lambda e, ti=ti, bl=bl, ev3=ev3: e.reduce_sum(
                numT[:, ti * 128 + bl * DBLK: ti * 128 + (bl + 1) * DBLK].unsqueeze(2),
                ev3, axis=AX.X))
    oT = p.enter_context(nc.sbuf_tensor("aa_oT", [128, T], BF16))
    P.chain("vector")
    P.do("vector", lambda e: e.reciprocal(zT[:], zT[:]))
    P.chain("vector")
    P.do("vector", lambda e: e.tensor_tensor(oT[:], numT[:], zT[:], ALU.mult))

    ofm = p.enter_context(nc.sbuf_tensor("aa_ofm", [128, T], BF16))
    mj = [p.enter_context(nc.sbuf_tensor(f"aa_mj{i}", [128, 128], BF16)) for i in range(2)]
    ps5 = ExitStack()
    psTo = ps5.enter_context(nc.psum_tensor("aa_psTo", [128, 4096], BF16))
    psRep = [ps5.enter_context(nc.psum_tensor(f"aa_psRep{i}", [128, 512], F32)) for i in range(2)]
    P.wait("tensor", "DV")
    for tc in range(NTT):
        if tc >= 4:
            P.wait("tensor", "DV", P.cnt["DV"] - 3)
        P.do("tensor", lambda e, tc=tc: e.transpose(
            psTo[:, (tc % 4) * 1024:(tc % 4) * 1024 + 128], oT[:, tc * 128:(tc + 1) * 128], idB[:]),
            inc="PE")
        P.wait("vector", "PE")
        P.do("vector", lambda e, tc=tc: e.tensor_copy(
            ofm[:, tc * 128:(tc + 1) * 128], psTo[:, (tc % 4) * 1024:(tc % 4) * 1024 + 128]),
            inc="DV")
    for j in range(NCH):
        mjb = mj[j % 2]
        if j >= 2:
            P.wait("vector", "PE")
        P.do("vector", lambda e, mjb=mjb: e.memset(mjb[:], 0.0), inc="DV")
        P.wait("gpsimd", "DV")
        P.dma("gpsimd", lambda e, j=j, mjb=mjb: e.dma_start(
            out=mjb[4 * j:4 * j + 4, :], in_=env["m0rep"][0:4, :]), f"mj{j % 2}")
        P.wslot("tensor", f"mj{j % 2}")
        P.wait("tensor", "DV")
        for th in range(2):
            P.do("tensor", lambda e, j=j, th=th, mjb=mjb: e.matmul(
                psRep[th][:], mjb[:], ofm[:, th * 512:(th + 1) * 512],
                start=True, stop=True), inc="PE")
        P.wait("vector", "PE")
        for th in range(2):
            P.do("vector", lambda e, j=j, th=th: e.tensor_tensor(
                xv(j, th * 512, 512), psRep[th][:], xv(j, th * 512, 512), ALU.add), inc="DV")
    ps5.close()
    p.close()


def emit_ffn(nc, P, env, call):
    xv = env["xv"]
    idB, idF = env["idB"], env["idF"]
    ones_col_b, ones_row_b = env["ones_col_b"], env["ones_row_b"]
    rows, rows_b = env["rows"], env["rows_b"]

    p = ExitStack()
    xln = p.enter_context(nc.sbuf_tensor(f"ff_xln{call}", [128, NCH * FT], BF16))
    rwT = p.enter_context(nc.sbuf_tensor(f"ff_rwT{call}", [128, 15 * NCH], BF16))
    sqb = [p.enter_context(nc.sbuf_tensor(f"ff_sq{call}{i}", [128, FT], BF16)) for i in range(2)]
    woc = [p.enter_context(nc.sbuf_tensor(f"ff_w{call}{i}", [128, 4096], BF16)) for i in range(2)]
    b2c = [p.enter_context(nc.sbuf_tensor(f"ff_b2{call}{i}", [128, 128], BF16)) for i in range(2)]
    atile = [p.enter_context(nc.sbuf_tensor(f"ff_at{call}{i}", [128, 128], BF16)) for i in range(2)]
    a2t = [p.enter_context(nc.sbuf_tensor(f"ff_a2t{call}{i}", [128, 128], BF16)) for i in range(2)]
    rbf_b = p.enter_context(nc.sbuf_tensor(f"ff_rb{call}", [128, FT], BF16))
    mrb_b = p.enter_context(nc.sbuf_tensor(f"ff_mrb{call}", [128, FT], BF16))
    lgT = p.enter_context(nc.sbuf_tensor(f"ff_lgT{call}", [128, 64], F32))
    mx8 = p.enter_context(nc.sbuf_tensor(f"ff_mx8{call}", [128, 8], F32))
    msk = p.enter_context(nc.sbuf_tensor(f"ff_msk{call}", [128, 16], F32))
    pmb = p.enter_context(nc.sbuf_tensor(f"ff_pm{call}", [128, 16], BF16))
    lg = p.enter_context(nc.sbuf_tensor(f"ff_lg{call}", [15, FT], F32))
    pmT = p.enter_context(nc.sbuf_tensor(f"ff_pmT{call}", [15, FT], BF16))
    wfm = p.enter_context(nc.sbuf_tensor(f"ff_wfm{call}", [8, FT], BF16))
    wsb = p.enter_context(nc.sbuf_tensor(f"ff_wsb{call}", [128, FT], BF16))
    lora = p.enter_context(nc.sbuf_tensor(f"ff_lora{call}", [128, FT], BF16))

    psO = [p.enter_context(nc.psum_tensor(f"ff_psO{call}{i}", [128, 512], F32)) for i in range(2)]
    psH = p.enter_context(nc.psum_tensor(f"ff_psH{call}", [128, 512], F32))
    psMisc = p.enter_context(nc.psum_tensor(f"ff_psM{call}", [128, 512], F32))
    psTb = p.enter_context(nc.psum_tensor(f"ff_psTb{call}", [128, 512], BF16))
    psSx = p.enter_context(nc.psum_tensor(f"ff_psSx{call}", [1, 512], F32))
    psSq = p.enter_context(nc.psum_tensor(f"ff_psSq{call}", [1, 512], F32))

    P.dma("sync", lambda e: e.dma_start(out=rwT[:], in_=env["rwt_dram"].ap()), f"frw{call}")

    for tt in range(NFT):
        tok = tt * FT
        if tt >= 1:
            P.wait("scalar", "PE")

        for dc in range(NCH):
            sq = sqb[dc % 2]
            if dc >= 2:
                P.wait("scalar", "PE")
            P.do("scalar", lambda e, dc=dc, tok=tok: e.activation(
                xln[:, dc * FT:(dc + 1) * FT], xv(dc, tok, FT), AF.Identity))
            P.chain("scalar")
            P.do("scalar", lambda e, dc=dc, sq=sq: e.activation(
                sq[:], xln[:, dc * FT:(dc + 1) * FT], AF.Square))
            P.wait("tensor", "AC")
            P.do("tensor", lambda e, dc=dc: e.matmul(
                psSx[0:1, :], ones_col_b[:], xln[:, dc * FT:(dc + 1) * FT],
                start=(dc == 0), stop=(dc == NCH - 1)), inc="PE")
            P.do("tensor", lambda e, dc=dc, sq=sq: e.matmul(
                psSq[0:1, :], ones_col_b[:], sq[:],
                start=(dc == 0), stop=(dc == NCH - 1)), inc="PE")
        P.wait("vector", "PE")
        rr = _ln_rows(nc, P, env, FT, psSx[0:1, :], psSq[0:1, :], "mr")
        P.chain("vector")
        P.do("vector", lambda e: e.tensor_copy(rows_b[0:1, 0:FT], rr["r"]))
        P.do("vector", lambda e: e.tensor_copy(rows_b[0:1, FT:2 * FT], rr["t3"]))
        P.wait("tensor", "DV")
        P.do("tensor", lambda e: e.matmul(psMisc[:], ones_row_b[0:1, 0:128],
                                          rows_b[0:1, 0:FT], start=True, stop=True), inc="PE")
        P.wait("vector", "PE")
        P.do("vector", lambda e: e.tensor_copy(rbf_b[:], psMisc[:]), inc="DV")
        P.wait("tensor", "DV")
        P.do("tensor", lambda e: e.matmul(psMisc[:], ones_row_b[0:1, 0:128],
                                          rows_b[0:1, FT:2 * FT], start=True, stop=True), inc="PE")
        P.wait("vector", "PE")
        P.do("vector", lambda e: e.tensor_copy(mrb_b[:], psMisc[:]), inc="DV")
        for dc in range(NCH):
            P.do("vector", lambda e, dc=dc: e.tensor_tensor(
                xln[:, dc * FT:(dc + 1) * FT], xln[:, dc * FT:(dc + 1) * FT], rbf_b[:], ALU.mult))
            P.chain("vector")
            P.do("vector", lambda e, dc=dc: e.tensor_tensor(
                xln[:, dc * FT:(dc + 1) * FT], xln[:, dc * FT:(dc + 1) * FT], mrb_b[:],
                ALU.subtract))

        P.wait("tensor", "DV")
        P.wslot("tensor", f"frw{call}")
        for dc in range(NCH):
            P.do("tensor", lambda e, dc=dc: e.matmul(
                psMisc[0:15, :], rwT[:, dc * 15:(dc + 1) * 15],
                xln[:, dc * FT:(dc + 1) * FT], start=(dc == 0), stop=(dc == NCH - 1)), inc="PE")
        P.wait("vector", "PE")
        P.do("vector", lambda e: e.tensor_copy(lg[:], psMisc[0:15, :]), inc="DV")
        for tc in range(4):
            P.wait("tensor", "DV")
            P.do("tensor", lambda e, tc=tc: e.transpose(
                psMisc[0:128, 0:15], lg[:, tc * 128:(tc + 1) * 128], idF[0:15, 0:15]), inc="PE")
            P.wait("vector", "PE")
            P.do("vector", lambda e: e.tensor_copy(lgT[:, 0:15], psMisc[0:128, 0:15]))
            P.chain("vector")
            P.do("vector", lambda e: e.tensor_tensor(lgT[:, 0:15], lgT[:, 0:15], env["rb_bc"][:],
                                                     ALU.add))
            P.wait("scalar", "DV")
            P.do("scalar", lambda e: e.activation(lgT[:, 16:31], lgT[:, 0:15], AF.Exp,
                                                  accum_out=lgT[:, 32:33]))
            P.wait("vector", "AC")
            P.do("vector", lambda e: e.reciprocal(lgT[:, 33:34], lgT[:, 32:33]))
            P.chain("vector")
            P.do("vector", lambda e: e.tensor_scalar(lgT[:, 16:31], lgT[:, 16:31],
                                                     lgT[:, 33:34], None, ALU.mult))
            P.chain("vector")
            P.do("vector", lambda e: e.max(out=mx8[:], in_=lgT[:, 16:31]))
            P.chain("vector")
            P.do("vector", lambda e: e.tensor_scalar(msk[:, 0:15], lgT[:, 16:31],
                                                     mx8[:, 7:8], None, ALU.is_ge))
            P.chain("vector")
            P.do("vector", lambda e: e.tensor_tensor(pmb[:, 0:15], lgT[:, 16:31], msk[:, 0:15],
                                                     ALU.mult))
            P.wait("tensor", "DV")
            P.do("tensor", lambda e, tc=tc: e.transpose(psTb[0:15, 0:128], pmb[:, 0:15], idB[:]), inc="PE")
            P.wait("vector", "PE")
            P.do("vector", lambda e, tc=tc: e.tensor_copy(pmT[:, tc * 128:(tc + 1) * 128],
                                                          psTb[0:15, 0:128]), inc="DV")
        P.wait("tensor", "DV")
        P.do("tensor", lambda e: e.matmul(psMisc[0:8, :], env["onehotT"][:], pmT[:],
                                          start=True, stop=True), inc="PE")
        P.wait("vector", "PE")
        P.do("vector", lambda e: e.tensor_copy(wfm[:], psMisc[0:8, :]), inc="DV")
        P.wait("tensor", "DV")
        P.do("tensor", lambda e: e.matmul(psMisc[:], env["eexp"][:], wfm[:],
                                          start=True, stop=True), inc="PE")
        P.wait("vector", "PE")
        P.do("vector", lambda e: e.tensor_copy(wsb[:], psMisc[:]), inc="DV")

        for dc in range(NCH):
            at, a2 = atile[dc % 2], a2t[dc % 2]
            if dc >= 2:
                P.wait("sync", "PE")
            P.dma("sync", lambda e, dc=dc, at=at: e.dma_start(
                out=at[:], in_=env["a_bf"].ap()[:, dc * 128:(dc + 1) * 128]), f"fa{dc % 2}")
            P.wslot("tensor", f"fa{dc % 2}")
            if dc >= 2:
                P.wait("tensor", "DV")
            P.do("tensor", lambda e, at=at: e.transpose(psTb[:, 0:128], at[:], idB[:]), inc="PE")
            P.wait("vector", "PE")
            P.do("vector", lambda e, a2=a2: e.tensor_copy(a2[:], psTb[:, 0:128]), inc="DV")
            P.wait("tensor", "DV")
            P.do("tensor", lambda e, dc=dc, a2=a2: e.matmul(
                psH[:], a2[:], xln[:, dc * FT:(dc + 1) * FT],
                start=(dc == 0), stop=(dc == NCH - 1)), inc="PE")
        P.wait("vector", "PE")
        P.chain("vector")
        P.do("vector", lambda e: e.tensor_tensor(lora[:], psH[:], wsb[:], ALU.mult))
        P.wait("tensor", "DV")
        P.wait("vector", "AC")
        if env.get("dbg") is not None and call == 0 and tt == 0:
            dbgap = env["dbg"]
            P.wait("gpsimd", "DV")
            P.dma("gpsimd", lambda e, dbgap=dbgap: e.dma_start(
                out=dbgap[:, 0:16384], in_=xln[:]), "dbgd")
            P.dma("gpsimd", lambda e, dbgap=dbgap: e.dma_start(
                out=dbgap[:, 16384:16896], in_=wsb[:]), "dbgd")
            P.dma("gpsimd", lambda e, dbgap=dbgap: e.dma_start(
                out=dbgap[:, 16896:17408], in_=lora[:]), "dbgd")
            P.dma("gpsimd", lambda e, dbgap=dbgap: e.dma_start(
                out=dbgap[0:15, 17408:17920], in_=pmT[:]), "dbgd")
            P.dma("gpsimd", lambda e, dbgap=dbgap: e.dma_start(
                out=dbgap[0:8, 17920:18432], in_=wfm[:]), "dbgd")
            P.wslot("scalar", "dbgd")
            P.wslot("vector", "dbgd")
            P.wslot("tensor", "dbgd")

        evac_dv = []
        for oc in range(NCH):
            w = woc[oc % 2]
            rr4 = oc // 4
            cc4 = oc % 4
            if oc >= 2:
                P.wait("sync", "PE")
            P.dma("sync", lambda e, rr4=rr4, cc4=cc4, w=w: e.dma_start(
                out=w[:], in_=bass.AP(
                    tensor=env["wt_full"],
                    offset=(rr4 * D) * OSL + cc4 * 128,
                    ap=[[OSL, 128], [OSL * 128, 32], [1, 128]])), f"fw{oc % 2}")
            bc = b2c[oc % 2]
            P.dma("sync", lambda e, oc=oc, bc=bc: e.dma_start(
                out=bc[:], in_=bass.AP(tensor=env["b2_full"],
                                       offset=(oc // 4) * 128 * 512 + (oc % 4) * 128,
                                       ap=[[512, 128], [1, 128]])), f"fb{oc % 2}")
            P.wslot("tensor", f"fw{oc % 2}")
            P.wslot("tensor", f"fb{oc % 2}")
            if oc >= 2:
                P.wait("tensor", "DV", evac_dv[oc - 2])
            po = psO[oc % 2]
            for dc in range(NCH):
                P.do("tensor", lambda e, dc=dc, w=w, po=po: e.matmul(
                    po[:], w[:, dc * 128:(dc + 1) * 128], xln[:, dc * FT:(dc + 1) * FT],
                    start=(dc == 0), stop=False), inc="PE")
            P.do("tensor", lambda e, oc=oc, po=po, bc=bc: e.matmul(
                po[:], bc[:], lora[:], start=False, stop=True), inc="PE")
            P.wait("vector", "PE")
            P.do("vector", lambda e, oc=oc, po=po, tok=tok: e.scalar_tensor_tensor(
                xv(oc, tok, FT), po[:], env["bb_fm"][:, oc:oc + 1], xv(oc, tok, FT),
                ALU.add, ALU.add))
            evac_dv.append(P.cnt["DV"])
    p.close()


def emit_attn_b(nc, P, env):
    xv = env["xv"]
    ones_col_b, ones_row_b = env["ones_col_b"], env["ones_row_b"]
    rows, rows_b = env["rows"], env["rows_b"]

    p = ExitStack()
    zt = [p.enter_context(nc.sbuf_tensor(f"ab_z{i}", [128, T], BF16)) for i in range(2)]
    sqb = [p.enter_context(nc.sbuf_tensor(f"ab_sq{i}", [128, T], BF16)) for i in range(2)]
    rbf_b = p.enter_context(nc.sbuf_tensor("ab_rb", [128, T], BF16))
    mrb_b = p.enter_context(nc.sbuf_tensor("ab_mrb", [128, T], BF16))
    gbuf = p.enter_context(nc.sbuf_tensor("ab_g", [128, 512], F32))
    junk = p.enter_context(nc.sbuf_tensor("ab_junk", [128, 256], F32))
    psSx = p.enter_context(nc.psum_tensor("ab_psSx", [1, 1024], F32))
    psSq = p.enter_context(nc.psum_tensor("ab_psSq", [1, 1024], F32))
    psMisc = p.enter_context(nc.psum_tensor("ab_psM", [128, 1024], F32))

    for dc in range(NCH):
        z, sq = zt[dc % 2], sqb[dc % 2]
        if dc >= 2:
            P.wait("scalar", "PE")
        P.do("scalar", lambda e, dc=dc, z=z: e.activation(z[:], xv(dc), AF.Identity))
        P.chain("scalar")
        P.do("scalar", lambda e, z=z, sq=sq: e.activation(sq[:], z[:], AF.Square))
        P.wait("tensor", "AC")
        for th in range(2):
            P.do("tensor", lambda e, dc=dc, th=th, z=z: e.matmul(
                psSx[0:1, th * 512:(th + 1) * 512], ones_col_b[:],
                z[:, th * 512:(th + 1) * 512], start=(dc == 0), stop=(dc == NCH - 1)), inc="PE")
            P.do("tensor", lambda e, dc=dc, th=th, sq=sq: e.matmul(
                psSq[0:1, th * 512:(th + 1) * 512], ones_col_b[:],
                sq[:, th * 512:(th + 1) * 512], start=(dc == 0), stop=(dc == NCH - 1)), inc="PE")
    P.wait("vector", "PE")
    rr = _ln_rows(nc, P, env, T, psSx[0:1, 0:T], psSq[0:1, 0:T], "mr")
    P.chain("vector")
    P.do("vector", lambda e: e.tensor_copy(rows_b[0:1, 0:T], rr["r"]))
    P.do("vector", lambda e: e.tensor_copy(rows_b[0:1, T:2 * T], rr["t3"]))
    P.wait("tensor", "DV")
    for th in range(2):
        P.do("tensor", lambda e, th=th: e.matmul(
            psMisc[:, th * 512:(th + 1) * 512], ones_row_b[:],
            rows_b[0:1, th * 512:(th + 1) * 512], start=True, stop=True), inc="PE")
    P.wait("vector", "PE")
    P.do("vector", lambda e: e.tensor_copy(rbf_b[:], psMisc[:, 0:T]), inc="DV")
    P.wait("tensor", "DV")
    for th in range(2):
        P.do("tensor", lambda e, th=th: e.matmul(
            psMisc[:, th * 512:(th + 1) * 512], ones_row_b[:],
            rows_b[0:1, T + th * 512:T + (th + 1) * 512], start=True, stop=True), inc="PE")
    P.wait("vector", "PE")
    P.do("vector", lambda e: e.tensor_copy(mrb_b[:], psMisc[:, 0:T]), inc="DV")

    def z_tile(lt, z):
        P.do("scalar", lambda e, lt=lt, z=z: e.activation(z[:], xv(lt), AF.Identity), inc="AC")
        P.wait("vector", "AC")
        P.do("vector", lambda e, z=z: e.tensor_tensor(z[:], z[:], rbf_b[:], ALU.mult))
        P.chain("vector")
        P.do("vector", lambda e, z=z: e.tensor_tensor(z[:], z[:], mrb_b[:], ALU.subtract))

    pairs_o = [(n, m) for n in range(4) for m in range(n + 1, 4)]
    for lt in range(NCH):
        z = zt[lt % 2]
        if lt >= 2:
            P.wait("scalar", "DV")
            P.wait("scalar", "AC")
        z_tile(lt, z)
        P.chain("vector")
        P.wait("scalar", "DV")
        for n in range(4):
            P.chain("scalar")
            P.do("scalar", lambda e, lt=lt, n=n, z=z: e.activation(
                sqb[lt % 2][:, 0:256], z[:, n * 256:n * 256 + 256], AF.Square,
                accum_out=gbuf[:, lt * 16 + 5 * n:lt * 16 + 5 * n + 1]))
        for (n, m) in pairs_o:
            P.chain("vector")
            P.do("vector", lambda e, n=n, m=m, z=z: e.tensor_tensor(
                junk[:], z[:, n * 256:n * 256 + 256], z[:, m * 256:m * 256 + 256], ALU.mult))
            P.chain("vector")
            P.do("vector", lambda e, lt=lt, n=n, m=m: e.reduce_sum(
                gbuf[:, lt * 16 + n * 4 + m:lt * 16 + n * 4 + m + 1], junk[:], axis=AX.X))
    P.wait("vector", "AC")
    P.chain("vector")
    for lt in range(NCH):
        for (n, m) in pairs_o:
            P.do("vector", lambda e, lt=lt, n=n, m=m: e.tensor_copy(
                gbuf[:, lt * 16 + m * 4 + n:lt * 16 + m * 4 + n + 1],
                gbuf[:, lt * 16 + n * 4 + m:lt * 16 + n * 4 + m + 1]))
    P.chain("vector")
    P.do("vector", lambda e: e.tensor_scalar_mul(gbuf[:], gbuf[:], GSCALE), inc="DV")
    P.wait("sync", "DV")
    P.dma("sync", lambda e: e.dma_start(
        out=bass.AP(tensor=env["g_part"], offset=0, ap=[[16, 128], [16 * 128, 32], [1, 16]]),
        in_=gbuf[:].rearrange("p (a b) -> p a b", b=16)), "gpt")
    P.wslot("gpsimd", "gpt")
    P.do("gpsimd", lambda e: e.collective_compute(
        "AllReduce", ALU.add, replica_groups=[list(range(NCORE))],
        ins=[env["g_part"].ap()], outs=[env["g_full"].ap()]), inc="CC")
    P.wait("sync", "CC")
    P.dma("sync", lambda e: e.dma_start(
        out=gbuf[:].rearrange("p (a b) -> p a b", b=16),
        in_=bass.AP(tensor=env["g_full"], offset=0, ap=[[16, 128], [16 * 128, 32], [1, 16]])),
        "gld")
    g3 = gbuf[:].rearrange("p (a m) -> p a m", m=4)
    P.wslot("vector", "gld")
    P.do("vector", lambda e: e.reduce_max(junk[:, 0:128].unsqueeze(2), g3, axis=AX.X,
                                          negate=True))
    P.chain("vector")
    P.do("vector", lambda e: e.tensor_tensor(
        g3, g3, junk[:, 0:128].unsqueeze(2).broadcast_to([128, 128, 4]), ALU.add))
    P.wait("scalar", "DV")
    P.do("scalar", lambda e: e.activation(gbuf[:], gbuf[:], AF.Exp))
    P.wait("vector", "AC")
    P.do("vector", lambda e: e.reduce_sum(junk[:, 0:128].unsqueeze(2), g3, axis=AX.X))
    P.chain("vector")
    P.do("vector", lambda e: e.reciprocal(junk[:, 0:128], junk[:, 0:128]))
    P.chain("vector")
    P.do("vector", lambda e: e.tensor_tensor(
        g3, g3, junk[:, 0:128].unsqueeze(2).broadcast_to([128, 128, 4]), ALU.mult))
    P.chain("vector")
    P.do("vector", lambda e: e.tensor_scalar_add(
        gbuf[:].rearrange("p (a m) -> p a m", m=4)[:, :, 0:1].rearrange("p a b -> p (a b)") if False
        else bass.AP(tensor=gbuf[:].tensor, offset=gbuf[:].offset, ap=[[512, 128], [16, 32], [5, 4]]),
        bass.AP(tensor=gbuf[:].tensor, offset=gbuf[:].offset, ap=[[512, 128], [16, 32], [5, 4]]),
        1.0))

    for lt in range(NCH):
        z = zt[lt % 2]
        if lt >= 2:
            P.wait("scalar", "DV")
        z_tile(lt, z)
        P.chain("vector")
        for m in range(4):
            for n in range(4):
                if m >= 1:
                    P.chain("vector", back=3)
                P.do("vector", lambda e, lt=lt, n=n, m=m, z=z: e.scalar_tensor_tensor(
                    xv(lt, n * 256, 256), z[:, m * 256:m * 256 + 256],
                    gbuf[:, lt * 16 + n * 4 + m:lt * 16 + n * 4 + m + 1],
                    xv(lt, n * 256, 256), ALU.mult, ALU.add))
    p.close()


def emit_phase5(nc, P, env, y_tok):
    xv = env["xv"]
    idF = env["idF"]
    p = ExitStack()
    ostage = [p.enter_context(nc.sbuf_tensor(f"p5_os{i}", [128, D], F32)) for i in range(2)]
    psO = p.enter_context(nc.psum_tensor("p5_psO", [128, 2048], F32))
    for tc in range(NTT):
        st = ostage[tc % 2]
        if tc >= 2:
            P.wslot("vector", f"ys{tc % 2}")
        for dc in range(NCH):
            if tc * NCH + dc >= 4:
                P.wait("tensor", "DV", P.cnt["DV"] - 3)
            P.do("tensor", lambda e, tc=tc, dc=dc: e.transpose(
                psO[:, (dc % 4) * 512:(dc % 4) * 512 + 128], xv(dc, tc * 128, 128), idF[:]),
                inc="PE")
            P.wait("vector", "PE")
            P.do("vector", lambda e, dc=dc, st=st: e.tensor_copy(
                st[:, dc * 128:(dc + 1) * 128], psO[:, (dc % 4) * 512:(dc % 4) * 512 + 128]),
                inc="DV")
        P.wait("sync", "DV")
        P.dma("sync", lambda e, tc=tc, st=st: e.dma_start(
            out=y_tok[tc * 128:(tc + 1) * 128, :], in_=st[:]), f"ys{tc % 2}")
    P.wslot("sync", "ys0")
    P.wslot("sync", "ys1")
    p.close()


_CACHE = {}


def kernel(**inputs):
    f32 = np.float32
    x = np.ascontiguousarray(inputs["L0_attn_a_x"], dtype=f32)
    Wq = np.ascontiguousarray(inputs["L0_attn_a_Wq"], dtype=f32)
    bq = np.ascontiguousarray(inputs["L0_attn_a_bq"], dtype=f32)
    Wk = np.ascontiguousarray(inputs["L0_attn_a_Wk"], dtype=f32).reshape(D, 128)
    bk = np.ascontiguousarray(inputs["L0_attn_a_bk"], dtype=f32).reshape(128, 1)
    Wv = np.ascontiguousarray(inputs["L0_attn_a_Wv"], dtype=f32).reshape(D, 128)
    bv = np.ascontiguousarray(inputs["L0_attn_a_bv"], dtype=f32).reshape(128, 1)
    Wb = np.ascontiguousarray(inputs["L0_ffn_base_W"], dtype=f32)
    bb = np.ascontiguousarray(inputs["L0_ffn_base_b"], dtype=f32).reshape(1, D)
    rW = np.ascontiguousarray(inputs["L0_ffn_router_W"], dtype=f32)
    rb = np.ascontiguousarray(inputs["L0_ffn_router_b"], dtype=f32).reshape(1, 15)
    A = np.ascontiguousarray(inputs["L0_ffn_A"], dtype=f32).reshape(128, D)
    Bm = np.ascontiguousarray(inputs["L0_ffn_B"], dtype=f32)
    emap = np.ascontiguousarray(inputs["L0_ffn_expert_map"], dtype=np.int32).reshape(15, 1)
    Wq2 = Wq.reshape(D, 128 * 32)

    key = "prog"
    if key not in _CACHE:
        _CACHE[key] = build_program()
    nc = _CACHE[key]

    in_maps = []
    for c in range(NCORE):
        xs = np.ascontiguousarray(x[:, c * SLOC:(c + 1) * SLOC, :].reshape(T, D))
        in_maps.append(dict(
            x_tok=xs,
            w_slice=np.ascontiguousarray(Wb[c * OSL:(c + 1) * OSL]),
            wq_slice=np.ascontiguousarray(Wq2[c * OSL:(c + 1) * OSL]),
            wk_in=Wk, wv_in=Wv, bq_in=bq, bk_in=bk, bv_in=bv, bb_in=bb,
            rw_in=rW, rb_in=rb, a_in=A,
            b_in=np.ascontiguousarray(Bm[:, c * 512:(c + 1) * 512, :].reshape(D, 16)),
            emap_in=emap,
        ))
    res = run_bass_kernel_spmd(nc, in_maps, core_ids=list(range(NCORE)))
    out = np.empty((B, S, D), dtype=np.float32)
    for c in range(NCORE):
        out[:, c * SLOC:(c + 1) * SLOC, :] = res.results[c]["y_tok"].reshape(B, SLOC, D)
    return out


if __name__ == "__main__":
    pass


# version 03
# speedup vs baseline: 1.2764x; 1.2764x over previous
import os
from contextlib import ExitStack

import numpy as np
import ml_dtypes
import concourse.bass as bass
import concourse.mybir as mybir
from concourse.bass_utils import run_bass_kernel_spmd

F32 = mybir.dt.float32
BF16 = mybir.dt.bfloat16
I32 = mybir.dt.int32
AF = mybir.ActivationFunctionType
ALU = mybir.AluOpType
AX = mybir.AxisListType

NCORE = 8
B, S, D = 4, 2048, 4096
SLOC = S // NCORE
T = B * SLOC
NCH = D // 128
NTT = T // 128
OSL = D // NCORE
FT = 512
NFT = T // FT
SCALE_A = float(1.0 / np.sqrt(128.0))
GSCALE = float(1.0 / np.sqrt(2048.0))
EPS = 1e-5

ENGINES = ("sync", "scalar", "vector", "tensor", "gpsimd")
SEMS = ["L", "GD", "PE", "DV", "AC", "PO", "CC"]


class Script:
    def __init__(self, nc, stack):
        self.nc = nc
        self.stack = stack
        self.steps = {e: [] for e in ENGINES}
        self.sem = {}
        self.cnt = {}

    def new_sem(self, name, stack=None):
        self.sem[name] = (stack or self.stack).enter_context(self.nc.semaphore(name))
        self.cnt[name] = 0

    def slot(self, name):
        name = "D" + name
        if name not in self.sem:
            self.new_sem(name)
        return name

    AUTO = {"vector": "DV", "scalar": "AC", "gpsimd": "PO"}

    def do(self, eng, fn, inc=None, inc_n=1):
        if inc is None and eng in self.AUTO:
            inc = self.AUTO[eng]
            inc_n = 1
        if inc is not None:
            self.cnt[inc] += inc_n
        sem = self.sem.get(inc)

        def step(e, fn=fn, sem=sem, n=inc_n if inc else 0):
            ins = fn(e)
            if sem is not None:
                ins.then_inc(sem, n)
        self.steps[eng].append(step)

    def chain(self, eng, back=0):
        sem = self.AUTO[eng]
        self.wait(eng, sem, self.cnt[sem] - back)

    def dma(self, eng, fn, slot):
        self.do(eng, fn, self.slot(slot), 16)

    def wslot(self, eng, name):
        self.wait(eng, self.slot(name))

    def wait(self, eng, sem, v=None):
        v = self.cnt[sem] if v is None else v
        if v <= 0:
            return
        s = self.sem[sem]
        self.steps[eng].append(lambda e, s=s, v=v: e.wait_ge(s, v))

    def barrier(self):
        snap = dict(self.cnt)
        for en in ENGINES:
            for s, v in snap.items():
                if s != "CC":
                    self.wait(en, s, v)

    def replay(self):
        with self.nc.Block() as block:
            for en in ENGINES:
                def runner(eng, en=en):
                    for st in self.steps[en]:
                        st(eng)
                getattr(block, en)(runner)


def _ln_rows(nc, P, env, n, sums_ps, sumsq_ps, third):
    rows = env["rows"]
    m = rows[:, 0:n]
    r = rows[:, n:2 * n]
    t3 = rows[:, 2 * n:3 * n]
    P.do("vector", lambda e: e.tensor_scalar_mul(m, sums_ps, 1.0 / D))
    P.do("vector", lambda e: e.tensor_scalar_mul(r, sumsq_ps, 1.0 / D))
    P.chain("vector")
    P.do("vector", lambda e: e.tensor_tensor(t3, m, m, ALU.mult))
    P.chain("vector")
    P.do("vector", lambda e: e.tensor_scalar_add(t3, t3, -EPS))
    P.chain("vector")
    P.do("vector", lambda e: e.tensor_tensor(r, r, t3, ALU.subtract))
    P.wait("scalar", "DV")
    P.do("scalar", lambda e: e.activation(r, r, AF.Sqrt))
    P.wait("vector", "AC")
    P.do("vector", lambda e: e.reciprocal(r, r))
    P.chain("vector")
    if third == "negm":
        P.do("vector", lambda e: e.tensor_scalar_mul(t3, m, -1.0))
    else:
        P.do("vector", lambda e: e.tensor_tensor(t3, m, r, ALU.mult))
    return dict(m=m, r=r, t3=t3)


def build_program():
    nc = bass.Bass("TRN2", target_bir_lowering=False)
    stop_phase = int(os.environ.get("KMK_STOP_PHASE", "99"))

    x_tok = nc.dram_tensor("x_tok", [T, D], F32, kind="ExternalInput").ap()
    w_slice = nc.dram_tensor("w_slice", [OSL, D], F32, kind="ExternalInput").ap()
    wq_slice = nc.dram_tensor("wq_slice", [OSL, D], F32, kind="ExternalInput").ap()
    wk_in = nc.dram_tensor("wk_in", [D, 128], F32, kind="ExternalInput").ap()
    wv_in = nc.dram_tensor("wv_in", [D, 128], F32, kind="ExternalInput").ap()
    bq_in = nc.dram_tensor("bq_in", [128, 32], F32, kind="ExternalInput").ap()
    bk_in = nc.dram_tensor("bk_in", [128, 1], F32, kind="ExternalInput").ap()
    bv_in = nc.dram_tensor("bv_in", [128, 1], F32, kind="ExternalInput").ap()
    bb_in = nc.dram_tensor("bb_in", [1, D], F32, kind="ExternalInput").ap()
    rw_in = nc.dram_tensor("rw_in", [15, D], F32, kind="ExternalInput").ap()
    rb_in = nc.dram_tensor("rb_in", [1, 15], F32, kind="ExternalInput").ap()
    a_in = nc.dram_tensor("a_in", [128, D], F32, kind="ExternalInput").ap()
    b_in = nc.dram_tensor("b_in", [D, 16], F32, kind="ExternalInput").ap()
    emap_in = nc.dram_tensor("emap_in", [15, 1], I32, kind="ExternalInput").ap()
    y_tok = nc.dram_tensor("y_tok", [T, D], F32, kind="ExternalOutput").ap()
    dbg = nc.dram_tensor("dbg", [128, 18432], F32, kind="ExternalOutput").ap() \
        if os.environ.get("KMK_DBG") else None

    wt_part = nc.dram_tensor("wt_part", [D, OSL], BF16)
    wqe_part = nc.dram_tensor("wqe_part", [OSL, 128], BF16)
    wsum_part = nc.dram_tensor("wsum_part", [OSL, 1], F32)
    wqs_part = nc.dram_tensor("wqs_part", [1, 128], F32)
    wt_full = nc.dram_tensor("wt_full", [NCORE * D, OSL], BF16)
    wqe_full = nc.dram_tensor("wqe_full", [D, 128], BF16)
    wsum_full = nc.dram_tensor("wsum_full", [D, 1], F32)
    wqs_full = nc.dram_tensor("wqs_full", [1, 128], F32)
    wk_bf = nc.dram_tensor("wk_bf", [D, 128], BF16)
    wv_bf = nc.dram_tensor("wv_bf", [D, 128], BF16)
    a_bf = nc.dram_tensor("a_bf", [128, D], BF16)
    b2_part = nc.dram_tensor("b2_part", [128, 512], BF16)
    b2_full = nc.dram_tensor("b2_full", [8 * 128, 512], BF16)
    g_part = nc.dram_tensor("g_part", [D, 16], F32)
    g_full = nc.dram_tensor("g_full", [D, 16], F32)
    rwt_dram = nc.dram_tensor("rwt_dram", [128, 15 * NCH], BF16)
    a2t_dram = nc.dram_tensor("a2t_dram", [D, 128], BF16)

    root = ExitStack()
    with root:
        P = Script(nc, root)
        for s in SEMS:
            P.new_sem(s, root)

        xfm = root.enter_context(nc.sbuf_tensor("xfm", [128, NCH * T], F32))
        idF = root.enter_context(nc.sbuf_tensor("idF", [128, 128], F32))
        idB = root.enter_context(nc.sbuf_tensor("idB", [128, 128], BF16))
        ones_col_b = root.enter_context(nc.sbuf_tensor("ones_col_b", [128, 1], BF16))
        ones_row_b = root.enter_context(nc.sbuf_tensor("ones_row_b", [1, 128], BF16))

        rb_bc = root.enter_context(nc.sbuf_tensor("rb_bc", [128, 15], F32))
        onehotT = root.enter_context(nc.sbuf_tensor("onehotT", [15, 8], BF16))
        eexp = root.enter_context(nc.sbuf_tensor("eexp", [8, 128], BF16))
        bb_fm = root.enter_context(nc.sbuf_tensor("bb_fm", [128, 32], F32))
        ones_f1 = root.enter_context(nc.sbuf_tensor("ones_f1", [1, 1], F32))
        bqe = root.enter_context(nc.sbuf_tensor("bqe", [128, 1], F32))
        bkc = root.enter_context(nc.sbuf_tensor("bkc", [128, 1], F32))
        bvc = root.enter_context(nc.sbuf_tensor("bvc", [128, 1], F32))
        wsums = root.enter_context(nc.sbuf_tensor("wsums", [1, 3 * 128], BF16))
        m0rep = root.enter_context(nc.sbuf_tensor("m0rep", [4, 128], BF16))
        rows = root.enter_context(nc.sbuf_tensor("rows", [1, 3072], F32))
        rows_b = root.enter_context(nc.sbuf_tensor("rows_b", [1, 2112], BF16))

        def xv(dc, lo=0, n=T):
            return xfm[:, dc * T + lo: dc * T + lo + n]

        env = dict(rows=rows, rows_b=rows_b, xv=xv, xfm=xfm, idF=idF, idB=idB,
                   ones_col_b=ones_col_b, ones_row_b=ones_row_b,
                   rb_bc=rb_bc, onehotT=onehotT, eexp=eexp, bb_fm=bb_fm, ones_f1=ones_f1,
                   rwt_dram=rwt_dram, a2t_dram=a2t_dram, b2_part=b2_part,
                   bqe=bqe, bkc=bkc, bvc=bvc, wsums=wsums, m0rep=m0rep,
                   wt_full=wt_full, wqe_full=wqe_full, wsum_full=wsum_full,
                   wqs_full=wqs_full, wk_bf=wk_bf, wv_bf=wv_bf, a_bf=a_bf,
                   b2_full=b2_full, g_part=g_part, g_full=g_full)

        env["dbg"] = dbg
        emit_phase0(nc, P, env, x_tok, w_slice, wq_slice, wk_in, wv_in, bq_in,
                    bk_in, bv_in, bb_in, rw_in, rb_in, a_in, b_in, emap_in,
                    wt_part, wqe_part, wsum_part, wqs_part)
        P.barrier()
        if stop_phase >= 1:
            emit_attn_a(nc, P, env)
            P.barrier()
        if stop_phase >= 2:
            emit_ffn(nc, P, env, 0)
            P.barrier()
        if stop_phase >= 3:
            emit_attn_b(nc, P, env)
            P.barrier()
        if stop_phase >= 4:
            emit_ffn(nc, P, env, 1)
            P.barrier()
        emit_phase5(nc, P, env, y_tok)
        P.replay()
    return nc


def emit_phase0(nc, P, env, x_tok, w_slice, wq_slice, wk_in, wv_in, bq_in,
                bk_in, bv_in, bb_in, rw_in, rb_in, a_in, b_in, emap_in,
                wt_part, wqe_part, wsum_part, wqs_part):
    xv = env["xv"]
    idF, idB = env["idF"], env["idB"]
    ones_col_b, ones_row_b = env["ones_col_b"], env["ones_row_b"]
    rows_b = env["rows_b"]

    p = ExitStack()
    pa = ExitStack()
    xstage = [pa.enter_context(nc.sbuf_tensor(f"p0_xs{i}", [128, D], F32)) for i in range(2)]
    bqt = pa.enter_context(nc.sbuf_tensor("p0_bqt", [128, 32], F32))
    bb_rowt = pa.enter_context(nc.sbuf_tensor("p0_bbrow", [1, D], F32))
    emap_f = pa.enter_context(nc.sbuf_tensor("p0_em", [15, 1], I32))
    emap_g = pa.enter_context(nc.sbuf_tensor("p0_emg", [15, 1], F32))
    psT = p.enter_context(nc.psum_tensor("p0_psT", [128, 2048], F32))
    psTb = p.enter_context(nc.psum_tensor("p0_psTb", [128, 2048], BF16))
    psK1 = p.enter_context(nc.psum_tensor("p0_psK1", [128, 512], F32))
    psWS = p.enter_context(nc.psum_tensor("p0_psWS", [1, 512], F32))

    P.do("vector", lambda e: e.memset(idF[:], 1.0), inc="DV")
    P.do("vector", lambda e: e.memset(idB[:], 1.0), inc="DV")
    P.wait("gpsimd", "DV")
    P.do("gpsimd", lambda e: e.affine_select(out=idF[:], in_=idF[:], pattern=[[1, 128]],
                                             compare_op=ALU.is_equal, fill=0.0, base=0,
                                             channel_multiplier=-1), inc="PO")
    P.do("gpsimd", lambda e: e.affine_select(out=idB[:], in_=idB[:], pattern=[[1, 128]],
                                             compare_op=ALU.is_equal, fill=0.0, base=0,
                                             channel_multiplier=-1), inc="PO")
    P.wait("tensor", "PO")
    P.do("vector", lambda e: e.memset(ones_col_b[:], 1.0))
    P.do("vector", lambda e: e.memset(ones_row_b[:], 1.0))
    P.do("vector", lambda e: e.memset(env["eexp"][:], 0.0))
    P.do("vector", lambda e: e.memset(env["m0rep"][:], 0.0))
    P.do("vector", lambda e: e.memset(rows_b[:, 2048:2064], 2.0))
    P.do("vector", lambda e: e.memset(rows_b[:, 2064:2096], 1.0), inc="DV")
    P.wait("gpsimd", "DV")
    for ee in range(8):
        P.dma("gpsimd", lambda e, ee=ee: e.dma_start(
            out=env["eexp"][ee:ee + 1, 16 * ee:16 * ee + 16], in_=rows_b[0:1, 2048:2064]), "cst")
    for dd in range(4):
        P.dma("gpsimd", lambda e, dd=dd: e.dma_start(
            out=env["m0rep"][dd:dd + 1, 32 * dd:32 * dd + 32], in_=rows_b[0:1, 2064:2096]), "cst")

    P.dma("sync", lambda e: e.dma_start(out=bqt[:], in_=bq_in), "sml")
    P.dma("sync", lambda e: e.dma_start(out=env["bkc"][:], in_=bk_in), "sml")
    P.dma("sync", lambda e: e.dma_start(out=env["bvc"][:], in_=bv_in), "sml")
    P.dma("sync", lambda e: e.dma_start(out=bb_rowt[:], in_=bb_in), "sml")
    P.dma("sync", lambda e: e.dma_start(out=env["rb_bc"][0:1, :], in_=rb_in), "sml")
    P.dma("sync", lambda e: e.dma_start(out=emap_f[:], in_=emap_in), "sml")
    P.wslot("vector", "sml")
    P.do("vector", lambda e: e.memset(env["ones_f1"][:], 1.0))
    P.do("vector", lambda e: e.reduce_sum(env["bqe"][:], bqt[:], axis=AX.X))
    P.chain("vector")
    P.do("vector", lambda e: e.tensor_scalar_mul(env["bqe"][:], env["bqe"][:], SCALE_A))
    P.do("vector", lambda e: e.tensor_copy(emap_g[:], emap_f[:]))
    P.chain("vector")
    for ee in range(8):
        P.do("vector", lambda e, ee=ee: e.tensor_scalar(env["onehotT"][:, ee:ee + 1], emap_g[:],
                                                        float(ee), None, ALU.is_equal))
    P.do("vector", lambda e: e.tensor_copy(rows_b[0:1, 2096:2111], env["rb_bc"][0:1, 0:15]), inc="DV")
    P.wait("tensor", "DV")
    P.do("tensor", lambda e: e.matmul(psK1[0:128, 0:15], ones_row_b[0:1, 0:128],
                                      rows_b[0:1, 2096:2111], start=True, stop=True), inc="PE")
    P.wait("vector", "PE")
    P.do("vector", lambda e: e.tensor_copy(env["rb_bc"][:], psK1[:, 0:15]), inc="DV")
    P.wait("tensor", "DV")
    for oc in range(32):
        P.do("tensor", lambda e, oc=oc: e.matmul(
            psK1[:, 128 + oc:128 + oc + 1], bb_rowt[0:1, oc * 128:(oc + 1) * 128],
            env["ones_f1"][:], start=True, stop=True), inc="PE")
    P.wait("vector", "PE")
    P.do("vector", lambda e: e.tensor_copy(env["bb_fm"][:], psK1[:, 128:160]), inc="DV")

    dv_tile_done = []
    for tc in range(NTT):
        st = xstage[tc % 2]
        if tc >= 2:
            P.wait("sync", "DV", dv_tile_done[tc - 2])
        P.dma("sync", lambda e, tc=tc, st=st: e.dma_start(
            out=st[:], in_=x_tok[tc * 128:(tc + 1) * 128, :]), f"xs{tc % 2}")
        P.wslot("tensor", f"xs{tc % 2}")
        for g4 in range(NCH // 4):
            if tc * NCH // 4 + g4 >= 1:
                P.wait("tensor", "DV")
            for q in range(4):
                dc = g4 * 4 + q
                P.do("tensor", lambda e, dc=dc, q=q, st=st: e.transpose(
                    psT[:, q * 512:q * 512 + 128],
                    st[:, dc * 128:(dc + 1) * 128], idF[:]), inc="PE")
            P.wait("vector", "PE")
            P.do("vector", lambda e, tc=tc, g4=g4: e.tensor_copy(
                bass.AP(tensor=env["xfm"][:].tensor, offset=(g4 * 4) * T + tc * 128,
                        ap=[[NCH * T, 128], [T, 4], [1, 128]]),
                psT[:].rearrange("p (s c) -> p s c", c=512)[:, :, 0:128]), inc="DV")
        dv_tile_done.append(P.cnt["DV"])
        P.wait("tensor", "DV")

    pa.close()
    wqout = p.enter_context(nc.sbuf_tensor("p0_wqo", [128, 128], F32))
    wqout_b = p.enter_context(nc.sbuf_tensor("p0_wqob", [128, 128], BF16))
    smt = p.enter_context(nc.sbuf_tensor("p0_smt", [128, 16], F32))
    castA = p.enter_context(nc.sbuf_tensor("p0_ca", [128, D], BF16))
    rw_sb = p.enter_context(nc.sbuf_tensor("p0_rw", [15, D], BF16))
    pb = ExitStack()
    wtileW = [pb.enter_context(nc.sbuf_tensor(f"p0_w{i}", [128, D], BF16)) for i in range(2)]
    wstage = pb.enter_context(nc.sbuf_tensor("p0_wst", [128, 256], BF16))
    wqtile = pb.enter_context(nc.sbuf_tensor("p0_wq", [128, 2048], F32))

    for i in range(4):
        for jh in range(2):
            P.wait("sync", "DV")
            P.dma("sync", lambda e, i=i, jh=jh: e.dma_start(
                out=wqtile[:], in_=wq_slice[i * 128:(i + 1) * 128,
                                            jh * 2048:(jh + 1) * 2048]), "wqt")
            P.wslot("vector", "wqt")
            P.do("vector", lambda e, jh=jh: e.reduce_sum(
                wqout[:, jh * 64:(jh + 1) * 64].unsqueeze(2),
                wqtile[:].rearrange("p (d h) -> p d h", h=32), axis=AX.X), inc="DV")
        P.chain("vector")
        P.do("vector", lambda e: e.tensor_scalar_mul(wqout[:], wqout[:], SCALE_A))
        P.chain("vector")
        P.wslot("vector", "gst")
        P.do("vector", lambda e: e.tensor_copy(wqout_b[:], wqout[:]))
        P.wait("gpsimd", "DV")
        P.dma("gpsimd", lambda e, i=i: e.dma_start(out=wqe_part.ap()[i * 128:(i + 1) * 128, :],
                                                   in_=wqout_b[:]), "gst")
    P.wslot("gpsimd", "gst")
    P.dma("gpsimd", lambda e: e.dma_start(out=castA[:, 0:512],
                                          in_=wqe_part.ap().rearrange("(a p) d -> p a d", p=128)),
          "gc")
    P.wslot("tensor", "gc")
    for g4 in range(4):
        P.do("tensor", lambda e, g4=g4: e.matmul(
            psWS[0:1, 0:128], ones_col_b[:], castA[:, g4 * 128:(g4 + 1) * 128],
            start=(g4 == 0), stop=(g4 == 3)), inc="PE")
    P.wait("vector", "PE")
    P.do("vector", lambda e: e.tensor_copy(wqout[0:1, 0:128], psWS[0:1, 0:128]), inc="DV")
    P.wait("gpsimd", "DV")
    P.dma("gpsimd", lambda e: e.dma_start(out=wqs_part.ap(), in_=wqout[0:1, 0:128]), "gc")
    P.wslot("gpsimd", "gst")
    P.wslot("gpsimd", "gc")
    P.do("gpsimd", lambda e: e.collective_compute(
        "AllGather", ALU.bypass, replica_groups=[list(range(NCORE))],
        ins=[wqe_part.ap()], outs=[env["wqe_full"].ap()]), inc="CC")
    P.do("gpsimd", lambda e: e.collective_compute(
        "AllReduce", ALU.add, replica_groups=[list(range(NCORE))],
        ins=[wqs_part.ap()], outs=[env["wqs_full"].ap()]), inc="CC")

    for (src, dst, coloff) in ((wk_in, env["wk_bf"], 1), (wv_in, env["wv_bf"], 2)):
        P.wslot("gpsimd", "gc")
        P.wait("gpsimd", "DV")
        P.dma("gpsimd", lambda e, src=src: e.dma_start(
            out=castA[:], in_=src.rearrange("(a p) d -> p a d", p=128)), "gc")
        P.wslot("gpsimd", "gc")
        P.dma("gpsimd", lambda e, dst=dst: e.dma_start(
            out=dst.ap().rearrange("(a p) d -> p a d", p=128), in_=castA[:]), "gc")
        P.wslot("vector", "gc")
        P.do("vector", lambda e: e.reduce_sum(
            wqout[:, 0:128].unsqueeze(2),
            castA[:].rearrange("p (a d) -> p d a", a=32), axis=AX.X))
        P.chain("vector")
        P.do("vector", lambda e: e.tensor_copy(wqout_b[:], wqout[:, 0:128]))
        P.wait("tensor", "DV")
        P.do("tensor", lambda e: e.matmul(psWS[0:1, 128:256], ones_col_b[:], wqout_b[:],
                                          start=True, stop=True), inc="PE")
        P.wait("vector", "PE")
        P.do("vector", lambda e, coloff=coloff: e.tensor_copy(
            env["wsums"][0:1, coloff * 128:(coloff + 1) * 128], psWS[0:1, 128:256]), inc="DV")

    for oh in range(2):
        P.wait("gpsimd", "PE")
        for ot in range(2):
            P.dma("gpsimd", lambda e, oh=oh, ot=ot: e.dma_start(
                out=wtileW[ot][:], in_=w_slice[(oh * 2 + ot) * 128:(oh * 2 + ot + 1) * 128, :]),
                "wtl")
        P.wslot("vector", "wtl")
        for ot in range(2):
            P.do("vector", lambda e, oh=oh, ot=ot: e.reduce_sum(
                smt[:, oh * 2 + ot:oh * 2 + ot + 1], wtileW[ot][:], axis=AX.X), inc="DV")
        P.wslot("tensor", "wtl")
        P.wait("tensor", "DV")
        for dc in range(NCH):
            if dc >= 1:
                P.wslot("tensor", "wst")
            for ot in range(2):
                P.do("tensor", lambda e, dc=dc, ot=ot: e.transpose(
                    psTb[:, ot * 128:(ot + 1) * 128],
                    wtileW[ot][:, dc * 128:(dc + 1) * 128], idB[:]), inc="PE")
            P.wait("vector", "PE")
            P.wslot("vector", "wst")
            P.do("vector", lambda e: e.tensor_copy(wstage[:], psTb[:, 0:256]))
            P.wait("sync", "DV")
            P.dma("sync", lambda e, dc=dc, oh=oh: e.dma_start(
                out=wt_part.ap()[dc * 128:(dc + 1) * 128, oh * 256:(oh + 1) * 256],
                in_=wstage[:]), "wst")
    P.wait("gpsimd", "DV")
    P.dma("gpsimd", lambda e: e.dma_start(
        out=wsum_part.ap().rearrange("(a p) b -> p a b", p=128), in_=smt[:, 0:4],
        allow_slow_non_contiguous=True), "gc")

    pb.close()
    P.wslot("gpsimd", "gc")
    P.wait("gpsimd", "DV")
    P.dma("gpsimd", lambda e: e.dma_start(out=castA[:], in_=a_in), "gc")
    P.wslot("gpsimd", "gc")
    P.dma("gpsimd", lambda e: e.dma_start(out=env["a_bf"].ap(), in_=castA[:]), "gc")
    P.wslot("gpsimd", "gc")
    P.dma("gpsimd", lambda e: e.dma_start(out=castA[:, 0:512],
                                          in_=b_in.rearrange("(a p) r -> p a r", p=128)), "gc")
    P.dma("gpsimd", lambda e: e.dma_start(out=rw_sb[:], in_=rw_in), "gc")
    P.wslot("tensor", "gc")
    rwt_loc = p.enter_context(nc.sbuf_tensor("p0_rwt", [128, 15 * NCH], BF16))
    for dc in range(NCH):
        P.do("tensor", lambda e, dc=dc: e.transpose(
            psTb[:, (dc % 2) * 1024:(dc % 2) * 1024 + 15],
            rw_sb[:, dc * 128:(dc + 1) * 128], idB[0:15, 0:15]), inc="PE")
        P.wait("vector", "PE")
        P.do("vector", lambda e, dc=dc: e.tensor_copy(
            rwt_loc[:, dc * 15:(dc + 1) * 15],
            psTb[:, (dc % 2) * 1024:(dc % 2) * 1024 + 15]), inc="DV")
        if dc >= 1:
            P.wait("tensor", "DV", P.cnt["DV"] - 1)
    P.wait("gpsimd", "DV")
    P.dma("gpsimd", lambda e: e.dma_start(out=env["rwt_dram"].ap(), in_=rwt_loc[:]), "gc")
    b2stg = p.enter_context(nc.sbuf_tensor("p0_b2s", [16, 1024], BF16))
    for ocl in range(4):
        if ocl >= 1:
            P.wslot("tensor", "gc")
        for ee in range(8):
            a_idx = ee * 4 + ocl
            P.do("tensor", lambda e, a_idx=a_idx: e.transpose(
                psTb[0:16, 0:128], castA[:, a_idx * 16:(a_idx + 1) * 16], idB[:]), inc="PE")
            P.wait("vector", "PE")
            P.do("vector", lambda e, ee=ee: e.tensor_copy(
                b2stg[:, ee * 128:(ee + 1) * 128], psTb[0:16, 0:128]), inc="DV")
            P.wait("tensor", "DV")
        P.wait("gpsimd", "DV")
        P.dma("gpsimd", lambda e, ocl=ocl: e.dma_start(
            out=bass.AP(tensor=env["b2_part"], offset=ocl * 128,
                        ap=[[512, 16], [16 * 512, 8], [1, 128]]),
            in_=b2stg[:].rearrange("p (a c) -> p a c", a=8)), "gc")

    P.wslot("gpsimd", "gc")
    P.wait("gpsimd", "DV")
    P.dma("gpsimd", lambda e: e.dma_start(out=castA[:], in_=env["a_bf"].ap()), "gc")
    P.wslot("tensor", "gc")
    for dc in range(NCH):
        P.do("tensor", lambda e, dc=dc: e.transpose(
            psTb[:, (dc % 2) * 1024:(dc % 2) * 1024 + 128],
            castA[:, dc * 128:(dc + 1) * 128], idB[:]), inc="PE")
        P.wait("vector", "PE")
        P.wslot("vector", "gc")
        P.do("vector", lambda e, dc=dc: e.tensor_copy(
            wqout_b[:], psTb[:, (dc % 2) * 1024:(dc % 2) * 1024 + 128]))
        P.wait("gpsimd", "DV")
        P.dma("gpsimd", lambda e, dc=dc: e.dma_start(
            out=env["a2t_dram"].ap()[dc * 128:(dc + 1) * 128, :], in_=wqout_b[:]), "gc")
        P.wait("tensor", "DV")
    for sl in ("gc", "wst", "wtl"):
        P.wslot("gpsimd", sl)
    for (pi, po) in ((wt_part, env["wt_full"]), (wsum_part, env["wsum_full"]),
                     (env["b2_part"], env["b2_full"])):
        P.do("gpsimd", lambda e, pi=pi, po=po: e.collective_compute(
            "AllGather", ALU.bypass, replica_groups=[list(range(NCORE))],
            ins=[pi.ap()], outs=[po.ap()]), inc="CC")
    P.wait("sync", "CC", 2)
    P.dma("sync", lambda e: e.dma_start(out=env["rows"][0:1, 0:128],
                                        in_=env["wqs_full"].ap()), "wqs")
    P.wslot("vector", "wqs")
    P.do("vector", lambda e: e.tensor_scalar_mul(env["rows"][0:1, 0:128],
                                                 env["rows"][0:1, 0:128], SCALE_A))
    P.chain("vector")
    P.do("vector", lambda e: e.tensor_copy(env["wsums"][0:1, 0:128],
                                           env["rows"][0:1, 0:128]))
    p.close()


def emit_attn_a(nc, P, env):
    xv = env["xv"]
    idF, idB = env["idF"], env["idB"]
    ones_col_b, ones_row_b = env["ones_col_b"], env["ones_row_b"]
    rows, rows_b = env["rows"], env["rows_b"]

    p = ExitStack()
    xb = [p.enter_context(nc.sbuf_tensor(f"aa_xb{i}", [128, T], BF16)) for i in range(2)]
    sqb = [p.enter_context(nc.sbuf_tensor("aa_sq0", [128, T], BF16))] * 2
    wq_t = [p.enter_context(nc.sbuf_tensor(f"aa_wq{i}", [128, 512], BF16)) for i in range(2)]
    wk_t = [p.enter_context(nc.sbuf_tensor(f"aa_wk{i}", [128, 512], BF16)) for i in range(2)]
    wv_t = [p.enter_context(nc.sbuf_tensor(f"aa_wv{i}", [128, 512], BF16)) for i in range(2)]
    afm = p.enter_context(nc.sbuf_tensor("aa_afm", [128, T], F32))
    kfm = p.enter_context(nc.sbuf_tensor("aa_kfm", [128, T], F32))
    vfm = p.enter_context(nc.sbuf_tensor("aa_vfm", [128, T], F32))
    rbf = p.enter_context(nc.sbuf_tensor("aa_rbf", [128, T], F32))

    ps1 = ExitStack()
    psSx = ps1.enter_context(nc.psum_tensor("aa_psSx", [1, 1024], F32))
    psSq = ps1.enter_context(nc.psum_tensor("aa_psSq", [1, 1024], F32))
    for dc in range(NCH):
        bb, sq = xb[dc % 2], sqb[dc % 2]
        if dc >= 1:
            P.wait("scalar", "PE")
        P.do("scalar", lambda e, dc=dc, bb=bb: e.activation(bb[:], xv(dc), AF.Identity))
        P.chain("scalar")
        P.do("scalar", lambda e, bb=bb, sq=sq: e.activation(sq[:], bb[:], AF.Square))
        P.wait("tensor", "AC")
        for th in range(2):
            P.do("tensor", lambda e, dc=dc, th=th, bb=bb: e.matmul(
                psSx[0:1, th * 512:(th + 1) * 512], ones_col_b[:],
                bb[:, th * 512:(th + 1) * 512],
                start=(dc == 0), stop=(dc == NCH - 1)), inc="PE")
            P.do("tensor", lambda e, dc=dc, th=th, sq=sq: e.matmul(
                psSq[0:1, th * 512:(th + 1) * 512], ones_col_b[:],
                sq[:, th * 512:(th + 1) * 512],
                start=(dc == 0), stop=(dc == NCH - 1)), inc="PE")
    P.wait("vector", "PE")
    rr = _ln_rows(nc, P, env, T, psSx[0:1, 0:T], psSq[0:1, 0:T], "negm")
    P.chain("vector")
    P.do("vector", lambda e: e.tensor_copy(rows_b[0:1, 0:T], rr["t3"]))
    ps1.close()

    ps2 = ExitStack()
    psR = ps2.enter_context(nc.psum_tensor("aa_psR", [128, 1024], F32))
    P.do("vector", lambda e: e.tensor_copy(rows_b[0:1, T:2 * T], rr["r"]), inc="DV")
    P.wait("tensor", "DV")
    for th in range(2):
        P.do("tensor", lambda e, th=th: e.matmul(
            psR[:, th * 512:(th + 1) * 512], ones_row_b[0:1, 0:128],
            rows_b[0:1, T + th * 512:T + (th + 1) * 512], start=True, stop=True), inc="PE")
    P.wait("vector", "PE")
    P.do("vector", lambda e: e.tensor_copy(rbf[:], psR[:, 0:T]), inc="DV")
    ps2.close()
    P.wait("tensor", "DV")

    P.wait("sync", "CC", 2)
    ps3 = ExitStack()
    psQ = ps3.enter_context(nc.psum_tensor("aa_psQ", [128, 1024], F32))
    psK = ps3.enter_context(nc.psum_tensor("aa_psK", [128, 1024], F32))
    psV = ps3.enter_context(nc.psum_tensor("aa_psV", [128, 1024], F32))
    psWSq = ps3.enter_context(nc.psum_tensor("aa_psWSq", [1, 512], F32))
    for g4 in range(8):
        wq, wk, wv = wq_t[g4 % 2], wk_t[g4 % 2], wv_t[g4 % 2]
        if g4 >= 2:
            P.wait("sync", "PE")
            P.wait("scalar", "PE")
        P.dma("sync", lambda e, g4=g4, wq=wq: e.dma_start(
            out=wq[:], in_=env["wqe_full"].ap().rearrange("(g p) d -> p g d", p=128)[
                :, g4 * 4:(g4 + 1) * 4, :]), f"aw{g4 % 2}")
        P.dma("sync", lambda e, g4=g4, wk=wk: e.dma_start(
            out=wk[:], in_=env["wk_bf"].ap().rearrange("(g p) d -> p g d", p=128)[
                :, g4 * 4:(g4 + 1) * 4, :]), f"aw{g4 % 2}")
        P.dma("sync", lambda e, g4=g4, wv=wv: e.dma_start(
            out=wv[:], in_=env["wv_bf"].ap().rearrange("(g p) d -> p g d", p=128)[
                :, g4 * 4:(g4 + 1) * 4, :]), f"aw{g4 % 2}")
        for i4 in range(4):
            dc = g4 * 4 + i4
            bb = xb[dc % 2]
            if dc >= 2:
                P.wait("scalar", "PE")
            P.do("scalar", lambda e, dc=dc, bb=bb: e.activation(bb[:], xv(dc), AF.Identity),
                 inc="AC")
            P.wait("tensor", "AC")
            P.wslot("tensor", f"aw{g4 % 2}")
            first, last = dc == 0, False
            for th in range(2):
                sl = slice(th * 512, (th + 1) * 512)
                P.do("tensor", lambda e, dc=dc, th=th, bb=bb, wq=wq, i4=i4: e.matmul(
                    psQ[:, th * 512:(th + 1) * 512], wq[:, i4 * 128:(i4 + 1) * 128],
                    bb[:, th * 512:(th + 1) * 512], start=(dc == 0), stop=False), inc="PE")
                P.do("tensor", lambda e, dc=dc, th=th, bb=bb, wk=wk, i4=i4: e.matmul(
                    psK[:, th * 512:(th + 1) * 512], wk[:, i4 * 128:(i4 + 1) * 128],
                    bb[:, th * 512:(th + 1) * 512], start=(dc == 0), stop=False), inc="PE")
                P.do("tensor", lambda e, dc=dc, th=th, bb=bb, wv=wv, i4=i4: e.matmul(
                    psV[:, th * 512:(th + 1) * 512], wv[:, i4 * 128:(i4 + 1) * 128],
                    bb[:, th * 512:(th + 1) * 512], start=(dc == 0), stop=False), inc="PE")
    for (psX, co) in ((psQ, 0), (psK, 1), (psV, 2)):
        for th in range(2):
            P.do("tensor", lambda e, psX=psX, co=co, th=th: e.matmul(
                psX[:, th * 512:(th + 1) * 512], env["wsums"][0:1, co * 128:(co + 1) * 128],
                rows_b[0:1, th * 512:(th + 1) * 512], start=False, stop=True), inc="PE")
    P.wait("vector", "PE")
    for (psX, dst, bias) in ((psQ, afm, env["bqe"]), (psK, kfm, env["bkc"]), (psV, vfm, env["bvc"])):
        P.do("vector", lambda e, psX=psX, dst=dst: e.tensor_tensor(dst[:], psX[:, 0:T], rbf[:],
                                                                   ALU.mult))
        P.chain("vector")
        P.do("vector", lambda e, dst=dst, bias=bias: e.tensor_scalar(
            dst[:], dst[:], bias[:], None, ALU.add))
    ps3.close()

    aT, kT, vT = afm, kfm, vfm
    ps4 = ExitStack()
    psTr = ps4.enter_context(nc.psum_tensor("aa_psTr", [128, 1024], F32))
    P.wait("tensor", "DV")
    first_tp = True
    for (src, dst) in ((afm, aT), (kfm, kT), (vfm, vT)):
        for tc in range(NTT):
            if not first_tp:
                P.wait("tensor", "DV", P.cnt["DV"] - 1)
            first_tp = False
            P.do("tensor", lambda e, src=src, tc=tc: e.transpose(
                psTr[:, (tc % 2) * 512:(tc % 2) * 512 + 128],
                src[:, tc * 128:(tc + 1) * 128], idF[:]), inc="PE")
            P.wait("vector", "PE")
            P.do("vector", lambda e, dst=dst, tc=tc: e.tensor_copy(
                dst[:, tc * 128:(tc + 1) * 128],
                psTr[:, (tc % 2) * 512:(tc % 2) * 512 + 128]), inc="DV")
    ps4.close()

    numT = p.enter_context(nc.sbuf_tensor("aa_numT", [128, T], F32))
    zT = p.enter_context(nc.sbuf_tensor("aa_zT", [128, T], F32))
    sc = [p.enter_context(nc.sbuf_tensor(f"aa_sc{i}", [128, 2048], F32)) for i in range(2)]
    ex = [p.enter_context(nc.sbuf_tensor(f"aa_ex{i}", [128, 2048], BF16)) for i in range(2)]
    ev = [t[:].bitcast(BF16) for t in sc]
    DBLK = 16
    NBLK = NTT * (128 // DBLK)
    P.wait("vector", "DV")
    P.wait("scalar", "DV")

    def blk(k):
        ti, bl = divmod(k, 128 // DBLK)
        a_b = aT[:, ti * 128 + bl * DBLK: ti * 128 + (bl + 1) * DBLK].unsqueeze(2).broadcast_to(
            [128, DBLK, 128])
        k_b = kT[:, ti * 128:(ti + 1) * 128].unsqueeze(1).broadcast_to([128, DBLK, 128])
        v_b = vT[:, ti * 128:(ti + 1) * 128].unsqueeze(1).broadcast_to([128, DBLK, 128])
        out_sl = slice(ti * 128 + bl * DBLK, ti * 128 + (bl + 1) * DBLK)
        return a_b, k_b, v_b, out_sl

    def emit_tt_exp(k):
        a_b, k_b, _, _ = blk(k)
        scb = sc[k % 2]
        sc3 = scb[:].rearrange("p (d e) -> p d e", e=128)
        if k >= 2:
            P.wait("vector", "AC", P.cnt["AC"] - 1)
            P.chain("vector")
        P.do("vector", lambda e, sc3=sc3, a_b=a_b, k_b=k_b: e.tensor_tensor(
            sc3, a_b, k_b, ALU.mult))
        P.wait("scalar", "DV")
        P.do("scalar", lambda e, scb=scb, k=k: e.activation(ex[k % 2][:], scb[:], AF.Exp))

    emit_tt_exp(0)
    for k in range(NBLK):
        if k + 1 < NBLK:
            emit_tt_exp(k + 1)
        _, _, v_bp, out_sl = blk(k)
        ex3 = ex[k % 2][:].rearrange("p (d e) -> p d e", e=128)
        evb3 = ev[k % 2][:, 0:2048].rearrange("p (d e) -> p d e", e=128)
        P.wait("vector", "AC", P.cnt["AC"] - 1 if k + 1 < NBLK else P.cnt["AC"])
        P.do("vector", lambda e, out_sl=out_sl, ex3=ex3: e.reduce_sum(
            zT[:, out_sl].unsqueeze(2), ex3, axis=AX.X))
        P.do("vector", lambda e, evb3=evb3, ex3=ex3, v_bp=v_bp: e.tensor_tensor(
            evb3, ex3, v_bp, ALU.mult))
        P.chain("vector")
        P.do("vector", lambda e, out_sl=out_sl, evb3=evb3: e.reduce_sum(
            numT[:, out_sl].unsqueeze(2), evb3, axis=AX.X))
    oT = sc[0][:].bitcast(BF16)[:, 0:T]
    P.chain("vector")
    P.do("vector", lambda e: e.reciprocal(zT[:], zT[:]))
    P.chain("vector")
    P.do("vector", lambda e: e.tensor_tensor(oT, numT[:], zT[:], ALU.mult))

    ofm = sc[0][:].bitcast(BF16)[:, T:2 * T]
    mj = [sc[1][:].bitcast(BF16)[:, i * 128:(i + 1) * 128] for i in range(2)]
    ps5 = ExitStack()
    psTo = ps5.enter_context(nc.psum_tensor("aa_psTo", [128, 4096], BF16))
    psRep = [ps5.enter_context(nc.psum_tensor(f"aa_psRep{i}", [128, 512], F32)) for i in range(2)]
    P.wait("tensor", "DV")
    for tc in range(NTT):
        if tc >= 4:
            P.wait("tensor", "DV", P.cnt["DV"] - 3)
        P.do("tensor", lambda e, tc=tc: e.transpose(
            psTo[:, (tc % 4) * 1024:(tc % 4) * 1024 + 128], oT[:, tc * 128:(tc + 1) * 128], idB[:]),
            inc="PE")
        P.wait("vector", "PE")
        P.do("vector", lambda e, tc=tc: e.tensor_copy(
            ofm[:, tc * 128:(tc + 1) * 128], psTo[:, (tc % 4) * 1024:(tc % 4) * 1024 + 128]),
            inc="DV")
    for j in range(NCH):
        mjb = mj[j % 2]
        if j >= 2:
            P.wait("vector", "PE")
        P.do("vector", lambda e, mjb=mjb: e.memset(mjb[:], 0.0), inc="DV")
        P.wait("gpsimd", "DV")
        P.dma("gpsimd", lambda e, j=j, mjb=mjb: e.dma_start(
            out=mjb[4 * j:4 * j + 4, :], in_=env["m0rep"][0:4, :]), f"mj{j % 2}")
        P.wslot("tensor", f"mj{j % 2}")
        P.wait("tensor", "DV")
        for th in range(2):
            P.do("tensor", lambda e, j=j, th=th, mjb=mjb: e.matmul(
                psRep[th][:], mjb, ofm[:, th * 512:(th + 1) * 512],
                start=True, stop=True), inc="PE")
        P.wait("vector", "PE")
        for th in range(2):
            P.do("vector", lambda e, j=j, th=th: e.tensor_tensor(
                xv(j, th * 512, 512), psRep[th][:], xv(j, th * 512, 512), ALU.add), inc="DV")
    ps5.close()
    p.close()


def emit_ffn(nc, P, env, call):
    xv = env["xv"]
    idB, idF = env["idB"], env["idF"]
    ones_col_b, ones_row_b = env["ones_col_b"], env["ones_row_b"]
    rows, rows_b = env["rows"], env["rows_b"]

    p = ExitStack()
    xln = p.enter_context(nc.sbuf_tensor(f"ff_xln{call}", [128, NCH * FT], BF16))
    rwT = p.enter_context(nc.sbuf_tensor(f"ff_rwT{call}", [128, 15 * NCH], BF16))
    sqb = [p.enter_context(nc.sbuf_tensor(f"ff_sq{call}{i}", [128, FT], BF16)) for i in range(2)]
    woc = [p.enter_context(nc.sbuf_tensor(f"ff_w{call}{i}", [128, 4096], BF16)) for i in range(2)]
    b2c = [p.enter_context(nc.sbuf_tensor(f"ff_b2{call}{i}", [128, 128], BF16)) for i in range(2)]
    a2t = [p.enter_context(nc.sbuf_tensor(f"ff_a2t{call}{i}", [128, 128], BF16)) for i in range(2)]
    rbf_b = p.enter_context(nc.sbuf_tensor(f"ff_rb{call}", [128, FT], BF16))
    mrb_b = p.enter_context(nc.sbuf_tensor(f"ff_mrb{call}", [128, FT], BF16))
    lgT = p.enter_context(nc.sbuf_tensor(f"ff_lgT{call}", [128, 64], F32))
    mx8 = p.enter_context(nc.sbuf_tensor(f"ff_mx8{call}", [128, 8], F32))
    msk = p.enter_context(nc.sbuf_tensor(f"ff_msk{call}", [128, 16], F32))
    pmb = p.enter_context(nc.sbuf_tensor(f"ff_pm{call}", [128, 16], BF16))
    lg = p.enter_context(nc.sbuf_tensor(f"ff_lg{call}", [15, FT], F32))
    pmT = p.enter_context(nc.sbuf_tensor(f"ff_pmT{call}", [15, FT], BF16))
    wfm = p.enter_context(nc.sbuf_tensor(f"ff_wfm{call}", [8, FT], BF16))
    wsb = p.enter_context(nc.sbuf_tensor(f"ff_wsb{call}", [128, FT], BF16))
    lora = p.enter_context(nc.sbuf_tensor(f"ff_lora{call}", [128, FT], BF16))

    psO = [p.enter_context(nc.psum_tensor(f"ff_psO{call}{i}", [128, 512], F32)) for i in range(2)]
    psH = p.enter_context(nc.psum_tensor(f"ff_psH{call}", [128, 512], F32))
    psMisc = p.enter_context(nc.psum_tensor(f"ff_psM{call}", [128, 512], F32))
    psTb = p.enter_context(nc.psum_tensor(f"ff_psTb{call}", [128, 512], BF16))
    psSx = p.enter_context(nc.psum_tensor(f"ff_psSx{call}", [1, 512], F32))
    psSq = p.enter_context(nc.psum_tensor(f"ff_psSq{call}", [1, 512], F32))

    P.wait("sync", "CC", 5)
    P.wait("scalar", "CC", 5)
    P.dma("sync", lambda e: e.dma_start(out=rwT[:], in_=env["rwt_dram"].ap()), f"frw{call}")

    evac_dv = []
    mm_pe = []
    for tt in range(NFT):
        tok = tt * FT
        if tt >= 1:
            P.wait("scalar", "PE")

        for dc in range(NCH):
            sq = sqb[dc % 2]
            if dc >= 2:
                P.wait("scalar", "PE")
            P.do("scalar", lambda e, dc=dc, tok=tok: e.activation(
                xln[:, dc * FT:(dc + 1) * FT], xv(dc, tok, FT), AF.Identity))
            P.chain("scalar")
            P.do("scalar", lambda e, dc=dc, sq=sq: e.activation(
                sq[:], xln[:, dc * FT:(dc + 1) * FT], AF.Square))
            P.wait("tensor", "AC")
            P.do("tensor", lambda e, dc=dc: e.matmul(
                psSx[0:1, :], ones_col_b[:], xln[:, dc * FT:(dc + 1) * FT],
                start=(dc == 0), stop=(dc == NCH - 1)), inc="PE")
            P.do("tensor", lambda e, dc=dc, sq=sq: e.matmul(
                psSq[0:1, :], ones_col_b[:], sq[:],
                start=(dc == 0), stop=(dc == NCH - 1)), inc="PE")
        P.wait("vector", "PE")
        rr = _ln_rows(nc, P, env, FT, psSx[0:1, :], psSq[0:1, :], "mr")
        P.chain("vector")
        P.do("vector", lambda e: e.tensor_copy(rows_b[0:1, 0:FT], rr["r"]))
        P.do("vector", lambda e: e.tensor_copy(rows_b[0:1, FT:2 * FT], rr["t3"]))
        P.wait("tensor", "DV")
        P.do("tensor", lambda e: e.matmul(psMisc[:], ones_row_b[0:1, 0:128],
                                          rows_b[0:1, 0:FT], start=True, stop=True), inc="PE")
        P.wait("vector", "PE")
        P.do("vector", lambda e: e.tensor_copy(rbf_b[:], psMisc[:]), inc="DV")
        P.wait("tensor", "DV")
        P.do("tensor", lambda e: e.matmul(psMisc[:], ones_row_b[0:1, 0:128],
                                          rows_b[0:1, FT:2 * FT], start=True, stop=True), inc="PE")
        P.wait("vector", "PE")
        P.do("vector", lambda e: e.tensor_copy(mrb_b[:], psMisc[:]), inc="DV")
        for dc in range(NCH):
            P.do("vector", lambda e, dc=dc: e.tensor_tensor(
                xln[:, dc * FT:(dc + 1) * FT], xln[:, dc * FT:(dc + 1) * FT], rbf_b[:], ALU.mult))
            P.chain("vector")
            P.do("vector", lambda e, dc=dc: e.tensor_tensor(
                xln[:, dc * FT:(dc + 1) * FT], xln[:, dc * FT:(dc + 1) * FT], mrb_b[:],
                ALU.subtract))

        P.wait("tensor", "DV")
        P.wslot("tensor", f"frw{call}")
        for dc in range(NCH):
            P.do("tensor", lambda e, dc=dc: e.matmul(
                psMisc[0:15, :], rwT[:, dc * 15:(dc + 1) * 15],
                xln[:, dc * FT:(dc + 1) * FT], start=(dc == 0), stop=(dc == NCH - 1)), inc="PE")
        P.wait("vector", "PE")
        P.do("vector", lambda e: e.tensor_copy(lg[:], psMisc[0:15, :]), inc="DV")
        for tc in range(4):
            P.wait("tensor", "DV")
            P.do("tensor", lambda e, tc=tc: e.transpose(
                psMisc[0:128, 0:15], lg[:, tc * 128:(tc + 1) * 128], idF[0:15, 0:15]), inc="PE")
            P.wait("vector", "PE")
            P.do("vector", lambda e: e.tensor_copy(lgT[:, 0:15], psMisc[0:128, 0:15]))
            P.chain("vector")
            P.do("vector", lambda e: e.tensor_tensor(lgT[:, 0:15], lgT[:, 0:15], env["rb_bc"][:],
                                                     ALU.add))
            P.wait("scalar", "DV")
            P.do("scalar", lambda e: e.activation(lgT[:, 16:31], lgT[:, 0:15], AF.Exp,
                                                  accum_out=lgT[:, 32:33]))
            P.wait("vector", "AC")
            P.do("vector", lambda e: e.reciprocal(lgT[:, 33:34], lgT[:, 32:33]))
            P.chain("vector")
            P.do("vector", lambda e: e.tensor_scalar(lgT[:, 16:31], lgT[:, 16:31],
                                                     lgT[:, 33:34], None, ALU.mult))
            P.chain("vector")
            P.do("vector", lambda e: e.max(out=mx8[:], in_=lgT[:, 16:31]))
            P.chain("vector")
            P.do("vector", lambda e: e.tensor_scalar(msk[:, 0:15], lgT[:, 16:31],
                                                     mx8[:, 7:8], None, ALU.is_ge))
            P.chain("vector")
            P.do("vector", lambda e: e.tensor_tensor(pmb[:, 0:15], lgT[:, 16:31], msk[:, 0:15],
                                                     ALU.mult))
            P.wait("tensor", "DV")
            P.do("tensor", lambda e, tc=tc: e.transpose(psTb[0:15, 0:128], pmb[:, 0:15], idB[:]), inc="PE")
            P.wait("vector", "PE")
            P.do("vector", lambda e, tc=tc: e.tensor_copy(pmT[:, tc * 128:(tc + 1) * 128],
                                                          psTb[0:15, 0:128]), inc="DV")
        P.wait("tensor", "DV")
        P.do("tensor", lambda e: e.matmul(psMisc[0:8, :], env["onehotT"][:], pmT[:],
                                          start=True, stop=True), inc="PE")
        P.wait("vector", "PE")
        P.do("vector", lambda e: e.tensor_copy(wfm[:], psMisc[0:8, :]), inc="DV")
        P.wait("tensor", "DV")
        P.do("tensor", lambda e: e.matmul(psMisc[:], env["eexp"][:], wfm[:],
                                          start=True, stop=True), inc="PE")
        P.wait("vector", "PE")
        P.do("vector", lambda e: e.tensor_copy(wsb[:], psMisc[:]), inc="DV")

        for dc in range(NCH):
            a2 = a2t[dc % 2]
            if dc >= 2:
                P.wait("scalar", "PE")
            P.dma("scalar", lambda e, dc=dc, a2=a2: e.dma_start(
                out=a2[:], in_=env["a2t_dram"].ap()[dc * 128:(dc + 1) * 128, :]), f"fa{dc % 2}")
            P.wslot("tensor", f"fa{dc % 2}")
            P.do("tensor", lambda e, dc=dc, a2=a2: e.matmul(
                psH[:], a2[:], xln[:, dc * FT:(dc + 1) * FT],
                start=(dc == 0), stop=(dc == NCH - 1)), inc="PE")
        P.wait("vector", "PE")
        P.chain("vector")
        P.do("vector", lambda e: e.tensor_tensor(lora[:], psH[:], wsb[:], ALU.mult))
        P.wait("tensor", "DV")
        P.wait("vector", "AC")
        if env.get("dbg") is not None and call == 0 and tt == 0:
            dbgap = env["dbg"]
            P.wait("gpsimd", "DV")
            P.dma("gpsimd", lambda e, dbgap=dbgap: e.dma_start(
                out=dbgap[:, 0:16384], in_=xln[:]), "dbgd")
            P.dma("gpsimd", lambda e, dbgap=dbgap: e.dma_start(
                out=dbgap[:, 16384:16896], in_=wsb[:]), "dbgd")
            P.dma("gpsimd", lambda e, dbgap=dbgap: e.dma_start(
                out=dbgap[:, 16896:17408], in_=lora[:]), "dbgd")
            P.dma("gpsimd", lambda e, dbgap=dbgap: e.dma_start(
                out=dbgap[0:15, 17408:17920], in_=pmT[:]), "dbgd")
            P.dma("gpsimd", lambda e, dbgap=dbgap: e.dma_start(
                out=dbgap[0:8, 17920:18432], in_=wfm[:]), "dbgd")
            P.wslot("scalar", "dbgd")
            P.wslot("vector", "dbgd")
            P.wslot("tensor", "dbgd")

        for oc in range(NCH):
            w = woc[oc % 2]
            rr4 = oc // 4
            cc4 = oc % 4
            if len(mm_pe) >= 2:
                P.wait("sync", "PE", mm_pe[-2])
                P.wait("scalar", "PE", mm_pe[-2])
            P.dma("sync", lambda e, rr4=rr4, cc4=cc4, w=w: e.dma_start(
                out=w[:], in_=bass.AP(
                    tensor=env["wt_full"],
                    offset=(rr4 * D) * OSL + cc4 * 128,
                    ap=[[OSL, 128], [OSL * 128, 32], [1, 128]])), f"fw{oc % 2}")
            bc = b2c[oc % 2]
            P.dma("scalar", lambda e, oc=oc, bc=bc: e.dma_start(
                out=bc[:], in_=bass.AP(tensor=env["b2_full"],
                                       offset=(oc // 4) * 128 * 512 + (oc % 4) * 128,
                                       ap=[[512, 128], [1, 128]])), f"fb{oc % 2}")
            P.wslot("tensor", f"fw{oc % 2}")
            P.wslot("tensor", f"fb{oc % 2}")
            if len(evac_dv) >= 2:
                P.wait("tensor", "DV", evac_dv[-2])
            po = psO[oc % 2]
            for dc in range(NCH):
                P.do("tensor", lambda e, dc=dc, w=w, po=po: e.matmul(
                    po[:], w[:, dc * 128:(dc + 1) * 128], xln[:, dc * FT:(dc + 1) * FT],
                    start=(dc == 0), stop=False), inc="PE")
            P.do("tensor", lambda e, oc=oc, po=po, bc=bc: e.matmul(
                po[:], bc[:], lora[:], start=False, stop=True), inc="PE")
            P.wait("vector", "PE")
            P.do("vector", lambda e, oc=oc, po=po, tok=tok: e.scalar_tensor_tensor(
                xv(oc, tok, FT), po[:], env["bb_fm"][:, oc:oc + 1], xv(oc, tok, FT),
                ALU.add, ALU.add))
            evac_dv.append(P.cnt["DV"])
            mm_pe.append(P.cnt["PE"])
    p.close()


def emit_attn_b(nc, P, env):
    xv = env["xv"]
    ones_col_b, ones_row_b = env["ones_col_b"], env["ones_row_b"]
    rows, rows_b = env["rows"], env["rows_b"]

    p = ExitStack()
    zt = [p.enter_context(nc.sbuf_tensor(f"ab_z{i}", [128, T], BF16)) for i in range(2)]
    sqb = [p.enter_context(nc.sbuf_tensor(f"ab_sq{i}", [128, T], BF16)) for i in range(2)]
    rbf_b = p.enter_context(nc.sbuf_tensor("ab_rb", [128, T], BF16))
    mrb_b = p.enter_context(nc.sbuf_tensor("ab_mrb", [128, T], BF16))
    gbuf = p.enter_context(nc.sbuf_tensor("ab_g", [128, 512], F32))
    junk = p.enter_context(nc.sbuf_tensor("ab_junk", [128, 256], F32))
    psSx = p.enter_context(nc.psum_tensor("ab_psSx", [1, 1024], F32))
    psSq = p.enter_context(nc.psum_tensor("ab_psSq", [1, 1024], F32))
    psMisc = p.enter_context(nc.psum_tensor("ab_psM", [128, 1024], F32))

    for dc in range(NCH):
        z, sq = zt[dc % 2], sqb[dc % 2]
        if dc >= 2:
            P.wait("scalar", "PE")
        P.do("scalar", lambda e, dc=dc, z=z: e.activation(z[:], xv(dc), AF.Identity))
        P.chain("scalar")
        P.do("scalar", lambda e, z=z, sq=sq: e.activation(sq[:], z[:], AF.Square))
        P.wait("tensor", "AC")
        for th in range(2):
            P.do("tensor", lambda e, dc=dc, th=th, z=z: e.matmul(
                psSx[0:1, th * 512:(th + 1) * 512], ones_col_b[:],
                z[:, th * 512:(th + 1) * 512], start=(dc == 0), stop=(dc == NCH - 1)), inc="PE")
            P.do("tensor", lambda e, dc=dc, th=th, sq=sq: e.matmul(
                psSq[0:1, th * 512:(th + 1) * 512], ones_col_b[:],
                sq[:, th * 512:(th + 1) * 512], start=(dc == 0), stop=(dc == NCH - 1)), inc="PE")
    P.wait("vector", "PE")
    rr = _ln_rows(nc, P, env, T, psSx[0:1, 0:T], psSq[0:1, 0:T], "mr")
    P.chain("vector")
    P.do("vector", lambda e: e.tensor_copy(rows_b[0:1, 0:T], rr["r"]))
    P.do("vector", lambda e: e.tensor_copy(rows_b[0:1, T:2 * T], rr["t3"]))
    P.wait("tensor", "DV")
    for th in range(2):
        P.do("tensor", lambda e, th=th: e.matmul(
            psMisc[:, th * 512:(th + 1) * 512], ones_row_b[:],
            rows_b[0:1, th * 512:(th + 1) * 512], start=True, stop=True), inc="PE")
    P.wait("vector", "PE")
    P.do("vector", lambda e: e.tensor_copy(rbf_b[:], psMisc[:, 0:T]), inc="DV")
    P.wait("tensor", "DV")
    for th in range(2):
        P.do("tensor", lambda e, th=th: e.matmul(
            psMisc[:, th * 512:(th + 1) * 512], ones_row_b[:],
            rows_b[0:1, T + th * 512:T + (th + 1) * 512], start=True, stop=True), inc="PE")
    P.wait("vector", "PE")
    P.do("vector", lambda e: e.tensor_copy(mrb_b[:], psMisc[:, 0:T]), inc="DV")

    def z_tile(lt, z):
        P.do("scalar", lambda e, lt=lt, z=z: e.activation(z[:], xv(lt), AF.Identity), inc="AC")
        P.wait("vector", "AC")
        P.do("vector", lambda e, z=z: e.tensor_tensor(z[:], z[:], rbf_b[:], ALU.mult))
        P.chain("vector")
        P.do("vector", lambda e, z=z: e.tensor_tensor(z[:], z[:], mrb_b[:], ALU.subtract))

    pairs_o = [(n, m) for n in range(4) for m in range(n + 1, 4)]
    for lt in range(NCH):
        z = zt[lt % 2]
        if lt >= 2:
            P.wait("scalar", "DV")
            P.wait("scalar", "AC")
        z_tile(lt, z)
        P.chain("vector")
        P.wait("scalar", "DV")
        for n in range(4):
            P.chain("scalar")
            P.do("scalar", lambda e, lt=lt, n=n, z=z: e.activation(
                sqb[lt % 2][:, 0:256], z[:, n * 256:n * 256 + 256], AF.Square,
                accum_out=gbuf[:, lt * 16 + 5 * n:lt * 16 + 5 * n + 1]))
        for (n, m) in pairs_o:
            P.chain("vector")
            P.do("vector", lambda e, n=n, m=m, z=z: e.tensor_tensor(
                junk[:], z[:, n * 256:n * 256 + 256], z[:, m * 256:m * 256 + 256], ALU.mult))
            P.chain("vector")
            P.do("vector", lambda e, lt=lt, n=n, m=m: e.reduce_sum(
                gbuf[:, lt * 16 + n * 4 + m:lt * 16 + n * 4 + m + 1], junk[:], axis=AX.X))
    P.wait("vector", "AC")
    P.chain("vector")
    for lt in range(NCH):
        for (n, m) in pairs_o:
            P.do("vector", lambda e, lt=lt, n=n, m=m: e.tensor_copy(
                gbuf[:, lt * 16 + m * 4 + n:lt * 16 + m * 4 + n + 1],
                gbuf[:, lt * 16 + n * 4 + m:lt * 16 + n * 4 + m + 1]))
    P.chain("vector")
    P.do("vector", lambda e: e.tensor_scalar_mul(gbuf[:], gbuf[:], GSCALE), inc="DV")
    P.wait("sync", "DV")
    P.dma("sync", lambda e: e.dma_start(
        out=bass.AP(tensor=env["g_part"], offset=0, ap=[[16, 128], [16 * 128, 32], [1, 16]]),
        in_=gbuf[:].rearrange("p (a b) -> p a b", b=16)), "gpt")
    P.wslot("gpsimd", "gpt")
    P.do("gpsimd", lambda e: e.collective_compute(
        "AllReduce", ALU.add, replica_groups=[list(range(NCORE))],
        ins=[env["g_part"].ap()], outs=[env["g_full"].ap()]), inc="CC")
    P.wait("sync", "CC")
    P.dma("sync", lambda e: e.dma_start(
        out=gbuf[:].rearrange("p (a b) -> p a b", b=16),
        in_=bass.AP(tensor=env["g_full"], offset=0, ap=[[16, 128], [16 * 128, 32], [1, 16]])),
        "gld")
    g3 = gbuf[:].rearrange("p (a m) -> p a m", m=4)
    P.wslot("vector", "gld")
    P.do("vector", lambda e: e.reduce_max(junk[:, 0:128].unsqueeze(2), g3, axis=AX.X,
                                          negate=True))
    P.chain("vector")
    P.do("vector", lambda e: e.tensor_tensor(
        g3, g3, junk[:, 0:128].unsqueeze(2).broadcast_to([128, 128, 4]), ALU.add))
    P.wait("scalar", "DV")
    P.do("scalar", lambda e: e.activation(gbuf[:], gbuf[:], AF.Exp))
    P.wait("vector", "AC")
    P.do("vector", lambda e: e.reduce_sum(junk[:, 0:128].unsqueeze(2), g3, axis=AX.X))
    P.chain("vector")
    P.do("vector", lambda e: e.reciprocal(junk[:, 0:128], junk[:, 0:128]))
    P.chain("vector")
    P.do("vector", lambda e: e.tensor_tensor(
        g3, g3, junk[:, 0:128].unsqueeze(2).broadcast_to([128, 128, 4]), ALU.mult))
    P.chain("vector")
    P.do("vector", lambda e: e.tensor_scalar_add(
        gbuf[:].rearrange("p (a m) -> p a m", m=4)[:, :, 0:1].rearrange("p a b -> p (a b)") if False
        else bass.AP(tensor=gbuf[:].tensor, offset=gbuf[:].offset, ap=[[512, 128], [16, 32], [5, 4]]),
        bass.AP(tensor=gbuf[:].tensor, offset=gbuf[:].offset, ap=[[512, 128], [16, 32], [5, 4]]),
        1.0))

    msga = p.enter_context(nc.sbuf_tensor("ab_msg", [128, T], BF16))
    for lt in range(NCH):
        z = zt[lt % 2]
        if lt >= 2:
            P.wait("scalar", "DV")
        z_tile(lt, z)
        P.chain("vector")
        for m in range(4):
            for n in range(4):
                if m == 0:
                    P.do("vector", lambda e, lt=lt, n=n, z=z: e.tensor_scalar(
                        msga[:, n * 256:n * 256 + 256], z[:, 0:256],
                        gbuf[:, lt * 16 + n * 4:lt * 16 + n * 4 + 1], None, ALU.mult))
                else:
                    P.chain("vector", back=3)
                    P.do("vector", lambda e, lt=lt, n=n, m=m, z=z: e.scalar_tensor_tensor(
                        msga[:, n * 256:n * 256 + 256], z[:, m * 256:m * 256 + 256],
                        gbuf[:, lt * 16 + n * 4 + m:lt * 16 + n * 4 + m + 1],
                        msga[:, n * 256:n * 256 + 256], ALU.mult, ALU.add))
        P.chain("vector")
        P.do("vector", lambda e, lt=lt: e.tensor_tensor(
            xv(lt), msga[:], xv(lt), ALU.add))
    p.close()


def emit_phase5(nc, P, env, y_tok):
    xv = env["xv"]
    idF = env["idF"]
    p = ExitStack()
    ostage = [p.enter_context(nc.sbuf_tensor(f"p5_os{i}", [128, D], F32)) for i in range(2)]
    psO = p.enter_context(nc.psum_tensor("p5_psO", [128, 2048], F32))
    for tc in range(NTT):
        st = ostage[tc % 2]
        if tc >= 2:
            P.wslot("vector", f"ys{tc % 2}")
        for g4 in range(NCH // 4):
            if tc * NCH // 4 + g4 >= 1:
                P.wait("tensor", "DV")
            for q in range(4):
                dc = g4 * 4 + q
                P.do("tensor", lambda e, tc=tc, dc=dc, q=q: e.transpose(
                    psO[:, q * 512:q * 512 + 128], xv(dc, tc * 128, 128), idF[:]), inc="PE")
            P.wait("vector", "PE")
            P.do("vector", lambda e, g4=g4, st=st: e.tensor_copy(
                st[:].rearrange("p (a c) -> p a c", c=128)[:, g4 * 4:(g4 + 1) * 4, :],
                psO[:].rearrange("p (s c) -> p s c", c=512)[:, :, 0:128]), inc="DV")
        P.wait("sync", "DV")
        P.dma("sync", lambda e, tc=tc, st=st: e.dma_start(
            out=y_tok[tc * 128:(tc + 1) * 128, :], in_=st[:]), f"ys{tc % 2}")
    P.wslot("sync", "ys0")
    P.wslot("sync", "ys1")
    p.close()


_CACHE = {}


def kernel(**inputs):
    f32 = np.float32
    x = np.ascontiguousarray(inputs["L0_attn_a_x"], dtype=f32)
    Wq = np.ascontiguousarray(inputs["L0_attn_a_Wq"], dtype=f32)
    bq = np.ascontiguousarray(inputs["L0_attn_a_bq"], dtype=f32)
    Wk = np.ascontiguousarray(inputs["L0_attn_a_Wk"], dtype=f32).reshape(D, 128)
    bk = np.ascontiguousarray(inputs["L0_attn_a_bk"], dtype=f32).reshape(128, 1)
    Wv = np.ascontiguousarray(inputs["L0_attn_a_Wv"], dtype=f32).reshape(D, 128)
    bv = np.ascontiguousarray(inputs["L0_attn_a_bv"], dtype=f32).reshape(128, 1)
    Wb = np.ascontiguousarray(inputs["L0_ffn_base_W"], dtype=f32)
    bb = np.ascontiguousarray(inputs["L0_ffn_base_b"], dtype=f32).reshape(1, D)
    rW = np.ascontiguousarray(inputs["L0_ffn_router_W"], dtype=f32)
    rb = np.ascontiguousarray(inputs["L0_ffn_router_b"], dtype=f32).reshape(1, 15)
    A = np.ascontiguousarray(inputs["L0_ffn_A"], dtype=f32).reshape(128, D)
    Bm = np.ascontiguousarray(inputs["L0_ffn_B"], dtype=f32)
    emap = np.ascontiguousarray(inputs["L0_ffn_expert_map"], dtype=np.int32).reshape(15, 1)
    Wq2 = Wq.reshape(D, 128 * 32)

    key = "prog"
    if key not in _CACHE:
        _CACHE[key] = build_program()
    nc = _CACHE[key]

    in_maps = []
    for c in range(NCORE):
        xs = np.ascontiguousarray(x[:, c * SLOC:(c + 1) * SLOC, :].reshape(T, D))
        in_maps.append(dict(
            x_tok=xs,
            w_slice=np.ascontiguousarray(Wb[c * OSL:(c + 1) * OSL]),
            wq_slice=np.ascontiguousarray(Wq2[c * OSL:(c + 1) * OSL]),
            wk_in=Wk, wv_in=Wv, bq_in=bq, bk_in=bk, bv_in=bv, bb_in=bb,
            rw_in=rW, rb_in=rb, a_in=A,
            b_in=np.ascontiguousarray(Bm[:, c * 512:(c + 1) * 512, :].reshape(D, 16)),
            emap_in=emap,
        ))
    res = run_bass_kernel_spmd(nc, in_maps, core_ids=list(range(NCORE)))
    out = np.empty((B, S, D), dtype=np.float32)
    for c in range(NCORE):
        out[:, c * SLOC:(c + 1) * SLOC, :] = res.results[c]["y_tok"].reshape(B, SLOC, D)
    return out


if __name__ == "__main__":
    pass


# version 04
# speedup vs baseline: 1.2819x; 1.0043x over previous
import os
from contextlib import ExitStack

import numpy as np
import ml_dtypes
import concourse.bass as bass
import concourse.mybir as mybir
from concourse.bass_utils import run_bass_kernel_spmd

F32 = mybir.dt.float32
BF16 = mybir.dt.bfloat16
I32 = mybir.dt.int32
AF = mybir.ActivationFunctionType
ALU = mybir.AluOpType
AX = mybir.AxisListType

NCORE = 8
B, S, D = 4, 2048, 4096
SLOC = S // NCORE
T = B * SLOC
NCH = D // 128
NTT = T // 128
OSL = D // NCORE
FT = 512
NFT = T // FT
SCALE_A = float(1.0 / np.sqrt(128.0))
GSCALE = float(1.0 / np.sqrt(2048.0))
EPS = 1e-5

ENGINES = ("sync", "scalar", "vector", "tensor", "gpsimd")
SEMS = ["L", "GD", "PE", "DV", "AC", "PO", "CC"]


class Script:
    def __init__(self, nc, stack):
        self.nc = nc
        self.stack = stack
        self.steps = {e: [] for e in ENGINES}
        self.sem = {}
        self.cnt = {}

    def new_sem(self, name, stack=None):
        self.sem[name] = (stack or self.stack).enter_context(self.nc.semaphore(name))
        self.cnt[name] = 0

    def slot(self, name):
        name = "D" + name
        if name not in self.sem:
            self.new_sem(name)
        return name

    AUTO = {"vector": "DV", "scalar": "AC", "gpsimd": "PO"}

    def do(self, eng, fn, inc=None, inc_n=1):
        if inc is None and eng in self.AUTO:
            inc = self.AUTO[eng]
            inc_n = 1
        if inc is not None:
            self.cnt[inc] += inc_n
        sem = self.sem.get(inc)

        def step(e, fn=fn, sem=sem, n=inc_n if inc else 0):
            ins = fn(e)
            if sem is not None:
                ins.then_inc(sem, n)
        self.steps[eng].append(step)

    def chain(self, eng, back=0):
        sem = self.AUTO[eng]
        self.wait(eng, sem, self.cnt[sem] - back)

    def dma(self, eng, fn, slot):
        self.do(eng, fn, self.slot(slot), 16)

    def wslot(self, eng, name):
        self.wait(eng, self.slot(name))

    def wait(self, eng, sem, v=None):
        v = self.cnt[sem] if v is None else v
        if v <= 0:
            return
        s = self.sem[sem]
        self.steps[eng].append(lambda e, s=s, v=v: e.wait_ge(s, v))

    def barrier(self):
        snap = dict(self.cnt)
        for en in ENGINES:
            for s, v in snap.items():
                if s != "CC":
                    self.wait(en, s, v)

    def replay(self):
        with self.nc.Block() as block:
            for en in ENGINES:
                def runner(eng, en=en):
                    for st in self.steps[en]:
                        st(eng)
                getattr(block, en)(runner)


def _ln_rows(nc, P, env, n, sums_ps, sumsq_ps, third):
    rows = env["rows"]
    m = rows[:, 0:n]
    r = rows[:, n:2 * n]
    t3 = rows[:, 2 * n:3 * n]
    P.do("vector", lambda e: e.tensor_scalar_mul(m, sums_ps, 1.0 / D))
    P.do("vector", lambda e: e.tensor_scalar_mul(r, sumsq_ps, 1.0 / D))
    P.chain("vector")
    P.do("vector", lambda e: e.tensor_tensor(t3, m, m, ALU.mult))
    P.chain("vector")
    P.do("vector", lambda e: e.tensor_scalar_add(t3, t3, -EPS))
    P.chain("vector")
    P.do("vector", lambda e: e.tensor_tensor(r, r, t3, ALU.subtract))
    P.wait("scalar", "DV")
    P.do("scalar", lambda e: e.activation(r, r, AF.Sqrt))
    P.wait("vector", "AC")
    P.do("vector", lambda e: e.reciprocal(r, r))
    P.chain("vector")
    if third == "negm":
        P.do("vector", lambda e: e.tensor_scalar_mul(t3, m, -1.0))
    else:
        P.do("vector", lambda e: e.tensor_tensor(t3, m, r, ALU.mult))
    return dict(m=m, r=r, t3=t3)


def build_program():
    nc = bass.Bass("TRN2", target_bir_lowering=False)
    stop_phase = int(os.environ.get("KMK_STOP_PHASE", "99"))

    x_tok = nc.dram_tensor("x_tok", [T, D], F32, kind="ExternalInput").ap()
    w_slice = nc.dram_tensor("w_slice", [OSL, D], F32, kind="ExternalInput").ap()
    wq_slice = nc.dram_tensor("wq_slice", [OSL, D], F32, kind="ExternalInput").ap()
    wk_in = nc.dram_tensor("wk_in", [D, 128], F32, kind="ExternalInput").ap()
    wv_in = nc.dram_tensor("wv_in", [D, 128], F32, kind="ExternalInput").ap()
    bq_in = nc.dram_tensor("bq_in", [128, 32], F32, kind="ExternalInput").ap()
    bk_in = nc.dram_tensor("bk_in", [128, 1], F32, kind="ExternalInput").ap()
    bv_in = nc.dram_tensor("bv_in", [128, 1], F32, kind="ExternalInput").ap()
    bb_in = nc.dram_tensor("bb_in", [1, D], F32, kind="ExternalInput").ap()
    rw_in = nc.dram_tensor("rw_in", [15, D], F32, kind="ExternalInput").ap()
    rb_in = nc.dram_tensor("rb_in", [1, 15], F32, kind="ExternalInput").ap()
    a_in = nc.dram_tensor("a_in", [128, D], F32, kind="ExternalInput").ap()
    b_in = nc.dram_tensor("b_in", [D, 16], F32, kind="ExternalInput").ap()
    emap_in = nc.dram_tensor("emap_in", [15, 1], I32, kind="ExternalInput").ap()
    y_tok = nc.dram_tensor("y_tok", [T, D], F32, kind="ExternalOutput").ap()
    dbg = nc.dram_tensor("dbg", [128, 18432], F32, kind="ExternalOutput").ap() \
        if os.environ.get("KMK_DBG") else None

    wt_part = nc.dram_tensor("wt_part", [D, OSL], BF16)
    wqe_part = nc.dram_tensor("wqe_part", [OSL, 128], BF16)
    wsum_part = nc.dram_tensor("wsum_part", [OSL, 1], F32)
    wqs_part = nc.dram_tensor("wqs_part", [1, 128], F32)
    wt_full = nc.dram_tensor("wt_full", [NCORE * D, OSL], BF16)
    wqe_full = nc.dram_tensor("wqe_full", [D, 128], BF16)
    wsum_full = nc.dram_tensor("wsum_full", [D, 1], F32)
    wqs_full = nc.dram_tensor("wqs_full", [1, 128], F32)
    wk_bf = nc.dram_tensor("wk_bf", [D, 128], BF16)
    wv_bf = nc.dram_tensor("wv_bf", [D, 128], BF16)
    a_bf = nc.dram_tensor("a_bf", [128, D], BF16)
    b2_part = nc.dram_tensor("b2_part", [128, 512], BF16)
    b2_full = nc.dram_tensor("b2_full", [8 * 128, 512], BF16)
    g_part = nc.dram_tensor("g_part", [D, 16], F32)
    g_full = nc.dram_tensor("g_full", [D, 16], F32)
    rwt_dram = nc.dram_tensor("rwt_dram", [128, 15 * NCH], BF16)
    a2t_dram = nc.dram_tensor("a2t_dram", [D, 128], BF16)

    root = ExitStack()
    with root:
        P = Script(nc, root)
        for s in SEMS:
            P.new_sem(s, root)

        xfm = root.enter_context(nc.sbuf_tensor("xfm", [128, NCH * T], F32))
        idF = root.enter_context(nc.sbuf_tensor("idF", [128, 128], F32))
        idB = root.enter_context(nc.sbuf_tensor("idB", [128, 128], BF16))
        ones_col_b = root.enter_context(nc.sbuf_tensor("ones_col_b", [128, 1], BF16))
        ones_row_b = root.enter_context(nc.sbuf_tensor("ones_row_b", [1, 128], BF16))

        rb_bc = root.enter_context(nc.sbuf_tensor("rb_bc", [128, 15], F32))
        onehotT = root.enter_context(nc.sbuf_tensor("onehotT", [15, 8], BF16))
        eexp = root.enter_context(nc.sbuf_tensor("eexp", [8, 128], BF16))
        bb_fm = root.enter_context(nc.sbuf_tensor("bb_fm", [128, 32], F32))
        ones_f1 = root.enter_context(nc.sbuf_tensor("ones_f1", [1, 1], F32))
        bqe = root.enter_context(nc.sbuf_tensor("bqe", [128, 1], F32))
        bkc = root.enter_context(nc.sbuf_tensor("bkc", [128, 1], F32))
        bvc = root.enter_context(nc.sbuf_tensor("bvc", [128, 1], F32))
        wsums = root.enter_context(nc.sbuf_tensor("wsums", [1, 3 * 128], BF16))
        m0rep = root.enter_context(nc.sbuf_tensor("m0rep", [4, 128], BF16))
        rows = root.enter_context(nc.sbuf_tensor("rows", [1, 3072], F32))
        rows_b = root.enter_context(nc.sbuf_tensor("rows_b", [1, 2112], BF16))

        def xv(dc, lo=0, n=T):
            return xfm[:, dc * T + lo: dc * T + lo + n]

        env = dict(rows=rows, rows_b=rows_b, xv=xv, xfm=xfm, idF=idF, idB=idB,
                   ones_col_b=ones_col_b, ones_row_b=ones_row_b,
                   rb_bc=rb_bc, onehotT=onehotT, eexp=eexp, bb_fm=bb_fm, ones_f1=ones_f1,
                   rwt_dram=rwt_dram, a2t_dram=a2t_dram, b2_part=b2_part,
                   bqe=bqe, bkc=bkc, bvc=bvc, wsums=wsums, m0rep=m0rep,
                   wt_full=wt_full, wqe_full=wqe_full, wsum_full=wsum_full,
                   wqs_full=wqs_full, wk_bf=wk_bf, wv_bf=wv_bf, a_bf=a_bf,
                   b2_full=b2_full, g_part=g_part, g_full=g_full)

        env["dbg"] = dbg
        emit_phase0(nc, P, env, x_tok, w_slice, wq_slice, wk_in, wv_in, bq_in,
                    bk_in, bv_in, bb_in, rw_in, rb_in, a_in, b_in, emap_in,
                    wt_part, wqe_part, wsum_part, wqs_part)
        P.barrier()
        if stop_phase >= 1:
            emit_attn_a(nc, P, env)
            P.barrier()
        if stop_phase >= 2:
            emit_ffn(nc, P, env, 0)
            P.barrier()
        if stop_phase >= 3:
            emit_attn_b(nc, P, env)
            P.barrier()
        if stop_phase >= 4:
            emit_ffn(nc, P, env, 1)
            P.barrier()
        emit_phase5(nc, P, env, y_tok)
        P.replay()
    return nc


def emit_phase0(nc, P, env, x_tok, w_slice, wq_slice, wk_in, wv_in, bq_in,
                bk_in, bv_in, bb_in, rw_in, rb_in, a_in, b_in, emap_in,
                wt_part, wqe_part, wsum_part, wqs_part):
    xv = env["xv"]
    idF, idB = env["idF"], env["idB"]
    ones_col_b, ones_row_b = env["ones_col_b"], env["ones_row_b"]
    rows_b = env["rows_b"]

    p = ExitStack()
    wqout = p.enter_context(nc.sbuf_tensor("p0_wqo", [128, 128], F32))
    wqout_b = p.enter_context(nc.sbuf_tensor("p0_wqob", [128, 128], BF16))
    smt = p.enter_context(nc.sbuf_tensor("p0_smt", [128, 16], F32))
    castA = p.enter_context(nc.sbuf_tensor("p0_ca", [128, D], BF16))
    rw_sb = p.enter_context(nc.sbuf_tensor("p0_rw", [15, D], BF16))
    pa = ExitStack()
    xstage = [pa.enter_context(nc.sbuf_tensor(f"p0_xs{i}", [128, D], F32)) for i in range(2)]
    bqt = pa.enter_context(nc.sbuf_tensor("p0_bqt", [128, 32], F32))
    bb_rowt = pa.enter_context(nc.sbuf_tensor("p0_bbrow", [1, 2048], F32))
    emap_f = pa.enter_context(nc.sbuf_tensor("p0_em", [15, 1], I32))
    emap_g = pa.enter_context(nc.sbuf_tensor("p0_emg", [15, 1], F32))
    psT = p.enter_context(nc.psum_tensor("p0_psT", [128, 2048], F32))
    psTb = p.enter_context(nc.psum_tensor("p0_psTb", [128, 2048], BF16))
    psK1 = p.enter_context(nc.psum_tensor("p0_psK1", [128, 512], F32))
    psWS = p.enter_context(nc.psum_tensor("p0_psWS", [1, 512], F32))

    P.do("vector", lambda e: e.memset(idF[:], 1.0), inc="DV")
    P.do("vector", lambda e: e.memset(idB[:], 1.0), inc="DV")
    P.wait("gpsimd", "DV")
    P.do("gpsimd", lambda e: e.affine_select(out=idF[:], in_=idF[:], pattern=[[1, 128]],
                                             compare_op=ALU.is_equal, fill=0.0, base=0,
                                             channel_multiplier=-1), inc="PO")
    P.do("gpsimd", lambda e: e.affine_select(out=idB[:], in_=idB[:], pattern=[[1, 128]],
                                             compare_op=ALU.is_equal, fill=0.0, base=0,
                                             channel_multiplier=-1), inc="PO")
    P.wait("tensor", "PO")
    P.do("vector", lambda e: e.memset(ones_col_b[:], 1.0))
    P.do("vector", lambda e: e.memset(ones_row_b[:], 1.0))
    P.do("vector", lambda e: e.memset(env["eexp"][:], 0.0))
    P.do("vector", lambda e: e.memset(env["m0rep"][:], 0.0))
    P.do("vector", lambda e: e.memset(rows_b[:, 2048:2064], 2.0))
    P.do("vector", lambda e: e.memset(rows_b[:, 2064:2096], 1.0), inc="DV")
    P.wait("gpsimd", "DV")
    for ee in range(8):
        P.dma("gpsimd", lambda e, ee=ee: e.dma_start(
            out=env["eexp"][ee:ee + 1, 16 * ee:16 * ee + 16], in_=rows_b[0:1, 2048:2064]), "cst")
    for dd in range(4):
        P.dma("gpsimd", lambda e, dd=dd: e.dma_start(
            out=env["m0rep"][dd:dd + 1, 32 * dd:32 * dd + 32], in_=rows_b[0:1, 2064:2096]), "cst")

    P.dma("sync", lambda e: e.dma_start(out=bqt[:], in_=bq_in), "sml")
    P.dma("sync", lambda e: e.dma_start(out=env["bkc"][:], in_=bk_in), "sml")
    P.dma("sync", lambda e: e.dma_start(out=env["bvc"][:], in_=bv_in), "sml")
    P.dma("sync", lambda e: e.dma_start(out=bb_rowt[:], in_=bb_in[0:1, 0:2048]), "sml")
    P.dma("sync", lambda e: e.dma_start(out=env["rb_bc"][0:1, :], in_=rb_in), "sml")
    P.dma("sync", lambda e: e.dma_start(out=emap_f[:], in_=emap_in), "sml")
    P.wslot("vector", "sml")
    P.do("vector", lambda e: e.memset(env["ones_f1"][:], 1.0))
    P.do("vector", lambda e: e.reduce_sum(env["bqe"][:], bqt[:], axis=AX.X))
    P.chain("vector")
    P.do("vector", lambda e: e.tensor_scalar_mul(env["bqe"][:], env["bqe"][:], SCALE_A))
    P.do("vector", lambda e: e.tensor_copy(emap_g[:], emap_f[:]))
    P.chain("vector")
    for ee in range(8):
        P.do("vector", lambda e, ee=ee: e.tensor_scalar(env["onehotT"][:, ee:ee + 1], emap_g[:],
                                                        float(ee), None, ALU.is_equal))
    P.do("vector", lambda e: e.tensor_copy(rows_b[0:1, 2096:2111], env["rb_bc"][0:1, 0:15]), inc="DV")
    P.wait("tensor", "DV")
    P.do("tensor", lambda e: e.matmul(psK1[0:128, 0:15], ones_row_b[0:1, 0:128],
                                      rows_b[0:1, 2096:2111], start=True, stop=True), inc="PE")
    P.wait("vector", "PE")
    P.do("vector", lambda e: e.tensor_copy(env["rb_bc"][:], psK1[:, 0:15]), inc="DV")
    P.wait("tensor", "DV")
    P.wslot("tensor", "sml")
    for oc in range(16):
        P.do("tensor", lambda e, oc=oc: e.matmul(
            psK1[:, 128 + oc:128 + oc + 1], bb_rowt[0:1, oc * 128:(oc + 1) * 128],
            env["ones_f1"][:], start=True, stop=True), inc="PE")
    P.wait("sync", "PE")
    P.dma("sync", lambda e: e.dma_start(out=bb_rowt[:], in_=bb_in[0:1, 2048:4096]), "sml")
    P.wslot("tensor", "sml")
    for oc in range(16):
        P.do("tensor", lambda e, oc=oc: e.matmul(
            psK1[:, 144 + oc:144 + oc + 1], bb_rowt[0:1, oc * 128:(oc + 1) * 128],
            env["ones_f1"][:], start=True, stop=True), inc="PE")
    P.wait("vector", "PE")
    P.do("vector", lambda e: e.tensor_copy(env["bb_fm"][:], psK1[:, 128:160]), inc="DV")

    for (src, dst, coloff) in ((wk_in, env["wk_bf"], 1), (wv_in, env["wv_bf"], 2)):
        P.wslot("gpsimd", "gc")
        P.wait("gpsimd", "DV")
        P.dma("gpsimd", lambda e, src=src: e.dma_start(
            out=castA[:], in_=src.rearrange("(a p) d -> p a d", p=128)), "gc")
        P.wslot("gpsimd", "gc")
        P.dma("gpsimd", lambda e, dst=dst: e.dma_start(
            out=dst.ap().rearrange("(a p) d -> p a d", p=128), in_=castA[:]), "gc")
        P.wslot("vector", "gc")
        P.do("vector", lambda e: e.reduce_sum(
            wqout[:, 0:128].unsqueeze(2),
            castA[:].rearrange("p (a d) -> p d a", a=32), axis=AX.X))
        P.chain("vector")
        P.do("vector", lambda e: e.tensor_copy(wqout_b[:], wqout[:, 0:128]))
        P.wait("tensor", "DV")
        P.do("tensor", lambda e: e.matmul(psWS[0:1, 128:256], ones_col_b[:], wqout_b[:],
                                          start=True, stop=True), inc="PE")
        P.wait("vector", "PE")
        P.do("vector", lambda e, coloff=coloff: e.tensor_copy(
            env["wsums"][0:1, coloff * 128:(coloff + 1) * 128], psWS[0:1, 128:256]), inc="DV")

    P.wslot("gpsimd", "gc")
    P.wait("gpsimd", "DV")
    P.dma("gpsimd", lambda e: e.dma_start(out=castA[:], in_=a_in), "gc")
    P.wslot("gpsimd", "gc")
    P.dma("gpsimd", lambda e: e.dma_start(out=env["a_bf"].ap(), in_=castA[:]), "gc")
    P.wslot("gpsimd", "gc")
    P.dma("gpsimd", lambda e: e.dma_start(out=castA[:, 0:512],
                                          in_=b_in.rearrange("(a p) r -> p a r", p=128)), "gc")
    P.dma("gpsimd", lambda e: e.dma_start(out=rw_sb[:], in_=rw_in), "gc")
    dv_tile_done = []
    for tc in range(NTT):
        st = xstage[tc % 2]
        if tc >= 2:
            P.wait("sync", "DV", dv_tile_done[tc - 2])
        P.dma("sync", lambda e, tc=tc, st=st: e.dma_start(
            out=st[:], in_=x_tok[tc * 128:(tc + 1) * 128, :]), f"xs{tc % 2}")
        P.wslot("tensor", f"xs{tc % 2}")
        for g4 in range(NCH // 4):
            if tc * NCH // 4 + g4 >= 1:
                P.wait("tensor", "DV")
            for q in range(4):
                dc = g4 * 4 + q
                P.do("tensor", lambda e, dc=dc, q=q, st=st: e.transpose(
                    psT[:, q * 512:q * 512 + 128],
                    st[:, dc * 128:(dc + 1) * 128], idF[:]), inc="PE")
            P.wait("vector", "PE")
            P.do("vector", lambda e, tc=tc, g4=g4: e.tensor_copy(
                bass.AP(tensor=env["xfm"][:].tensor, offset=(g4 * 4) * T + tc * 128,
                        ap=[[NCH * T, 128], [T, 4], [1, 128]]),
                psT[:].rearrange("p (s c) -> p s c", c=512)[:, :, 0:128]), inc="DV")
        dv_tile_done.append(P.cnt["DV"])
        P.wait("tensor", "DV")

    pa.close()
    pb = ExitStack()
    wtileW = [pb.enter_context(nc.sbuf_tensor(f"p0_w{i}", [128, D], BF16)) for i in range(2)]
    wstage = pb.enter_context(nc.sbuf_tensor("p0_wst", [128, 256], BF16))
    wqtile = pb.enter_context(nc.sbuf_tensor("p0_wq", [128, 2048], F32))

    for i in range(4):
        for jh in range(2):
            P.wait("sync", "DV")
            P.dma("sync", lambda e, i=i, jh=jh: e.dma_start(
                out=wqtile[:], in_=wq_slice[i * 128:(i + 1) * 128,
                                            jh * 2048:(jh + 1) * 2048]), "wqt")
            P.wslot("vector", "wqt")
            P.do("vector", lambda e, jh=jh: e.reduce_sum(
                wqout[:, jh * 64:(jh + 1) * 64].unsqueeze(2),
                wqtile[:].rearrange("p (d h) -> p d h", h=32), axis=AX.X), inc="DV")
        P.chain("vector")
        P.do("vector", lambda e: e.tensor_scalar_mul(wqout[:], wqout[:], SCALE_A))
        P.chain("vector")
        P.wslot("vector", "gst")
        P.wait("vector", "PE")
        P.do("vector", lambda e: e.tensor_copy(wqout_b[:], wqout[:]))
        P.wait("gpsimd", "DV")
        P.dma("gpsimd", lambda e, i=i: e.dma_start(out=wqe_part.ap()[i * 128:(i + 1) * 128, :],
                                                   in_=wqout_b[:]), "gst")
    P.wslot("gpsimd", "gst")
    P.wait("gpsimd", "DV")
    P.dma("gpsimd", lambda e: e.dma_start(out=wqtile[:].bitcast(BF16)[:, 0:512],
                                          in_=wqe_part.ap().rearrange("(a p) d -> p a d", p=128)),
          "gc")
    P.wslot("tensor", "gc")
    for g4 in range(4):
        P.do("tensor", lambda e, g4=g4: e.matmul(
            psWS[0:1, 0:128], ones_col_b[:], wqtile[:].bitcast(BF16)[:, g4 * 128:(g4 + 1) * 128],
            start=(g4 == 0), stop=(g4 == 3)), inc="PE")
    P.wait("vector", "PE")
    P.do("vector", lambda e: e.tensor_copy(wqout[0:1, 0:128], psWS[0:1, 0:128]), inc="DV")
    P.wait("gpsimd", "DV")
    P.dma("gpsimd", lambda e: e.dma_start(out=wqs_part.ap(), in_=wqout[0:1, 0:128]), "gc")
    P.wslot("gpsimd", "gst")
    P.wslot("gpsimd", "gc")
    P.do("gpsimd", lambda e: e.collective_compute(
        "AllGather", ALU.bypass, replica_groups=[list(range(NCORE))],
        ins=[wqe_part.ap()], outs=[env["wqe_full"].ap()]), inc="CC")
    P.do("gpsimd", lambda e: e.collective_compute(
        "AllReduce", ALU.add, replica_groups=[list(range(NCORE))],
        ins=[wqs_part.ap()], outs=[env["wqs_full"].ap()]), inc="CC")

    for oh in range(2):
        P.wait("gpsimd", "PE")
        for ot in range(2):
            P.dma("gpsimd", lambda e, oh=oh, ot=ot: e.dma_start(
                out=wtileW[ot][:], in_=w_slice[(oh * 2 + ot) * 128:(oh * 2 + ot + 1) * 128, :]),
                "wtl")
        P.wslot("vector", "wtl")
        for ot in range(2):
            P.do("vector", lambda e, oh=oh, ot=ot: e.reduce_sum(
                smt[:, oh * 2 + ot:oh * 2 + ot + 1], wtileW[ot][:], axis=AX.X), inc="DV")
        P.wslot("tensor", "wtl")
        P.wait("tensor", "DV")
        for dc in range(NCH):
            if dc >= 1:
                P.wslot("tensor", "wst")
            for ot in range(2):
                P.do("tensor", lambda e, dc=dc, ot=ot: e.transpose(
                    psTb[:, ot * 128:(ot + 1) * 128],
                    wtileW[ot][:, dc * 128:(dc + 1) * 128], idB[:]), inc="PE")
            P.wait("vector", "PE")
            P.wslot("vector", "wst")
            P.do("vector", lambda e: e.tensor_copy(wstage[:], psTb[:, 0:256]))
            P.wait("sync", "DV")
            P.dma("sync", lambda e, dc=dc, oh=oh: e.dma_start(
                out=wt_part.ap()[dc * 128:(dc + 1) * 128, oh * 256:(oh + 1) * 256],
                in_=wstage[:]), "wst")
    pb.close()
    P.wait("gpsimd", "DV")
    P.dma("gpsimd", lambda e: e.dma_start(
        out=wsum_part.ap().rearrange("(a p) b -> p a b", p=128), in_=smt[:, 0:4],
        allow_slow_non_contiguous=True), "gc")

    P.wslot("tensor", "gc")
    rwt_loc = p.enter_context(nc.sbuf_tensor("p0_rwt", [128, 15 * NCH], BF16))
    for dc in range(NCH):
        P.do("tensor", lambda e, dc=dc: e.transpose(
            psTb[:, (dc % 2) * 1024:(dc % 2) * 1024 + 15],
            rw_sb[:, dc * 128:(dc + 1) * 128], idB[0:15, 0:15]), inc="PE")
        P.wait("vector", "PE")
        P.do("vector", lambda e, dc=dc: e.tensor_copy(
            rwt_loc[:, dc * 15:(dc + 1) * 15],
            psTb[:, (dc % 2) * 1024:(dc % 2) * 1024 + 15]), inc="DV")
        if dc >= 1:
            P.wait("tensor", "DV", P.cnt["DV"] - 1)
    P.wait("gpsimd", "DV")
    P.dma("gpsimd", lambda e: e.dma_start(out=env["rwt_dram"].ap(), in_=rwt_loc[:]), "gc")
    b2stg = p.enter_context(nc.sbuf_tensor("p0_b2s", [16, 1024], BF16))
    for ocl in range(4):
        if ocl >= 1:
            P.wslot("tensor", "gc")
        for ee in range(8):
            a_idx = ee * 4 + ocl
            P.do("tensor", lambda e, a_idx=a_idx: e.transpose(
                psTb[0:16, 0:128], castA[:, a_idx * 16:(a_idx + 1) * 16], idB[:]), inc="PE")
            P.wait("vector", "PE")
            P.do("vector", lambda e, ee=ee: e.tensor_copy(
                b2stg[:, ee * 128:(ee + 1) * 128], psTb[0:16, 0:128]), inc="DV")
            P.wait("tensor", "DV")
        P.wait("gpsimd", "DV")
        P.dma("gpsimd", lambda e, ocl=ocl: e.dma_start(
            out=bass.AP(tensor=env["b2_part"], offset=ocl * 128,
                        ap=[[512, 16], [16 * 512, 8], [1, 128]]),
            in_=b2stg[:].rearrange("p (a c) -> p a c", a=8)), "gc")

    P.wslot("gpsimd", "gc")
    P.wait("gpsimd", "DV")
    P.dma("gpsimd", lambda e: e.dma_start(out=castA[:], in_=env["a_bf"].ap()), "gc")
    P.wslot("tensor", "gc")
    for dc in range(NCH):
        P.do("tensor", lambda e, dc=dc: e.transpose(
            psTb[:, (dc % 2) * 1024:(dc % 2) * 1024 + 128],
            castA[:, dc * 128:(dc + 1) * 128], idB[:]), inc="PE")
        P.wait("vector", "PE")
        P.wslot("vector", "gc")
        P.do("vector", lambda e, dc=dc: e.tensor_copy(
            wqout_b[:], psTb[:, (dc % 2) * 1024:(dc % 2) * 1024 + 128]))
        P.wait("gpsimd", "DV")
        P.dma("gpsimd", lambda e, dc=dc: e.dma_start(
            out=env["a2t_dram"].ap()[dc * 128:(dc + 1) * 128, :], in_=wqout_b[:]), "gc")
        P.wait("tensor", "DV")
    for sl in ("gc", "wst", "wtl"):
        P.wslot("gpsimd", sl)
    for (pi, po) in ((wt_part, env["wt_full"]), (wsum_part, env["wsum_full"]),
                     (env["b2_part"], env["b2_full"])):
        P.do("gpsimd", lambda e, pi=pi, po=po: e.collective_compute(
            "AllGather", ALU.bypass, replica_groups=[list(range(NCORE))],
            ins=[pi.ap()], outs=[po.ap()]), inc="CC")
    P.wait("sync", "CC", 2)
    P.dma("sync", lambda e: e.dma_start(out=env["rows"][0:1, 0:128],
                                        in_=env["wqs_full"].ap()), "wqs")
    P.wslot("vector", "wqs")
    P.do("vector", lambda e: e.tensor_scalar_mul(env["rows"][0:1, 0:128],
                                                 env["rows"][0:1, 0:128], SCALE_A))
    P.chain("vector")
    P.do("vector", lambda e: e.tensor_copy(env["wsums"][0:1, 0:128],
                                           env["rows"][0:1, 0:128]))
    p.close()


def emit_attn_a(nc, P, env):
    xv = env["xv"]
    idF, idB = env["idF"], env["idB"]
    ones_col_b, ones_row_b = env["ones_col_b"], env["ones_row_b"]
    rows, rows_b = env["rows"], env["rows_b"]

    p = ExitStack()
    xb = [p.enter_context(nc.sbuf_tensor(f"aa_xb{i}", [128, T], BF16)) for i in range(2)]
    sqb = [p.enter_context(nc.sbuf_tensor("aa_sq0", [128, T], BF16))] * 2
    wq_t = [p.enter_context(nc.sbuf_tensor(f"aa_wq{i}", [128, 512], BF16)) for i in range(2)]
    wk_t = [p.enter_context(nc.sbuf_tensor(f"aa_wk{i}", [128, 512], BF16)) for i in range(2)]
    wv_t = [p.enter_context(nc.sbuf_tensor(f"aa_wv{i}", [128, 512], BF16)) for i in range(2)]
    afm = p.enter_context(nc.sbuf_tensor("aa_afm", [128, T], F32))
    kfm = p.enter_context(nc.sbuf_tensor("aa_kfm", [128, T], F32))
    vfm = p.enter_context(nc.sbuf_tensor("aa_vfm", [128, T], F32))
    rbf = p.enter_context(nc.sbuf_tensor("aa_rbf", [128, T], F32))

    ps1 = ExitStack()
    psSx = ps1.enter_context(nc.psum_tensor("aa_psSx", [1, 1024], F32))
    psSq = ps1.enter_context(nc.psum_tensor("aa_psSq", [1, 1024], F32))
    for dc in range(NCH):
        bb, sq = xb[dc % 2], sqb[dc % 2]
        if dc >= 1:
            P.wait("scalar", "PE")
        P.do("scalar", lambda e, dc=dc, bb=bb: e.activation(bb[:], xv(dc), AF.Identity))
        P.chain("scalar")
        P.do("scalar", lambda e, bb=bb, sq=sq: e.activation(sq[:], bb[:], AF.Square))
        P.wait("tensor", "AC")
        for th in range(2):
            P.do("tensor", lambda e, dc=dc, th=th, bb=bb: e.matmul(
                psSx[0:1, th * 512:(th + 1) * 512], ones_col_b[:],
                bb[:, th * 512:(th + 1) * 512],
                start=(dc == 0), stop=(dc == NCH - 1)), inc="PE")
            P.do("tensor", lambda e, dc=dc, th=th, sq=sq: e.matmul(
                psSq[0:1, th * 512:(th + 1) * 512], ones_col_b[:],
                sq[:, th * 512:(th + 1) * 512],
                start=(dc == 0), stop=(dc == NCH - 1)), inc="PE")
    P.wait("vector", "PE")
    rr = _ln_rows(nc, P, env, T, psSx[0:1, 0:T], psSq[0:1, 0:T], "negm")
    P.chain("vector")
    P.do("vector", lambda e: e.tensor_copy(rows_b[0:1, 0:T], rr["t3"]))
    ps1.close()

    ps2 = ExitStack()
    psR = ps2.enter_context(nc.psum_tensor("aa_psR", [128, 1024], F32))
    P.do("vector", lambda e: e.tensor_copy(rows_b[0:1, T:2 * T], rr["r"]), inc="DV")
    P.wait("tensor", "DV")
    for th in range(2):
        P.do("tensor", lambda e, th=th: e.matmul(
            psR[:, th * 512:(th + 1) * 512], ones_row_b[0:1, 0:128],
            rows_b[0:1, T + th * 512:T + (th + 1) * 512], start=True, stop=True), inc="PE")
    P.wait("vector", "PE")
    P.do("vector", lambda e: e.tensor_copy(rbf[:], psR[:, 0:T]), inc="DV")
    ps2.close()
    P.wait("tensor", "DV")

    P.wait("sync", "CC", 2)
    ps3 = ExitStack()
    psQ = ps3.enter_context(nc.psum_tensor("aa_psQ", [128, 1024], F32))
    psK = ps3.enter_context(nc.psum_tensor("aa_psK", [128, 1024], F32))
    psV = ps3.enter_context(nc.psum_tensor("aa_psV", [128, 1024], F32))
    psWSq = ps3.enter_context(nc.psum_tensor("aa_psWSq", [1, 512], F32))
    for g4 in range(8):
        wq, wk, wv = wq_t[g4 % 2], wk_t[g4 % 2], wv_t[g4 % 2]
        if g4 >= 2:
            P.wait("sync", "PE")
            P.wait("scalar", "PE")
        P.dma("sync", lambda e, g4=g4, wq=wq: e.dma_start(
            out=wq[:], in_=env["wqe_full"].ap().rearrange("(g p) d -> p g d", p=128)[
                :, g4 * 4:(g4 + 1) * 4, :]), f"aw{g4 % 2}")
        P.dma("sync", lambda e, g4=g4, wk=wk: e.dma_start(
            out=wk[:], in_=env["wk_bf"].ap().rearrange("(g p) d -> p g d", p=128)[
                :, g4 * 4:(g4 + 1) * 4, :]), f"aw{g4 % 2}")
        P.dma("sync", lambda e, g4=g4, wv=wv: e.dma_start(
            out=wv[:], in_=env["wv_bf"].ap().rearrange("(g p) d -> p g d", p=128)[
                :, g4 * 4:(g4 + 1) * 4, :]), f"aw{g4 % 2}")
        for i4 in range(4):
            dc = g4 * 4 + i4
            bb = xb[dc % 2]
            if dc >= 2:
                P.wait("scalar", "PE")
            P.do("scalar", lambda e, dc=dc, bb=bb: e.activation(bb[:], xv(dc), AF.Identity),
                 inc="AC")
            P.wait("tensor", "AC")
            P.wslot("tensor", f"aw{g4 % 2}")
            first, last = dc == 0, False
            for th in range(2):
                sl = slice(th * 512, (th + 1) * 512)
                P.do("tensor", lambda e, dc=dc, th=th, bb=bb, wq=wq, i4=i4: e.matmul(
                    psQ[:, th * 512:(th + 1) * 512], wq[:, i4 * 128:(i4 + 1) * 128],
                    bb[:, th * 512:(th + 1) * 512], start=(dc == 0), stop=False), inc="PE")
                P.do("tensor", lambda e, dc=dc, th=th, bb=bb, wk=wk, i4=i4: e.matmul(
                    psK[:, th * 512:(th + 1) * 512], wk[:, i4 * 128:(i4 + 1) * 128],
                    bb[:, th * 512:(th + 1) * 512], start=(dc == 0), stop=False), inc="PE")
                P.do("tensor", lambda e, dc=dc, th=th, bb=bb, wv=wv, i4=i4: e.matmul(
                    psV[:, th * 512:(th + 1) * 512], wv[:, i4 * 128:(i4 + 1) * 128],
                    bb[:, th * 512:(th + 1) * 512], start=(dc == 0), stop=False), inc="PE")
    for (psX, co) in ((psQ, 0), (psK, 1), (psV, 2)):
        for th in range(2):
            P.do("tensor", lambda e, psX=psX, co=co, th=th: e.matmul(
                psX[:, th * 512:(th + 1) * 512], env["wsums"][0:1, co * 128:(co + 1) * 128],
                rows_b[0:1, th * 512:(th + 1) * 512], start=False, stop=True), inc="PE")
    P.wait("vector", "PE")
    for (psX, dst, bias) in ((psQ, afm, env["bqe"]), (psK, kfm, env["bkc"]), (psV, vfm, env["bvc"])):
        P.do("vector", lambda e, psX=psX, dst=dst: e.tensor_tensor(dst[:], psX[:, 0:T], rbf[:],
                                                                   ALU.mult))
        P.chain("vector")
        P.do("vector", lambda e, dst=dst, bias=bias: e.tensor_scalar(
            dst[:], dst[:], bias[:], None, ALU.add))
    ps3.close()

    aT, kT, vT = afm, kfm, vfm
    ps4 = ExitStack()
    psTr = ps4.enter_context(nc.psum_tensor("aa_psTr", [128, 1024], F32))
    P.wait("tensor", "DV")
    first_tp = True
    for (src, dst) in ((afm, aT), (kfm, kT), (vfm, vT)):
        for tc in range(NTT):
            if not first_tp:
                P.wait("tensor", "DV", P.cnt["DV"] - 1)
            first_tp = False
            P.do("tensor", lambda e, src=src, tc=tc: e.transpose(
                psTr[:, (tc % 2) * 512:(tc % 2) * 512 + 128],
                src[:, tc * 128:(tc + 1) * 128], idF[:]), inc="PE")
            P.wait("vector", "PE")
            P.do("vector", lambda e, dst=dst, tc=tc: e.tensor_copy(
                dst[:, tc * 128:(tc + 1) * 128],
                psTr[:, (tc % 2) * 512:(tc % 2) * 512 + 128]), inc="DV")
    ps4.close()

    numT = p.enter_context(nc.sbuf_tensor("aa_numT", [128, T], F32))
    zT = p.enter_context(nc.sbuf_tensor("aa_zT", [128, T], F32))
    sc = [p.enter_context(nc.sbuf_tensor(f"aa_sc{i}", [128, 2048], F32)) for i in range(2)]
    ex = [p.enter_context(nc.sbuf_tensor(f"aa_ex{i}", [128, 2048], BF16)) for i in range(2)]
    ev = [t[:].bitcast(BF16) for t in sc]
    DBLK = 16
    NBLK = NTT * (128 // DBLK)
    P.wait("vector", "DV")
    P.wait("scalar", "DV")

    def blk(k):
        ti, bl = divmod(k, 128 // DBLK)
        a_b = aT[:, ti * 128 + bl * DBLK: ti * 128 + (bl + 1) * DBLK].unsqueeze(2).broadcast_to(
            [128, DBLK, 128])
        k_b = kT[:, ti * 128:(ti + 1) * 128].unsqueeze(1).broadcast_to([128, DBLK, 128])
        v_b = vT[:, ti * 128:(ti + 1) * 128].unsqueeze(1).broadcast_to([128, DBLK, 128])
        out_sl = slice(ti * 128 + bl * DBLK, ti * 128 + (bl + 1) * DBLK)
        return a_b, k_b, v_b, out_sl

    def emit_tt_exp(k):
        a_b, k_b, _, _ = blk(k)
        scb = sc[k % 2]
        sc3 = scb[:].rearrange("p (d e) -> p d e", e=128)
        if k >= 2:
            P.wait("vector", "AC", P.cnt["AC"] - 1)
            P.chain("vector")
        P.do("vector", lambda e, sc3=sc3, a_b=a_b, k_b=k_b: e.tensor_tensor(
            sc3, a_b, k_b, ALU.mult))
        P.wait("scalar", "DV")
        P.do("scalar", lambda e, scb=scb, k=k: e.activation(ex[k % 2][:], scb[:], AF.Exp))

    emit_tt_exp(0)
    for k in range(NBLK):
        if k + 1 < NBLK:
            emit_tt_exp(k + 1)
        _, _, v_bp, out_sl = blk(k)
        ex3 = ex[k % 2][:].rearrange("p (d e) -> p d e", e=128)
        evb3 = ev[k % 2][:, 0:2048].rearrange("p (d e) -> p d e", e=128)
        P.wait("vector", "AC", P.cnt["AC"] - 1 if k + 1 < NBLK else P.cnt["AC"])
        P.do("vector", lambda e, out_sl=out_sl, ex3=ex3: e.reduce_sum(
            zT[:, out_sl].unsqueeze(2), ex3, axis=AX.X))
        P.do("vector", lambda e, evb3=evb3, ex3=ex3, v_bp=v_bp: e.tensor_tensor(
            evb3, ex3, v_bp, ALU.mult))
        P.chain("vector")
        P.do("vector", lambda e, out_sl=out_sl, evb3=evb3: e.reduce_sum(
            numT[:, out_sl].unsqueeze(2), evb3, axis=AX.X))
    oT = sc[0][:].bitcast(BF16)[:, 0:T]
    P.chain("vector")
    P.do("vector", lambda e: e.reciprocal(zT[:], zT[:]))
    P.chain("vector")
    P.do("vector", lambda e: e.tensor_tensor(oT, numT[:], zT[:], ALU.mult))

    ofm = sc[0][:].bitcast(BF16)[:, T:2 * T]
    mj = [sc[1][:].bitcast(BF16)[:, i * 128:(i + 1) * 128] for i in range(2)]
    ps5 = ExitStack()
    psTo = ps5.enter_context(nc.psum_tensor("aa_psTo", [128, 4096], BF16))
    psRep = [ps5.enter_context(nc.psum_tensor(f"aa_psRep{i}", [128, 512], F32)) for i in range(2)]
    P.wait("tensor", "DV")
    for tc in range(NTT):
        if tc >= 4:
            P.wait("tensor", "DV", P.cnt["DV"] - 3)
        P.do("tensor", lambda e, tc=tc: e.transpose(
            psTo[:, (tc % 4) * 1024:(tc % 4) * 1024 + 128], oT[:, tc * 128:(tc + 1) * 128], idB[:]),
            inc="PE")
        P.wait("vector", "PE")
        P.do("vector", lambda e, tc=tc: e.tensor_copy(
            ofm[:, tc * 128:(tc + 1) * 128], psTo[:, (tc % 4) * 1024:(tc % 4) * 1024 + 128]),
            inc="DV")
    for j in range(NCH):
        mjb = mj[j % 2]
        if j >= 2:
            P.wait("vector", "PE")
        P.do("vector", lambda e, mjb=mjb: e.memset(mjb[:], 0.0), inc="DV")
        P.wait("gpsimd", "DV")
        P.dma("gpsimd", lambda e, j=j, mjb=mjb: e.dma_start(
            out=mjb[4 * j:4 * j + 4, :], in_=env["m0rep"][0:4, :]), f"mj{j % 2}")
        P.wslot("tensor", f"mj{j % 2}")
        P.wait("tensor", "DV")
        for th in range(2):
            P.do("tensor", lambda e, j=j, th=th, mjb=mjb: e.matmul(
                psRep[th][:], mjb, ofm[:, th * 512:(th + 1) * 512],
                start=True, stop=True), inc="PE")
        P.wait("vector", "PE")
        for th in range(2):
            P.do("vector", lambda e, j=j, th=th: e.tensor_tensor(
                xv(j, th * 512, 512), psRep[th][:], xv(j, th * 512, 512), ALU.add), inc="DV")
    ps5.close()
    p.close()


def emit_ffn(nc, P, env, call):
    xv = env["xv"]
    idB, idF = env["idB"], env["idF"]
    ones_col_b, ones_row_b = env["ones_col_b"], env["ones_row_b"]
    rows, rows_b = env["rows"], env["rows_b"]

    p = ExitStack()
    xln = p.enter_context(nc.sbuf_tensor(f"ff_xln{call}", [128, NCH * FT], BF16))
    xcb = [p.enter_context(nc.sbuf_tensor(f"ff_xc{call}{i}", [128, FT], BF16)) for i in range(2)]
    rwT = p.enter_context(nc.sbuf_tensor(f"ff_rwT{call}", [128, 15 * NCH], BF16))
    sqb = [p.enter_context(nc.sbuf_tensor(f"ff_sq{call}0", [128, FT], BF16))] * 2
    woc = [p.enter_context(nc.sbuf_tensor(f"ff_w{call}{i}", [128, 4096], BF16)) for i in range(2)]
    b2c = [p.enter_context(nc.sbuf_tensor(f"ff_b2{call}{i}", [128, 128], BF16)) for i in range(2)]
    a2t = [p.enter_context(nc.sbuf_tensor(f"ff_a2t{call}{i}", [128, 128], BF16)) for i in range(2)]
    rbf_b = p.enter_context(nc.sbuf_tensor(f"ff_rb{call}", [128, FT], BF16))
    mrb_b = p.enter_context(nc.sbuf_tensor(f"ff_mrb{call}", [128, FT], BF16))
    lgT = p.enter_context(nc.sbuf_tensor(f"ff_lgT{call}", [128, 64], F32))
    mx8 = p.enter_context(nc.sbuf_tensor(f"ff_mx8{call}", [128, 8], F32))
    msk = p.enter_context(nc.sbuf_tensor(f"ff_msk{call}", [128, 16], F32))
    pmb = p.enter_context(nc.sbuf_tensor(f"ff_pm{call}", [128, 16], BF16))
    lg = p.enter_context(nc.sbuf_tensor(f"ff_lg{call}", [15, FT], F32))
    pmT = p.enter_context(nc.sbuf_tensor(f"ff_pmT{call}", [15, FT], BF16))
    wfm = p.enter_context(nc.sbuf_tensor(f"ff_wfm{call}", [8, FT], BF16))
    wsb = xcb[0]
    lora = p.enter_context(nc.sbuf_tensor(f"ff_lora{call}", [128, FT], BF16))

    psO = [p.enter_context(nc.psum_tensor(f"ff_psO{call}{i}", [128, 512], F32)) for i in range(2)]
    psH = p.enter_context(nc.psum_tensor(f"ff_psH{call}", [128, 512], F32))
    psMisc = p.enter_context(nc.psum_tensor(f"ff_psM{call}", [128, 512], F32))
    psTb = p.enter_context(nc.psum_tensor(f"ff_psTb{call}", [128, 512], BF16))
    psSx = p.enter_context(nc.psum_tensor(f"ff_psSx{call}", [1, 512], F32))
    psSq = p.enter_context(nc.psum_tensor(f"ff_psSq{call}", [1, 512], F32))

    P.wait("sync", "CC", 5)
    P.wait("scalar", "CC", 5)
    P.dma("sync", lambda e: e.dma_start(out=rwT[:], in_=env["rwt_dram"].ap()), f"frw{call}")

    evac_dv = []
    mm_pe = []
    for tt in range(NFT):
        tok = tt * FT

        P.wait("scalar", "DV")
        for dc in range(NCH):
            xc, sq = xcb[dc % 2], sqb[0]
            if tt > 0 or dc >= 1:
                P.wait("scalar", "PE")
            P.do("scalar", lambda e, dc=dc, tok=tok, xc=xc: e.activation(
                xc[:], xv(dc, tok, FT), AF.Identity))
            P.chain("scalar")
            P.do("scalar", lambda e, xc=xc, sq=sq: e.activation(sq[:], xc[:], AF.Square))
            P.wait("tensor", "AC")
            P.do("tensor", lambda e, dc=dc, xc=xc: e.matmul(
                psSx[0:1, :], ones_col_b[:], xc[:],
                start=(dc == 0), stop=(dc == NCH - 1)), inc="PE")
            P.do("tensor", lambda e, dc=dc, sq=sq: e.matmul(
                psSq[0:1, :], ones_col_b[:], sq[:],
                start=(dc == 0), stop=(dc == NCH - 1)), inc="PE")
        P.wait("vector", "PE")
        rr = _ln_rows(nc, P, env, FT, psSx[0:1, :], psSq[0:1, :], "mr")
        P.chain("vector")
        P.do("vector", lambda e: e.tensor_copy(rows_b[0:1, 0:FT], rr["r"]))
        P.do("vector", lambda e: e.tensor_copy(rows_b[0:1, FT:2 * FT], rr["t3"]))
        P.wait("tensor", "DV")
        P.do("tensor", lambda e: e.matmul(psMisc[:], ones_row_b[0:1, 0:128],
                                          rows_b[0:1, 0:FT], start=True, stop=True), inc="PE")
        P.wait("vector", "PE")
        P.do("vector", lambda e: e.tensor_copy(rbf_b[:], psMisc[:]), inc="DV")
        P.wait("tensor", "DV")
        P.do("tensor", lambda e: e.matmul(psMisc[:], ones_row_b[0:1, 0:128],
                                          rows_b[0:1, FT:2 * FT], start=True, stop=True), inc="PE")
        P.wait("vector", "PE")
        P.do("vector", lambda e: e.tensor_copy(mrb_b[:], psMisc[:]), inc="DV")
        if tt >= 1:
            P.wait("scalar", "PE", mm_pe[-1])
        P.wait("vector", "DV")
        for dc in range(NCH):
            P.do("scalar", lambda e, dc=dc, tok=tok: e.activation(
                xln[:, dc * FT:(dc + 1) * FT], xv(dc, tok, FT), AF.Identity))
            P.wait("vector", "AC")
            P.do("vector", lambda e, dc=dc: e.tensor_tensor(
                xln[:, dc * FT:(dc + 1) * FT], xln[:, dc * FT:(dc + 1) * FT], rbf_b[:], ALU.mult))
            P.chain("vector")
            P.do("vector", lambda e, dc=dc: e.tensor_tensor(
                xln[:, dc * FT:(dc + 1) * FT], xln[:, dc * FT:(dc + 1) * FT], mrb_b[:],
                ALU.subtract))

        P.wait("tensor", "DV")
        P.wslot("tensor", f"frw{call}")
        for dc in range(NCH):
            P.do("tensor", lambda e, dc=dc: e.matmul(
                psMisc[0:15, :], rwT[:, dc * 15:(dc + 1) * 15],
                xln[:, dc * FT:(dc + 1) * FT], start=(dc == 0), stop=(dc == NCH - 1)), inc="PE")
        P.wait("vector", "PE")
        P.do("vector", lambda e: e.tensor_copy(lg[:], psMisc[0:15, :]), inc="DV")
        for tc in range(4):
            P.wait("tensor", "DV")
            P.do("tensor", lambda e, tc=tc: e.transpose(
                psMisc[0:128, 0:15], lg[:, tc * 128:(tc + 1) * 128], idF[0:15, 0:15]), inc="PE")
            P.wait("vector", "PE")
            P.do("vector", lambda e: e.tensor_copy(lgT[:, 0:15], psMisc[0:128, 0:15]))
            P.chain("vector")
            P.do("vector", lambda e: e.tensor_tensor(lgT[:, 0:15], lgT[:, 0:15], env["rb_bc"][:],
                                                     ALU.add))
            P.wait("scalar", "DV")
            P.do("scalar", lambda e: e.activation(lgT[:, 16:31], lgT[:, 0:15], AF.Exp,
                                                  accum_out=lgT[:, 32:33]))
            P.wait("vector", "AC")
            P.do("vector", lambda e: e.reciprocal(lgT[:, 33:34], lgT[:, 32:33]))
            P.chain("vector")
            P.do("vector", lambda e: e.tensor_scalar(lgT[:, 16:31], lgT[:, 16:31],
                                                     lgT[:, 33:34], None, ALU.mult))
            P.chain("vector")
            P.do("vector", lambda e: e.max(out=mx8[:], in_=lgT[:, 16:31]))
            P.chain("vector")
            P.do("vector", lambda e: e.tensor_scalar(msk[:, 0:15], lgT[:, 16:31],
                                                     mx8[:, 7:8], None, ALU.is_ge))
            P.chain("vector")
            P.do("vector", lambda e: e.tensor_tensor(pmb[:, 0:15], lgT[:, 16:31], msk[:, 0:15],
                                                     ALU.mult))
            P.wait("tensor", "DV")
            P.do("tensor", lambda e, tc=tc: e.transpose(psTb[0:15, 0:128], pmb[:, 0:15], idB[:]), inc="PE")
            P.wait("vector", "PE")
            P.do("vector", lambda e, tc=tc: e.tensor_copy(pmT[:, tc * 128:(tc + 1) * 128],
                                                          psTb[0:15, 0:128]), inc="DV")
        P.wait("tensor", "DV")
        P.do("tensor", lambda e: e.matmul(psMisc[0:8, :], env["onehotT"][:], pmT[:],
                                          start=True, stop=True), inc="PE")
        P.wait("vector", "PE")
        P.do("vector", lambda e: e.tensor_copy(wfm[:], psMisc[0:8, :]), inc="DV")
        P.wait("tensor", "DV")
        P.do("tensor", lambda e: e.matmul(psMisc[:], env["eexp"][:], wfm[:],
                                          start=True, stop=True), inc="PE")
        P.wait("vector", "PE")
        P.do("vector", lambda e: e.tensor_copy(wsb[:], psMisc[:]), inc="DV")

        for dc in range(NCH):
            a2 = a2t[dc % 2]
            if dc >= 2:
                P.wait("scalar", "PE")
            P.dma("scalar", lambda e, dc=dc, a2=a2: e.dma_start(
                out=a2[:], in_=env["a2t_dram"].ap()[dc * 128:(dc + 1) * 128, :]), f"fa{dc % 2}")
            P.wslot("tensor", f"fa{dc % 2}")
            P.do("tensor", lambda e, dc=dc, a2=a2: e.matmul(
                psH[:], a2[:], xln[:, dc * FT:(dc + 1) * FT],
                start=(dc == 0), stop=(dc == NCH - 1)), inc="PE")
        P.wait("vector", "PE")
        P.chain("vector")
        P.do("vector", lambda e: e.tensor_tensor(lora[:], psH[:], wsb[:], ALU.mult))
        P.wait("tensor", "DV")
        P.wait("vector", "AC")
        if env.get("dbg") is not None and call == 0 and tt == 0:
            dbgap = env["dbg"]
            P.wait("gpsimd", "DV")
            P.dma("gpsimd", lambda e, dbgap=dbgap: e.dma_start(
                out=dbgap[:, 0:16384], in_=xln[:]), "dbgd")
            P.dma("gpsimd", lambda e, dbgap=dbgap: e.dma_start(
                out=dbgap[:, 16384:16896], in_=wsb[:]), "dbgd")
            P.dma("gpsimd", lambda e, dbgap=dbgap: e.dma_start(
                out=dbgap[:, 16896:17408], in_=lora[:]), "dbgd")
            P.dma("gpsimd", lambda e, dbgap=dbgap: e.dma_start(
                out=dbgap[0:15, 17408:17920], in_=pmT[:]), "dbgd")
            P.dma("gpsimd", lambda e, dbgap=dbgap: e.dma_start(
                out=dbgap[0:8, 17920:18432], in_=wfm[:]), "dbgd")
            P.wslot("scalar", "dbgd")
            P.wslot("vector", "dbgd")
            P.wslot("tensor", "dbgd")

        for oc in range(NCH):
            w = woc[oc % 2]
            rr4 = oc // 4
            cc4 = oc % 4
            if len(mm_pe) >= 2:
                P.wait("sync", "PE", mm_pe[-2])
                P.wait("scalar", "PE", mm_pe[-2])
            P.dma("sync", lambda e, rr4=rr4, cc4=cc4, w=w: e.dma_start(
                out=w[:], in_=bass.AP(
                    tensor=env["wt_full"],
                    offset=(rr4 * D) * OSL + cc4 * 128,
                    ap=[[OSL, 128], [OSL * 128, 32], [1, 128]])), f"fw{oc % 2}")
            bc = b2c[oc % 2]
            P.dma("scalar", lambda e, oc=oc, bc=bc: e.dma_start(
                out=bc[:], in_=bass.AP(tensor=env["b2_full"],
                                       offset=(oc // 4) * 128 * 512 + (oc % 4) * 128,
                                       ap=[[512, 128], [1, 128]])), f"fb{oc % 2}")
            P.wslot("tensor", f"fw{oc % 2}")
            P.wslot("tensor", f"fb{oc % 2}")
            if len(evac_dv) >= 2:
                P.wait("tensor", "DV", evac_dv[-2])
            po = psO[oc % 2]
            for dc in range(NCH):
                P.do("tensor", lambda e, dc=dc, w=w, po=po: e.matmul(
                    po[:], w[:, dc * 128:(dc + 1) * 128], xln[:, dc * FT:(dc + 1) * FT],
                    start=(dc == 0), stop=False), inc="PE")
            P.do("tensor", lambda e, oc=oc, po=po, bc=bc: e.matmul(
                po[:], bc[:], lora[:], start=False, stop=True), inc="PE")
            P.wait("vector", "PE")
            P.do("vector", lambda e, oc=oc, po=po, tok=tok: e.scalar_tensor_tensor(
                xv(oc, tok, FT), po[:], env["bb_fm"][:, oc:oc + 1], xv(oc, tok, FT),
                ALU.add, ALU.add))
            evac_dv.append(P.cnt["DV"])
            mm_pe.append(P.cnt["PE"])
    p.close()


def emit_attn_b(nc, P, env):
    xv = env["xv"]
    ones_col_b, ones_row_b = env["ones_col_b"], env["ones_row_b"]
    rows, rows_b = env["rows"], env["rows_b"]

    p = ExitStack()
    zt = [p.enter_context(nc.sbuf_tensor(f"ab_z{i}", [128, T], BF16)) for i in range(2)]
    sqb = [p.enter_context(nc.sbuf_tensor(f"ab_sq{i}", [128, T], BF16)) for i in range(2)]
    rbf_b = p.enter_context(nc.sbuf_tensor("ab_rb", [128, T], BF16))
    mrb_b = p.enter_context(nc.sbuf_tensor("ab_mrb", [128, T], BF16))
    gbuf = p.enter_context(nc.sbuf_tensor("ab_g", [128, 512], F32))
    junk = p.enter_context(nc.sbuf_tensor("ab_junk", [128, 256], F32))
    psSx = p.enter_context(nc.psum_tensor("ab_psSx", [1, 1024], F32))
    psSq = p.enter_context(nc.psum_tensor("ab_psSq", [1, 1024], F32))
    psMisc = p.enter_context(nc.psum_tensor("ab_psM", [128, 1024], F32))

    for dc in range(NCH):
        z, sq = zt[dc % 2], sqb[dc % 2]
        if dc >= 2:
            P.wait("scalar", "PE")
        P.do("scalar", lambda e, dc=dc, z=z: e.activation(z[:], xv(dc), AF.Identity))
        P.chain("scalar")
        P.do("scalar", lambda e, z=z, sq=sq: e.activation(sq[:], z[:], AF.Square))
        P.wait("tensor", "AC")
        for th in range(2):
            P.do("tensor", lambda e, dc=dc, th=th, z=z: e.matmul(
                psSx[0:1, th * 512:(th + 1) * 512], ones_col_b[:],
                z[:, th * 512:(th + 1) * 512], start=(dc == 0), stop=(dc == NCH - 1)), inc="PE")
            P.do("tensor", lambda e, dc=dc, th=th, sq=sq: e.matmul(
                psSq[0:1, th * 512:(th + 1) * 512], ones_col_b[:],
                sq[:, th * 512:(th + 1) * 512], start=(dc == 0), stop=(dc == NCH - 1)), inc="PE")
    P.wait("vector", "PE")
    rr = _ln_rows(nc, P, env, T, psSx[0:1, 0:T], psSq[0:1, 0:T], "mr")
    P.chain("vector")
    P.do("vector", lambda e: e.tensor_copy(rows_b[0:1, 0:T], rr["r"]))
    P.do("vector", lambda e: e.tensor_copy(rows_b[0:1, T:2 * T], rr["t3"]))
    P.wait("tensor", "DV")
    for th in range(2):
        P.do("tensor", lambda e, th=th: e.matmul(
            psMisc[:, th * 512:(th + 1) * 512], ones_row_b[:],
            rows_b[0:1, th * 512:(th + 1) * 512], start=True, stop=True), inc="PE")
    P.wait("vector", "PE")
    P.do("vector", lambda e: e.tensor_copy(rbf_b[:], psMisc[:, 0:T]), inc="DV")
    P.wait("tensor", "DV")
    for th in range(2):
        P.do("tensor", lambda e, th=th: e.matmul(
            psMisc[:, th * 512:(th + 1) * 512], ones_row_b[:],
            rows_b[0:1, T + th * 512:T + (th + 1) * 512], start=True, stop=True), inc="PE")
    P.wait("vector", "PE")
    P.do("vector", lambda e: e.tensor_copy(mrb_b[:], psMisc[:, 0:T]), inc="DV")

    def z_tile(lt, z):
        P.do("scalar", lambda e, lt=lt, z=z: e.activation(z[:], xv(lt), AF.Identity), inc="AC")
        P.wait("vector", "AC")
        P.do("vector", lambda e, z=z: e.tensor_tensor(z[:], z[:], rbf_b[:], ALU.mult))
        P.chain("vector")
        P.do("vector", lambda e, z=z: e.tensor_tensor(z[:], z[:], mrb_b[:], ALU.subtract))

    pairs_o = [(n, m) for n in range(4) for m in range(n + 1, 4)]
    for lt in range(NCH):
        z = zt[lt % 2]
        if lt >= 2:
            P.wait("scalar", "DV")
            P.wait("scalar", "AC")
        z_tile(lt, z)
        P.chain("vector")
        P.wait("scalar", "DV")
        for n in range(4):
            P.chain("scalar")
            P.do("scalar", lambda e, lt=lt, n=n, z=z: e.activation(
                sqb[lt % 2][:, 0:256], z[:, n * 256:n * 256 + 256], AF.Square,
                accum_out=gbuf[:, lt * 16 + 5 * n:lt * 16 + 5 * n + 1]))
        for (n, m) in pairs_o:
            P.chain("vector")
            P.do("vector", lambda e, n=n, m=m, z=z: e.tensor_tensor(
                junk[:], z[:, n * 256:n * 256 + 256], z[:, m * 256:m * 256 + 256], ALU.mult))
            P.chain("vector")
            P.do("vector", lambda e, lt=lt, n=n, m=m: e.reduce_sum(
                gbuf[:, lt * 16 + n * 4 + m:lt * 16 + n * 4 + m + 1], junk[:], axis=AX.X))
    P.wait("vector", "AC")
    P.chain("vector")
    for lt in range(NCH):
        for (n, m) in pairs_o:
            P.do("vector", lambda e, lt=lt, n=n, m=m: e.tensor_copy(
                gbuf[:, lt * 16 + m * 4 + n:lt * 16 + m * 4 + n + 1],
                gbuf[:, lt * 16 + n * 4 + m:lt * 16 + n * 4 + m + 1]))
    P.chain("vector")
    P.do("vector", lambda e: e.tensor_scalar_mul(gbuf[:], gbuf[:], GSCALE), inc="DV")
    P.wait("sync", "DV")
    P.dma("sync", lambda e: e.dma_start(
        out=bass.AP(tensor=env["g_part"], offset=0, ap=[[16, 128], [16 * 128, 32], [1, 16]]),
        in_=gbuf[:].rearrange("p (a b) -> p a b", b=16)), "gpt")
    P.wslot("gpsimd", "gpt")
    P.do("gpsimd", lambda e: e.collective_compute(
        "AllReduce", ALU.add, replica_groups=[list(range(NCORE))],
        ins=[env["g_part"].ap()], outs=[env["g_full"].ap()]), inc="CC")
    P.wait("sync", "CC")
    P.dma("sync", lambda e: e.dma_start(
        out=gbuf[:].rearrange("p (a b) -> p a b", b=16),
        in_=bass.AP(tensor=env["g_full"], offset=0, ap=[[16, 128], [16 * 128, 32], [1, 16]])),
        "gld")
    g3 = gbuf[:].rearrange("p (a m) -> p a m", m=4)
    P.wslot("vector", "gld")
    P.do("vector", lambda e: e.reduce_max(junk[:, 0:128].unsqueeze(2), g3, axis=AX.X,
                                          negate=True))
    P.chain("vector")
    P.do("vector", lambda e: e.tensor_tensor(
        g3, g3, junk[:, 0:128].unsqueeze(2).broadcast_to([128, 128, 4]), ALU.add))
    P.wait("scalar", "DV")
    P.do("scalar", lambda e: e.activation(gbuf[:], gbuf[:], AF.Exp))
    P.wait("vector", "AC")
    P.do("vector", lambda e: e.reduce_sum(junk[:, 0:128].unsqueeze(2), g3, axis=AX.X))
    P.chain("vector")
    P.do("vector", lambda e: e.reciprocal(junk[:, 0:128], junk[:, 0:128]))
    P.chain("vector")
    P.do("vector", lambda e: e.tensor_tensor(
        g3, g3, junk[:, 0:128].unsqueeze(2).broadcast_to([128, 128, 4]), ALU.mult))
    P.chain("vector")
    P.do("vector", lambda e: e.tensor_scalar_add(
        gbuf[:].rearrange("p (a m) -> p a m", m=4)[:, :, 0:1].rearrange("p a b -> p (a b)") if False
        else bass.AP(tensor=gbuf[:].tensor, offset=gbuf[:].offset, ap=[[512, 128], [16, 32], [5, 4]]),
        bass.AP(tensor=gbuf[:].tensor, offset=gbuf[:].offset, ap=[[512, 128], [16, 32], [5, 4]]),
        1.0))

    msga = p.enter_context(nc.sbuf_tensor("ab_msg", [128, T], BF16))
    for lt in range(NCH):
        z = zt[lt % 2]
        if lt >= 2:
            P.wait("scalar", "DV")
        z_tile(lt, z)
        P.chain("vector")
        for m in range(4):
            for n in range(4):
                if m == 0:
                    P.do("vector", lambda e, lt=lt, n=n, z=z: e.tensor_scalar(
                        msga[:, n * 256:n * 256 + 256], z[:, 0:256],
                        gbuf[:, lt * 16 + n * 4:lt * 16 + n * 4 + 1], None, ALU.mult))
                else:
                    P.chain("vector", back=3)
                    P.do("vector", lambda e, lt=lt, n=n, m=m, z=z: e.scalar_tensor_tensor(
                        msga[:, n * 256:n * 256 + 256], z[:, m * 256:m * 256 + 256],
                        gbuf[:, lt * 16 + n * 4 + m:lt * 16 + n * 4 + m + 1],
                        msga[:, n * 256:n * 256 + 256], ALU.mult, ALU.add))
        P.chain("vector")
        P.do("vector", lambda e, lt=lt: e.tensor_tensor(
            xv(lt), msga[:], xv(lt), ALU.add))
    p.close()


def emit_phase5(nc, P, env, y_tok):
    xv = env["xv"]
    idF = env["idF"]
    p = ExitStack()
    ostage = [p.enter_context(nc.sbuf_tensor(f"p5_os{i}", [128, D], F32)) for i in range(2)]
    psO = p.enter_context(nc.psum_tensor("p5_psO", [128, 2048], F32))
    for tc in range(NTT):
        st = ostage[tc % 2]
        if tc >= 2:
            P.wslot("vector", f"ys{tc % 2}")
        for g4 in range(NCH // 4):
            if tc * NCH // 4 + g4 >= 1:
                P.wait("tensor", "DV")
            for q in range(4):
                dc = g4 * 4 + q
                P.do("tensor", lambda e, tc=tc, dc=dc, q=q: e.transpose(
                    psO[:, q * 512:q * 512 + 128], xv(dc, tc * 128, 128), idF[:]), inc="PE")
            P.wait("vector", "PE")
            P.do("vector", lambda e, g4=g4, st=st: e.tensor_copy(
                st[:].rearrange("p (a c) -> p a c", c=128)[:, g4 * 4:(g4 + 1) * 4, :],
                psO[:].rearrange("p (s c) -> p s c", c=512)[:, :, 0:128]), inc="DV")
        P.wait("sync", "DV")
        P.dma("sync", lambda e, tc=tc, st=st: e.dma_start(
            out=y_tok[tc * 128:(tc + 1) * 128, :], in_=st[:]), f"ys{tc % 2}")
    P.wslot("sync", "ys0")
    P.wslot("sync", "ys1")
    p.close()


_CACHE = {}


def kernel(**inputs):
    f32 = np.float32
    x = np.ascontiguousarray(inputs["L0_attn_a_x"], dtype=f32)
    Wq = np.ascontiguousarray(inputs["L0_attn_a_Wq"], dtype=f32)
    bq = np.ascontiguousarray(inputs["L0_attn_a_bq"], dtype=f32)
    Wk = np.ascontiguousarray(inputs["L0_attn_a_Wk"], dtype=f32).reshape(D, 128)
    bk = np.ascontiguousarray(inputs["L0_attn_a_bk"], dtype=f32).reshape(128, 1)
    Wv = np.ascontiguousarray(inputs["L0_attn_a_Wv"], dtype=f32).reshape(D, 128)
    bv = np.ascontiguousarray(inputs["L0_attn_a_bv"], dtype=f32).reshape(128, 1)
    Wb = np.ascontiguousarray(inputs["L0_ffn_base_W"], dtype=f32)
    bb = np.ascontiguousarray(inputs["L0_ffn_base_b"], dtype=f32).reshape(1, D)
    rW = np.ascontiguousarray(inputs["L0_ffn_router_W"], dtype=f32)
    rb = np.ascontiguousarray(inputs["L0_ffn_router_b"], dtype=f32).reshape(1, 15)
    A = np.ascontiguousarray(inputs["L0_ffn_A"], dtype=f32).reshape(128, D)
    Bm = np.ascontiguousarray(inputs["L0_ffn_B"], dtype=f32)
    emap = np.ascontiguousarray(inputs["L0_ffn_expert_map"], dtype=np.int32).reshape(15, 1)
    Wq2 = Wq.reshape(D, 128 * 32)

    key = "prog"
    if key not in _CACHE:
        _CACHE[key] = build_program()
    nc = _CACHE[key]

    in_maps = []
    for c in range(NCORE):
        xs = np.ascontiguousarray(x[:, c * SLOC:(c + 1) * SLOC, :].reshape(T, D))
        in_maps.append(dict(
            x_tok=xs,
            w_slice=np.ascontiguousarray(Wb[c * OSL:(c + 1) * OSL]),
            wq_slice=np.ascontiguousarray(Wq2[c * OSL:(c + 1) * OSL]),
            wk_in=Wk, wv_in=Wv, bq_in=bq, bk_in=bk, bv_in=bv, bb_in=bb,
            rw_in=rW, rb_in=rb, a_in=A,
            b_in=np.ascontiguousarray(Bm[:, c * 512:(c + 1) * 512, :].reshape(D, 16)),
            emap_in=emap,
        ))
    res = run_bass_kernel_spmd(nc, in_maps, core_ids=list(range(NCORE)))
    out = np.empty((B, S, D), dtype=np.float32)
    for c in range(NCORE):
        out[:, c * SLOC:(c + 1) * SLOC, :] = res.results[c]["y_tok"].reshape(B, SLOC, D)
    return out


if __name__ == "__main__":
    pass


# version 05
# speedup vs baseline: 1.3588x; 1.0600x over previous
import os
from contextlib import ExitStack

import numpy as np
import ml_dtypes
import concourse.bass as bass
import concourse.mybir as mybir
from concourse.bass_utils import run_bass_kernel_spmd

F32 = mybir.dt.float32
BF16 = mybir.dt.bfloat16
I32 = mybir.dt.int32
AF = mybir.ActivationFunctionType
ALU = mybir.AluOpType
AX = mybir.AxisListType

NCORE = 8
B, S, D = 4, 2048, 4096
SLOC = S // NCORE
T = B * SLOC
NCH = D // 128
NTT = T // 128
OSL = D // NCORE
FT = 512
NFT = T // FT
SCALE_A = float(1.0 / np.sqrt(128.0))
GSCALE = float(1.0 / np.sqrt(2048.0))
EPS = 1e-5

ENGINES = ("sync", "scalar", "vector", "tensor", "gpsimd")
SEMS = ["L", "GD", "PE", "DV", "AC", "PO", "CC"]


class Script:
    def __init__(self, nc, stack):
        self.nc = nc
        self.stack = stack
        self.steps = {e: [] for e in ENGINES}
        self.sem = {}
        self.cnt = {}

    def new_sem(self, name, stack=None):
        self.sem[name] = (stack or self.stack).enter_context(self.nc.semaphore(name))
        self.cnt[name] = 0

    def slot(self, name):
        name = "D" + name
        if name not in self.sem:
            self.new_sem(name)
        return name

    AUTO = {"vector": "DV", "scalar": "AC", "gpsimd": "PO"}

    def do(self, eng, fn, inc=None, inc_n=1):
        if inc is None and eng in self.AUTO:
            inc = self.AUTO[eng]
            inc_n = 1
        if inc is not None:
            self.cnt[inc] += inc_n
        sem = self.sem.get(inc)

        def step(e, fn=fn, sem=sem, n=inc_n if inc else 0):
            ins = fn(e)
            if sem is not None:
                ins.then_inc(sem, n)
        self.steps[eng].append(step)

    def chain(self, eng, back=0):
        sem = self.AUTO[eng]
        self.wait(eng, sem, self.cnt[sem] - back)

    def dma(self, eng, fn, slot):
        self.do(eng, fn, self.slot(slot), 16)

    def wslot(self, eng, name):
        self.wait(eng, self.slot(name))

    def wait(self, eng, sem, v=None):
        v = self.cnt[sem] if v is None else v
        if v <= 0:
            return
        s = self.sem[sem]
        self.steps[eng].append(lambda e, s=s, v=v: e.wait_ge(s, v))

    def barrier(self):
        snap = dict(self.cnt)
        for en in ENGINES:
            for s, v in snap.items():
                if s != "CC":
                    self.wait(en, s, v)

    def replay(self):
        with self.nc.Block() as block:
            for en in ENGINES:
                def runner(eng, en=en):
                    for st in self.steps[en]:
                        st(eng)
                getattr(block, en)(runner)


def _ln_rows(nc, P, env, n, sums_ps, sumsq_ps, third):
    rows = env["rows"]
    m = rows[:, 0:n]
    r = rows[:, n:2 * n]
    t3 = rows[:, 2 * n:3 * n]
    P.do("vector", lambda e: e.tensor_scalar_mul(m, sums_ps, 1.0 / D))
    P.do("vector", lambda e: e.tensor_scalar_mul(r, sumsq_ps, 1.0 / D))
    P.chain("vector")
    P.do("vector", lambda e: e.tensor_tensor(t3, m, m, ALU.mult))
    P.chain("vector")
    P.do("vector", lambda e: e.tensor_scalar_add(t3, t3, -EPS))
    P.chain("vector")
    P.do("vector", lambda e: e.tensor_tensor(r, r, t3, ALU.subtract))
    P.wait("scalar", "DV")
    P.do("scalar", lambda e: e.activation(r, r, AF.Sqrt))
    P.wait("vector", "AC")
    P.do("vector", lambda e: e.reciprocal(r, r))
    P.chain("vector")
    if third == "negm":
        P.do("vector", lambda e: e.tensor_scalar_mul(t3, m, -1.0))
    else:
        P.do("vector", lambda e: e.tensor_tensor(t3, m, r, ALU.mult))
    return dict(m=m, r=r, t3=t3)


def build_program():
    nc = bass.Bass("TRN2", target_bir_lowering=False)
    stop_phase = int(os.environ.get("KMK_STOP_PHASE", "99"))

    x_tok = nc.dram_tensor("x_tok", [T, D], F32, kind="ExternalInput").ap()
    w_slice = nc.dram_tensor("w_slice", [OSL, D], F32, kind="ExternalInput").ap()
    wq_slice = nc.dram_tensor("wq_slice", [OSL, D], F32, kind="ExternalInput").ap()
    wk_in = nc.dram_tensor("wk_in", [D, 128], F32, kind="ExternalInput").ap()
    wv_in = nc.dram_tensor("wv_in", [D, 128], F32, kind="ExternalInput").ap()
    bq_in = nc.dram_tensor("bq_in", [128, 32], F32, kind="ExternalInput").ap()
    bk_in = nc.dram_tensor("bk_in", [128, 1], F32, kind="ExternalInput").ap()
    bv_in = nc.dram_tensor("bv_in", [128, 1], F32, kind="ExternalInput").ap()
    bb_in = nc.dram_tensor("bb_in", [1, D], F32, kind="ExternalInput").ap()
    rw_in = nc.dram_tensor("rw_in", [15, D], F32, kind="ExternalInput").ap()
    rb_in = nc.dram_tensor("rb_in", [1, 15], F32, kind="ExternalInput").ap()
    a_in = nc.dram_tensor("a_in", [128, D], F32, kind="ExternalInput").ap()
    b_in = nc.dram_tensor("b_in", [D, 16], F32, kind="ExternalInput").ap()
    emap_in = nc.dram_tensor("emap_in", [15, 1], I32, kind="ExternalInput").ap()
    y_tok = nc.dram_tensor("y_tok", [T, D], F32, kind="ExternalOutput").ap()
    dbg = nc.dram_tensor("dbg", [128, 18432], F32, kind="ExternalOutput").ap() \
        if os.environ.get("KMK_DBG") else None

    wt_part = nc.dram_tensor("wt_part", [D, OSL], BF16)
    wqe_part = nc.dram_tensor("wqe_part", [OSL, 128], BF16)
    wsum_part = nc.dram_tensor("wsum_part", [OSL, 1], F32)
    wqs_part = nc.dram_tensor("wqs_part", [1, 128], F32)
    wt_full = nc.dram_tensor("wt_full", [NCORE * D, OSL], BF16)
    wqe_full = nc.dram_tensor("wqe_full", [D, 128], BF16)
    wsum_full = nc.dram_tensor("wsum_full", [D, 1], F32)
    wqs_full = nc.dram_tensor("wqs_full", [1, 128], F32)
    wk_bf = nc.dram_tensor("wk_bf", [D, 128], BF16)
    wv_bf = nc.dram_tensor("wv_bf", [D, 128], BF16)
    a_bf = nc.dram_tensor("a_bf", [128, D], BF16)
    b2_part = nc.dram_tensor("b2_part", [128, 512], BF16)
    b2_full = nc.dram_tensor("b2_full", [8 * 128, 512], BF16)
    g_part = nc.dram_tensor("g_part", [D, 16], F32)
    g_full = nc.dram_tensor("g_full", [D, 16], F32)
    rwt_dram = nc.dram_tensor("rwt_dram", [128, 15 * NCH], BF16)
    a2t_dram = nc.dram_tensor("a2t_dram", [D, 128], BF16)

    root = ExitStack()
    with root:
        P = Script(nc, root)
        for s in SEMS:
            P.new_sem(s, root)

        xfm = root.enter_context(nc.sbuf_tensor("xfm", [128, NCH * T], F32))
        idF = root.enter_context(nc.sbuf_tensor("idF", [128, 128], F32))
        idB = root.enter_context(nc.sbuf_tensor("idB", [128, 128], BF16))
        ones_col_b = root.enter_context(nc.sbuf_tensor("ones_col_b", [128, 1], BF16))
        ones_row_b = root.enter_context(nc.sbuf_tensor("ones_row_b", [1, 128], BF16))

        rb_bc = root.enter_context(nc.sbuf_tensor("rb_bc", [128, 15], F32))
        onehotT = root.enter_context(nc.sbuf_tensor("onehotT", [15, 8], BF16))
        eexp = root.enter_context(nc.sbuf_tensor("eexp", [8, 128], BF16))
        bb_fm = root.enter_context(nc.sbuf_tensor("bb_fm", [128, 32], F32))
        ones_f1 = root.enter_context(nc.sbuf_tensor("ones_f1", [1, 1], F32))
        bqe = root.enter_context(nc.sbuf_tensor("bqe", [128, 1], F32))
        bkc = root.enter_context(nc.sbuf_tensor("bkc", [128, 1], F32))
        bvc = root.enter_context(nc.sbuf_tensor("bvc", [128, 1], F32))
        wsums = root.enter_context(nc.sbuf_tensor("wsums", [1, 3 * 128], BF16))
        m0rep = root.enter_context(nc.sbuf_tensor("m0rep", [4, 128], BF16))
        rows = root.enter_context(nc.sbuf_tensor("rows", [1, 3072], F32))
        rows_b = root.enter_context(nc.sbuf_tensor("rows_b", [1, 2112], BF16))

        def xv(dc, lo=0, n=T):
            return xfm[:, dc * T + lo: dc * T + lo + n]

        env = dict(rows=rows, rows_b=rows_b, xv=xv, xfm=xfm, idF=idF, idB=idB,
                   ones_col_b=ones_col_b, ones_row_b=ones_row_b,
                   rb_bc=rb_bc, onehotT=onehotT, eexp=eexp, bb_fm=bb_fm, ones_f1=ones_f1,
                   rwt_dram=rwt_dram, a2t_dram=a2t_dram, b2_part=b2_part,
                   bqe=bqe, bkc=bkc, bvc=bvc, wsums=wsums, m0rep=m0rep,
                   wt_full=wt_full, wqe_full=wqe_full, wsum_full=wsum_full,
                   wqs_full=wqs_full, wk_bf=wk_bf, wv_bf=wv_bf, a_bf=a_bf,
                   b2_full=b2_full, g_part=g_part, g_full=g_full)

        env["dbg"] = dbg
        emit_phase0(nc, P, env, x_tok, w_slice, wq_slice, wk_in, wv_in, bq_in,
                    bk_in, bv_in, bb_in, rw_in, rb_in, a_in, b_in, emap_in,
                    wt_part, wqe_part, wsum_part, wqs_part)
        P.barrier()
        if stop_phase >= 1:
            emit_attn_a(nc, P, env)
            P.barrier()
        if stop_phase >= 2:
            emit_ffn(nc, P, env, 0)
            P.barrier()
        if stop_phase >= 3:
            emit_attn_b(nc, P, env)
            P.barrier()
        if stop_phase >= 4:
            emit_ffn(nc, P, env, 1)
            P.barrier()
        emit_phase5(nc, P, env, y_tok)
        P.replay()
    return nc


def emit_phase0(nc, P, env, x_tok, w_slice, wq_slice, wk_in, wv_in, bq_in,
                bk_in, bv_in, bb_in, rw_in, rb_in, a_in, b_in, emap_in,
                wt_part, wqe_part, wsum_part, wqs_part):
    xv = env["xv"]
    idF, idB = env["idF"], env["idB"]
    ones_col_b, ones_row_b = env["ones_col_b"], env["ones_row_b"]
    rows_b = env["rows_b"]

    p = ExitStack()
    wqout = p.enter_context(nc.sbuf_tensor("p0_wqo", [128, 128], F32))
    wqout_b = p.enter_context(nc.sbuf_tensor("p0_wqob", [128, 128], BF16))
    smt = p.enter_context(nc.sbuf_tensor("p0_smt", [128, 16], F32))
    castA = p.enter_context(nc.sbuf_tensor("p0_ca", [128, D], BF16))
    rw_sb = p.enter_context(nc.sbuf_tensor("p0_rw", [15, D], BF16))
    pa = ExitStack()
    xstage = [pa.enter_context(nc.sbuf_tensor(f"p0_xs{i}", [128, D], F32)) for i in range(2)]
    bqt = pa.enter_context(nc.sbuf_tensor("p0_bqt", [128, 32], F32))
    bb_rowt = pa.enter_context(nc.sbuf_tensor("p0_bbrow", [1, 2048], F32))
    emap_f = pa.enter_context(nc.sbuf_tensor("p0_em", [15, 1], I32))
    emap_g = pa.enter_context(nc.sbuf_tensor("p0_emg", [15, 1], F32))
    psT = p.enter_context(nc.psum_tensor("p0_psT", [128, 2048], F32))
    psTb = p.enter_context(nc.psum_tensor("p0_psTb", [128, 2048], BF16))
    psK1 = p.enter_context(nc.psum_tensor("p0_psK1", [128, 512], F32))
    psWS = p.enter_context(nc.psum_tensor("p0_psWS", [1, 512], F32))

    P.do("vector", lambda e: e.memset(idF[:], 1.0), inc="DV")
    P.do("vector", lambda e: e.memset(idB[:], 1.0), inc="DV")
    P.wait("gpsimd", "DV")
    P.do("gpsimd", lambda e: e.affine_select(out=idF[:], in_=idF[:], pattern=[[1, 128]],
                                             compare_op=ALU.is_equal, fill=0.0, base=0,
                                             channel_multiplier=-1), inc="PO")
    P.do("gpsimd", lambda e: e.affine_select(out=idB[:], in_=idB[:], pattern=[[1, 128]],
                                             compare_op=ALU.is_equal, fill=0.0, base=0,
                                             channel_multiplier=-1), inc="PO")
    P.wait("tensor", "PO")
    P.do("vector", lambda e: e.memset(ones_col_b[:], 1.0))
    P.do("vector", lambda e: e.memset(ones_row_b[:], 1.0))
    P.do("vector", lambda e: e.memset(env["eexp"][:], 0.0))
    P.do("vector", lambda e: e.memset(env["m0rep"][:], 0.0))
    P.do("vector", lambda e: e.memset(rows_b[:, 2048:2064], 2.0))
    P.do("vector", lambda e: e.memset(rows_b[:, 2064:2096], 1.0), inc="DV")
    P.wait("gpsimd", "DV")
    for ee in range(8):
        P.dma("gpsimd", lambda e, ee=ee: e.dma_start(
            out=env["eexp"][ee:ee + 1, 16 * ee:16 * ee + 16], in_=rows_b[0:1, 2048:2064]), "cst")
    for dd in range(4):
        P.dma("gpsimd", lambda e, dd=dd: e.dma_start(
            out=env["m0rep"][dd:dd + 1, 32 * dd:32 * dd + 32], in_=rows_b[0:1, 2064:2096]), "cst")

    P.dma("sync", lambda e: e.dma_start(out=bqt[:], in_=bq_in), "sml")
    P.dma("sync", lambda e: e.dma_start(out=env["bkc"][:], in_=bk_in), "sml")
    P.dma("sync", lambda e: e.dma_start(out=env["bvc"][:], in_=bv_in), "sml")
    P.dma("sync", lambda e: e.dma_start(out=bb_rowt[:], in_=bb_in[0:1, 0:2048]), "sml")
    P.dma("sync", lambda e: e.dma_start(out=env["rb_bc"][0:1, :], in_=rb_in), "sml")
    P.dma("sync", lambda e: e.dma_start(out=emap_f[:], in_=emap_in), "sml")
    P.wslot("vector", "sml")
    P.do("vector", lambda e: e.memset(env["ones_f1"][:], 1.0))
    P.do("vector", lambda e: e.reduce_sum(env["bqe"][:], bqt[:], axis=AX.X))
    P.chain("vector")
    P.do("vector", lambda e: e.tensor_scalar_mul(env["bqe"][:], env["bqe"][:], SCALE_A))
    P.do("vector", lambda e: e.tensor_copy(emap_g[:], emap_f[:]))
    P.chain("vector")
    for ee in range(8):
        P.do("vector", lambda e, ee=ee: e.tensor_scalar(env["onehotT"][:, ee:ee + 1], emap_g[:],
                                                        float(ee), None, ALU.is_equal))
    P.do("vector", lambda e: e.tensor_copy(rows_b[0:1, 2096:2111], env["rb_bc"][0:1, 0:15]), inc="DV")
    P.wait("tensor", "DV")
    P.do("tensor", lambda e: e.matmul(psK1[0:128, 0:15], ones_row_b[0:1, 0:128],
                                      rows_b[0:1, 2096:2111], start=True, stop=True), inc="PE")
    P.wait("vector", "PE")
    P.do("vector", lambda e: e.tensor_copy(env["rb_bc"][:], psK1[:, 0:15]), inc="DV")
    P.wait("tensor", "DV")
    P.wslot("tensor", "sml")
    for oc in range(16):
        P.do("tensor", lambda e, oc=oc: e.matmul(
            psK1[:, 128 + oc:128 + oc + 1], bb_rowt[0:1, oc * 128:(oc + 1) * 128],
            env["ones_f1"][:], start=True, stop=True), inc="PE")
    P.wait("sync", "PE")
    P.dma("sync", lambda e: e.dma_start(out=bb_rowt[:], in_=bb_in[0:1, 2048:4096]), "sml")
    P.wslot("tensor", "sml")
    for oc in range(16):
        P.do("tensor", lambda e, oc=oc: e.matmul(
            psK1[:, 144 + oc:144 + oc + 1], bb_rowt[0:1, oc * 128:(oc + 1) * 128],
            env["ones_f1"][:], start=True, stop=True), inc="PE")
    P.wait("vector", "PE")
    P.do("vector", lambda e: e.tensor_copy(env["bb_fm"][:], psK1[:, 128:160]), inc="DV")

    for (src, dst, coloff) in ((wk_in, env["wk_bf"], 1), (wv_in, env["wv_bf"], 2)):
        P.wslot("gpsimd", "gc")
        P.wait("gpsimd", "DV")
        P.dma("gpsimd", lambda e, src=src: e.dma_start(
            out=castA[:], in_=src.rearrange("(a p) d -> p a d", p=128)), "gc")
        P.wslot("gpsimd", "gc")
        P.dma("gpsimd", lambda e, dst=dst: e.dma_start(
            out=dst.ap().rearrange("(a p) d -> p a d", p=128), in_=castA[:]), "gc")
        P.wslot("vector", "gc")
        P.do("vector", lambda e: e.reduce_sum(
            wqout[:, 0:128].unsqueeze(2),
            castA[:].rearrange("p (a d) -> p d a", a=32), axis=AX.X))
        P.chain("vector")
        P.do("vector", lambda e: e.tensor_copy(wqout_b[:], wqout[:, 0:128]))
        P.wait("tensor", "DV")
        P.do("tensor", lambda e: e.matmul(psWS[0:1, 128:256], ones_col_b[:], wqout_b[:],
                                          start=True, stop=True), inc="PE")
        P.wait("vector", "PE")
        P.do("vector", lambda e, coloff=coloff: e.tensor_copy(
            env["wsums"][0:1, coloff * 128:(coloff + 1) * 128], psWS[0:1, 128:256]), inc="DV")

    P.wslot("gpsimd", "gc")
    P.wait("gpsimd", "DV")
    P.dma("gpsimd", lambda e: e.dma_start(out=castA[:], in_=a_in), "gc")
    P.wslot("gpsimd", "gc")
    P.dma("gpsimd", lambda e: e.dma_start(out=env["a_bf"].ap(), in_=castA[:]), "gc")
    P.wslot("gpsimd", "gc")
    P.dma("gpsimd", lambda e: e.dma_start(out=castA[:, 0:512],
                                          in_=b_in.rearrange("(a p) r -> p a r", p=128)), "gc")
    P.dma("gpsimd", lambda e: e.dma_start(out=rw_sb[:], in_=rw_in), "gc")
    dv_tile_done = []
    for tc in range(NTT):
        st = xstage[tc % 2]
        if tc >= 2:
            P.wait("sync", "DV", dv_tile_done[tc - 2])
        P.dma("sync", lambda e, tc=tc, st=st: e.dma_start(
            out=st[:], in_=x_tok[tc * 128:(tc + 1) * 128, :]), f"xs{tc % 2}")
        P.wslot("tensor", f"xs{tc % 2}")
        for g4 in range(NCH // 4):
            if tc * NCH // 4 + g4 >= 1:
                P.wait("tensor", "DV")
            for q in range(4):
                dc = g4 * 4 + q
                P.do("tensor", lambda e, dc=dc, q=q, st=st: e.transpose(
                    psT[:, q * 512:q * 512 + 128],
                    st[:, dc * 128:(dc + 1) * 128], idF[:]), inc="PE")
            P.wait("vector", "PE")
            P.do("vector", lambda e, tc=tc, g4=g4: e.tensor_copy(
                bass.AP(tensor=env["xfm"][:].tensor, offset=(g4 * 4) * T + tc * 128,
                        ap=[[NCH * T, 128], [T, 4], [1, 128]]),
                psT[:].rearrange("p (s c) -> p s c", c=512)[:, :, 0:128]), inc="DV")
        dv_tile_done.append(P.cnt["DV"])
        P.wait("tensor", "DV")

    pa.close()
    pb = ExitStack()
    wtileW = [pb.enter_context(nc.sbuf_tensor(f"p0_w{i}", [128, D], BF16)) for i in range(2)]
    wstage = pb.enter_context(nc.sbuf_tensor("p0_wst", [128, 256], BF16))
    wqtile = pb.enter_context(nc.sbuf_tensor("p0_wq", [128, 2048], F32))

    for i in range(4):
        for jh in range(2):
            P.wait("sync", "DV")
            P.dma("sync", lambda e, i=i, jh=jh: e.dma_start(
                out=wqtile[:], in_=wq_slice[i * 128:(i + 1) * 128,
                                            jh * 2048:(jh + 1) * 2048]), "wqt")
            P.wslot("vector", "wqt")
            P.do("vector", lambda e, jh=jh: e.reduce_sum(
                wqout[:, jh * 64:(jh + 1) * 64].unsqueeze(2),
                wqtile[:].rearrange("p (d h) -> p d h", h=32), axis=AX.X), inc="DV")
        P.chain("vector")
        P.do("vector", lambda e: e.tensor_scalar_mul(wqout[:], wqout[:], SCALE_A))
        P.chain("vector")
        P.wslot("vector", "gst")
        P.wait("vector", "PE")
        P.do("vector", lambda e: e.tensor_copy(wqout_b[:], wqout[:]))
        P.wait("gpsimd", "DV")
        P.dma("gpsimd", lambda e, i=i: e.dma_start(out=wqe_part.ap()[i * 128:(i + 1) * 128, :],
                                                   in_=wqout_b[:]), "gst")
    P.wslot("gpsimd", "gst")
    P.wait("gpsimd", "DV")
    P.dma("gpsimd", lambda e: e.dma_start(out=wqtile[:].bitcast(BF16)[:, 0:512],
                                          in_=wqe_part.ap().rearrange("(a p) d -> p a d", p=128)),
          "gc")
    P.wslot("tensor", "gc")
    for g4 in range(4):
        P.do("tensor", lambda e, g4=g4: e.matmul(
            psWS[0:1, 0:128], ones_col_b[:], wqtile[:].bitcast(BF16)[:, g4 * 128:(g4 + 1) * 128],
            start=(g4 == 0), stop=(g4 == 3)), inc="PE")
    P.wait("vector", "PE")
    P.do("vector", lambda e: e.tensor_copy(wqout[0:1, 0:128], psWS[0:1, 0:128]), inc="DV")
    P.wait("gpsimd", "DV")
    P.dma("gpsimd", lambda e: e.dma_start(out=wqs_part.ap(), in_=wqout[0:1, 0:128]), "gc")
    P.wslot("gpsimd", "gst")
    P.wslot("gpsimd", "gc")
    P.do("gpsimd", lambda e: e.collective_compute(
        "AllGather", ALU.bypass, replica_groups=[list(range(NCORE))],
        ins=[wqe_part.ap()], outs=[env["wqe_full"].ap()]), inc="CC")
    P.do("gpsimd", lambda e: e.collective_compute(
        "AllReduce", ALU.add, replica_groups=[list(range(NCORE))],
        ins=[wqs_part.ap()], outs=[env["wqs_full"].ap()]), inc="CC")

    for oh in range(2):
        P.wait("gpsimd", "PE")
        for ot in range(2):
            P.dma("gpsimd", lambda e, oh=oh, ot=ot: e.dma_start(
                out=wtileW[ot][:], in_=w_slice[(oh * 2 + ot) * 128:(oh * 2 + ot + 1) * 128, :]),
                "wtl")
        P.wslot("vector", "wtl")
        for ot in range(2):
            P.do("vector", lambda e, oh=oh, ot=ot: e.reduce_sum(
                smt[:, oh * 2 + ot:oh * 2 + ot + 1], wtileW[ot][:], axis=AX.X), inc="DV")
        P.wslot("tensor", "wtl")
        P.wait("tensor", "DV")
        for dc in range(NCH):
            if dc >= 1:
                P.wslot("tensor", "wst")
            for ot in range(2):
                P.do("tensor", lambda e, dc=dc, ot=ot: e.transpose(
                    psTb[:, ot * 128:(ot + 1) * 128],
                    wtileW[ot][:, dc * 128:(dc + 1) * 128], idB[:]), inc="PE")
            P.wait("vector", "PE")
            P.wslot("vector", "wst")
            P.do("vector", lambda e: e.tensor_copy(wstage[:], psTb[:, 0:256]))
            P.wait("sync", "DV")
            P.dma("sync", lambda e, dc=dc, oh=oh: e.dma_start(
                out=wt_part.ap()[dc * 128:(dc + 1) * 128, oh * 256:(oh + 1) * 256],
                in_=wstage[:]), "wst")
    pb.close()
    P.wait("gpsimd", "DV")
    P.dma("gpsimd", lambda e: e.dma_start(
        out=wsum_part.ap().rearrange("(a p) b -> p a b", p=128), in_=smt[:, 0:4],
        allow_slow_non_contiguous=True), "gc")

    P.wslot("tensor", "gc")
    rwt_loc = p.enter_context(nc.sbuf_tensor("p0_rwt", [128, 15 * NCH], BF16))
    for dc in range(NCH):
        P.do("tensor", lambda e, dc=dc: e.transpose(
            psTb[:, (dc % 2) * 1024:(dc % 2) * 1024 + 15],
            rw_sb[:, dc * 128:(dc + 1) * 128], idB[0:15, 0:15]), inc="PE")
        P.wait("vector", "PE")
        P.do("vector", lambda e, dc=dc: e.tensor_copy(
            rwt_loc[:, dc * 15:(dc + 1) * 15],
            psTb[:, (dc % 2) * 1024:(dc % 2) * 1024 + 15]), inc="DV")
        if dc >= 1:
            P.wait("tensor", "DV", P.cnt["DV"] - 1)
    P.wait("gpsimd", "DV")
    P.dma("gpsimd", lambda e: e.dma_start(out=env["rwt_dram"].ap(), in_=rwt_loc[:]), "gc")
    b2stg = p.enter_context(nc.sbuf_tensor("p0_b2s", [16, 1024], BF16))
    for ocl in range(4):
        if ocl >= 1:
            P.wslot("tensor", "gc")
        for ee in range(8):
            a_idx = ee * 4 + ocl
            P.do("tensor", lambda e, a_idx=a_idx: e.transpose(
                psTb[0:16, 0:128], castA[:, a_idx * 16:(a_idx + 1) * 16], idB[:]), inc="PE")
            P.wait("vector", "PE")
            P.do("vector", lambda e, ee=ee: e.tensor_copy(
                b2stg[:, ee * 128:(ee + 1) * 128], psTb[0:16, 0:128]), inc="DV")
            P.wait("tensor", "DV")
        P.wait("gpsimd", "DV")
        P.dma("gpsimd", lambda e, ocl=ocl: e.dma_start(
            out=bass.AP(tensor=env["b2_part"], offset=ocl * 128,
                        ap=[[512, 16], [16 * 512, 8], [1, 128]]),
            in_=b2stg[:].rearrange("p (a c) -> p a c", a=8)), "gc")

    P.wslot("gpsimd", "gc")
    P.wait("gpsimd", "DV")
    P.dma("gpsimd", lambda e: e.dma_start(out=castA[:], in_=env["a_bf"].ap()), "gc")
    P.wslot("tensor", "gc")
    for dc in range(NCH):
        P.do("tensor", lambda e, dc=dc: e.transpose(
            psTb[:, (dc % 2) * 1024:(dc % 2) * 1024 + 128],
            castA[:, dc * 128:(dc + 1) * 128], idB[:]), inc="PE")
        P.wait("vector", "PE")
        P.wslot("vector", "gc")
        P.do("vector", lambda e, dc=dc: e.tensor_copy(
            wqout_b[:], psTb[:, (dc % 2) * 1024:(dc % 2) * 1024 + 128]))
        P.wait("gpsimd", "DV")
        P.dma("gpsimd", lambda e, dc=dc: e.dma_start(
            out=env["a2t_dram"].ap()[dc * 128:(dc + 1) * 128, :], in_=wqout_b[:]), "gc")
        P.wait("tensor", "DV")
    for sl in ("gc", "wst", "wtl"):
        P.wslot("gpsimd", sl)
    for (pi, po) in ((wt_part, env["wt_full"]), (wsum_part, env["wsum_full"]),
                     (env["b2_part"], env["b2_full"])):
        P.do("gpsimd", lambda e, pi=pi, po=po: e.collective_compute(
            "AllGather", ALU.bypass, replica_groups=[list(range(NCORE))],
            ins=[pi.ap()], outs=[po.ap()]), inc="CC")
    P.wait("sync", "CC", 2)
    P.dma("sync", lambda e: e.dma_start(out=env["rows"][0:1, 0:128],
                                        in_=env["wqs_full"].ap()), "wqs")
    P.wslot("vector", "wqs")
    P.do("vector", lambda e: e.tensor_scalar_mul(env["rows"][0:1, 0:128],
                                                 env["rows"][0:1, 0:128], SCALE_A))
    P.chain("vector")
    P.do("vector", lambda e: e.tensor_copy(env["wsums"][0:1, 0:128],
                                           env["rows"][0:1, 0:128]))
    p.close()


def emit_attn_a(nc, P, env):
    xv = env["xv"]
    idF, idB = env["idF"], env["idB"]
    ones_col_b, ones_row_b = env["ones_col_b"], env["ones_row_b"]
    rows, rows_b = env["rows"], env["rows_b"]

    p = ExitStack()
    xb = [p.enter_context(nc.sbuf_tensor(f"aa_xb{i}", [128, T], BF16)) for i in range(2)]
    sqb = [p.enter_context(nc.sbuf_tensor("aa_sq0", [128, T], BF16))] * 2
    wq_t = [p.enter_context(nc.sbuf_tensor(f"aa_wq{i}", [128, 512], BF16)) for i in range(2)]
    wk_t = [p.enter_context(nc.sbuf_tensor(f"aa_wk{i}", [128, 512], BF16)) for i in range(2)]
    wv_t = [p.enter_context(nc.sbuf_tensor(f"aa_wv{i}", [128, 512], BF16)) for i in range(2)]
    afm = p.enter_context(nc.sbuf_tensor("aa_afm", [128, T], F32))
    kfm = p.enter_context(nc.sbuf_tensor("aa_kfm", [128, T], F32))
    vfm = p.enter_context(nc.sbuf_tensor("aa_vfm", [128, T], F32))
    rbf = p.enter_context(nc.sbuf_tensor("aa_rbf", [128, T], F32))

    ps1 = ExitStack()
    psSx = ps1.enter_context(nc.psum_tensor("aa_psSx", [1, 1024], F32))
    psSq = ps1.enter_context(nc.psum_tensor("aa_psSq", [1, 1024], F32))
    for dc in range(NCH):
        bb, sq = xb[dc % 2], sqb[dc % 2]
        if dc >= 1:
            P.wait("scalar", "PE")
        P.do("scalar", lambda e, dc=dc, bb=bb: e.activation(bb[:], xv(dc), AF.Identity))
        P.chain("scalar")
        P.do("scalar", lambda e, bb=bb, sq=sq: e.activation(sq[:], bb[:], AF.Square))
        P.wait("tensor", "AC")
        for th in range(2):
            P.do("tensor", lambda e, dc=dc, th=th, bb=bb: e.matmul(
                psSx[0:1, th * 512:(th + 1) * 512], ones_col_b[:],
                bb[:, th * 512:(th + 1) * 512],
                start=(dc == 0), stop=(dc == NCH - 1)), inc="PE")
            P.do("tensor", lambda e, dc=dc, th=th, sq=sq: e.matmul(
                psSq[0:1, th * 512:(th + 1) * 512], ones_col_b[:],
                sq[:, th * 512:(th + 1) * 512],
                start=(dc == 0), stop=(dc == NCH - 1)), inc="PE")
    P.wait("vector", "PE")
    rr = _ln_rows(nc, P, env, T, psSx[0:1, 0:T], psSq[0:1, 0:T], "negm")
    P.chain("vector")
    P.do("vector", lambda e: e.tensor_copy(rows_b[0:1, 0:T], rr["t3"]))
    ps1.close()

    ps2 = ExitStack()
    psR = ps2.enter_context(nc.psum_tensor("aa_psR", [128, 1024], F32))
    P.do("vector", lambda e: e.tensor_copy(rows_b[0:1, T:2 * T], rr["r"]), inc="DV")
    P.wait("tensor", "DV")
    for th in range(2):
        P.do("tensor", lambda e, th=th: e.matmul(
            psR[:, th * 512:(th + 1) * 512], ones_row_b[0:1, 0:128],
            rows_b[0:1, T + th * 512:T + (th + 1) * 512], start=True, stop=True), inc="PE")
    P.wait("vector", "PE")
    P.do("vector", lambda e: e.tensor_copy(rbf[:], psR[:, 0:T]), inc="DV")
    ps2.close()
    P.wait("tensor", "DV")

    P.wait("sync", "CC", 2)
    ps3 = ExitStack()
    psQ = ps3.enter_context(nc.psum_tensor("aa_psQ", [128, 1024], F32))
    psK = ps3.enter_context(nc.psum_tensor("aa_psK", [128, 1024], F32))
    psV = ps3.enter_context(nc.psum_tensor("aa_psV", [128, 1024], F32))
    psWSq = ps3.enter_context(nc.psum_tensor("aa_psWSq", [1, 512], F32))
    for g4 in range(8):
        wq, wk, wv = wq_t[g4 % 2], wk_t[g4 % 2], wv_t[g4 % 2]
        if g4 >= 2:
            P.wait("sync", "PE")
            P.wait("scalar", "PE")
        P.dma("sync", lambda e, g4=g4, wq=wq: e.dma_start(
            out=wq[:], in_=env["wqe_full"].ap().rearrange("(g p) d -> p g d", p=128)[
                :, g4 * 4:(g4 + 1) * 4, :]), f"aw{g4 % 2}")
        P.dma("sync", lambda e, g4=g4, wk=wk: e.dma_start(
            out=wk[:], in_=env["wk_bf"].ap().rearrange("(g p) d -> p g d", p=128)[
                :, g4 * 4:(g4 + 1) * 4, :]), f"aw{g4 % 2}")
        P.dma("sync", lambda e, g4=g4, wv=wv: e.dma_start(
            out=wv[:], in_=env["wv_bf"].ap().rearrange("(g p) d -> p g d", p=128)[
                :, g4 * 4:(g4 + 1) * 4, :]), f"aw{g4 % 2}")
        for i4 in range(4):
            dc = g4 * 4 + i4
            bb = xb[dc % 2]
            if dc >= 2:
                P.wait("scalar", "PE")
            P.do("scalar", lambda e, dc=dc, bb=bb: e.activation(bb[:], xv(dc), AF.Identity),
                 inc="AC")
            P.wait("tensor", "AC")
            P.wslot("tensor", f"aw{g4 % 2}")
            first, last = dc == 0, False
            for th in range(2):
                sl = slice(th * 512, (th + 1) * 512)
                P.do("tensor", lambda e, dc=dc, th=th, bb=bb, wq=wq, i4=i4: e.matmul(
                    psQ[:, th * 512:(th + 1) * 512], wq[:, i4 * 128:(i4 + 1) * 128],
                    bb[:, th * 512:(th + 1) * 512], start=(dc == 0), stop=False), inc="PE")
                P.do("tensor", lambda e, dc=dc, th=th, bb=bb, wk=wk, i4=i4: e.matmul(
                    psK[:, th * 512:(th + 1) * 512], wk[:, i4 * 128:(i4 + 1) * 128],
                    bb[:, th * 512:(th + 1) * 512], start=(dc == 0), stop=False), inc="PE")
                P.do("tensor", lambda e, dc=dc, th=th, bb=bb, wv=wv, i4=i4: e.matmul(
                    psV[:, th * 512:(th + 1) * 512], wv[:, i4 * 128:(i4 + 1) * 128],
                    bb[:, th * 512:(th + 1) * 512], start=(dc == 0), stop=False), inc="PE")
    for (psX, co) in ((psQ, 0), (psK, 1), (psV, 2)):
        for th in range(2):
            P.do("tensor", lambda e, psX=psX, co=co, th=th: e.matmul(
                psX[:, th * 512:(th + 1) * 512], env["wsums"][0:1, co * 128:(co + 1) * 128],
                rows_b[0:1, th * 512:(th + 1) * 512], start=False, stop=True), inc="PE")
    P.wait("vector", "PE")
    for (psX, dst, bias) in ((psQ, afm, env["bqe"]), (psK, kfm, env["bkc"]), (psV, vfm, env["bvc"])):
        P.do("vector", lambda e, psX=psX, dst=dst: e.tensor_tensor(dst[:], psX[:, 0:T], rbf[:],
                                                                   ALU.mult))
        P.chain("vector")
        P.do("vector", lambda e, dst=dst, bias=bias: e.tensor_scalar(
            dst[:], dst[:], bias[:], None, ALU.add))
    ps3.close()

    aT, kT, vT = afm, kfm, vfm
    ps4 = ExitStack()
    psTr = ps4.enter_context(nc.psum_tensor("aa_psTr", [128, 1024], F32))
    P.wait("tensor", "DV")
    first_tp = True
    for (src, dst) in ((afm, aT), (kfm, kT), (vfm, vT)):
        for tc in range(NTT):
            if not first_tp:
                P.wait("tensor", "DV", P.cnt["DV"] - 1)
            first_tp = False
            P.do("tensor", lambda e, src=src, tc=tc: e.transpose(
                psTr[:, (tc % 2) * 512:(tc % 2) * 512 + 128],
                src[:, tc * 128:(tc + 1) * 128], idF[:]), inc="PE")
            P.wait("vector", "PE")
            P.do("vector", lambda e, dst=dst, tc=tc: e.tensor_copy(
                dst[:, tc * 128:(tc + 1) * 128],
                psTr[:, (tc % 2) * 512:(tc % 2) * 512 + 128]), inc="DV")
    ps4.close()

    numT = p.enter_context(nc.sbuf_tensor("aa_numT", [128, T], F32))
    zT = p.enter_context(nc.sbuf_tensor("aa_zT", [128, T], F32))
    sc = [p.enter_context(nc.sbuf_tensor(f"aa_sc{i}", [128, 2048], F32)) for i in range(2)]
    ex = [p.enter_context(nc.sbuf_tensor(f"aa_ex{i}", [128, 2048], BF16)) for i in range(2)]
    ev = [t[:].bitcast(BF16) for t in sc]
    DBLK = 16
    NBLK = NTT * (128 // DBLK)
    P.wait("vector", "DV")
    P.wait("scalar", "DV")

    def blk(k):
        ti, bl = divmod(k, 128 // DBLK)
        a_b = aT[:, ti * 128 + bl * DBLK: ti * 128 + (bl + 1) * DBLK].unsqueeze(2).broadcast_to(
            [128, DBLK, 128])
        k_b = kT[:, ti * 128:(ti + 1) * 128].unsqueeze(1).broadcast_to([128, DBLK, 128])
        v_b = vT[:, ti * 128:(ti + 1) * 128].unsqueeze(1).broadcast_to([128, DBLK, 128])
        out_sl = slice(ti * 128 + bl * DBLK, ti * 128 + (bl + 1) * DBLK)
        return a_b, k_b, v_b, out_sl

    def emit_tt_exp(k):
        a_b, k_b, _, _ = blk(k)
        scb = sc[k % 2]
        sc3 = scb[:].rearrange("p (d e) -> p d e", e=128)
        if k >= 2:
            P.wait("vector", "AC", P.cnt["AC"] - 1)
            P.chain("vector")
        P.do("vector", lambda e, sc3=sc3, a_b=a_b, k_b=k_b: e.tensor_tensor(
            sc3, a_b, k_b, ALU.mult))
        P.wait("scalar", "DV")
        P.do("scalar", lambda e, scb=scb, k=k: e.activation(ex[k % 2][:], scb[:], AF.Exp))

    emit_tt_exp(0)
    for k in range(NBLK):
        if k + 1 < NBLK:
            emit_tt_exp(k + 1)
        _, _, v_bp, out_sl = blk(k)
        ex3 = ex[k % 2][:].rearrange("p (d e) -> p d e", e=128)
        evb3 = ev[k % 2][:, 0:2048].rearrange("p (d e) -> p d e", e=128)
        P.wait("vector", "AC", P.cnt["AC"] - 1 if k + 1 < NBLK else P.cnt["AC"])
        P.do("vector", lambda e, out_sl=out_sl, ex3=ex3: e.reduce_sum(
            zT[:, out_sl].unsqueeze(2), ex3, axis=AX.X))
        P.do("vector", lambda e, evb3=evb3, ex3=ex3, v_bp=v_bp: e.tensor_tensor(
            evb3, ex3, v_bp, ALU.mult))
        P.chain("vector")
        P.do("vector", lambda e, out_sl=out_sl, evb3=evb3: e.reduce_sum(
            numT[:, out_sl].unsqueeze(2), evb3, axis=AX.X))
    oT = sc[0][:].bitcast(BF16)[:, 0:T]
    P.chain("vector")
    P.do("vector", lambda e: e.reciprocal(zT[:], zT[:]))
    P.chain("vector")
    P.do("vector", lambda e: e.tensor_tensor(oT, numT[:], zT[:], ALU.mult))

    ofm = sc[0][:].bitcast(BF16)[:, T:2 * T]
    mj = [sc[1][:].bitcast(BF16)[:, i * 128:(i + 1) * 128] for i in range(2)]
    ps5 = ExitStack()
    psTo = ps5.enter_context(nc.psum_tensor("aa_psTo", [128, 4096], BF16))
    psRep = [ps5.enter_context(nc.psum_tensor(f"aa_psRep{i}", [128, 512], F32)) for i in range(2)]
    P.wait("tensor", "DV")
    for tc in range(NTT):
        if tc >= 4:
            P.wait("tensor", "DV", P.cnt["DV"] - 3)
        P.do("tensor", lambda e, tc=tc: e.transpose(
            psTo[:, (tc % 4) * 1024:(tc % 4) * 1024 + 128], oT[:, tc * 128:(tc + 1) * 128], idB[:]),
            inc="PE")
        P.wait("vector", "PE")
        P.do("vector", lambda e, tc=tc: e.tensor_copy(
            ofm[:, tc * 128:(tc + 1) * 128], psTo[:, (tc % 4) * 1024:(tc % 4) * 1024 + 128]),
            inc="DV")
    for j in range(NCH):
        mjb = mj[j % 2]
        if j >= 2:
            P.wait("vector", "PE")
        P.do("vector", lambda e, mjb=mjb: e.memset(mjb[:], 0.0), inc="DV")
        P.wait("gpsimd", "DV")
        P.dma("gpsimd", lambda e, j=j, mjb=mjb: e.dma_start(
            out=mjb[4 * j:4 * j + 4, :], in_=env["m0rep"][0:4, :]), f"mj{j % 2}")
        P.wslot("tensor", f"mj{j % 2}")
        P.wait("tensor", "DV")
        for th in range(2):
            P.do("tensor", lambda e, j=j, th=th, mjb=mjb: e.matmul(
                psRep[th][:], mjb, ofm[:, th * 512:(th + 1) * 512],
                start=True, stop=True), inc="PE")
        P.wait("vector", "PE")
        for th in range(2):
            P.do("vector", lambda e, j=j, th=th: e.tensor_tensor(
                xv(j, th * 512, 512), psRep[th][:], xv(j, th * 512, 512), ALU.add), inc="DV")
    ps5.close()
    p.close()


def emit_ffn(nc, P, env, call):
    xv = env["xv"]
    idB, idF = env["idB"], env["idF"]
    ones_col_b, ones_row_b = env["ones_col_b"], env["ones_row_b"]
    rows, rows_b = env["rows"], env["rows_b"]

    p = ExitStack()
    xln = p.enter_context(nc.sbuf_tensor(f"ff_xln{call}", [128, NCH * FT], BF16))
    xcb = [p.enter_context(nc.sbuf_tensor(f"ff_xc{call}{i}", [128, FT], BF16)) for i in range(2)]
    rwT = p.enter_context(nc.sbuf_tensor(f"ff_rwT{call}", [128, 15 * NCH], BF16))
    sqb = [p.enter_context(nc.sbuf_tensor(f"ff_sq{call}0", [128, FT], BF16))] * 2
    woc = [p.enter_context(nc.sbuf_tensor(f"ff_w{call}{i}", [128, 4096], BF16)) for i in range(2)]
    b2c = [p.enter_context(nc.sbuf_tensor(f"ff_b2{call}{i}", [128, 128], BF16)) for i in range(2)]
    rbf_b = p.enter_context(nc.sbuf_tensor(f"ff_rb{call}", [128, FT], BF16))
    mrb_b = p.enter_context(nc.sbuf_tensor(f"ff_mrb{call}", [128, FT], BF16))
    lgT = p.enter_context(nc.sbuf_tensor(f"ff_lgT{call}", [128, 64], F32))
    mx8 = p.enter_context(nc.sbuf_tensor(f"ff_mx8{call}", [128, 8], F32))
    msk = p.enter_context(nc.sbuf_tensor(f"ff_msk{call}", [128, 16], F32))
    pmb = p.enter_context(nc.sbuf_tensor(f"ff_pm{call}", [128, 16], BF16))
    lg = p.enter_context(nc.sbuf_tensor(f"ff_lg{call}", [15, FT], F32))
    pmT = p.enter_context(nc.sbuf_tensor(f"ff_pmT{call}", [15, FT], BF16))
    wfm = p.enter_context(nc.sbuf_tensor(f"ff_wfm{call}", [8, FT], BF16))
    wsb = xcb[0]
    lora = p.enter_context(nc.sbuf_tensor(f"ff_lora{call}", [128, FT], BF16))

    psO = [p.enter_context(nc.psum_tensor(f"ff_psO{call}{i}", [128, 512], F32)) for i in range(2)]
    psH = p.enter_context(nc.psum_tensor(f"ff_psH{call}", [128, 512], F32))
    psMisc = p.enter_context(nc.psum_tensor(f"ff_psM{call}", [128, 512], F32))
    psTb = p.enter_context(nc.psum_tensor(f"ff_psTb{call}", [128, 512], BF16))
    psSx = p.enter_context(nc.psum_tensor(f"ff_psSx{call}", [1, 512], F32))
    psSq = p.enter_context(nc.psum_tensor(f"ff_psSq{call}", [1, 512], F32))

    P.wait("sync", "CC", 5)
    P.wait("scalar", "CC", 5)
    P.dma("sync", lambda e: e.dma_start(out=rwT[:], in_=env["rwt_dram"].ap()), f"frw{call}")

    evac_dv = []
    mm_pe = []
    for tt in range(NFT):
        tok = tt * FT

        P.wait("scalar", "DV")
        for dc in range(NCH):
            xc, sq = xcb[dc % 2], sqb[0]
            if tt > 0 or dc >= 1:
                P.wait("scalar", "PE")
            P.do("scalar", lambda e, dc=dc, tok=tok, xc=xc: e.activation(
                xc[:], xv(dc, tok, FT), AF.Identity))
            P.chain("scalar")
            P.do("scalar", lambda e, xc=xc, sq=sq: e.activation(sq[:], xc[:], AF.Square))
            P.wait("tensor", "AC")
            P.do("tensor", lambda e, dc=dc, xc=xc: e.matmul(
                psSx[0:1, :], ones_col_b[:], xc[:],
                start=(dc == 0), stop=(dc == NCH - 1)), inc="PE")
            P.do("tensor", lambda e, dc=dc, sq=sq: e.matmul(
                psSq[0:1, :], ones_col_b[:], sq[:],
                start=(dc == 0), stop=(dc == NCH - 1)), inc="PE")
        P.wait("vector", "PE")
        rr = _ln_rows(nc, P, env, FT, psSx[0:1, :], psSq[0:1, :], "mr")
        P.chain("vector")
        P.do("vector", lambda e: e.tensor_copy(rows_b[0:1, 0:FT], rr["r"]))
        P.do("vector", lambda e: e.tensor_copy(rows_b[0:1, FT:2 * FT], rr["t3"]))
        P.wait("tensor", "DV")
        P.do("tensor", lambda e: e.matmul(psMisc[:], ones_row_b[0:1, 0:128],
                                          rows_b[0:1, 0:FT], start=True, stop=True), inc="PE")
        P.wait("vector", "PE")
        P.do("vector", lambda e: e.tensor_copy(rbf_b[:], psMisc[:]), inc="DV")
        P.wait("tensor", "DV")
        P.do("tensor", lambda e: e.matmul(psMisc[:], ones_row_b[0:1, 0:128],
                                          rows_b[0:1, FT:2 * FT], start=True, stop=True), inc="PE")
        P.wait("vector", "PE")
        P.do("vector", lambda e: e.tensor_copy(mrb_b[:], psMisc[:]), inc="DV")
        if tt >= 1:
            P.wait("scalar", "PE", mm_pe[-1])
        P.wait("vector", "DV")
        for dc in range(NCH):
            P.do("scalar", lambda e, dc=dc, tok=tok: e.activation(
                xln[:, dc * FT:(dc + 1) * FT], xv(dc, tok, FT), AF.Identity))
            P.wait("vector", "AC")
            P.do("vector", lambda e, dc=dc: e.tensor_tensor(
                xln[:, dc * FT:(dc + 1) * FT], xln[:, dc * FT:(dc + 1) * FT], rbf_b[:], ALU.mult))
            P.chain("vector")
            P.do("vector", lambda e, dc=dc: e.tensor_tensor(
                xln[:, dc * FT:(dc + 1) * FT], xln[:, dc * FT:(dc + 1) * FT], mrb_b[:],
                ALU.subtract))

        P.wait("tensor", "DV")
        P.wslot("tensor", f"frw{call}")
        for dc in range(NCH):
            P.do("tensor", lambda e, dc=dc: e.matmul(
                psMisc[0:15, :], rwT[:, dc * 15:(dc + 1) * 15],
                xln[:, dc * FT:(dc + 1) * FT], start=(dc == 0), stop=(dc == NCH - 1)), inc="PE")
        P.wait("vector", "PE")
        P.do("vector", lambda e: e.tensor_copy(lg[:], psMisc[0:15, :]), inc="DV")
        for tc in range(4):
            P.wait("tensor", "DV")
            P.do("tensor", lambda e, tc=tc: e.transpose(
                psMisc[0:128, 0:15], lg[:, tc * 128:(tc + 1) * 128], idF[0:15, 0:15]), inc="PE")
            P.wait("vector", "PE")
            P.do("vector", lambda e: e.tensor_copy(lgT[:, 0:15], psMisc[0:128, 0:15]))
            P.chain("vector")
            P.do("vector", lambda e: e.tensor_tensor(lgT[:, 0:15], lgT[:, 0:15], env["rb_bc"][:],
                                                     ALU.add))
            P.wait("scalar", "DV")
            P.do("scalar", lambda e: e.activation(lgT[:, 16:31], lgT[:, 0:15], AF.Exp,
                                                  accum_out=lgT[:, 32:33]))
            P.wait("vector", "AC")
            P.do("vector", lambda e: e.reciprocal(lgT[:, 33:34], lgT[:, 32:33]))
            P.chain("vector")
            P.do("vector", lambda e: e.tensor_scalar(lgT[:, 16:31], lgT[:, 16:31],
                                                     lgT[:, 33:34], None, ALU.mult))
            P.chain("vector")
            P.do("vector", lambda e: e.max(out=mx8[:], in_=lgT[:, 16:31]))
            P.chain("vector")
            P.do("vector", lambda e: e.tensor_scalar(msk[:, 0:15], lgT[:, 16:31],
                                                     mx8[:, 7:8], None, ALU.is_ge))
            P.chain("vector")
            P.do("vector", lambda e: e.tensor_tensor(pmb[:, 0:15], lgT[:, 16:31], msk[:, 0:15],
                                                     ALU.mult))
            P.wait("tensor", "DV")
            P.do("tensor", lambda e, tc=tc: e.transpose(psTb[0:15, 0:128], pmb[:, 0:15], idB[:]), inc="PE")
            P.wait("vector", "PE")
            P.do("vector", lambda e, tc=tc: e.tensor_copy(pmT[:, tc * 128:(tc + 1) * 128],
                                                          psTb[0:15, 0:128]), inc="DV")
        P.wait("tensor", "DV")
        P.do("tensor", lambda e: e.matmul(psMisc[0:8, :], env["onehotT"][:], pmT[:],
                                          start=True, stop=True), inc="PE")
        P.wait("vector", "PE")
        P.do("vector", lambda e: e.tensor_copy(wfm[:], psMisc[0:8, :]), inc="DV")
        P.wait("tensor", "DV")
        P.do("tensor", lambda e: e.matmul(psMisc[:], env["eexp"][:], wfm[:],
                                          start=True, stop=True), inc="PE")
        P.wait("vector", "PE")
        P.do("vector", lambda e: e.tensor_copy(wsb[:], psMisc[:]), inc="DV")

        if mm_pe:
            P.wait("scalar", "PE", mm_pe[-1])
        P.dma("scalar", lambda e: e.dma_start(
            out=woc[0][:], in_=env["a2t_dram"].ap().rearrange("(g p) d -> p g d", p=128)),
            f"fa{call}")
        P.wslot("tensor", f"fa{call}")
        for dc in range(NCH):
            P.do("tensor", lambda e, dc=dc: e.matmul(
                psH[:], woc[0][:, dc * 128:(dc + 1) * 128], xln[:, dc * FT:(dc + 1) * FT],
                start=(dc == 0), stop=(dc == NCH - 1)), inc="PE")
        h_pe = P.cnt["PE"]
        P.wait("vector", "PE")
        P.chain("vector")
        P.do("vector", lambda e: e.tensor_tensor(lora[:], psH[:], wsb[:], ALU.mult))
        P.wait("tensor", "DV")
        P.wait("vector", "AC")
        if env.get("dbg") is not None and call == 0 and tt == 0:
            dbgap = env["dbg"]
            P.wait("gpsimd", "DV")
            P.dma("gpsimd", lambda e, dbgap=dbgap: e.dma_start(
                out=dbgap[:, 0:16384], in_=xln[:]), "dbgd")
            P.dma("gpsimd", lambda e, dbgap=dbgap: e.dma_start(
                out=dbgap[:, 16384:16896], in_=wsb[:]), "dbgd")
            P.dma("gpsimd", lambda e, dbgap=dbgap: e.dma_start(
                out=dbgap[:, 16896:17408], in_=lora[:]), "dbgd")
            P.dma("gpsimd", lambda e, dbgap=dbgap: e.dma_start(
                out=dbgap[0:15, 17408:17920], in_=pmT[:]), "dbgd")
            P.dma("gpsimd", lambda e, dbgap=dbgap: e.dma_start(
                out=dbgap[0:8, 17920:18432], in_=wfm[:]), "dbgd")
            P.wslot("scalar", "dbgd")
            P.wslot("vector", "dbgd")
            P.wslot("tensor", "dbgd")

        def b2dma(oc):
            bc = b2c[oc % 2]
            P.dma("scalar", lambda e, oc=oc, bc=bc: e.dma_start(
                out=bc[:], in_=bass.AP(tensor=env["b2_full"],
                                       offset=(oc // 4) * 128 * 512 + (oc % 4) * 128,
                                       ap=[[512, 128], [1, 128]])), f"fb{oc % 2}")
        for oc in range(NCH):
            w = woc[oc % 2]
            rr4 = oc // 4
            cc4 = oc % 4
            if oc < 2:
                P.wait("sync", "PE", h_pe)
            if len(mm_pe) >= 2:
                P.wait("sync", "PE", mm_pe[-2])
            P.dma("sync", lambda e, rr4=rr4, cc4=cc4, w=w: e.dma_start(
                out=w[:], in_=bass.AP(
                    tensor=env["wt_full"],
                    offset=(rr4 * D) * OSL + cc4 * 128,
                    ap=[[OSL, 128], [OSL * 128, 32], [1, 128]])), f"fw{oc % 2}")
            if oc == 0:
                P.wait("scalar", "PE", h_pe)
                b2dma(0)
            if oc + 1 < NCH:
                if len(mm_pe) >= 1:
                    P.wait("scalar", "PE", mm_pe[-1])
                b2dma(oc + 1)
            P.wslot("tensor", f"fw{oc % 2}")
            if len(evac_dv) >= 2:
                P.wait("tensor", "DV", evac_dv[-2])
            po = psO[oc % 2]
            for dc in range(NCH):
                P.do("tensor", lambda e, dc=dc, w=w, po=po: e.matmul(
                    po[:], w[:, dc * 128:(dc + 1) * 128], xln[:, dc * FT:(dc + 1) * FT],
                    start=(dc == 0), stop=False), inc="PE")
            P.wslot("tensor", f"fb{oc % 2}")
            P.do("tensor", lambda e, oc=oc, po=po, bc=b2c[oc % 2]: e.matmul(
                po[:], bc[:], lora[:], start=False, stop=True), inc="PE")
            P.wait("vector", "PE")
            P.do("vector", lambda e, oc=oc, po=po, tok=tok: e.scalar_tensor_tensor(
                xv(oc, tok, FT), po[:], env["bb_fm"][:, oc:oc + 1], xv(oc, tok, FT),
                ALU.add, ALU.add))
            evac_dv.append(P.cnt["DV"])
            mm_pe.append(P.cnt["PE"])
    p.close()


def emit_attn_b(nc, P, env):
    xv = env["xv"]
    ones_col_b, ones_row_b = env["ones_col_b"], env["ones_row_b"]
    rows, rows_b = env["rows"], env["rows_b"]

    p = ExitStack()
    zt = [p.enter_context(nc.sbuf_tensor(f"ab_z{i}", [128, T], BF16)) for i in range(2)]
    sqb = [p.enter_context(nc.sbuf_tensor(f"ab_sq{i}", [128, T], BF16)) for i in range(2)]
    rbf_b = p.enter_context(nc.sbuf_tensor("ab_rb", [128, T], BF16))
    mrb_b = p.enter_context(nc.sbuf_tensor("ab_mrb", [128, T], BF16))
    gbuf = p.enter_context(nc.sbuf_tensor("ab_g", [128, 512], F32))
    junk = p.enter_context(nc.sbuf_tensor("ab_junk", [128, 256], F32))
    psSx = p.enter_context(nc.psum_tensor("ab_psSx", [1, 1024], F32))
    psSq = p.enter_context(nc.psum_tensor("ab_psSq", [1, 1024], F32))
    psMisc = p.enter_context(nc.psum_tensor("ab_psM", [128, 1024], F32))

    for dc in range(NCH):
        z, sq = zt[dc % 2], sqb[dc % 2]
        if dc >= 2:
            P.wait("scalar", "PE")
        P.do("scalar", lambda e, dc=dc, z=z: e.activation(z[:], xv(dc), AF.Identity))
        P.chain("scalar")
        P.do("scalar", lambda e, z=z, sq=sq: e.activation(sq[:], z[:], AF.Square))
        P.wait("tensor", "AC")
        for th in range(2):
            P.do("tensor", lambda e, dc=dc, th=th, z=z: e.matmul(
                psSx[0:1, th * 512:(th + 1) * 512], ones_col_b[:],
                z[:, th * 512:(th + 1) * 512], start=(dc == 0), stop=(dc == NCH - 1)), inc="PE")
            P.do("tensor", lambda e, dc=dc, th=th, sq=sq: e.matmul(
                psSq[0:1, th * 512:(th + 1) * 512], ones_col_b[:],
                sq[:, th * 512:(th + 1) * 512], start=(dc == 0), stop=(dc == NCH - 1)), inc="PE")
    P.wait("vector", "PE")
    rr = _ln_rows(nc, P, env, T, psSx[0:1, 0:T], psSq[0:1, 0:T], "mr")
    P.chain("vector")
    P.do("vector", lambda e: e.tensor_copy(rows_b[0:1, 0:T], rr["r"]))
    P.do("vector", lambda e: e.tensor_copy(rows_b[0:1, T:2 * T], rr["t3"]))
    P.wait("tensor", "DV")
    for th in range(2):
        P.do("tensor", lambda e, th=th: e.matmul(
            psMisc[:, th * 512:(th + 1) * 512], ones_row_b[:],
            rows_b[0:1, th * 512:(th + 1) * 512], start=True, stop=True), inc="PE")
    P.wait("vector", "PE")
    P.do("vector", lambda e: e.tensor_copy(rbf_b[:], psMisc[:, 0:T]), inc="DV")
    P.wait("tensor", "DV")
    for th in range(2):
        P.do("tensor", lambda e, th=th: e.matmul(
            psMisc[:, th * 512:(th + 1) * 512], ones_row_b[:],
            rows_b[0:1, T + th * 512:T + (th + 1) * 512], start=True, stop=True), inc="PE")
    P.wait("vector", "PE")
    P.do("vector", lambda e: e.tensor_copy(mrb_b[:], psMisc[:, 0:T]), inc="DV")

    def z_tile(lt, z):
        P.do("scalar", lambda e, lt=lt, z=z: e.activation(z[:], xv(lt), AF.Identity), inc="AC")
        P.wait("vector", "AC")
        P.do("vector", lambda e, z=z: e.tensor_tensor(z[:], z[:], rbf_b[:], ALU.mult))
        P.chain("vector")
        P.do("vector", lambda e, z=z: e.tensor_tensor(z[:], z[:], mrb_b[:], ALU.subtract))

    pairs_o = [(n, m) for n in range(4) for m in range(n + 1, 4)]
    for lt in range(NCH):
        z = zt[lt % 2]
        if lt >= 2:
            P.wait("scalar", "DV")
            P.wait("scalar", "AC")
        z_tile(lt, z)
        P.chain("vector")
        P.wait("scalar", "DV")
        for n in range(4):
            P.chain("scalar")
            P.do("scalar", lambda e, lt=lt, n=n, z=z: e.activation(
                sqb[lt % 2][:, 0:256], z[:, n * 256:n * 256 + 256], AF.Square,
                accum_out=gbuf[:, lt * 16 + 5 * n:lt * 16 + 5 * n + 1]))
        for (n, m) in pairs_o:
            P.chain("vector")
            P.do("vector", lambda e, n=n, m=m, z=z: e.tensor_tensor(
                junk[:], z[:, n * 256:n * 256 + 256], z[:, m * 256:m * 256 + 256], ALU.mult))
            P.chain("vector")
            P.do("vector", lambda e, lt=lt, n=n, m=m: e.reduce_sum(
                gbuf[:, lt * 16 + n * 4 + m:lt * 16 + n * 4 + m + 1], junk[:], axis=AX.X))
    P.wait("vector", "AC")
    P.chain("vector")
    for lt in range(NCH):
        for (n, m) in pairs_o:
            P.do("vector", lambda e, lt=lt, n=n, m=m: e.tensor_copy(
                gbuf[:, lt * 16 + m * 4 + n:lt * 16 + m * 4 + n + 1],
                gbuf[:, lt * 16 + n * 4 + m:lt * 16 + n * 4 + m + 1]))
    P.chain("vector")
    P.do("vector", lambda e: e.tensor_scalar_mul(gbuf[:], gbuf[:], GSCALE), inc="DV")
    P.wait("sync", "DV")
    P.dma("sync", lambda e: e.dma_start(
        out=bass.AP(tensor=env["g_part"], offset=0, ap=[[16, 128], [16 * 128, 32], [1, 16]]),
        in_=gbuf[:].rearrange("p (a b) -> p a b", b=16)), "gpt")
    P.wslot("gpsimd", "gpt")
    P.do("gpsimd", lambda e: e.collective_compute(
        "AllReduce", ALU.add, replica_groups=[list(range(NCORE))],
        ins=[env["g_part"].ap()], outs=[env["g_full"].ap()]), inc="CC")
    P.wait("sync", "CC")
    P.dma("sync", lambda e: e.dma_start(
        out=gbuf[:].rearrange("p (a b) -> p a b", b=16),
        in_=bass.AP(tensor=env["g_full"], offset=0, ap=[[16, 128], [16 * 128, 32], [1, 16]])),
        "gld")
    g3 = gbuf[:].rearrange("p (a m) -> p a m", m=4)
    P.wslot("vector", "gld")
    P.do("vector", lambda e: e.reduce_max(junk[:, 0:128].unsqueeze(2), g3, axis=AX.X,
                                          negate=True))
    P.chain("vector")
    P.do("vector", lambda e: e.tensor_tensor(
        g3, g3, junk[:, 0:128].unsqueeze(2).broadcast_to([128, 128, 4]), ALU.add))
    P.wait("scalar", "DV")
    P.do("scalar", lambda e: e.activation(gbuf[:], gbuf[:], AF.Exp))
    P.wait("vector", "AC")
    P.do("vector", lambda e: e.reduce_sum(junk[:, 0:128].unsqueeze(2), g3, axis=AX.X))
    P.chain("vector")
    P.do("vector", lambda e: e.reciprocal(junk[:, 0:128], junk[:, 0:128]))
    P.chain("vector")
    P.do("vector", lambda e: e.tensor_tensor(
        g3, g3, junk[:, 0:128].unsqueeze(2).broadcast_to([128, 128, 4]), ALU.mult))
    P.chain("vector")
    P.do("vector", lambda e: e.tensor_scalar_add(
        gbuf[:].rearrange("p (a m) -> p a m", m=4)[:, :, 0:1].rearrange("p a b -> p (a b)") if False
        else bass.AP(tensor=gbuf[:].tensor, offset=gbuf[:].offset, ap=[[512, 128], [16, 32], [5, 4]]),
        bass.AP(tensor=gbuf[:].tensor, offset=gbuf[:].offset, ap=[[512, 128], [16, 32], [5, 4]]),
        1.0))

    msga = p.enter_context(nc.sbuf_tensor("ab_msg", [128, T], BF16))
    for lt in range(NCH):
        z = zt[lt % 2]
        if lt >= 2:
            P.wait("scalar", "DV")
        z_tile(lt, z)
        P.chain("vector")
        for m in range(4):
            for n in range(4):
                if m == 0:
                    P.do("vector", lambda e, lt=lt, n=n, z=z: e.tensor_scalar(
                        msga[:, n * 256:n * 256 + 256], z[:, 0:256],
                        gbuf[:, lt * 16 + n * 4:lt * 16 + n * 4 + 1], None, ALU.mult))
                else:
                    P.chain("vector", back=3)
                    P.do("vector", lambda e, lt=lt, n=n, m=m, z=z: e.scalar_tensor_tensor(
                        msga[:, n * 256:n * 256 + 256], z[:, m * 256:m * 256 + 256],
                        gbuf[:, lt * 16 + n * 4 + m:lt * 16 + n * 4 + m + 1],
                        msga[:, n * 256:n * 256 + 256], ALU.mult, ALU.add))
        P.chain("vector")
        P.do("vector", lambda e, lt=lt: e.tensor_tensor(
            xv(lt), msga[:], xv(lt), ALU.add))
    p.close()


def emit_phase5(nc, P, env, y_tok):
    xv = env["xv"]
    idF = env["idF"]
    p = ExitStack()
    ostage = [p.enter_context(nc.sbuf_tensor(f"p5_os{i}", [128, D], F32)) for i in range(2)]
    psO = p.enter_context(nc.psum_tensor("p5_psO", [128, 2048], F32))
    for tc in range(NTT):
        st = ostage[tc % 2]
        if tc >= 2:
            P.wslot("vector", f"ys{tc % 2}")
        for g4 in range(NCH // 4):
            if tc * NCH // 4 + g4 >= 1:
                P.wait("tensor", "DV")
            for q in range(4):
                dc = g4 * 4 + q
                P.do("tensor", lambda e, tc=tc, dc=dc, q=q: e.transpose(
                    psO[:, q * 512:q * 512 + 128], xv(dc, tc * 128, 128), idF[:]), inc="PE")
            P.wait("vector", "PE")
            P.do("vector", lambda e, g4=g4, st=st: e.tensor_copy(
                st[:].rearrange("p (a c) -> p a c", c=128)[:, g4 * 4:(g4 + 1) * 4, :],
                psO[:].rearrange("p (s c) -> p s c", c=512)[:, :, 0:128]), inc="DV")
        P.wait("sync", "DV")
        P.dma("sync", lambda e, tc=tc, st=st: e.dma_start(
            out=y_tok[tc * 128:(tc + 1) * 128, :], in_=st[:]), f"ys{tc % 2}")
    P.wslot("sync", "ys0")
    P.wslot("sync", "ys1")
    p.close()


_CACHE = {}


def kernel(**inputs):
    f32 = np.float32
    x = np.ascontiguousarray(inputs["L0_attn_a_x"], dtype=f32)
    Wq = np.ascontiguousarray(inputs["L0_attn_a_Wq"], dtype=f32)
    bq = np.ascontiguousarray(inputs["L0_attn_a_bq"], dtype=f32)
    Wk = np.ascontiguousarray(inputs["L0_attn_a_Wk"], dtype=f32).reshape(D, 128)
    bk = np.ascontiguousarray(inputs["L0_attn_a_bk"], dtype=f32).reshape(128, 1)
    Wv = np.ascontiguousarray(inputs["L0_attn_a_Wv"], dtype=f32).reshape(D, 128)
    bv = np.ascontiguousarray(inputs["L0_attn_a_bv"], dtype=f32).reshape(128, 1)
    Wb = np.ascontiguousarray(inputs["L0_ffn_base_W"], dtype=f32)
    bb = np.ascontiguousarray(inputs["L0_ffn_base_b"], dtype=f32).reshape(1, D)
    rW = np.ascontiguousarray(inputs["L0_ffn_router_W"], dtype=f32)
    rb = np.ascontiguousarray(inputs["L0_ffn_router_b"], dtype=f32).reshape(1, 15)
    A = np.ascontiguousarray(inputs["L0_ffn_A"], dtype=f32).reshape(128, D)
    Bm = np.ascontiguousarray(inputs["L0_ffn_B"], dtype=f32)
    emap = np.ascontiguousarray(inputs["L0_ffn_expert_map"], dtype=np.int32).reshape(15, 1)
    Wq2 = Wq.reshape(D, 128 * 32)

    key = "prog"
    if key not in _CACHE:
        _CACHE[key] = build_program()
    nc = _CACHE[key]

    in_maps = []
    for c in range(NCORE):
        xs = np.ascontiguousarray(x[:, c * SLOC:(c + 1) * SLOC, :].reshape(T, D))
        in_maps.append(dict(
            x_tok=xs,
            w_slice=np.ascontiguousarray(Wb[c * OSL:(c + 1) * OSL]),
            wq_slice=np.ascontiguousarray(Wq2[c * OSL:(c + 1) * OSL]),
            wk_in=Wk, wv_in=Wv, bq_in=bq, bk_in=bk, bv_in=bv, bb_in=bb,
            rw_in=rW, rb_in=rb, a_in=A,
            b_in=np.ascontiguousarray(Bm[:, c * 512:(c + 1) * 512, :].reshape(D, 16)),
            emap_in=emap,
        ))
    res = run_bass_kernel_spmd(nc, in_maps, core_ids=list(range(NCORE)))
    out = np.empty((B, S, D), dtype=np.float32)
    for c in range(NCORE):
        out[:, c * SLOC:(c + 1) * SLOC, :] = res.results[c]["y_tok"].reshape(B, SLOC, D)
    return out


if __name__ == "__main__":
    pass


# version 06 (final)
# speedup vs baseline: 1.3905x; 1.0234x over previous
import os
from contextlib import ExitStack

import numpy as np
import ml_dtypes
import concourse.bass as bass
import concourse.mybir as mybir
from concourse.bass_utils import run_bass_kernel_spmd

F32 = mybir.dt.float32
BF16 = mybir.dt.bfloat16
I32 = mybir.dt.int32
AF = mybir.ActivationFunctionType
ALU = mybir.AluOpType
AX = mybir.AxisListType

NCORE = 8
B, S, D = 4, 2048, 4096
SLOC = S // NCORE
T = B * SLOC
NCH = D // 128
NTT = T // 128
OSL = D // NCORE
FT = 512
NFT = T // FT
SCALE_A = float(1.0 / np.sqrt(128.0))
GSCALE = float(1.0 / np.sqrt(2048.0))
EPS = 1e-5

ENGINES = ("sync", "scalar", "vector", "tensor", "gpsimd")
SEMS = ["L", "GD", "PE", "DV", "AC", "PO", "CC"]


class Script:
    def __init__(self, nc, stack):
        self.nc = nc
        self.stack = stack
        self.steps = {e: [] for e in ENGINES}
        self.sem = {}
        self.cnt = {}

    def new_sem(self, name, stack=None):
        self.sem[name] = (stack or self.stack).enter_context(self.nc.semaphore(name))
        self.cnt[name] = 0

    def slot(self, name):
        name = "D" + name
        if name not in self.sem:
            self.new_sem(name)
        return name

    AUTO = {"vector": "DV", "scalar": "AC", "gpsimd": "PO"}

    def do(self, eng, fn, inc=None, inc_n=1):
        if inc is None and eng in self.AUTO:
            inc = self.AUTO[eng]
            inc_n = 1
        if inc is not None:
            self.cnt[inc] += inc_n
        sem = self.sem.get(inc)

        def step(e, fn=fn, sem=sem, n=inc_n if inc else 0):
            ins = fn(e)
            if sem is not None:
                ins.then_inc(sem, n)
        self.steps[eng].append(step)

    def chain(self, eng, back=0):
        sem = self.AUTO[eng]
        self.wait(eng, sem, self.cnt[sem] - back)

    def dma(self, eng, fn, slot):
        self.do(eng, fn, self.slot(slot), 16)

    def wslot(self, eng, name):
        self.wait(eng, self.slot(name))

    def wait(self, eng, sem, v=None):
        v = self.cnt[sem] if v is None else v
        if v <= 0:
            return
        s = self.sem[sem]
        self.steps[eng].append(lambda e, s=s, v=v: e.wait_ge(s, v))

    def barrier(self):
        snap = dict(self.cnt)
        for en in ENGINES:
            for s, v in snap.items():
                if s != "CC":
                    self.wait(en, s, v)

    def replay(self):
        with self.nc.Block() as block:
            for en in ENGINES:
                def runner(eng, en=en):
                    for st in self.steps[en]:
                        st(eng)
                getattr(block, en)(runner)


def _ln_rows(nc, P, env, n, sums_ps, sumsq_ps, third):
    rows = env["rows"]
    m = rows[:, 0:n]
    r = rows[:, n:2 * n]
    t3 = rows[:, 2 * n:3 * n]
    P.do("vector", lambda e: e.tensor_scalar_mul(m, sums_ps, 1.0 / D))
    P.do("vector", lambda e: e.tensor_scalar_mul(r, sumsq_ps, 1.0 / D))
    P.chain("vector")
    P.do("vector", lambda e: e.tensor_tensor(t3, m, m, ALU.mult))
    P.chain("vector")
    P.do("vector", lambda e: e.tensor_scalar_add(t3, t3, -EPS))
    P.chain("vector")
    P.do("vector", lambda e: e.tensor_tensor(r, r, t3, ALU.subtract))
    P.wait("scalar", "DV")
    P.do("scalar", lambda e: e.activation(r, r, AF.Sqrt))
    P.wait("vector", "AC")
    P.do("vector", lambda e: e.reciprocal(r, r))
    P.chain("vector")
    if third == "negm":
        P.do("vector", lambda e: e.tensor_scalar_mul(t3, m, -1.0))
    else:
        P.do("vector", lambda e: e.tensor_tensor(t3, m, r, ALU.mult))
    return dict(m=m, r=r, t3=t3)


def build_program():
    nc = bass.Bass("TRN2", target_bir_lowering=False)
    stop_phase = int(os.environ.get("KMK_STOP_PHASE", "99"))

    x_tok = nc.dram_tensor("x_tok", [T, D], F32, kind="ExternalInput").ap()
    w_slice = nc.dram_tensor("w_slice", [OSL, D], F32, kind="ExternalInput").ap()
    wq_slice = nc.dram_tensor("wq_slice", [OSL, D], F32, kind="ExternalInput").ap()
    wk_in = nc.dram_tensor("wk_in", [D, 128], F32, kind="ExternalInput").ap()
    wv_in = nc.dram_tensor("wv_in", [D, 128], F32, kind="ExternalInput").ap()
    bq_in = nc.dram_tensor("bq_in", [128, 32], F32, kind="ExternalInput").ap()
    bk_in = nc.dram_tensor("bk_in", [128, 1], F32, kind="ExternalInput").ap()
    bv_in = nc.dram_tensor("bv_in", [128, 1], F32, kind="ExternalInput").ap()
    bb_in = nc.dram_tensor("bb_in", [1, D], F32, kind="ExternalInput").ap()
    rw_in = nc.dram_tensor("rw_in", [15, D], F32, kind="ExternalInput").ap()
    rb_in = nc.dram_tensor("rb_in", [1, 15], F32, kind="ExternalInput").ap()
    a_in = nc.dram_tensor("a_in", [128, D], F32, kind="ExternalInput").ap()
    b_in = nc.dram_tensor("b_in", [D, 16], F32, kind="ExternalInput").ap()
    emap_in = nc.dram_tensor("emap_in", [15, 1], I32, kind="ExternalInput").ap()
    y_tok = nc.dram_tensor("y_tok", [T, D], F32, kind="ExternalOutput").ap()
    dbg = nc.dram_tensor("dbg", [128, 18432], F32, kind="ExternalOutput").ap() \
        if os.environ.get("KMK_DBG") else None

    wt_part = nc.dram_tensor("wt_part", [D, OSL], BF16)
    wqe_part = nc.dram_tensor("wqe_part", [OSL, 128], BF16)
    wsum_part = nc.dram_tensor("wsum_part", [OSL, 1], F32)
    wqs_part = nc.dram_tensor("wqs_part", [1, 128], F32)
    wt_full = nc.dram_tensor("wt_full", [NCORE * D, OSL], BF16)
    wqe_full = nc.dram_tensor("wqe_full", [D, 128], BF16)
    wsum_full = nc.dram_tensor("wsum_full", [D, 1], F32)
    wqs_full = nc.dram_tensor("wqs_full", [1, 128], F32)
    wk_bf = nc.dram_tensor("wk_bf", [D, 128], BF16)
    wv_bf = nc.dram_tensor("wv_bf", [D, 128], BF16)
    a_bf = nc.dram_tensor("a_bf", [128, D], BF16)
    b2_part = nc.dram_tensor("b2_part", [128, 512], BF16)
    b2_full = nc.dram_tensor("b2_full", [8 * 128, 512], BF16)
    g_part = nc.dram_tensor("g_part", [D, 16], F32)
    g_full = nc.dram_tensor("g_full", [D, 16], F32)
    rwt_dram = nc.dram_tensor("rwt_dram", [128, 15 * NCH], BF16)
    a2t_dram = nc.dram_tensor("a2t_dram", [D, 128], BF16)

    root = ExitStack()
    with root:
        P = Script(nc, root)
        for s in SEMS:
            P.new_sem(s, root)

        xfm = root.enter_context(nc.sbuf_tensor("xfm", [128, NCH * T], F32))
        idF = root.enter_context(nc.sbuf_tensor("idF", [128, 128], F32))
        idB = root.enter_context(nc.sbuf_tensor("idB", [128, 128], BF16))
        ones_col_b = root.enter_context(nc.sbuf_tensor("ones_col_b", [128, 1], BF16))
        ones_row_b = root.enter_context(nc.sbuf_tensor("ones_row_b", [1, 128], BF16))

        rb_bc = root.enter_context(nc.sbuf_tensor("rb_bc", [128, 15], F32))
        onehotT = root.enter_context(nc.sbuf_tensor("onehotT", [15, 8], BF16))
        eexp = root.enter_context(nc.sbuf_tensor("eexp", [8, 128], BF16))
        bb_fm = root.enter_context(nc.sbuf_tensor("bb_fm", [128, 32], F32))
        ones_f1 = root.enter_context(nc.sbuf_tensor("ones_f1", [1, 1], F32))
        bqe = root.enter_context(nc.sbuf_tensor("bqe", [128, 1], F32))
        bkc = root.enter_context(nc.sbuf_tensor("bkc", [128, 1], F32))
        bvc = root.enter_context(nc.sbuf_tensor("bvc", [128, 1], F32))
        wsums = root.enter_context(nc.sbuf_tensor("wsums", [1, 3 * 128], BF16))
        m0rep = root.enter_context(nc.sbuf_tensor("m0rep", [4, 128], BF16))
        rows = root.enter_context(nc.sbuf_tensor("rows", [1, 3072], F32))
        rows_b = root.enter_context(nc.sbuf_tensor("rows_b", [1, 2112], BF16))

        def xv(dc, lo=0, n=T):
            return xfm[:, dc * T + lo: dc * T + lo + n]

        env = dict(rows=rows, rows_b=rows_b, xv=xv, xfm=xfm, idF=idF, idB=idB,
                   ones_col_b=ones_col_b, ones_row_b=ones_row_b,
                   rb_bc=rb_bc, onehotT=onehotT, eexp=eexp, bb_fm=bb_fm, ones_f1=ones_f1,
                   rwt_dram=rwt_dram, a2t_dram=a2t_dram, b2_part=b2_part,
                   bqe=bqe, bkc=bkc, bvc=bvc, wsums=wsums, m0rep=m0rep,
                   wt_full=wt_full, wqe_full=wqe_full, wsum_full=wsum_full,
                   wqs_full=wqs_full, wk_bf=wk_bf, wv_bf=wv_bf, a_bf=a_bf,
                   b2_full=b2_full, g_part=g_part, g_full=g_full)

        env["dbg"] = dbg
        emit_phase0(nc, P, env, x_tok, w_slice, wq_slice, wk_in, wv_in, bq_in,
                    bk_in, bv_in, bb_in, rw_in, rb_in, a_in, b_in, emap_in,
                    wt_part, wqe_part, wsum_part, wqs_part)
        P.barrier()
        if stop_phase >= 1:
            emit_attn_a(nc, P, env)
            P.barrier()
        if stop_phase >= 2:
            emit_ffn(nc, P, env, 0)
            P.barrier()
        if stop_phase >= 3:
            emit_attn_b(nc, P, env)
            P.barrier()
        if stop_phase >= 4:
            emit_ffn(nc, P, env, 1)
            P.barrier()
        emit_phase5(nc, P, env, y_tok)
        P.replay()
    return nc


def emit_phase0(nc, P, env, x_tok, w_slice, wq_slice, wk_in, wv_in, bq_in,
                bk_in, bv_in, bb_in, rw_in, rb_in, a_in, b_in, emap_in,
                wt_part, wqe_part, wsum_part, wqs_part):
    xv = env["xv"]
    idF, idB = env["idF"], env["idB"]
    ones_col_b, ones_row_b = env["ones_col_b"], env["ones_row_b"]
    rows_b = env["rows_b"]

    p = ExitStack()
    wqout = p.enter_context(nc.sbuf_tensor("p0_wqo", [128, 128], F32))
    wqout_b = p.enter_context(nc.sbuf_tensor("p0_wqob", [128, 128], BF16))
    smt = p.enter_context(nc.sbuf_tensor("p0_smt", [128, 16], F32))
    castA = p.enter_context(nc.sbuf_tensor("p0_ca", [128, D], BF16))
    rw_sb = p.enter_context(nc.sbuf_tensor("p0_rw", [15, D], BF16))
    pa = ExitStack()
    xstage = [pa.enter_context(nc.sbuf_tensor(f"p0_xs{i}", [128, D], F32)) for i in range(2)]
    bqt = pa.enter_context(nc.sbuf_tensor("p0_bqt", [128, 32], F32))
    bb_rowt = pa.enter_context(nc.sbuf_tensor("p0_bbrow", [1, 2048], F32))
    emap_f = pa.enter_context(nc.sbuf_tensor("p0_em", [15, 1], I32))
    emap_g = pa.enter_context(nc.sbuf_tensor("p0_emg", [15, 1], F32))
    psT = p.enter_context(nc.psum_tensor("p0_psT", [128, 2048], F32))
    psTb = p.enter_context(nc.psum_tensor("p0_psTb", [128, 2048], BF16))
    psK1 = p.enter_context(nc.psum_tensor("p0_psK1", [128, 512], F32))
    psWS = p.enter_context(nc.psum_tensor("p0_psWS", [1, 512], F32))

    P.do("vector", lambda e: e.memset(idF[:], 1.0), inc="DV")
    P.do("vector", lambda e: e.memset(idB[:], 1.0), inc="DV")
    P.wait("gpsimd", "DV")
    P.do("gpsimd", lambda e: e.affine_select(out=idF[:], in_=idF[:], pattern=[[1, 128]],
                                             compare_op=ALU.is_equal, fill=0.0, base=0,
                                             channel_multiplier=-1), inc="PO")
    P.do("gpsimd", lambda e: e.affine_select(out=idB[:], in_=idB[:], pattern=[[1, 128]],
                                             compare_op=ALU.is_equal, fill=0.0, base=0,
                                             channel_multiplier=-1), inc="PO")
    P.wait("tensor", "PO")
    P.do("vector", lambda e: e.memset(ones_col_b[:], 1.0))
    P.do("vector", lambda e: e.memset(ones_row_b[:], 1.0))
    P.do("vector", lambda e: e.memset(env["eexp"][:], 0.0))
    P.do("vector", lambda e: e.memset(env["m0rep"][:], 0.0))
    P.do("vector", lambda e: e.memset(rows_b[:, 2048:2064], 2.0))
    P.do("vector", lambda e: e.memset(rows_b[:, 2064:2096], 1.0), inc="DV")
    P.wait("gpsimd", "DV")
    for ee in range(8):
        P.dma("gpsimd", lambda e, ee=ee: e.dma_start(
            out=env["eexp"][ee:ee + 1, 16 * ee:16 * ee + 16], in_=rows_b[0:1, 2048:2064]), "cst")
    for dd in range(4):
        P.dma("gpsimd", lambda e, dd=dd: e.dma_start(
            out=env["m0rep"][dd:dd + 1, 32 * dd:32 * dd + 32], in_=rows_b[0:1, 2064:2096]), "cst")

    P.dma("sync", lambda e: e.dma_start(out=bqt[:], in_=bq_in), "sml")
    P.dma("sync", lambda e: e.dma_start(out=env["bkc"][:], in_=bk_in), "sml")
    P.dma("sync", lambda e: e.dma_start(out=env["bvc"][:], in_=bv_in), "sml")
    P.dma("sync", lambda e: e.dma_start(out=bb_rowt[:], in_=bb_in[0:1, 0:2048]), "sml")
    P.dma("sync", lambda e: e.dma_start(out=env["rb_bc"][0:1, :], in_=rb_in), "sml")
    P.dma("sync", lambda e: e.dma_start(out=emap_f[:], in_=emap_in), "sml")
    P.wslot("vector", "sml")
    P.do("vector", lambda e: e.memset(env["ones_f1"][:], 1.0))
    P.do("vector", lambda e: e.reduce_sum(env["bqe"][:], bqt[:], axis=AX.X))
    P.chain("vector")
    P.do("vector", lambda e: e.tensor_scalar_mul(env["bqe"][:], env["bqe"][:], SCALE_A))
    P.do("vector", lambda e: e.tensor_copy(emap_g[:], emap_f[:]))
    P.chain("vector")
    for ee in range(8):
        P.do("vector", lambda e, ee=ee: e.tensor_scalar(env["onehotT"][:, ee:ee + 1], emap_g[:],
                                                        float(ee), None, ALU.is_equal))
    P.do("vector", lambda e: e.tensor_copy(rows_b[0:1, 2096:2111], env["rb_bc"][0:1, 0:15]), inc="DV")
    P.wait("tensor", "DV")
    P.do("tensor", lambda e: e.matmul(psK1[0:128, 0:15], ones_row_b[0:1, 0:128],
                                      rows_b[0:1, 2096:2111], start=True, stop=True), inc="PE")
    P.wait("vector", "PE")
    P.do("vector", lambda e: e.tensor_copy(env["rb_bc"][:], psK1[:, 0:15]), inc="DV")
    P.wait("tensor", "DV")
    P.wslot("tensor", "sml")
    for oc in range(16):
        P.do("tensor", lambda e, oc=oc: e.matmul(
            psK1[:, 128 + oc:128 + oc + 1], bb_rowt[0:1, oc * 128:(oc + 1) * 128],
            env["ones_f1"][:], start=True, stop=True), inc="PE")
    P.wait("sync", "PE")
    P.dma("sync", lambda e: e.dma_start(out=bb_rowt[:], in_=bb_in[0:1, 2048:4096]), "sml")
    P.wslot("tensor", "sml")
    for oc in range(16):
        P.do("tensor", lambda e, oc=oc: e.matmul(
            psK1[:, 144 + oc:144 + oc + 1], bb_rowt[0:1, oc * 128:(oc + 1) * 128],
            env["ones_f1"][:], start=True, stop=True), inc="PE")
    P.wait("vector", "PE")
    P.do("vector", lambda e: e.tensor_copy(env["bb_fm"][:], psK1[:, 128:160]), inc="DV")

    for (src, dst, coloff) in ((wk_in, env["wk_bf"], 1), (wv_in, env["wv_bf"], 2)):
        P.wslot("gpsimd", "gc")
        P.wait("gpsimd", "DV")
        P.dma("gpsimd", lambda e, src=src: e.dma_start(
            out=castA[:], in_=src.rearrange("(a p) d -> p a d", p=128)), "gc")
        P.wslot("gpsimd", "gc")
        P.dma("gpsimd", lambda e, dst=dst: e.dma_start(
            out=dst.ap().rearrange("(a p) d -> p a d", p=128), in_=castA[:]), "gc")
        P.wslot("vector", "gc")
        P.do("vector", lambda e: e.reduce_sum(
            wqout[:, 0:128].unsqueeze(2),
            castA[:].rearrange("p (a d) -> p d a", a=32), axis=AX.X))
        P.chain("vector")
        P.do("vector", lambda e: e.tensor_copy(wqout_b[:], wqout[:, 0:128]))
        P.wait("tensor", "DV")
        P.do("tensor", lambda e: e.matmul(psWS[0:1, 128:256], ones_col_b[:], wqout_b[:],
                                          start=True, stop=True), inc="PE")
        P.wait("vector", "PE")
        P.do("vector", lambda e, coloff=coloff: e.tensor_copy(
            env["wsums"][0:1, coloff * 128:(coloff + 1) * 128], psWS[0:1, 128:256]), inc="DV")

    P.wslot("gpsimd", "gc")
    P.wait("gpsimd", "DV")
    P.dma("gpsimd", lambda e: e.dma_start(out=castA[:], in_=a_in), "gc")
    P.wslot("gpsimd", "gc")
    P.dma("gpsimd", lambda e: e.dma_start(out=env["a_bf"].ap(), in_=castA[:]), "gc")
    P.wslot("gpsimd", "gc")
    P.dma("gpsimd", lambda e: e.dma_start(out=castA[:, 0:512],
                                          in_=b_in.rearrange("(a p) r -> p a r", p=128)), "gc")
    P.dma("gpsimd", lambda e: e.dma_start(out=rw_sb[:], in_=rw_in), "gc")
    dv_tile_done = []
    for tc in range(NTT):
        st = xstage[tc % 2]
        if tc >= 2:
            P.wait("sync", "DV", dv_tile_done[tc - 2])
        P.dma("sync", lambda e, tc=tc, st=st: e.dma_start(
            out=st[:], in_=x_tok[tc * 128:(tc + 1) * 128, :]), f"xs{tc % 2}")
        P.wslot("tensor", f"xs{tc % 2}")
        for g4 in range(NCH // 4):
            if tc * NCH // 4 + g4 >= 1:
                P.wait("tensor", "DV")
            for q in range(4):
                dc = g4 * 4 + q
                P.do("tensor", lambda e, dc=dc, q=q, st=st: e.transpose(
                    psT[:, q * 512:q * 512 + 128],
                    st[:, dc * 128:(dc + 1) * 128], idF[:]), inc="PE")
            P.wait("vector", "PE")
            P.do("vector", lambda e, tc=tc, g4=g4: e.tensor_copy(
                bass.AP(tensor=env["xfm"][:].tensor, offset=(g4 * 4) * T + tc * 128,
                        ap=[[NCH * T, 128], [T, 4], [1, 128]]),
                psT[:].rearrange("p (s c) -> p s c", c=512)[:, :, 0:128]), inc="DV")
        dv_tile_done.append(P.cnt["DV"])
        P.wait("tensor", "DV")

    pa.close()
    pb = ExitStack()
    wtileW = [pb.enter_context(nc.sbuf_tensor(f"p0_w{i}", [128, D], BF16)) for i in range(2)]
    wstage = [pb.enter_context(nc.sbuf_tensor(f"p0_wst{i}", [128, 1024], BF16)) for i in range(2)]
    wqtile = pb.enter_context(nc.sbuf_tensor("p0_wq", [128, 2048], F32))

    for i in range(4):
        for jh in range(2):
            P.wait("sync", "DV")
            P.dma("sync", lambda e, i=i, jh=jh: e.dma_start(
                out=wqtile[:], in_=wq_slice[i * 128:(i + 1) * 128,
                                            jh * 2048:(jh + 1) * 2048]), "wqt")
            P.wslot("vector", "wqt")
            P.do("vector", lambda e, jh=jh: e.reduce_sum(
                wqout[:, jh * 64:(jh + 1) * 64].unsqueeze(2),
                wqtile[:].rearrange("p (d h) -> p d h", h=32), axis=AX.X), inc="DV")
        P.chain("vector")
        P.do("vector", lambda e: e.tensor_scalar_mul(wqout[:], wqout[:], SCALE_A))
        P.chain("vector")
        P.wslot("vector", "gst")
        P.wait("vector", "PE")
        P.do("vector", lambda e: e.tensor_copy(wqout_b[:], wqout[:]))
        P.wait("gpsimd", "DV")
        P.dma("gpsimd", lambda e, i=i: e.dma_start(out=wqe_part.ap()[i * 128:(i + 1) * 128, :],
                                                   in_=wqout_b[:]), "gst")
    P.wslot("gpsimd", "gst")
    P.wait("gpsimd", "DV")
    P.dma("gpsimd", lambda e: e.dma_start(out=wqtile[:].bitcast(BF16)[:, 0:512],
                                          in_=wqe_part.ap().rearrange("(a p) d -> p a d", p=128)),
          "gc")
    P.wslot("tensor", "gc")
    for g4 in range(4):
        P.do("tensor", lambda e, g4=g4: e.matmul(
            psWS[0:1, 0:128], ones_col_b[:], wqtile[:].bitcast(BF16)[:, g4 * 128:(g4 + 1) * 128],
            start=(g4 == 0), stop=(g4 == 3)), inc="PE")
    P.wait("vector", "PE")
    P.do("vector", lambda e: e.tensor_copy(wqout[0:1, 0:128], psWS[0:1, 0:128]), inc="DV")
    P.wait("gpsimd", "DV")
    P.dma("gpsimd", lambda e: e.dma_start(out=wqs_part.ap(), in_=wqout[0:1, 0:128]), "gc")
    P.wslot("gpsimd", "gst")
    P.wslot("gpsimd", "gc")
    P.do("gpsimd", lambda e: e.collective_compute(
        "AllGather", ALU.bypass, replica_groups=[list(range(NCORE))],
        ins=[wqe_part.ap()], outs=[env["wqe_full"].ap()]), inc="CC")
    P.do("gpsimd", lambda e: e.collective_compute(
        "AllReduce", ALU.add, replica_groups=[list(range(NCORE))],
        ins=[wqs_part.ap()], outs=[env["wqs_full"].ap()]), inc="CC")

    for oh in range(2):
        P.wait("gpsimd", "PE")
        for ot in range(2):
            P.dma("gpsimd", lambda e, oh=oh, ot=ot: e.dma_start(
                out=wtileW[ot][:], in_=w_slice[(oh * 2 + ot) * 128:(oh * 2 + ot + 1) * 128, :]),
                "wtl")
        P.wslot("vector", "wtl")
        for ot in range(2):
            P.do("vector", lambda e, oh=oh, ot=ot: e.reduce_sum(
                smt[:, oh * 2 + ot:oh * 2 + ot + 1], wtileW[ot][:], axis=AX.X), inc="DV")
        P.wslot("tensor", "wtl")
        P.wait("tensor", "DV")
        for dc in range(NCH):
            ws = wstage[(dc // 4) % 2]
            for ot in range(2):
                P.do("tensor", lambda e, dc=dc, ot=ot: e.transpose(
                    psTb[:, ot * 128:(ot + 1) * 128],
                    wtileW[ot][:, dc * 128:(dc + 1) * 128], idB[:]), inc="PE")
            P.wait("vector", "PE")
            if dc % 4 == 0:
                P.wslot("vector", "wst")
            P.do("vector", lambda e, dc=dc, ws=ws: e.tensor_copy(
                ws[:, (dc % 4) * 256:(dc % 4) * 256 + 256], psTb[:, 0:256]))
            P.wait("tensor", "DV")
            if dc % 4 == 3:
                P.wait("sync", "DV")
                P.dma("sync", lambda e, dc=dc, oh=oh, ws=ws: e.dma_start(
                    out=bass.AP(tensor=wt_part, offset=(dc - 3) * 128 * OSL + oh * 256,
                                ap=[[OSL, 128], [128 * OSL, 4], [1, 256]]),
                    in_=ws[:].rearrange("p (g c) -> p g c", c=256)), "wst")
    pb.close()
    P.wait("gpsimd", "DV")
    P.dma("gpsimd", lambda e: e.dma_start(
        out=wsum_part.ap().rearrange("(a p) b -> p a b", p=128), in_=smt[:, 0:4],
        allow_slow_non_contiguous=True), "gc")

    P.wslot("tensor", "gc")
    rwt_loc = p.enter_context(nc.sbuf_tensor("p0_rwt", [128, 15 * NCH], BF16))
    for dc in range(NCH):
        P.do("tensor", lambda e, dc=dc: e.transpose(
            psTb[:, (dc % 2) * 1024:(dc % 2) * 1024 + 15],
            rw_sb[:, dc * 128:(dc + 1) * 128], idB[0:15, 0:15]), inc="PE")
        P.wait("vector", "PE")
        P.do("vector", lambda e, dc=dc: e.tensor_copy(
            rwt_loc[:, dc * 15:(dc + 1) * 15],
            psTb[:, (dc % 2) * 1024:(dc % 2) * 1024 + 15]), inc="DV")
        if dc >= 1:
            P.wait("tensor", "DV", P.cnt["DV"] - 1)
    P.wait("gpsimd", "DV")
    P.dma("gpsimd", lambda e: e.dma_start(out=env["rwt_dram"].ap(), in_=rwt_loc[:]), "gc")
    b2stg = p.enter_context(nc.sbuf_tensor("p0_b2s", [16, 1024], BF16))
    for ocl in range(4):
        if ocl >= 1:
            P.wslot("tensor", "gc")
        for ee in range(8):
            a_idx = ee * 4 + ocl
            P.do("tensor", lambda e, a_idx=a_idx: e.transpose(
                psTb[0:16, 0:128], castA[:, a_idx * 16:(a_idx + 1) * 16], idB[:]), inc="PE")
            P.wait("vector", "PE")
            P.do("vector", lambda e, ee=ee: e.tensor_copy(
                b2stg[:, ee * 128:(ee + 1) * 128], psTb[0:16, 0:128]), inc="DV")
            P.wait("tensor", "DV")
        P.wait("gpsimd", "DV")
        P.dma("gpsimd", lambda e, ocl=ocl: e.dma_start(
            out=bass.AP(tensor=env["b2_part"], offset=ocl * 128,
                        ap=[[512, 16], [16 * 512, 8], [1, 128]]),
            in_=b2stg[:].rearrange("p (a c) -> p a c", a=8)), "gc")

    P.wslot("gpsimd", "gc")
    P.wait("gpsimd", "DV")
    P.dma("gpsimd", lambda e: e.dma_start(out=castA[:], in_=env["a_bf"].ap()), "gc")
    P.wslot("tensor", "gc")
    for dc in range(NCH):
        P.do("tensor", lambda e, dc=dc: e.transpose(
            psTb[:, (dc % 2) * 1024:(dc % 2) * 1024 + 128],
            castA[:, dc * 128:(dc + 1) * 128], idB[:]), inc="PE")
        P.wait("vector", "PE")
        P.wslot("vector", "gc")
        P.do("vector", lambda e, dc=dc: e.tensor_copy(
            wqout_b[:], psTb[:, (dc % 2) * 1024:(dc % 2) * 1024 + 128]))
        P.wait("gpsimd", "DV")
        P.dma("gpsimd", lambda e, dc=dc: e.dma_start(
            out=env["a2t_dram"].ap()[dc * 128:(dc + 1) * 128, :], in_=wqout_b[:]), "gc")
        P.wait("tensor", "DV")
    for sl in ("gc", "wst", "wtl"):
        P.wslot("gpsimd", sl)
    for (pi, po) in ((wt_part, env["wt_full"]), (wsum_part, env["wsum_full"]),
                     (env["b2_part"], env["b2_full"])):
        P.do("gpsimd", lambda e, pi=pi, po=po: e.collective_compute(
            "AllGather", ALU.bypass, replica_groups=[list(range(NCORE))],
            ins=[pi.ap()], outs=[po.ap()]), inc="CC")
    P.wait("sync", "CC", 2)
    P.dma("sync", lambda e: e.dma_start(out=env["rows"][0:1, 0:128],
                                        in_=env["wqs_full"].ap()), "wqs")
    P.wslot("vector", "wqs")
    P.do("vector", lambda e: e.tensor_scalar_mul(env["rows"][0:1, 0:128],
                                                 env["rows"][0:1, 0:128], SCALE_A))
    P.chain("vector")
    P.do("vector", lambda e: e.tensor_copy(env["wsums"][0:1, 0:128],
                                           env["rows"][0:1, 0:128]))
    p.close()


def emit_attn_a(nc, P, env):
    xv = env["xv"]
    idF, idB = env["idF"], env["idB"]
    ones_col_b, ones_row_b = env["ones_col_b"], env["ones_row_b"]
    rows, rows_b = env["rows"], env["rows_b"]

    p = ExitStack()
    xb = [p.enter_context(nc.sbuf_tensor(f"aa_xb{i}", [128, T], BF16)) for i in range(2)]
    sqb = [p.enter_context(nc.sbuf_tensor("aa_sq0", [128, T], BF16))] * 2
    wq_t = [p.enter_context(nc.sbuf_tensor(f"aa_wq{i}", [128, 512], BF16)) for i in range(2)]
    wk_t = [p.enter_context(nc.sbuf_tensor(f"aa_wk{i}", [128, 512], BF16)) for i in range(2)]
    wv_t = [p.enter_context(nc.sbuf_tensor(f"aa_wv{i}", [128, 512], BF16)) for i in range(2)]
    afm = p.enter_context(nc.sbuf_tensor("aa_afm", [128, T], F32))
    kfm = p.enter_context(nc.sbuf_tensor("aa_kfm", [128, T], F32))
    vfm = p.enter_context(nc.sbuf_tensor("aa_vfm", [128, T], F32))
    rbf = p.enter_context(nc.sbuf_tensor("aa_rbf", [128, T], F32))

    ps1 = ExitStack()
    psSx = ps1.enter_context(nc.psum_tensor("aa_psSx", [1, 1024], F32))
    psSq = ps1.enter_context(nc.psum_tensor("aa_psSq", [1, 1024], F32))
    for dc in range(NCH):
        bb, sq = xb[dc % 2], sqb[dc % 2]
        if dc >= 1:
            P.wait("scalar", "PE")
        P.do("scalar", lambda e, dc=dc, bb=bb: e.activation(bb[:], xv(dc), AF.Identity))
        P.chain("scalar")
        P.do("scalar", lambda e, bb=bb, sq=sq: e.activation(sq[:], bb[:], AF.Square))
        P.wait("tensor", "AC")
        for th in range(2):
            P.do("tensor", lambda e, dc=dc, th=th, bb=bb: e.matmul(
                psSx[0:1, th * 512:(th + 1) * 512], ones_col_b[:],
                bb[:, th * 512:(th + 1) * 512],
                start=(dc == 0), stop=(dc == NCH - 1)), inc="PE")
            P.do("tensor", lambda e, dc=dc, th=th, sq=sq: e.matmul(
                psSq[0:1, th * 512:(th + 1) * 512], ones_col_b[:],
                sq[:, th * 512:(th + 1) * 512],
                start=(dc == 0), stop=(dc == NCH - 1)), inc="PE")
    P.wait("vector", "PE")
    rr = _ln_rows(nc, P, env, T, psSx[0:1, 0:T], psSq[0:1, 0:T], "negm")
    P.chain("vector")
    P.do("vector", lambda e: e.tensor_copy(rows_b[0:1, 0:T], rr["t3"]))
    ps1.close()

    ps2 = ExitStack()
    psR = ps2.enter_context(nc.psum_tensor("aa_psR", [128, 1024], F32))
    P.do("vector", lambda e: e.tensor_copy(rows_b[0:1, T:2 * T], rr["r"]), inc="DV")
    P.wait("tensor", "DV")
    for th in range(2):
        P.do("tensor", lambda e, th=th: e.matmul(
            psR[:, th * 512:(th + 1) * 512], ones_row_b[0:1, 0:128],
            rows_b[0:1, T + th * 512:T + (th + 1) * 512], start=True, stop=True), inc="PE")
    P.wait("vector", "PE")
    P.do("vector", lambda e: e.tensor_copy(rbf[:], psR[:, 0:T]), inc="DV")
    ps2.close()
    P.wait("tensor", "DV")

    P.wait("sync", "CC", 2)
    ps3 = ExitStack()
    psQ = ps3.enter_context(nc.psum_tensor("aa_psQ", [128, 1024], F32))
    psK = ps3.enter_context(nc.psum_tensor("aa_psK", [128, 1024], F32))
    psV = ps3.enter_context(nc.psum_tensor("aa_psV", [128, 1024], F32))
    psWSq = ps3.enter_context(nc.psum_tensor("aa_psWSq", [1, 512], F32))
    for g4 in range(8):
        wq, wk, wv = wq_t[g4 % 2], wk_t[g4 % 2], wv_t[g4 % 2]
        if g4 >= 2:
            P.wait("sync", "PE")
            P.wait("scalar", "PE")
        P.dma("sync", lambda e, g4=g4, wq=wq: e.dma_start(
            out=wq[:], in_=env["wqe_full"].ap().rearrange("(g p) d -> p g d", p=128)[
                :, g4 * 4:(g4 + 1) * 4, :]), f"aw{g4 % 2}")
        P.dma("sync", lambda e, g4=g4, wk=wk: e.dma_start(
            out=wk[:], in_=env["wk_bf"].ap().rearrange("(g p) d -> p g d", p=128)[
                :, g4 * 4:(g4 + 1) * 4, :]), f"aw{g4 % 2}")
        P.dma("sync", lambda e, g4=g4, wv=wv: e.dma_start(
            out=wv[:], in_=env["wv_bf"].ap().rearrange("(g p) d -> p g d", p=128)[
                :, g4 * 4:(g4 + 1) * 4, :]), f"aw{g4 % 2}")
        for i4 in range(4):
            dc = g4 * 4 + i4
            bb = xb[dc % 2]
            if dc >= 2:
                P.wait("scalar", "PE")
            P.do("scalar", lambda e, dc=dc, bb=bb: e.activation(bb[:], xv(dc), AF.Identity),
                 inc="AC")
            P.wait("tensor", "AC")
            P.wslot("tensor", f"aw{g4 % 2}")
            first, last = dc == 0, False
            for th in range(2):
                sl = slice(th * 512, (th + 1) * 512)
                P.do("tensor", lambda e, dc=dc, th=th, bb=bb, wq=wq, i4=i4: e.matmul(
                    psQ[:, th * 512:(th + 1) * 512], wq[:, i4 * 128:(i4 + 1) * 128],
                    bb[:, th * 512:(th + 1) * 512], start=(dc == 0), stop=False), inc="PE")
                P.do("tensor", lambda e, dc=dc, th=th, bb=bb, wk=wk, i4=i4: e.matmul(
                    psK[:, th * 512:(th + 1) * 512], wk[:, i4 * 128:(i4 + 1) * 128],
                    bb[:, th * 512:(th + 1) * 512], start=(dc == 0), stop=False), inc="PE")
                P.do("tensor", lambda e, dc=dc, th=th, bb=bb, wv=wv, i4=i4: e.matmul(
                    psV[:, th * 512:(th + 1) * 512], wv[:, i4 * 128:(i4 + 1) * 128],
                    bb[:, th * 512:(th + 1) * 512], start=(dc == 0), stop=False), inc="PE")
    for (psX, co) in ((psQ, 0), (psK, 1), (psV, 2)):
        for th in range(2):
            P.do("tensor", lambda e, psX=psX, co=co, th=th: e.matmul(
                psX[:, th * 512:(th + 1) * 512], env["wsums"][0:1, co * 128:(co + 1) * 128],
                rows_b[0:1, th * 512:(th + 1) * 512], start=False, stop=True), inc="PE")
    P.wait("vector", "PE")
    for (psX, dst, bias) in ((psQ, afm, env["bqe"]), (psK, kfm, env["bkc"]), (psV, vfm, env["bvc"])):
        P.do("vector", lambda e, psX=psX, dst=dst: e.tensor_tensor(dst[:], psX[:, 0:T], rbf[:],
                                                                   ALU.mult))
        P.chain("vector")
        P.do("vector", lambda e, dst=dst, bias=bias: e.tensor_scalar(
            dst[:], dst[:], bias[:], None, ALU.add))
    ps3.close()

    aT, kT, vT = afm, kfm, vfm
    ps4 = ExitStack()
    psTr = ps4.enter_context(nc.psum_tensor("aa_psTr", [128, 1024], F32))
    P.wait("tensor", "DV")
    first_tp = True
    for (src, dst) in ((afm, aT), (kfm, kT), (vfm, vT)):
        for tc in range(NTT):
            if not first_tp:
                P.wait("tensor", "DV", P.cnt["DV"] - 1)
            first_tp = False
            P.do("tensor", lambda e, src=src, tc=tc: e.transpose(
                psTr[:, (tc % 2) * 512:(tc % 2) * 512 + 128],
                src[:, tc * 128:(tc + 1) * 128], idF[:]), inc="PE")
            P.wait("vector", "PE")
            P.do("vector", lambda e, dst=dst, tc=tc: e.tensor_copy(
                dst[:, tc * 128:(tc + 1) * 128],
                psTr[:, (tc % 2) * 512:(tc % 2) * 512 + 128]), inc="DV")
    ps4.close()

    numT = p.enter_context(nc.sbuf_tensor("aa_numT", [128, T], F32))
    zT = p.enter_context(nc.sbuf_tensor("aa_zT", [128, T], F32))
    sc = [p.enter_context(nc.sbuf_tensor(f"aa_sc{i}", [128, 2048], F32)) for i in range(2)]
    ex = [p.enter_context(nc.sbuf_tensor(f"aa_ex{i}", [128, 2048], BF16)) for i in range(2)]
    ev = [t[:].bitcast(BF16) for t in sc]
    DBLK = 16
    NBLK = NTT * (128 // DBLK)
    P.wait("vector", "DV")
    P.wait("scalar", "DV")

    def blk(k):
        ti, bl = divmod(k, 128 // DBLK)
        a_b = aT[:, ti * 128 + bl * DBLK: ti * 128 + (bl + 1) * DBLK].unsqueeze(2).broadcast_to(
            [128, DBLK, 128])
        k_b = kT[:, ti * 128:(ti + 1) * 128].unsqueeze(1).broadcast_to([128, DBLK, 128])
        v_b = vT[:, ti * 128:(ti + 1) * 128].unsqueeze(1).broadcast_to([128, DBLK, 128])
        out_sl = slice(ti * 128 + bl * DBLK, ti * 128 + (bl + 1) * DBLK)
        return a_b, k_b, v_b, out_sl

    def emit_tt_exp(k):
        a_b, k_b, _, _ = blk(k)
        scb = sc[k % 2]
        sc3 = scb[:].rearrange("p (d e) -> p d e", e=128)
        if k >= 2:
            P.wait("vector", "AC", P.cnt["AC"] - 1)
            P.chain("vector")
        P.do("vector", lambda e, sc3=sc3, a_b=a_b, k_b=k_b: e.tensor_tensor(
            sc3, a_b, k_b, ALU.mult))
        P.wait("scalar", "DV")
        P.do("scalar", lambda e, scb=scb, k=k: e.activation(ex[k % 2][:], scb[:], AF.Exp))

    emit_tt_exp(0)
    for k in range(NBLK):
        if k + 1 < NBLK:
            emit_tt_exp(k + 1)
        _, _, v_bp, out_sl = blk(k)
        ex3 = ex[k % 2][:].rearrange("p (d e) -> p d e", e=128)
        evb3 = ev[k % 2][:, 0:2048].rearrange("p (d e) -> p d e", e=128)
        P.wait("vector", "AC", P.cnt["AC"] - 1 if k + 1 < NBLK else P.cnt["AC"])
        P.do("vector", lambda e, out_sl=out_sl, ex3=ex3: e.reduce_sum(
            zT[:, out_sl].unsqueeze(2), ex3, axis=AX.X))
        P.do("vector", lambda e, evb3=evb3, ex3=ex3, v_bp=v_bp: e.tensor_tensor(
            evb3, ex3, v_bp, ALU.mult))
        P.chain("vector")
        P.do("vector", lambda e, out_sl=out_sl, evb3=evb3: e.reduce_sum(
            numT[:, out_sl].unsqueeze(2), evb3, axis=AX.X))
    oT = sc[0][:].bitcast(BF16)[:, 0:T]
    P.chain("vector")
    P.do("vector", lambda e: e.reciprocal(zT[:], zT[:]))
    P.chain("vector")
    P.do("vector", lambda e: e.tensor_tensor(oT, numT[:], zT[:], ALU.mult))

    ofm = sc[0][:].bitcast(BF16)[:, T:2 * T]
    mj = [sc[1][:].bitcast(BF16)[:, i * 128:(i + 1) * 128] for i in range(2)]
    ps5 = ExitStack()
    psTo = ps5.enter_context(nc.psum_tensor("aa_psTo", [128, 4096], BF16))
    psRep = [ps5.enter_context(nc.psum_tensor(f"aa_psRep{i}", [128, 512], F32)) for i in range(2)]
    P.wait("tensor", "DV")
    for tc in range(NTT):
        if tc >= 4:
            P.wait("tensor", "DV", P.cnt["DV"] - 3)
        P.do("tensor", lambda e, tc=tc: e.transpose(
            psTo[:, (tc % 4) * 1024:(tc % 4) * 1024 + 128], oT[:, tc * 128:(tc + 1) * 128], idB[:]),
            inc="PE")
        P.wait("vector", "PE")
        P.do("vector", lambda e, tc=tc: e.tensor_copy(
            ofm[:, tc * 128:(tc + 1) * 128], psTo[:, (tc % 4) * 1024:(tc % 4) * 1024 + 128]),
            inc="DV")
    for j in range(NCH):
        mjb = mj[j % 2]
        if j >= 2:
            P.wait("vector", "PE")
        P.do("vector", lambda e, mjb=mjb: e.memset(mjb[:], 0.0), inc="DV")
        P.wait("gpsimd", "DV")
        P.dma("gpsimd", lambda e, j=j, mjb=mjb: e.dma_start(
            out=mjb[4 * j:4 * j + 4, :], in_=env["m0rep"][0:4, :]), f"mj{j % 2}")
        P.wslot("tensor", f"mj{j % 2}")
        P.wait("tensor", "DV")
        for th in range(2):
            P.do("tensor", lambda e, j=j, th=th, mjb=mjb: e.matmul(
                psRep[th][:], mjb, ofm[:, th * 512:(th + 1) * 512],
                start=True, stop=True), inc="PE")
        P.wait("vector", "PE")
        for th in range(2):
            P.do("vector", lambda e, j=j, th=th: e.tensor_tensor(
                xv(j, th * 512, 512), psRep[th][:], xv(j, th * 512, 512), ALU.add), inc="DV")
    ps5.close()
    p.close()


def emit_ffn(nc, P, env, call):
    xv = env["xv"]
    idB, idF = env["idB"], env["idF"]
    ones_col_b, ones_row_b = env["ones_col_b"], env["ones_row_b"]
    rows, rows_b = env["rows"], env["rows_b"]

    p = ExitStack()
    xln = p.enter_context(nc.sbuf_tensor(f"ff_xln{call}", [128, NCH * FT], BF16))
    xcb = [p.enter_context(nc.sbuf_tensor(f"ff_xc{call}{i}", [128, FT], BF16)) for i in range(2)]
    rwT = p.enter_context(nc.sbuf_tensor(f"ff_rwT{call}", [128, 15 * NCH], BF16))
    sqb = [p.enter_context(nc.sbuf_tensor(f"ff_sq{call}0", [128, FT], BF16))] * 2
    woc = [p.enter_context(nc.sbuf_tensor(f"ff_w{call}{i}", [128, 4096], BF16)) for i in range(2)]
    b2c = [p.enter_context(nc.sbuf_tensor(f"ff_b2{call}{i}", [128, 128], BF16)) for i in range(2)]
    rbf_b = p.enter_context(nc.sbuf_tensor(f"ff_rb{call}", [128, FT], BF16))
    mrb_b = p.enter_context(nc.sbuf_tensor(f"ff_mrb{call}", [128, FT], BF16))
    lgT = p.enter_context(nc.sbuf_tensor(f"ff_lgT{call}", [128, 64], F32))
    mx8 = p.enter_context(nc.sbuf_tensor(f"ff_mx8{call}", [128, 8], F32))
    msk = p.enter_context(nc.sbuf_tensor(f"ff_msk{call}", [128, 16], F32))
    pmb = p.enter_context(nc.sbuf_tensor(f"ff_pm{call}", [128, 16], BF16))
    lg = p.enter_context(nc.sbuf_tensor(f"ff_lg{call}", [15, FT], F32))
    pmT = p.enter_context(nc.sbuf_tensor(f"ff_pmT{call}", [15, FT], BF16))
    wfm = p.enter_context(nc.sbuf_tensor(f"ff_wfm{call}", [8, FT], BF16))
    wsb = xcb[0]
    lora = p.enter_context(nc.sbuf_tensor(f"ff_lora{call}", [128, FT], BF16))

    psO = [p.enter_context(nc.psum_tensor(f"ff_psO{call}{i}", [128, 512], F32)) for i in range(2)]
    psH = p.enter_context(nc.psum_tensor(f"ff_psH{call}", [128, 512], F32))
    psMisc = p.enter_context(nc.psum_tensor(f"ff_psM{call}", [128, 512], F32))
    psTb = p.enter_context(nc.psum_tensor(f"ff_psTb{call}", [128, 512], BF16))
    psSx = p.enter_context(nc.psum_tensor(f"ff_psSx{call}", [1, 512], F32))
    psSq = p.enter_context(nc.psum_tensor(f"ff_psSq{call}", [1, 512], F32))

    P.wait("sync", "CC", 5)
    P.wait("scalar", "CC", 5)
    P.dma("sync", lambda e: e.dma_start(out=rwT[:], in_=env["rwt_dram"].ap()), f"frw{call}")

    evac_dv = []
    mm_pe = []
    for tt in range(NFT):
        tok = tt * FT

        P.wait("scalar", "DV")
        for dc in range(NCH):
            xc, sq = xcb[dc % 2], sqb[0]
            if tt > 0 or dc >= 1:
                P.wait("scalar", "PE")
            P.do("scalar", lambda e, dc=dc, tok=tok, xc=xc: e.activation(
                xc[:], xv(dc, tok, FT), AF.Identity))
            P.chain("scalar")
            P.do("scalar", lambda e, xc=xc, sq=sq: e.activation(sq[:], xc[:], AF.Square))
            P.wait("tensor", "AC")
            P.do("tensor", lambda e, dc=dc, xc=xc: e.matmul(
                psSx[0:1, :], ones_col_b[:], xc[:],
                start=(dc == 0), stop=(dc == NCH - 1)), inc="PE")
            P.do("tensor", lambda e, dc=dc, sq=sq: e.matmul(
                psSq[0:1, :], ones_col_b[:], sq[:],
                start=(dc == 0), stop=(dc == NCH - 1)), inc="PE")
        P.wait("vector", "PE")
        rr = _ln_rows(nc, P, env, FT, psSx[0:1, :], psSq[0:1, :], "mr")
        P.chain("vector")
        P.do("vector", lambda e: e.tensor_copy(rows_b[0:1, 0:FT], rr["r"]))
        P.do("vector", lambda e: e.tensor_copy(rows_b[0:1, FT:2 * FT], rr["t3"]))
        P.wait("tensor", "DV")
        P.do("tensor", lambda e: e.matmul(psMisc[:], ones_row_b[0:1, 0:128],
                                          rows_b[0:1, 0:FT], start=True, stop=True), inc="PE")
        P.wait("vector", "PE")
        P.do("vector", lambda e: e.tensor_copy(rbf_b[:], psMisc[:]), inc="DV")
        P.wait("tensor", "DV")
        P.do("tensor", lambda e: e.matmul(psMisc[:], ones_row_b[0:1, 0:128],
                                          rows_b[0:1, FT:2 * FT], start=True, stop=True), inc="PE")
        P.wait("vector", "PE")
        P.do("vector", lambda e: e.tensor_copy(mrb_b[:], psMisc[:]), inc="DV")
        if tt >= 1:
            P.wait("scalar", "PE", mm_pe[-1])
        P.wait("vector", "DV")
        for dc in range(NCH):
            P.do("scalar", lambda e, dc=dc, tok=tok: e.activation(
                xln[:, dc * FT:(dc + 1) * FT], xv(dc, tok, FT), AF.Identity))
            P.wait("vector", "AC")
            P.do("vector", lambda e, dc=dc: e.tensor_tensor(
                xln[:, dc * FT:(dc + 1) * FT], xln[:, dc * FT:(dc + 1) * FT], rbf_b[:], ALU.mult))
            P.chain("vector")
            P.do("vector", lambda e, dc=dc: e.tensor_tensor(
                xln[:, dc * FT:(dc + 1) * FT], xln[:, dc * FT:(dc + 1) * FT], mrb_b[:],
                ALU.subtract))

        P.wait("tensor", "DV")
        P.wslot("tensor", f"frw{call}")
        for dc in range(NCH):
            P.do("tensor", lambda e, dc=dc: e.matmul(
                psMisc[0:15, :], rwT[:, dc * 15:(dc + 1) * 15],
                xln[:, dc * FT:(dc + 1) * FT], start=(dc == 0), stop=(dc == NCH - 1)), inc="PE")
        P.wait("vector", "PE")
        P.do("vector", lambda e: e.tensor_copy(lg[:], psMisc[0:15, :]), inc="DV")
        for tc in range(4):
            P.wait("tensor", "DV")
            P.do("tensor", lambda e, tc=tc: e.transpose(
                psMisc[0:128, 0:15], lg[:, tc * 128:(tc + 1) * 128], idF[0:15, 0:15]), inc="PE")
            P.wait("vector", "PE")
            P.do("vector", lambda e: e.tensor_copy(lgT[:, 0:15], psMisc[0:128, 0:15]))
            P.chain("vector")
            P.do("vector", lambda e: e.tensor_tensor(lgT[:, 0:15], lgT[:, 0:15], env["rb_bc"][:],
                                                     ALU.add))
            P.wait("scalar", "DV")
            P.do("scalar", lambda e: e.activation(lgT[:, 16:31], lgT[:, 0:15], AF.Exp,
                                                  accum_out=lgT[:, 32:33]))
            P.wait("vector", "AC")
            P.do("vector", lambda e: e.reciprocal(lgT[:, 33:34], lgT[:, 32:33]))
            P.chain("vector")
            P.do("vector", lambda e: e.tensor_scalar(lgT[:, 16:31], lgT[:, 16:31],
                                                     lgT[:, 33:34], None, ALU.mult))
            P.chain("vector")
            P.do("vector", lambda e: e.max(out=mx8[:], in_=lgT[:, 16:31]))
            P.chain("vector")
            P.do("vector", lambda e: e.tensor_scalar(msk[:, 0:15], lgT[:, 16:31],
                                                     mx8[:, 7:8], None, ALU.is_ge))
            P.chain("vector")
            P.do("vector", lambda e: e.tensor_tensor(pmb[:, 0:15], lgT[:, 16:31], msk[:, 0:15],
                                                     ALU.mult))
            P.wait("tensor", "DV")
            P.do("tensor", lambda e, tc=tc: e.transpose(psTb[0:15, 0:128], pmb[:, 0:15], idB[:]), inc="PE")
            P.wait("vector", "PE")
            P.do("vector", lambda e, tc=tc: e.tensor_copy(pmT[:, tc * 128:(tc + 1) * 128],
                                                          psTb[0:15, 0:128]), inc="DV")
        P.wait("tensor", "DV")
        P.do("tensor", lambda e: e.matmul(psMisc[0:8, :], env["onehotT"][:], pmT[:],
                                          start=True, stop=True), inc="PE")
        P.wait("vector", "PE")
        P.do("vector", lambda e: e.tensor_copy(wfm[:], psMisc[0:8, :]), inc="DV")
        P.wait("tensor", "DV")
        P.do("tensor", lambda e: e.matmul(psMisc[:], env["eexp"][:], wfm[:],
                                          start=True, stop=True), inc="PE")
        P.wait("vector", "PE")
        P.do("vector", lambda e: e.tensor_copy(wsb[:], psMisc[:]), inc="DV")

        if mm_pe:
            P.wait("scalar", "PE", mm_pe[-1])
        P.dma("scalar", lambda e: e.dma_start(
            out=woc[0][:], in_=env["a2t_dram"].ap().rearrange("(g p) d -> p g d", p=128)),
            f"fa{call}")
        P.wslot("tensor", f"fa{call}")
        for dc in range(NCH):
            P.do("tensor", lambda e, dc=dc: e.matmul(
                psH[:], woc[0][:, dc * 128:(dc + 1) * 128], xln[:, dc * FT:(dc + 1) * FT],
                start=(dc == 0), stop=(dc == NCH - 1)), inc="PE")
        h_pe = P.cnt["PE"]
        P.wait("vector", "PE")
        P.chain("vector")
        P.do("vector", lambda e: e.tensor_tensor(lora[:], psH[:], wsb[:], ALU.mult))
        P.wait("tensor", "DV")
        P.wait("vector", "AC")
        if env.get("dbg") is not None and call == 0 and tt == 0:
            dbgap = env["dbg"]
            P.wait("gpsimd", "DV")
            P.dma("gpsimd", lambda e, dbgap=dbgap: e.dma_start(
                out=dbgap[:, 0:16384], in_=xln[:]), "dbgd")
            P.dma("gpsimd", lambda e, dbgap=dbgap: e.dma_start(
                out=dbgap[:, 16384:16896], in_=wsb[:]), "dbgd")
            P.dma("gpsimd", lambda e, dbgap=dbgap: e.dma_start(
                out=dbgap[:, 16896:17408], in_=lora[:]), "dbgd")
            P.dma("gpsimd", lambda e, dbgap=dbgap: e.dma_start(
                out=dbgap[0:15, 17408:17920], in_=pmT[:]), "dbgd")
            P.dma("gpsimd", lambda e, dbgap=dbgap: e.dma_start(
                out=dbgap[0:8, 17920:18432], in_=wfm[:]), "dbgd")
            P.wslot("scalar", "dbgd")
            P.wslot("vector", "dbgd")
            P.wslot("tensor", "dbgd")

        def b2dma(oc):
            bc = b2c[oc % 2]
            P.dma("scalar", lambda e, oc=oc, bc=bc: e.dma_start(
                out=bc[:], in_=bass.AP(tensor=env["b2_full"],
                                       offset=(oc // 4) * 128 * 512 + (oc % 4) * 128,
                                       ap=[[512, 128], [1, 128]])), f"fb{oc % 2}")
        for oc in range(NCH):
            w = woc[oc % 2]
            rr4 = oc // 4
            cc4 = oc % 4
            if oc < 2:
                P.wait("sync", "PE", h_pe)
            if len(mm_pe) >= 2:
                P.wait("sync", "PE", mm_pe[-2])
            P.dma("sync", lambda e, rr4=rr4, cc4=cc4, w=w: e.dma_start(
                out=w[:], in_=bass.AP(
                    tensor=env["wt_full"],
                    offset=(rr4 * D) * OSL + cc4 * 128,
                    ap=[[OSL, 128], [OSL * 128, 32], [1, 128]])), f"fw{oc % 2}")
            if oc == 0:
                P.wait("scalar", "PE", h_pe)
                b2dma(0)
            if oc + 1 < NCH:
                if len(mm_pe) >= 1:
                    P.wait("scalar", "PE", mm_pe[-1])
                b2dma(oc + 1)
            P.wslot("tensor", f"fw{oc % 2}")
            if len(evac_dv) >= 2:
                P.wait("tensor", "DV", evac_dv[-2])
            po = psO[oc % 2]
            for dc in range(NCH):
                P.do("tensor", lambda e, dc=dc, w=w, po=po: e.matmul(
                    po[:], w[:, dc * 128:(dc + 1) * 128], xln[:, dc * FT:(dc + 1) * FT],
                    start=(dc == 0), stop=False), inc="PE")
            P.wslot("tensor", f"fb{oc % 2}")
            P.do("tensor", lambda e, oc=oc, po=po, bc=b2c[oc % 2]: e.matmul(
                po[:], bc[:], lora[:], start=False, stop=True), inc="PE")
            P.wait("vector", "PE")
            P.do("vector", lambda e, oc=oc, po=po, tok=tok: e.scalar_tensor_tensor(
                xv(oc, tok, FT), po[:], env["bb_fm"][:, oc:oc + 1], xv(oc, tok, FT),
                ALU.add, ALU.add))
            evac_dv.append(P.cnt["DV"])
            mm_pe.append(P.cnt["PE"])
    p.close()


def emit_attn_b(nc, P, env):
    xv = env["xv"]
    ones_col_b, ones_row_b = env["ones_col_b"], env["ones_row_b"]
    rows, rows_b = env["rows"], env["rows_b"]

    p = ExitStack()
    zt = [p.enter_context(nc.sbuf_tensor(f"ab_z{i}", [128, T], BF16)) for i in range(2)]
    sqb = [p.enter_context(nc.sbuf_tensor(f"ab_sq{i}", [128, T], BF16)) for i in range(2)]
    rbf_b = p.enter_context(nc.sbuf_tensor("ab_rb", [128, T], BF16))
    mrb_b = p.enter_context(nc.sbuf_tensor("ab_mrb", [128, T], BF16))
    gbuf = p.enter_context(nc.sbuf_tensor("ab_g", [128, 512], F32))
    junk = p.enter_context(nc.sbuf_tensor("ab_junk", [128, 256], F32))
    psSx = p.enter_context(nc.psum_tensor("ab_psSx", [1, 1024], F32))
    psSq = p.enter_context(nc.psum_tensor("ab_psSq", [1, 1024], F32))
    psMisc = p.enter_context(nc.psum_tensor("ab_psM", [128, 1024], F32))

    for dc in range(NCH):
        z, sq = zt[dc % 2], sqb[dc % 2]
        if dc >= 2:
            P.wait("scalar", "PE")
        P.do("scalar", lambda e, dc=dc, z=z: e.activation(z[:], xv(dc), AF.Identity))
        P.chain("scalar")
        P.do("scalar", lambda e, z=z, sq=sq: e.activation(sq[:], z[:], AF.Square))
        P.wait("tensor", "AC")
        for th in range(2):
            P.do("tensor", lambda e, dc=dc, th=th, z=z: e.matmul(
                psSx[0:1, th * 512:(th + 1) * 512], ones_col_b[:],
                z[:, th * 512:(th + 1) * 512], start=(dc == 0), stop=(dc == NCH - 1)), inc="PE")
            P.do("tensor", lambda e, dc=dc, th=th, sq=sq: e.matmul(
                psSq[0:1, th * 512:(th + 1) * 512], ones_col_b[:],
                sq[:, th * 512:(th + 1) * 512], start=(dc == 0), stop=(dc == NCH - 1)), inc="PE")
    P.wait("vector", "PE")
    rr = _ln_rows(nc, P, env, T, psSx[0:1, 0:T], psSq[0:1, 0:T], "mr")
    P.chain("vector")
    P.do("vector", lambda e: e.tensor_copy(rows_b[0:1, 0:T], rr["r"]))
    P.do("vector", lambda e: e.tensor_copy(rows_b[0:1, T:2 * T], rr["t3"]))
    P.wait("tensor", "DV")
    for th in range(2):
        P.do("tensor", lambda e, th=th: e.matmul(
            psMisc[:, th * 512:(th + 1) * 512], ones_row_b[:],
            rows_b[0:1, th * 512:(th + 1) * 512], start=True, stop=True), inc="PE")
    P.wait("vector", "PE")
    P.do("vector", lambda e: e.tensor_copy(rbf_b[:], psMisc[:, 0:T]), inc="DV")
    P.wait("tensor", "DV")
    for th in range(2):
        P.do("tensor", lambda e, th=th: e.matmul(
            psMisc[:, th * 512:(th + 1) * 512], ones_row_b[:],
            rows_b[0:1, T + th * 512:T + (th + 1) * 512], start=True, stop=True), inc="PE")
    P.wait("vector", "PE")
    P.do("vector", lambda e: e.tensor_copy(mrb_b[:], psMisc[:, 0:T]), inc="DV")

    def z_tile(lt, z):
        P.do("scalar", lambda e, lt=lt, z=z: e.activation(z[:], xv(lt), AF.Identity), inc="AC")
        P.wait("vector", "AC")
        P.do("vector", lambda e, z=z: e.tensor_tensor(z[:], z[:], rbf_b[:], ALU.mult))
        P.chain("vector")
        P.do("vector", lambda e, z=z: e.tensor_tensor(z[:], z[:], mrb_b[:], ALU.subtract))

    pairs_o = [(n, m) for n in range(4) for m in range(n + 1, 4)]
    for lt in range(NCH):
        z = zt[lt % 2]
        if lt >= 2:
            P.wait("scalar", "DV")
            P.wait("scalar", "AC")
        z_tile(lt, z)
        P.chain("vector")
        P.wait("scalar", "DV")
        for n in range(4):
            P.chain("scalar")
            P.do("scalar", lambda e, lt=lt, n=n, z=z: e.activation(
                sqb[lt % 2][:, 0:256], z[:, n * 256:n * 256 + 256], AF.Square,
                accum_out=gbuf[:, lt * 16 + 5 * n:lt * 16 + 5 * n + 1]))
        for (n, m) in pairs_o:
            P.chain("vector")
            P.do("vector", lambda e, n=n, m=m, z=z: e.tensor_tensor(
                junk[:], z[:, n * 256:n * 256 + 256], z[:, m * 256:m * 256 + 256], ALU.mult))
            P.chain("vector")
            P.do("vector", lambda e, lt=lt, n=n, m=m: e.reduce_sum(
                gbuf[:, lt * 16 + n * 4 + m:lt * 16 + n * 4 + m + 1], junk[:], axis=AX.X))
    P.wait("vector", "AC")
    P.chain("vector")
    for lt in range(NCH):
        for (n, m) in pairs_o:
            P.do("vector", lambda e, lt=lt, n=n, m=m: e.tensor_copy(
                gbuf[:, lt * 16 + m * 4 + n:lt * 16 + m * 4 + n + 1],
                gbuf[:, lt * 16 + n * 4 + m:lt * 16 + n * 4 + m + 1]))
    P.chain("vector")
    P.do("vector", lambda e: e.tensor_scalar_mul(gbuf[:], gbuf[:], GSCALE), inc="DV")
    P.wait("sync", "DV")
    P.dma("sync", lambda e: e.dma_start(
        out=bass.AP(tensor=env["g_part"], offset=0, ap=[[16, 128], [16 * 128, 32], [1, 16]]),
        in_=gbuf[:].rearrange("p (a b) -> p a b", b=16)), "gpt")
    P.wslot("gpsimd", "gpt")
    P.do("gpsimd", lambda e: e.collective_compute(
        "AllReduce", ALU.add, replica_groups=[list(range(NCORE))],
        ins=[env["g_part"].ap()], outs=[env["g_full"].ap()]), inc="CC")
    P.wait("sync", "CC")
    P.dma("sync", lambda e: e.dma_start(
        out=gbuf[:].rearrange("p (a b) -> p a b", b=16),
        in_=bass.AP(tensor=env["g_full"], offset=0, ap=[[16, 128], [16 * 128, 32], [1, 16]])),
        "gld")
    g3 = gbuf[:].rearrange("p (a m) -> p a m", m=4)
    P.wslot("vector", "gld")
    P.do("vector", lambda e: e.reduce_max(junk[:, 0:128].unsqueeze(2), g3, axis=AX.X,
                                          negate=True))
    P.chain("vector")
    P.do("vector", lambda e: e.tensor_tensor(
        g3, g3, junk[:, 0:128].unsqueeze(2).broadcast_to([128, 128, 4]), ALU.add))
    P.wait("scalar", "DV")
    P.do("scalar", lambda e: e.activation(gbuf[:], gbuf[:], AF.Exp))
    P.wait("vector", "AC")
    P.do("vector", lambda e: e.reduce_sum(junk[:, 0:128].unsqueeze(2), g3, axis=AX.X))
    P.chain("vector")
    P.do("vector", lambda e: e.reciprocal(junk[:, 0:128], junk[:, 0:128]))
    P.chain("vector")
    P.do("vector", lambda e: e.tensor_tensor(
        g3, g3, junk[:, 0:128].unsqueeze(2).broadcast_to([128, 128, 4]), ALU.mult))
    P.chain("vector")
    P.do("vector", lambda e: e.tensor_scalar_add(
        gbuf[:].rearrange("p (a m) -> p a m", m=4)[:, :, 0:1].rearrange("p a b -> p (a b)") if False
        else bass.AP(tensor=gbuf[:].tensor, offset=gbuf[:].offset, ap=[[512, 128], [16, 32], [5, 4]]),
        bass.AP(tensor=gbuf[:].tensor, offset=gbuf[:].offset, ap=[[512, 128], [16, 32], [5, 4]]),
        1.0))

    msga = p.enter_context(nc.sbuf_tensor("ab_msg", [128, T], BF16))
    for lt in range(NCH):
        z = zt[lt % 2]
        if lt >= 2:
            P.wait("scalar", "DV")
        z_tile(lt, z)
        P.chain("vector")
        for m in range(4):
            for n in range(4):
                if m == 0:
                    P.do("vector", lambda e, lt=lt, n=n, z=z: e.tensor_scalar(
                        msga[:, n * 256:n * 256 + 256], z[:, 0:256],
                        gbuf[:, lt * 16 + n * 4:lt * 16 + n * 4 + 1], None, ALU.mult))
                else:
                    P.chain("vector", back=3)
                    P.do("vector", lambda e, lt=lt, n=n, m=m, z=z: e.scalar_tensor_tensor(
                        msga[:, n * 256:n * 256 + 256], z[:, m * 256:m * 256 + 256],
                        gbuf[:, lt * 16 + n * 4 + m:lt * 16 + n * 4 + m + 1],
                        msga[:, n * 256:n * 256 + 256], ALU.mult, ALU.add))
        P.chain("vector")
        P.do("vector", lambda e, lt=lt: e.tensor_tensor(
            xv(lt), msga[:], xv(lt), ALU.add))
    p.close()


def emit_phase5(nc, P, env, y_tok):
    xv = env["xv"]
    idF = env["idF"]
    p = ExitStack()
    ostage = [p.enter_context(nc.sbuf_tensor(f"p5_os{i}", [128, D], F32)) for i in range(2)]
    psO = p.enter_context(nc.psum_tensor("p5_psO", [128, 2048], F32))
    for tc in range(NTT):
        st = ostage[tc % 2]
        if tc >= 2:
            P.wslot("vector", f"ys{tc % 2}")
        for g4 in range(NCH // 4):
            if tc * NCH // 4 + g4 >= 1:
                P.wait("tensor", "DV")
            for q in range(4):
                dc = g4 * 4 + q
                P.do("tensor", lambda e, tc=tc, dc=dc, q=q: e.transpose(
                    psO[:, q * 512:q * 512 + 128], xv(dc, tc * 128, 128), idF[:]), inc="PE")
            P.wait("vector", "PE")
            P.do("vector", lambda e, g4=g4, st=st: e.tensor_copy(
                st[:].rearrange("p (a c) -> p a c", c=128)[:, g4 * 4:(g4 + 1) * 4, :],
                psO[:].rearrange("p (s c) -> p s c", c=512)[:, :, 0:128]), inc="DV")
        P.wait("sync", "DV")
        P.dma("sync", lambda e, tc=tc, st=st: e.dma_start(
            out=y_tok[tc * 128:(tc + 1) * 128, :], in_=st[:]), f"ys{tc % 2}")
    P.wslot("sync", "ys0")
    P.wslot("sync", "ys1")
    p.close()


_CACHE = {}


def kernel(**inputs):
    f32 = np.float32
    x = np.ascontiguousarray(inputs["L0_attn_a_x"], dtype=f32)
    Wq = np.ascontiguousarray(inputs["L0_attn_a_Wq"], dtype=f32)
    bq = np.ascontiguousarray(inputs["L0_attn_a_bq"], dtype=f32)
    Wk = np.ascontiguousarray(inputs["L0_attn_a_Wk"], dtype=f32).reshape(D, 128)
    bk = np.ascontiguousarray(inputs["L0_attn_a_bk"], dtype=f32).reshape(128, 1)
    Wv = np.ascontiguousarray(inputs["L0_attn_a_Wv"], dtype=f32).reshape(D, 128)
    bv = np.ascontiguousarray(inputs["L0_attn_a_bv"], dtype=f32).reshape(128, 1)
    Wb = np.ascontiguousarray(inputs["L0_ffn_base_W"], dtype=f32)
    bb = np.ascontiguousarray(inputs["L0_ffn_base_b"], dtype=f32).reshape(1, D)
    rW = np.ascontiguousarray(inputs["L0_ffn_router_W"], dtype=f32)
    rb = np.ascontiguousarray(inputs["L0_ffn_router_b"], dtype=f32).reshape(1, 15)
    A = np.ascontiguousarray(inputs["L0_ffn_A"], dtype=f32).reshape(128, D)
    Bm = np.ascontiguousarray(inputs["L0_ffn_B"], dtype=f32)
    emap = np.ascontiguousarray(inputs["L0_ffn_expert_map"], dtype=np.int32).reshape(15, 1)
    Wq2 = Wq.reshape(D, 128 * 32)

    key = "prog"
    if key not in _CACHE:
        _CACHE[key] = build_program()
    nc = _CACHE[key]

    in_maps = []
    for c in range(NCORE):
        xs = np.ascontiguousarray(x[:, c * SLOC:(c + 1) * SLOC, :].reshape(T, D))
        in_maps.append(dict(
            x_tok=xs,
            w_slice=np.ascontiguousarray(Wb[c * OSL:(c + 1) * OSL]),
            wq_slice=np.ascontiguousarray(Wq2[c * OSL:(c + 1) * OSL]),
            wk_in=Wk, wv_in=Wv, bq_in=bq, bk_in=bk, bv_in=bv, bb_in=bb,
            rw_in=rW, rb_in=rb, a_in=A,
            b_in=np.ascontiguousarray(Bm[:, c * 512:(c + 1) * 512, :].reshape(D, 16)),
            emap_in=emap,
        ))
    res = run_bass_kernel_spmd(nc, in_maps, core_ids=list(range(NCORE)))
    out = np.empty((B, S, D), dtype=np.float32)
    for c in range(NCORE):
        out[:, c * SLOC:(c + 1) * SLOC, :] = res.results[c]["y_tok"].reshape(B, SLOC, D)
    return out


if __name__ == "__main__":
    pass


# version 07
# speedup vs baseline: 1.3923x; 1.0013x over previous
import os
from contextlib import ExitStack

import numpy as np
import ml_dtypes
import concourse.bass as bass
import concourse.mybir as mybir
from concourse.bass_utils import run_bass_kernel_spmd

F32 = mybir.dt.float32
BF16 = mybir.dt.bfloat16
I32 = mybir.dt.int32
AF = mybir.ActivationFunctionType
ALU = mybir.AluOpType
AX = mybir.AxisListType

NCORE = 8
B, S, D = 4, 2048, 4096
SLOC = S // NCORE
T = B * SLOC
NCH = D // 128
NTT = T // 128
OSL = D // NCORE
FT = 512
NFT = T // FT
SCALE_A = float(1.0 / np.sqrt(128.0))
GSCALE = float(1.0 / np.sqrt(2048.0))
EPS = 1e-5

ENGINES = ("sync", "scalar", "vector", "tensor", "gpsimd")
SEMS = ["L", "GD", "PE", "DV", "AC", "PO", "CC"]


class Script:
    def __init__(self, nc, stack):
        self.nc = nc
        self.stack = stack
        self.steps = {e: [] for e in ENGINES}
        self.sem = {}
        self.cnt = {}

    def new_sem(self, name, stack=None):
        self.sem[name] = (stack or self.stack).enter_context(self.nc.semaphore(name))
        self.cnt[name] = 0

    def slot(self, name):
        name = "D" + name
        if name not in self.sem:
            self.new_sem(name)
        return name

    AUTO = {"vector": "DV", "scalar": "AC", "gpsimd": "PO"}

    def do(self, eng, fn, inc=None, inc_n=1):
        if inc is None and eng in self.AUTO:
            inc = self.AUTO[eng]
            inc_n = 1
        if inc is not None:
            self.cnt[inc] += inc_n
        sem = self.sem.get(inc)

        def step(e, fn=fn, sem=sem, n=inc_n if inc else 0):
            ins = fn(e)
            if sem is not None:
                ins.then_inc(sem, n)
        self.steps[eng].append(step)

    def chain(self, eng, back=0):
        sem = self.AUTO[eng]
        self.wait(eng, sem, self.cnt[sem] - back)

    def dma(self, eng, fn, slot):
        self.do(eng, fn, self.slot(slot), 16)

    def wslot(self, eng, name):
        self.wait(eng, self.slot(name))

    def wait(self, eng, sem, v=None):
        v = self.cnt[sem] if v is None else v
        if v <= 0:
            return
        s = self.sem[sem]
        self.steps[eng].append(lambda e, s=s, v=v: e.wait_ge(s, v))

    def barrier(self):
        snap = dict(self.cnt)
        for en in ENGINES:
            for s, v in snap.items():
                if s != "CC":
                    self.wait(en, s, v)

    def replay(self):
        with self.nc.Block() as block:
            for en in ENGINES:
                def runner(eng, en=en):
                    for st in self.steps[en]:
                        st(eng)
                getattr(block, en)(runner)


def _ln_rows(nc, P, env, n, sums_ps, sumsq_ps, third):
    rows = env["rows"]
    m = rows[:, 0:n]
    r = rows[:, n:2 * n]
    t3 = rows[:, 2 * n:3 * n]
    P.do("vector", lambda e: e.tensor_scalar_mul(m, sums_ps, 1.0 / D))
    P.do("vector", lambda e: e.tensor_scalar_mul(r, sumsq_ps, 1.0 / D))
    P.chain("vector")
    P.do("vector", lambda e: e.tensor_tensor(t3, m, m, ALU.mult))
    P.chain("vector")
    P.do("vector", lambda e: e.tensor_scalar_add(t3, t3, -EPS))
    P.chain("vector")
    P.do("vector", lambda e: e.tensor_tensor(r, r, t3, ALU.subtract))
    P.wait("scalar", "DV")
    P.do("scalar", lambda e: e.activation(r, r, AF.Sqrt))
    P.wait("vector", "AC")
    P.do("vector", lambda e: e.reciprocal(r, r))
    P.chain("vector")
    if third == "negm":
        P.do("vector", lambda e: e.tensor_scalar_mul(t3, m, -1.0))
    else:
        P.do("vector", lambda e: e.tensor_tensor(t3, m, r, ALU.mult))
    return dict(m=m, r=r, t3=t3)


def build_program():
    nc = bass.Bass("TRN2", target_bir_lowering=False)
    stop_phase = int(os.environ.get("KMK_STOP_PHASE", "99"))

    x_tok = nc.dram_tensor("x_tok", [T, D], F32, kind="ExternalInput").ap()
    w_slice = nc.dram_tensor("w_slice", [OSL, D], F32, kind="ExternalInput").ap()
    wq_slice = nc.dram_tensor("wq_slice", [OSL, D], F32, kind="ExternalInput").ap()
    wk_in = nc.dram_tensor("wk_in", [D, 128], F32, kind="ExternalInput").ap()
    wv_in = nc.dram_tensor("wv_in", [D, 128], F32, kind="ExternalInput").ap()
    bq_in = nc.dram_tensor("bq_in", [128, 32], F32, kind="ExternalInput").ap()
    bk_in = nc.dram_tensor("bk_in", [128, 1], F32, kind="ExternalInput").ap()
    bv_in = nc.dram_tensor("bv_in", [128, 1], F32, kind="ExternalInput").ap()
    bb_in = nc.dram_tensor("bb_in", [1, D], F32, kind="ExternalInput").ap()
    rw_in = nc.dram_tensor("rw_in", [15, D], F32, kind="ExternalInput").ap()
    rb_in = nc.dram_tensor("rb_in", [1, 15], F32, kind="ExternalInput").ap()
    a_in = nc.dram_tensor("a_in", [128, D], F32, kind="ExternalInput").ap()
    b_in = nc.dram_tensor("b_in", [D, 16], F32, kind="ExternalInput").ap()
    emap_in = nc.dram_tensor("emap_in", [15, 1], I32, kind="ExternalInput").ap()
    y_tok = nc.dram_tensor("y_tok", [T, D], F32, kind="ExternalOutput").ap()
    dbg = nc.dram_tensor("dbg", [128, 18432], F32, kind="ExternalOutput").ap() \
        if os.environ.get("KMK_DBG") else None

    wt_part = nc.dram_tensor("wt_part", [D, OSL], BF16)
    wqe_part = nc.dram_tensor("wqe_part", [OSL, 128], BF16)
    wsum_part = nc.dram_tensor("wsum_part", [OSL, 1], F32)
    wqs_part = nc.dram_tensor("wqs_part", [1, 128], F32)
    wt_full = nc.dram_tensor("wt_full", [NCORE * D, OSL], BF16)
    wqe_full = nc.dram_tensor("wqe_full", [D, 128], BF16)
    wsum_full = nc.dram_tensor("wsum_full", [D, 1], F32)
    wqs_full = nc.dram_tensor("wqs_full", [1, 128], F32)
    wk_bf = nc.dram_tensor("wk_bf", [D, 128], BF16)
    wv_bf = nc.dram_tensor("wv_bf", [D, 128], BF16)
    a_bf = nc.dram_tensor("a_bf", [128, D], BF16)
    b2_part = nc.dram_tensor("b2_part", [128, 512], BF16)
    b2_full = nc.dram_tensor("b2_full", [8 * 128, 512], BF16)
    g_part = nc.dram_tensor("g_part", [D, 16], F32)
    g_full = nc.dram_tensor("g_full", [D, 16], F32)
    rwt_dram = nc.dram_tensor("rwt_dram", [128, 15 * NCH], BF16)
    a2t_dram = nc.dram_tensor("a2t_dram", [D, 128], BF16)

    root = ExitStack()
    with root:
        P = Script(nc, root)
        for s in SEMS:
            P.new_sem(s, root)

        xfm = root.enter_context(nc.sbuf_tensor("xfm", [128, NCH * T], F32))
        idF = root.enter_context(nc.sbuf_tensor("idF", [128, 128], F32))
        idB = root.enter_context(nc.sbuf_tensor("idB", [128, 128], BF16))
        ones_col_b = root.enter_context(nc.sbuf_tensor("ones_col_b", [128, 1], BF16))
        ones_row_b = root.enter_context(nc.sbuf_tensor("ones_row_b", [1, 128], BF16))

        rb_bc = root.enter_context(nc.sbuf_tensor("rb_bc", [128, 15], F32))
        onehotT = root.enter_context(nc.sbuf_tensor("onehotT", [15, 8], BF16))
        eexp = root.enter_context(nc.sbuf_tensor("eexp", [8, 128], BF16))
        bb_fm = root.enter_context(nc.sbuf_tensor("bb_fm", [128, 32], F32))
        ones_f1 = root.enter_context(nc.sbuf_tensor("ones_f1", [1, 1], F32))
        bqe = root.enter_context(nc.sbuf_tensor("bqe", [128, 1], F32))
        bkc = root.enter_context(nc.sbuf_tensor("bkc", [128, 1], F32))
        bvc = root.enter_context(nc.sbuf_tensor("bvc", [128, 1], F32))
        wsums = root.enter_context(nc.sbuf_tensor("wsums", [1, 3 * 128], BF16))
        m0rep = root.enter_context(nc.sbuf_tensor("m0rep", [4, 128], BF16))
        rows = root.enter_context(nc.sbuf_tensor("rows", [1, 3072], F32))
        rows_b = root.enter_context(nc.sbuf_tensor("rows_b", [1, 2112], BF16))

        def xv(dc, lo=0, n=T):
            return xfm[:, dc * T + lo: dc * T + lo + n]

        env = dict(rows=rows, rows_b=rows_b, xv=xv, xfm=xfm, idF=idF, idB=idB,
                   ones_col_b=ones_col_b, ones_row_b=ones_row_b,
                   rb_bc=rb_bc, onehotT=onehotT, eexp=eexp, bb_fm=bb_fm, ones_f1=ones_f1,
                   rwt_dram=rwt_dram, a2t_dram=a2t_dram, b2_part=b2_part,
                   bqe=bqe, bkc=bkc, bvc=bvc, wsums=wsums, m0rep=m0rep,
                   wt_full=wt_full, wqe_full=wqe_full, wsum_full=wsum_full,
                   wqs_full=wqs_full, wk_bf=wk_bf, wv_bf=wv_bf, a_bf=a_bf,
                   b2_full=b2_full, g_part=g_part, g_full=g_full)

        env["dbg"] = dbg
        emit_phase0(nc, P, env, x_tok, w_slice, wq_slice, wk_in, wv_in, bq_in,
                    bk_in, bv_in, bb_in, rw_in, rb_in, a_in, b_in, emap_in,
                    wt_part, wqe_part, wsum_part, wqs_part)
        P.barrier()
        if stop_phase >= 1:
            emit_attn_a(nc, P, env)
            P.barrier()
        if stop_phase >= 2:
            emit_ffn(nc, P, env, 0)
            P.barrier()
        if stop_phase >= 3:
            emit_attn_b(nc, P, env)
            P.barrier()
        if stop_phase >= 4:
            emit_ffn(nc, P, env, 1)
            P.barrier()
        emit_phase5(nc, P, env, y_tok)
        P.replay()
    return nc


def emit_phase0(nc, P, env, x_tok, w_slice, wq_slice, wk_in, wv_in, bq_in,
                bk_in, bv_in, bb_in, rw_in, rb_in, a_in, b_in, emap_in,
                wt_part, wqe_part, wsum_part, wqs_part):
    xv = env["xv"]
    idF, idB = env["idF"], env["idB"]
    ones_col_b, ones_row_b = env["ones_col_b"], env["ones_row_b"]
    rows_b = env["rows_b"]

    p = ExitStack()
    wqout = p.enter_context(nc.sbuf_tensor("p0_wqo", [128, 128], F32))
    wqout_b = p.enter_context(nc.sbuf_tensor("p0_wqob", [128, 128], BF16))
    smt = p.enter_context(nc.sbuf_tensor("p0_smt", [128, 16], F32))
    castA = p.enter_context(nc.sbuf_tensor("p0_ca", [128, D], BF16))
    rw_sb = p.enter_context(nc.sbuf_tensor("p0_rw", [15, D], BF16))
    pa = ExitStack()
    xstage = [pa.enter_context(nc.sbuf_tensor(f"p0_xs{i}", [128, D], F32)) for i in range(2)]
    bqt = pa.enter_context(nc.sbuf_tensor("p0_bqt", [128, 32], F32))
    bb_rowt = pa.enter_context(nc.sbuf_tensor("p0_bbrow", [1, 2048], F32))
    emap_f = pa.enter_context(nc.sbuf_tensor("p0_em", [15, 1], I32))
    emap_g = pa.enter_context(nc.sbuf_tensor("p0_emg", [15, 1], F32))
    psT = p.enter_context(nc.psum_tensor("p0_psT", [128, 2048], F32))
    psTb = p.enter_context(nc.psum_tensor("p0_psTb", [128, 2048], BF16))
    psK1 = p.enter_context(nc.psum_tensor("p0_psK1", [128, 512], F32))
    psWS = p.enter_context(nc.psum_tensor("p0_psWS", [1, 512], F32))

    P.do("vector", lambda e: e.memset(idF[:], 1.0), inc="DV")
    P.do("vector", lambda e: e.memset(idB[:], 1.0), inc="DV")
    P.wait("gpsimd", "DV")
    P.do("gpsimd", lambda e: e.affine_select(out=idF[:], in_=idF[:], pattern=[[1, 128]],
                                             compare_op=ALU.is_equal, fill=0.0, base=0,
                                             channel_multiplier=-1), inc="PO")
    P.do("gpsimd", lambda e: e.affine_select(out=idB[:], in_=idB[:], pattern=[[1, 128]],
                                             compare_op=ALU.is_equal, fill=0.0, base=0,
                                             channel_multiplier=-1), inc="PO")
    P.wait("tensor", "PO")
    P.do("vector", lambda e: e.memset(ones_col_b[:], 1.0))
    P.do("vector", lambda e: e.memset(ones_row_b[:], 1.0))
    P.do("vector", lambda e: e.memset(env["eexp"][:], 0.0))
    P.do("vector", lambda e: e.memset(env["m0rep"][:], 0.0))
    P.do("vector", lambda e: e.memset(rows_b[:, 2048:2064], 2.0))
    P.do("vector", lambda e: e.memset(rows_b[:, 2064:2096], 1.0), inc="DV")
    P.wait("gpsimd", "DV")
    for ee in range(8):
        P.dma("gpsimd", lambda e, ee=ee: e.dma_start(
            out=env["eexp"][ee:ee + 1, 16 * ee:16 * ee + 16], in_=rows_b[0:1, 2048:2064]), "cst")
    for dd in range(4):
        P.dma("gpsimd", lambda e, dd=dd: e.dma_start(
            out=env["m0rep"][dd:dd + 1, 32 * dd:32 * dd + 32], in_=rows_b[0:1, 2064:2096]), "cst")

    P.dma("sync", lambda e: e.dma_start(out=bqt[:], in_=bq_in), "sml")
    P.dma("sync", lambda e: e.dma_start(out=env["bkc"][:], in_=bk_in), "sml")
    P.dma("sync", lambda e: e.dma_start(out=env["bvc"][:], in_=bv_in), "sml")
    P.dma("sync", lambda e: e.dma_start(out=bb_rowt[:], in_=bb_in[0:1, 0:2048]), "sml")
    P.dma("sync", lambda e: e.dma_start(out=env["rb_bc"][0:1, :], in_=rb_in), "sml")
    P.dma("sync", lambda e: e.dma_start(out=emap_f[:], in_=emap_in), "sml")
    P.wslot("vector", "sml")
    P.do("vector", lambda e: e.memset(env["ones_f1"][:], 1.0))
    P.do("vector", lambda e: e.reduce_sum(env["bqe"][:], bqt[:], axis=AX.X))
    P.chain("vector")
    P.do("vector", lambda e: e.tensor_scalar_mul(env["bqe"][:], env["bqe"][:], SCALE_A))
    P.do("vector", lambda e: e.tensor_copy(emap_g[:], emap_f[:]))
    P.chain("vector")
    for ee in range(8):
        P.do("vector", lambda e, ee=ee: e.tensor_scalar(env["onehotT"][:, ee:ee + 1], emap_g[:],
                                                        float(ee), None, ALU.is_equal))
    P.do("vector", lambda e: e.tensor_copy(rows_b[0:1, 2096:2111], env["rb_bc"][0:1, 0:15]), inc="DV")
    P.wait("tensor", "DV")
    P.do("tensor", lambda e: e.matmul(psK1[0:128, 0:15], ones_row_b[0:1, 0:128],
                                      rows_b[0:1, 2096:2111], start=True, stop=True), inc="PE")
    P.wait("vector", "PE")
    P.do("vector", lambda e: e.tensor_copy(env["rb_bc"][:], psK1[:, 0:15]), inc="DV")
    P.wait("tensor", "DV")
    P.wslot("tensor", "sml")
    for oc in range(16):
        P.do("tensor", lambda e, oc=oc: e.matmul(
            psK1[:, 128 + oc:128 + oc + 1], bb_rowt[0:1, oc * 128:(oc + 1) * 128],
            env["ones_f1"][:], start=True, stop=True), inc="PE")
    P.wait("sync", "PE")
    P.dma("sync", lambda e: e.dma_start(out=bb_rowt[:], in_=bb_in[0:1, 2048:4096]), "sml")
    P.wslot("tensor", "sml")
    for oc in range(16):
        P.do("tensor", lambda e, oc=oc: e.matmul(
            psK1[:, 144 + oc:144 + oc + 1], bb_rowt[0:1, oc * 128:(oc + 1) * 128],
            env["ones_f1"][:], start=True, stop=True), inc="PE")
    P.wait("vector", "PE")
    P.do("vector", lambda e: e.tensor_copy(env["bb_fm"][:], psK1[:, 128:160]), inc="DV")

    for (src, dst, coloff) in ((wk_in, env["wk_bf"], 1), (wv_in, env["wv_bf"], 2)):
        P.wslot("gpsimd", "gc")
        P.wait("gpsimd", "DV")
        P.dma("gpsimd", lambda e, src=src: e.dma_start(
            out=castA[:], in_=src.rearrange("(a p) d -> p a d", p=128)), "gc")
        P.wslot("gpsimd", "gc")
        P.dma("gpsimd", lambda e, dst=dst: e.dma_start(
            out=dst.ap().rearrange("(a p) d -> p a d", p=128), in_=castA[:]), "gc")
        P.wslot("vector", "gc")
        P.do("vector", lambda e: e.reduce_sum(
            wqout[:, 0:128].unsqueeze(2),
            castA[:].rearrange("p (a d) -> p d a", a=32), axis=AX.X))
        P.chain("vector")
        P.do("vector", lambda e: e.tensor_copy(wqout_b[:], wqout[:, 0:128]))
        P.wait("tensor", "DV")
        P.do("tensor", lambda e: e.matmul(psWS[0:1, 128:256], ones_col_b[:], wqout_b[:],
                                          start=True, stop=True), inc="PE")
        P.wait("vector", "PE")
        P.do("vector", lambda e, coloff=coloff: e.tensor_copy(
            env["wsums"][0:1, coloff * 128:(coloff + 1) * 128], psWS[0:1, 128:256]), inc="DV")

    P.wslot("gpsimd", "gc")
    P.wait("gpsimd", "DV")
    P.dma("gpsimd", lambda e: e.dma_start(out=castA[:], in_=a_in), "gc")
    P.wslot("gpsimd", "gc")
    P.dma("gpsimd", lambda e: e.dma_start(out=env["a_bf"].ap(), in_=castA[:]), "gc")
    P.wslot("gpsimd", "gc")
    P.dma("gpsimd", lambda e: e.dma_start(out=castA[:, 0:512],
                                          in_=b_in.rearrange("(a p) r -> p a r", p=128)), "gc")
    P.dma("gpsimd", lambda e: e.dma_start(out=rw_sb[:], in_=rw_in), "gc")
    dv_tile_done = []
    for tc in range(NTT):
        st = xstage[tc % 2]
        if tc >= 2:
            P.wait("sync", "DV", dv_tile_done[tc - 2])
        P.dma("sync", lambda e, tc=tc, st=st: e.dma_start(
            out=st[:], in_=x_tok[tc * 128:(tc + 1) * 128, :]), f"xs{tc % 2}")
        P.wslot("tensor", f"xs{tc % 2}")
        for g4 in range(NCH // 4):
            if tc * NCH // 4 + g4 >= 1:
                P.wait("tensor", "DV")
            for q in range(4):
                dc = g4 * 4 + q
                P.do("tensor", lambda e, dc=dc, q=q, st=st: e.transpose(
                    psT[:, q * 512:q * 512 + 128],
                    st[:, dc * 128:(dc + 1) * 128], idF[:]), inc="PE")
            P.wait("vector", "PE")
            P.do("vector", lambda e, tc=tc, g4=g4: e.tensor_copy(
                bass.AP(tensor=env["xfm"][:].tensor, offset=(g4 * 4) * T + tc * 128,
                        ap=[[NCH * T, 128], [T, 4], [1, 128]]),
                psT[:].rearrange("p (s c) -> p s c", c=512)[:, :, 0:128]), inc="DV")
        dv_tile_done.append(P.cnt["DV"])
        P.wait("tensor", "DV")

    pa.close()
    pb = ExitStack()
    wtileW = [pb.enter_context(nc.sbuf_tensor(f"p0_w{i}", [128, D], BF16)) for i in range(2)]
    wstage = [pb.enter_context(nc.sbuf_tensor(f"p0_wst{i}", [128, 1024], BF16)) for i in range(2)]
    wqtile = pb.enter_context(nc.sbuf_tensor("p0_wq", [128, 2048], F32))

    for i in range(4):
        for jh in range(2):
            P.wait("sync", "DV")
            P.dma("sync", lambda e, i=i, jh=jh: e.dma_start(
                out=wqtile[:], in_=wq_slice[i * 128:(i + 1) * 128,
                                            jh * 2048:(jh + 1) * 2048]), "wqt")
            P.wslot("vector", "wqt")
            P.do("vector", lambda e, jh=jh: e.reduce_sum(
                wqout[:, jh * 64:(jh + 1) * 64].unsqueeze(2),
                wqtile[:].rearrange("p (d h) -> p d h", h=32), axis=AX.X), inc="DV")
        P.chain("vector")
        P.do("vector", lambda e: e.tensor_scalar_mul(wqout[:], wqout[:], SCALE_A))
        P.chain("vector")
        P.wslot("vector", "gst")
        P.wait("vector", "PE")
        P.do("vector", lambda e: e.tensor_copy(wqout_b[:], wqout[:]))
        P.wait("gpsimd", "DV")
        P.dma("gpsimd", lambda e, i=i: e.dma_start(out=wqe_part.ap()[i * 128:(i + 1) * 128, :],
                                                   in_=wqout_b[:]), "gst")
    P.wslot("gpsimd", "gst")
    P.wait("gpsimd", "DV")
    P.dma("gpsimd", lambda e: e.dma_start(out=wqtile[:].bitcast(BF16)[:, 0:512],
                                          in_=wqe_part.ap().rearrange("(a p) d -> p a d", p=128)),
          "gc")
    P.wslot("tensor", "gc")
    for g4 in range(4):
        P.do("tensor", lambda e, g4=g4: e.matmul(
            psWS[0:1, 0:128], ones_col_b[:], wqtile[:].bitcast(BF16)[:, g4 * 128:(g4 + 1) * 128],
            start=(g4 == 0), stop=(g4 == 3)), inc="PE")
    P.wait("vector", "PE")
    P.do("vector", lambda e: e.tensor_copy(wqout[0:1, 0:128], psWS[0:1, 0:128]), inc="DV")
    P.wait("gpsimd", "DV")
    P.dma("gpsimd", lambda e: e.dma_start(out=wqs_part.ap(), in_=wqout[0:1, 0:128]), "gc")
    P.wslot("gpsimd", "gst")
    P.wslot("gpsimd", "gc")
    P.do("gpsimd", lambda e: e.collective_compute(
        "AllGather", ALU.bypass, replica_groups=[list(range(NCORE))],
        ins=[wqe_part.ap()], outs=[env["wqe_full"].ap()]), inc="CC")
    P.do("gpsimd", lambda e: e.collective_compute(
        "AllReduce", ALU.add, replica_groups=[list(range(NCORE))],
        ins=[wqs_part.ap()], outs=[env["wqs_full"].ap()]), inc="CC")

    for oh in range(2):
        P.wait("gpsimd", "PE")
        for ot in range(2):
            P.dma("gpsimd", lambda e, oh=oh, ot=ot: e.dma_start(
                out=wtileW[ot][:], in_=w_slice[(oh * 2 + ot) * 128:(oh * 2 + ot + 1) * 128, :]),
                "wtl")
        P.wslot("vector", "wtl")
        for ot in range(2):
            P.do("vector", lambda e, oh=oh, ot=ot: e.reduce_sum(
                smt[:, oh * 2 + ot:oh * 2 + ot + 1], wtileW[ot][:], axis=AX.X), inc="DV")
        P.wslot("tensor", "wtl")
        P.wait("tensor", "DV")
        for dc in range(NCH):
            ws = wstage[(dc // 4) % 2]
            for ot in range(2):
                P.do("tensor", lambda e, dc=dc, ot=ot: e.transpose(
                    psTb[:, ot * 128:(ot + 1) * 128],
                    wtileW[ot][:, dc * 128:(dc + 1) * 128], idB[:]), inc="PE")
            P.wait("vector", "PE")
            if dc % 4 == 0:
                P.wslot("vector", "wst")
            P.do("vector", lambda e, dc=dc, ws=ws: e.tensor_copy(
                ws[:, (dc % 4) * 256:(dc % 4) * 256 + 256], psTb[:, 0:256]))
            P.wait("tensor", "DV")
            if dc % 4 == 3:
                P.wait("sync", "DV")
                P.dma("sync", lambda e, dc=dc, oh=oh, ws=ws: e.dma_start(
                    out=bass.AP(tensor=wt_part, offset=(dc - 3) * 128 * OSL + oh * 256,
                                ap=[[OSL, 128], [128 * OSL, 4], [1, 256]]),
                    in_=ws[:].rearrange("p (g c) -> p g c", c=256)), "wst")
    pb.close()
    P.wait("gpsimd", "DV")
    P.dma("gpsimd", lambda e: e.dma_start(
        out=wsum_part.ap().rearrange("(a p) b -> p a b", p=128), in_=smt[:, 0:4],
        allow_slow_non_contiguous=True), "gc")

    P.wslot("tensor", "gc")
    rwt_loc = p.enter_context(nc.sbuf_tensor("p0_rwt", [128, 15 * NCH], BF16))
    for dc in range(NCH):
        P.do("tensor", lambda e, dc=dc: e.transpose(
            psTb[:, (dc % 2) * 1024:(dc % 2) * 1024 + 15],
            rw_sb[:, dc * 128:(dc + 1) * 128], idB[0:15, 0:15]), inc="PE")
        P.wait("vector", "PE")
        P.do("vector", lambda e, dc=dc: e.tensor_copy(
            rwt_loc[:, dc * 15:(dc + 1) * 15],
            psTb[:, (dc % 2) * 1024:(dc % 2) * 1024 + 15]), inc="DV")
        if dc >= 1:
            P.wait("tensor", "DV", P.cnt["DV"] - 1)
    P.wait("gpsimd", "DV")
    P.dma("gpsimd", lambda e: e.dma_start(out=env["rwt_dram"].ap(), in_=rwt_loc[:]), "gc")
    b2stg = p.enter_context(nc.sbuf_tensor("p0_b2s", [16, 1024], BF16))
    for ocl in range(4):
        if ocl >= 1:
            P.wslot("tensor", "gc")
        for ee in range(8):
            a_idx = ee * 4 + ocl
            P.do("tensor", lambda e, a_idx=a_idx: e.transpose(
                psTb[0:16, 0:128], castA[:, a_idx * 16:(a_idx + 1) * 16], idB[:]), inc="PE")
            P.wait("vector", "PE")
            P.do("vector", lambda e, ee=ee: e.tensor_copy(
                b2stg[:, ee * 128:(ee + 1) * 128], psTb[0:16, 0:128]), inc="DV")
            P.wait("tensor", "DV")
        P.wait("gpsimd", "DV")
        P.dma("gpsimd", lambda e, ocl=ocl: e.dma_start(
            out=bass.AP(tensor=env["b2_part"], offset=ocl * 128,
                        ap=[[512, 16], [16 * 512, 8], [1, 128]]),
            in_=b2stg[:].rearrange("p (a c) -> p a c", a=8)), "gc")

    P.wslot("gpsimd", "gc")
    P.wait("gpsimd", "DV")
    P.dma("gpsimd", lambda e: e.dma_start(out=castA[:], in_=env["a_bf"].ap()), "gc")
    P.wslot("tensor", "gc")
    for dc in range(NCH):
        P.do("tensor", lambda e, dc=dc: e.transpose(
            psTb[:, (dc % 2) * 1024:(dc % 2) * 1024 + 128],
            castA[:, dc * 128:(dc + 1) * 128], idB[:]), inc="PE")
        P.wait("vector", "PE")
        P.wslot("vector", "gc")
        P.do("vector", lambda e, dc=dc: e.tensor_copy(
            wqout_b[:], psTb[:, (dc % 2) * 1024:(dc % 2) * 1024 + 128]))
        P.wait("gpsimd", "DV")
        P.dma("gpsimd", lambda e, dc=dc: e.dma_start(
            out=env["a2t_dram"].ap()[dc * 128:(dc + 1) * 128, :], in_=wqout_b[:]), "gc")
        P.wait("tensor", "DV")
    for sl in ("gc", "wst", "wtl"):
        P.wslot("gpsimd", sl)
    for (pi, po) in ((wt_part, env["wt_full"]), (wsum_part, env["wsum_full"]),
                     (env["b2_part"], env["b2_full"])):
        P.do("gpsimd", lambda e, pi=pi, po=po: e.collective_compute(
            "AllGather", ALU.bypass, replica_groups=[list(range(NCORE))],
            ins=[pi.ap()], outs=[po.ap()]), inc="CC")
    P.wait("sync", "CC", 2)
    P.dma("sync", lambda e: e.dma_start(out=env["rows"][0:1, 0:128],
                                        in_=env["wqs_full"].ap()), "wqs")
    P.wslot("vector", "wqs")
    P.do("vector", lambda e: e.tensor_scalar_mul(env["rows"][0:1, 0:128],
                                                 env["rows"][0:1, 0:128], SCALE_A))
    P.chain("vector")
    P.do("vector", lambda e: e.tensor_copy(env["wsums"][0:1, 0:128],
                                           env["rows"][0:1, 0:128]))
    p.close()


def emit_attn_a(nc, P, env):
    xv = env["xv"]
    idF, idB = env["idF"], env["idB"]
    ones_col_b, ones_row_b = env["ones_col_b"], env["ones_row_b"]
    rows, rows_b = env["rows"], env["rows_b"]

    p = ExitStack()
    xb = [p.enter_context(nc.sbuf_tensor(f"aa_xb{i}", [128, T], BF16)) for i in range(2)]
    sqb = [p.enter_context(nc.sbuf_tensor("aa_sq0", [128, T], BF16))] * 2
    wq_t = [p.enter_context(nc.sbuf_tensor(f"aa_wq{i}", [128, 512], BF16)) for i in range(2)]
    wk_t = [p.enter_context(nc.sbuf_tensor(f"aa_wk{i}", [128, 512], BF16)) for i in range(2)]
    wv_t = [p.enter_context(nc.sbuf_tensor(f"aa_wv{i}", [128, 512], BF16)) for i in range(2)]
    afm = p.enter_context(nc.sbuf_tensor("aa_afm", [128, T], F32))
    kfm = p.enter_context(nc.sbuf_tensor("aa_kfm", [128, T], F32))
    vfm = p.enter_context(nc.sbuf_tensor("aa_vfm", [128, T], F32))
    rbf = p.enter_context(nc.sbuf_tensor("aa_rbf", [128, T], F32))

    ps1 = ExitStack()
    psSx = ps1.enter_context(nc.psum_tensor("aa_psSx", [1, 1024], F32))
    psSq = ps1.enter_context(nc.psum_tensor("aa_psSq", [1, 1024], F32))
    for dc in range(NCH):
        bb, sq = xb[dc % 2], sqb[dc % 2]
        if dc >= 1:
            P.wait("scalar", "PE")
        P.do("scalar", lambda e, dc=dc, bb=bb: e.activation(bb[:], xv(dc), AF.Identity))
        P.chain("scalar")
        P.do("scalar", lambda e, bb=bb, sq=sq: e.activation(sq[:], bb[:], AF.Square))
        P.wait("tensor", "AC")
        for th in range(2):
            P.do("tensor", lambda e, dc=dc, th=th, bb=bb: e.matmul(
                psSx[0:1, th * 512:(th + 1) * 512], ones_col_b[:],
                bb[:, th * 512:(th + 1) * 512],
                start=(dc == 0), stop=(dc == NCH - 1)), inc="PE")
            P.do("tensor", lambda e, dc=dc, th=th, sq=sq: e.matmul(
                psSq[0:1, th * 512:(th + 1) * 512], ones_col_b[:],
                sq[:, th * 512:(th + 1) * 512],
                start=(dc == 0), stop=(dc == NCH - 1)), inc="PE")
    P.wait("vector", "PE")
    rr = _ln_rows(nc, P, env, T, psSx[0:1, 0:T], psSq[0:1, 0:T], "negm")
    P.chain("vector")
    P.do("vector", lambda e: e.tensor_copy(rows_b[0:1, 0:T], rr["t3"]))
    ps1.close()

    ps2 = ExitStack()
    psR = ps2.enter_context(nc.psum_tensor("aa_psR", [128, 1024], F32))
    P.do("vector", lambda e: e.tensor_copy(rows_b[0:1, T:2 * T], rr["r"]), inc="DV")
    P.wait("tensor", "DV")
    for th in range(2):
        P.do("tensor", lambda e, th=th: e.matmul(
            psR[:, th * 512:(th + 1) * 512], ones_row_b[0:1, 0:128],
            rows_b[0:1, T + th * 512:T + (th + 1) * 512], start=True, stop=True), inc="PE")
    P.wait("vector", "PE")
    P.do("vector", lambda e: e.tensor_copy(rbf[:], psR[:, 0:T]), inc="DV")
    ps2.close()
    P.wait("tensor", "DV")

    P.wait("sync", "CC", 2)
    ps3 = ExitStack()
    psQ = ps3.enter_context(nc.psum_tensor("aa_psQ", [128, 1024], F32))
    psK = ps3.enter_context(nc.psum_tensor("aa_psK", [128, 1024], F32))
    psV = ps3.enter_context(nc.psum_tensor("aa_psV", [128, 1024], F32))
    psWSq = ps3.enter_context(nc.psum_tensor("aa_psWSq", [1, 512], F32))
    for g4 in range(8):
        wq, wk, wv = wq_t[g4 % 2], wk_t[g4 % 2], wv_t[g4 % 2]
        if g4 >= 2:
            P.wait("sync", "PE")
            P.wait("scalar", "PE")
        P.dma("sync", lambda e, g4=g4, wq=wq: e.dma_start(
            out=wq[:], in_=env["wqe_full"].ap().rearrange("(g p) d -> p g d", p=128)[
                :, g4 * 4:(g4 + 1) * 4, :]), f"aw{g4 % 2}")
        P.dma("sync", lambda e, g4=g4, wk=wk: e.dma_start(
            out=wk[:], in_=env["wk_bf"].ap().rearrange("(g p) d -> p g d", p=128)[
                :, g4 * 4:(g4 + 1) * 4, :]), f"aw{g4 % 2}")
        P.dma("sync", lambda e, g4=g4, wv=wv: e.dma_start(
            out=wv[:], in_=env["wv_bf"].ap().rearrange("(g p) d -> p g d", p=128)[
                :, g4 * 4:(g4 + 1) * 4, :]), f"aw{g4 % 2}")
        for i4 in range(4):
            dc = g4 * 4 + i4
            bb = xb[dc % 2]
            if dc >= 2:
                P.wait("scalar", "PE")
            P.do("scalar", lambda e, dc=dc, bb=bb: e.activation(bb[:], xv(dc), AF.Identity),
                 inc="AC")
            P.wait("tensor", "AC")
            P.wslot("tensor", f"aw{g4 % 2}")
            first, last = dc == 0, False
            for th in range(2):
                sl = slice(th * 512, (th + 1) * 512)
                P.do("tensor", lambda e, dc=dc, th=th, bb=bb, wq=wq, i4=i4: e.matmul(
                    psQ[:, th * 512:(th + 1) * 512], wq[:, i4 * 128:(i4 + 1) * 128],
                    bb[:, th * 512:(th + 1) * 512], start=(dc == 0), stop=False), inc="PE")
                P.do("tensor", lambda e, dc=dc, th=th, bb=bb, wk=wk, i4=i4: e.matmul(
                    psK[:, th * 512:(th + 1) * 512], wk[:, i4 * 128:(i4 + 1) * 128],
                    bb[:, th * 512:(th + 1) * 512], start=(dc == 0), stop=False), inc="PE")
                P.do("tensor", lambda e, dc=dc, th=th, bb=bb, wv=wv, i4=i4: e.matmul(
                    psV[:, th * 512:(th + 1) * 512], wv[:, i4 * 128:(i4 + 1) * 128],
                    bb[:, th * 512:(th + 1) * 512], start=(dc == 0), stop=False), inc="PE")
    for (psX, co) in ((psQ, 0), (psK, 1), (psV, 2)):
        for th in range(2):
            P.do("tensor", lambda e, psX=psX, co=co, th=th: e.matmul(
                psX[:, th * 512:(th + 1) * 512], env["wsums"][0:1, co * 128:(co + 1) * 128],
                rows_b[0:1, th * 512:(th + 1) * 512], start=False, stop=True), inc="PE")
    P.wait("vector", "PE")
    for (psX, dst, bias) in ((psQ, afm, env["bqe"]), (psK, kfm, env["bkc"]), (psV, vfm, env["bvc"])):
        P.do("vector", lambda e, psX=psX, dst=dst: e.tensor_tensor(dst[:], psX[:, 0:T], rbf[:],
                                                                   ALU.mult))
        P.chain("vector")
        P.do("vector", lambda e, dst=dst, bias=bias: e.tensor_scalar(
            dst[:], dst[:], bias[:], None, ALU.add))
    ps3.close()

    aT, kT, vT = afm, kfm, vfm
    ps4 = ExitStack()
    psTr = ps4.enter_context(nc.psum_tensor("aa_psTr", [128, 1024], F32))
    P.wait("tensor", "DV")
    first_tp = True
    for (src, dst) in ((afm, aT), (kfm, kT), (vfm, vT)):
        for tc in range(NTT):
            if not first_tp:
                P.wait("tensor", "DV", P.cnt["DV"] - 1)
            first_tp = False
            P.do("tensor", lambda e, src=src, tc=tc: e.transpose(
                psTr[:, (tc % 2) * 512:(tc % 2) * 512 + 128],
                src[:, tc * 128:(tc + 1) * 128], idF[:]), inc="PE")
            P.wait("vector", "PE")
            P.do("vector", lambda e, dst=dst, tc=tc: e.tensor_copy(
                dst[:, tc * 128:(tc + 1) * 128],
                psTr[:, (tc % 2) * 512:(tc % 2) * 512 + 128]), inc="DV")
    ps4.close()

    numT = p.enter_context(nc.sbuf_tensor("aa_numT", [128, T], F32))
    zT = p.enter_context(nc.sbuf_tensor("aa_zT", [128, T], F32))
    sc = [p.enter_context(nc.sbuf_tensor(f"aa_sc{i}", [128, 2048], F32)) for i in range(2)]
    ex = [p.enter_context(nc.sbuf_tensor(f"aa_ex{i}", [128, 2048], BF16)) for i in range(2)]
    ev = [t[:].bitcast(BF16) for t in sc]
    psHeat = p.enter_context(nc.psum_tensor("aa_psHeat", [1, 512], F32))
    DBLK = 16
    NBLK = NTT * (128 // DBLK)
    P.wait("vector", "DV")
    P.wait("scalar", "DV")

    def blk(k):
        ti, bl = divmod(k, 128 // DBLK)
        a_b = aT[:, ti * 128 + bl * DBLK: ti * 128 + (bl + 1) * DBLK].unsqueeze(2).broadcast_to(
            [128, DBLK, 128])
        k_b = kT[:, ti * 128:(ti + 1) * 128].unsqueeze(1).broadcast_to([128, DBLK, 128])
        v_b = vT[:, ti * 128:(ti + 1) * 128].unsqueeze(1).broadcast_to([128, DBLK, 128])
        out_sl = slice(ti * 128 + bl * DBLK, ti * 128 + (bl + 1) * DBLK)
        return a_b, k_b, v_b, out_sl

    def emit_tt_exp(k):
        a_b, k_b, _, _ = blk(k)
        scb = sc[k % 2]
        sc3 = scb[:].rearrange("p (d e) -> p d e", e=128)
        if k >= 2:
            P.wait("vector", "AC", P.cnt["AC"] - 1)
            P.chain("vector")
        P.do("vector", lambda e, sc3=sc3, a_b=a_b, k_b=k_b: e.tensor_tensor(
            sc3, a_b, k_b, ALU.mult))
        P.wait("scalar", "DV")
        P.do("scalar", lambda e, scb=scb, k=k: e.activation(ex[k % 2][:], scb[:], AF.Exp))
        P.wait("tensor", "DV")
        for _h in range(3):
            P.do("tensor", lambda e: e.matmul(psHeat[0:1, 0:1], ones_col_b[:], ones_col_b[:],
                                              start=True, stop=True), inc="PE")

    emit_tt_exp(0)
    for k in range(NBLK):
        if k + 1 < NBLK:
            emit_tt_exp(k + 1)
        _, _, v_bp, out_sl = blk(k)
        ex3 = ex[k % 2][:].rearrange("p (d e) -> p d e", e=128)
        evb3 = ev[k % 2][:, 0:2048].rearrange("p (d e) -> p d e", e=128)
        P.wait("vector", "AC", P.cnt["AC"] - 1 if k + 1 < NBLK else P.cnt["AC"])
        P.do("vector", lambda e, out_sl=out_sl, ex3=ex3: e.reduce_sum(
            zT[:, out_sl].unsqueeze(2), ex3, axis=AX.X))
        P.do("vector", lambda e, evb3=evb3, ex3=ex3, v_bp=v_bp: e.tensor_tensor(
            evb3, ex3, v_bp, ALU.mult))
        P.chain("vector")
        P.do("vector", lambda e, out_sl=out_sl, evb3=evb3: e.reduce_sum(
            numT[:, out_sl].unsqueeze(2), evb3, axis=AX.X))
        P.wait("tensor", "DV")
        for _h in range(2):
            P.do("tensor", lambda e: e.matmul(psHeat[0:1, 0:1], ones_col_b[:], ones_col_b[:],
                                              start=True, stop=True), inc="PE")
    oT = sc[0][:].bitcast(BF16)[:, 0:T]
    P.chain("vector")
    P.do("vector", lambda e: e.reciprocal(zT[:], zT[:]))
    P.chain("vector")
    P.do("vector", lambda e: e.tensor_tensor(oT, numT[:], zT[:], ALU.mult))

    ofm = sc[0][:].bitcast(BF16)[:, T:2 * T]
    mj = [sc[1][:].bitcast(BF16)[:, i * 128:(i + 1) * 128] for i in range(2)]
    ps5 = ExitStack()
    psTo = ps5.enter_context(nc.psum_tensor("aa_psTo", [128, 4096], BF16))
    psRep = [ps5.enter_context(nc.psum_tensor(f"aa_psRep{i}", [128, 512], F32)) for i in range(2)]
    P.wait("tensor", "DV")
    for tc in range(NTT):
        if tc >= 4:
            P.wait("tensor", "DV", P.cnt["DV"] - 3)
        P.do("tensor", lambda e, tc=tc: e.transpose(
            psTo[:, (tc % 4) * 1024:(tc % 4) * 1024 + 128], oT[:, tc * 128:(tc + 1) * 128], idB[:]),
            inc="PE")
        P.wait("vector", "PE")
        P.do("vector", lambda e, tc=tc: e.tensor_copy(
            ofm[:, tc * 128:(tc + 1) * 128], psTo[:, (tc % 4) * 1024:(tc % 4) * 1024 + 128]),
            inc="DV")
    for j in range(NCH):
        mjb = mj[j % 2]
        if j >= 2:
            P.wait("vector", "PE")
        P.do("vector", lambda e, mjb=mjb: e.memset(mjb[:], 0.0), inc="DV")
        P.wait("gpsimd", "DV")
        P.dma("gpsimd", lambda e, j=j, mjb=mjb: e.dma_start(
            out=mjb[4 * j:4 * j + 4, :], in_=env["m0rep"][0:4, :]), f"mj{j % 2}")
        P.wslot("tensor", f"mj{j % 2}")
        P.wait("tensor", "DV")
        for th in range(2):
            P.do("tensor", lambda e, j=j, th=th, mjb=mjb: e.matmul(
                psRep[th][:], mjb, ofm[:, th * 512:(th + 1) * 512],
                start=True, stop=True), inc="PE")
        P.wait("vector", "PE")
        for th in range(2):
            P.do("vector", lambda e, j=j, th=th: e.tensor_tensor(
                xv(j, th * 512, 512), psRep[th][:], xv(j, th * 512, 512), ALU.add), inc="DV")
    ps5.close()
    p.close()


def emit_ffn(nc, P, env, call):
    xv = env["xv"]
    idB, idF = env["idB"], env["idF"]
    ones_col_b, ones_row_b = env["ones_col_b"], env["ones_row_b"]
    rows, rows_b = env["rows"], env["rows_b"]

    p = ExitStack()
    xln = p.enter_context(nc.sbuf_tensor(f"ff_xln{call}", [128, NCH * FT], BF16))
    xcb = [p.enter_context(nc.sbuf_tensor(f"ff_xc{call}{i}", [128, FT], BF16)) for i in range(2)]
    rwT = p.enter_context(nc.sbuf_tensor(f"ff_rwT{call}", [128, 15 * NCH], BF16))
    sqb = [p.enter_context(nc.sbuf_tensor(f"ff_sq{call}0", [128, FT], BF16))] * 2
    woc = [p.enter_context(nc.sbuf_tensor(f"ff_w{call}{i}", [128, 4096], BF16)) for i in range(2)]
    b2c = [p.enter_context(nc.sbuf_tensor(f"ff_b2{call}{i}", [128, 128], BF16)) for i in range(2)]
    rbf_b = p.enter_context(nc.sbuf_tensor(f"ff_rb{call}", [128, FT], BF16))
    mrb_b = p.enter_context(nc.sbuf_tensor(f"ff_mrb{call}", [128, FT], BF16))
    lgT = p.enter_context(nc.sbuf_tensor(f"ff_lgT{call}", [128, 64], F32))
    mx8 = p.enter_context(nc.sbuf_tensor(f"ff_mx8{call}", [128, 8], F32))
    msk = p.enter_context(nc.sbuf_tensor(f"ff_msk{call}", [128, 16], F32))
    pmb = p.enter_context(nc.sbuf_tensor(f"ff_pm{call}", [128, 16], BF16))
    lg = p.enter_context(nc.sbuf_tensor(f"ff_lg{call}", [15, FT], F32))
    pmT = p.enter_context(nc.sbuf_tensor(f"ff_pmT{call}", [15, FT], BF16))
    wfm = p.enter_context(nc.sbuf_tensor(f"ff_wfm{call}", [8, FT], BF16))
    wsb = xcb[0]
    lora = p.enter_context(nc.sbuf_tensor(f"ff_lora{call}", [128, FT], BF16))

    psO = [p.enter_context(nc.psum_tensor(f"ff_psO{call}{i}", [128, 512], F32)) for i in range(2)]
    psH = p.enter_context(nc.psum_tensor(f"ff_psH{call}", [128, 512], F32))
    psMisc = p.enter_context(nc.psum_tensor(f"ff_psM{call}", [128, 512], F32))
    psTb = p.enter_context(nc.psum_tensor(f"ff_psTb{call}", [128, 512], BF16))
    psSx = p.enter_context(nc.psum_tensor(f"ff_psSx{call}", [1, 512], F32))
    psSq = p.enter_context(nc.psum_tensor(f"ff_psSq{call}", [1, 512], F32))

    P.wait("sync", "CC", 5)
    P.wait("scalar", "CC", 5)
    P.dma("sync", lambda e: e.dma_start(out=rwT[:], in_=env["rwt_dram"].ap()), f"frw{call}")

    evac_dv = []
    mm_pe = []
    for tt in range(NFT):
        tok = tt * FT

        P.wait("scalar", "DV")
        for dc in range(NCH):
            xc, sq = xcb[dc % 2], sqb[0]
            if tt > 0 or dc >= 1:
                P.wait("scalar", "PE")
            P.do("scalar", lambda e, dc=dc, tok=tok, xc=xc: e.activation(
                xc[:], xv(dc, tok, FT), AF.Identity))
            P.chain("scalar")
            P.do("scalar", lambda e, xc=xc, sq=sq: e.activation(sq[:], xc[:], AF.Square))
            P.wait("tensor", "AC")
            P.do("tensor", lambda e, dc=dc, xc=xc: e.matmul(
                psSx[0:1, :], ones_col_b[:], xc[:],
                start=(dc == 0), stop=(dc == NCH - 1)), inc="PE")
            P.do("tensor", lambda e, dc=dc, sq=sq: e.matmul(
                psSq[0:1, :], ones_col_b[:], sq[:],
                start=(dc == 0), stop=(dc == NCH - 1)), inc="PE")
        P.wait("vector", "PE")
        rr = _ln_rows(nc, P, env, FT, psSx[0:1, :], psSq[0:1, :], "mr")
        P.chain("vector")
        P.do("vector", lambda e: e.tensor_copy(rows_b[0:1, 0:FT], rr["r"]))
        P.do("vector", lambda e: e.tensor_copy(rows_b[0:1, FT:2 * FT], rr["t3"]))
        P.wait("tensor", "DV")
        P.do("tensor", lambda e: e.matmul(psMisc[:], ones_row_b[0:1, 0:128],
                                          rows_b[0:1, 0:FT], start=True, stop=True), inc="PE")
        P.wait("vector", "PE")
        P.do("vector", lambda e: e.tensor_copy(rbf_b[:], psMisc[:]), inc="DV")
        P.wait("tensor", "DV")
        P.do("tensor", lambda e: e.matmul(psMisc[:], ones_row_b[0:1, 0:128],
                                          rows_b[0:1, FT:2 * FT], start=True, stop=True), inc="PE")
        P.wait("vector", "PE")
        P.do("vector", lambda e: e.tensor_copy(mrb_b[:], psMisc[:]), inc="DV")
        if tt >= 1:
            P.wait("scalar", "PE", mm_pe[-1])
        P.wait("vector", "DV")
        for dc in range(NCH):
            P.do("scalar", lambda e, dc=dc, tok=tok: e.activation(
                xln[:, dc * FT:(dc + 1) * FT], xv(dc, tok, FT), AF.Identity))
            P.wait("vector", "AC")
            P.do("vector", lambda e, dc=dc: e.tensor_tensor(
                xln[:, dc * FT:(dc + 1) * FT], xln[:, dc * FT:(dc + 1) * FT], rbf_b[:], ALU.mult))
            P.chain("vector")
            P.do("vector", lambda e, dc=dc: e.tensor_tensor(
                xln[:, dc * FT:(dc + 1) * FT], xln[:, dc * FT:(dc + 1) * FT], mrb_b[:],
                ALU.subtract))

        P.wait("tensor", "DV")
        P.wslot("tensor", f"frw{call}")
        for dc in range(NCH):
            P.do("tensor", lambda e, dc=dc: e.matmul(
                psMisc[0:15, :], rwT[:, dc * 15:(dc + 1) * 15],
                xln[:, dc * FT:(dc + 1) * FT], start=(dc == 0), stop=(dc == NCH - 1)), inc="PE")
        P.wait("vector", "PE")
        P.do("vector", lambda e: e.tensor_copy(lg[:], psMisc[0:15, :]), inc="DV")
        for tc in range(4):
            P.wait("tensor", "DV")
            P.do("tensor", lambda e, tc=tc: e.transpose(
                psMisc[0:128, 0:15], lg[:, tc * 128:(tc + 1) * 128], idF[0:15, 0:15]), inc="PE")
            P.wait("vector", "PE")
            P.do("vector", lambda e: e.tensor_copy(lgT[:, 0:15], psMisc[0:128, 0:15]))
            P.chain("vector")
            P.do("vector", lambda e: e.tensor_tensor(lgT[:, 0:15], lgT[:, 0:15], env["rb_bc"][:],
                                                     ALU.add))
            P.wait("scalar", "DV")
            P.do("scalar", lambda e: e.activation(lgT[:, 16:31], lgT[:, 0:15], AF.Exp,
                                                  accum_out=lgT[:, 32:33]))
            P.wait("vector", "AC")
            P.do("vector", lambda e: e.reciprocal(lgT[:, 33:34], lgT[:, 32:33]))
            P.chain("vector")
            P.do("vector", lambda e: e.tensor_scalar(lgT[:, 16:31], lgT[:, 16:31],
                                                     lgT[:, 33:34], None, ALU.mult))
            P.chain("vector")
            P.do("vector", lambda e: e.max(out=mx8[:], in_=lgT[:, 16:31]))
            P.chain("vector")
            P.do("vector", lambda e: e.tensor_scalar(msk[:, 0:15], lgT[:, 16:31],
                                                     mx8[:, 7:8], None, ALU.is_ge))
            P.chain("vector")
            P.do("vector", lambda e: e.tensor_tensor(pmb[:, 0:15], lgT[:, 16:31], msk[:, 0:15],
                                                     ALU.mult))
            P.wait("tensor", "DV")
            P.do("tensor", lambda e, tc=tc: e.transpose(psTb[0:15, 0:128], pmb[:, 0:15], idB[:]), inc="PE")
            P.wait("vector", "PE")
            P.do("vector", lambda e, tc=tc: e.tensor_copy(pmT[:, tc * 128:(tc + 1) * 128],
                                                          psTb[0:15, 0:128]), inc="DV")
        P.wait("tensor", "DV")
        P.do("tensor", lambda e: e.matmul(psMisc[0:8, :], env["onehotT"][:], pmT[:],
                                          start=True, stop=True), inc="PE")
        P.wait("vector", "PE")
        P.do("vector", lambda e: e.tensor_copy(wfm[:], psMisc[0:8, :]), inc="DV")
        P.wait("tensor", "DV")
        P.do("tensor", lambda e: e.matmul(psMisc[:], env["eexp"][:], wfm[:],
                                          start=True, stop=True), inc="PE")
        P.wait("vector", "PE")
        P.do("vector", lambda e: e.tensor_copy(wsb[:], psMisc[:]), inc="DV")

        if mm_pe:
            P.wait("scalar", "PE", mm_pe[-1])
        P.dma("scalar", lambda e: e.dma_start(
            out=woc[0][:], in_=env["a2t_dram"].ap().rearrange("(g p) d -> p g d", p=128)),
            f"fa{call}")
        P.wslot("tensor", f"fa{call}")
        for dc in range(NCH):
            P.do("tensor", lambda e, dc=dc: e.matmul(
                psH[:], woc[0][:, dc * 128:(dc + 1) * 128], xln[:, dc * FT:(dc + 1) * FT],
                start=(dc == 0), stop=(dc == NCH - 1)), inc="PE")
        h_pe = P.cnt["PE"]
        P.wait("vector", "PE")
        P.chain("vector")
        P.do("vector", lambda e: e.tensor_tensor(lora[:], psH[:], wsb[:], ALU.mult))
        P.wait("tensor", "DV")
        P.wait("vector", "AC")
        if env.get("dbg") is not None and call == 0 and tt == 0:
            dbgap = env["dbg"]
            P.wait("gpsimd", "DV")
            P.dma("gpsimd", lambda e, dbgap=dbgap: e.dma_start(
                out=dbgap[:, 0:16384], in_=xln[:]), "dbgd")
            P.dma("gpsimd", lambda e, dbgap=dbgap: e.dma_start(
                out=dbgap[:, 16384:16896], in_=wsb[:]), "dbgd")
            P.dma("gpsimd", lambda e, dbgap=dbgap: e.dma_start(
                out=dbgap[:, 16896:17408], in_=lora[:]), "dbgd")
            P.dma("gpsimd", lambda e, dbgap=dbgap: e.dma_start(
                out=dbgap[0:15, 17408:17920], in_=pmT[:]), "dbgd")
            P.dma("gpsimd", lambda e, dbgap=dbgap: e.dma_start(
                out=dbgap[0:8, 17920:18432], in_=wfm[:]), "dbgd")
            P.wslot("scalar", "dbgd")
            P.wslot("vector", "dbgd")
            P.wslot("tensor", "dbgd")

        def b2dma(oc):
            bc = b2c[oc % 2]
            P.dma("scalar", lambda e, oc=oc, bc=bc: e.dma_start(
                out=bc[:], in_=bass.AP(tensor=env["b2_full"],
                                       offset=(oc // 4) * 128 * 512 + (oc % 4) * 128,
                                       ap=[[512, 128], [1, 128]])), f"fb{oc % 2}")
        for oc in range(NCH):
            w = woc[oc % 2]
            rr4 = oc // 4
            cc4 = oc % 4
            if oc < 2:
                P.wait("sync", "PE", h_pe)
            if len(mm_pe) >= 2:
                P.wait("sync", "PE", mm_pe[-2])
            P.dma("sync", lambda e, rr4=rr4, cc4=cc4, w=w: e.dma_start(
                out=w[:], in_=bass.AP(
                    tensor=env["wt_full"],
                    offset=(rr4 * D) * OSL + cc4 * 128,
                    ap=[[OSL, 128], [OSL * 128, 32], [1, 128]])), f"fw{oc % 2}")
            if oc == 0:
                P.wait("scalar", "PE", h_pe)
                b2dma(0)
            if oc + 1 < NCH:
                if len(mm_pe) >= 1:
                    P.wait("scalar", "PE", mm_pe[-1])
                b2dma(oc + 1)
            P.wslot("tensor", f"fw{oc % 2}")
            if len(evac_dv) >= 2:
                P.wait("tensor", "DV", evac_dv[-2])
            po = psO[oc % 2]
            for dc in range(NCH):
                P.do("tensor", lambda e, dc=dc, w=w, po=po: e.matmul(
                    po[:], w[:, dc * 128:(dc + 1) * 128], xln[:, dc * FT:(dc + 1) * FT],
                    start=(dc == 0), stop=False), inc="PE")
            P.wslot("tensor", f"fb{oc % 2}")
            P.do("tensor", lambda e, oc=oc, po=po, bc=b2c[oc % 2]: e.matmul(
                po[:], bc[:], lora[:], start=False, stop=True), inc="PE")
            P.wait("vector", "PE")
            P.do("vector", lambda e, oc=oc, po=po, tok=tok: e.scalar_tensor_tensor(
                xv(oc, tok, FT), po[:], env["bb_fm"][:, oc:oc + 1], xv(oc, tok, FT),
                ALU.add, ALU.add))
            evac_dv.append(P.cnt["DV"])
            mm_pe.append(P.cnt["PE"])
    p.close()


def emit_attn_b(nc, P, env):
    xv = env["xv"]
    ones_col_b, ones_row_b = env["ones_col_b"], env["ones_row_b"]
    rows, rows_b = env["rows"], env["rows_b"]

    p = ExitStack()
    zt = [p.enter_context(nc.sbuf_tensor(f"ab_z{i}", [128, T], BF16)) for i in range(2)]
    sqb = [p.enter_context(nc.sbuf_tensor(f"ab_sq{i}", [128, T], BF16)) for i in range(2)]
    rbf_b = p.enter_context(nc.sbuf_tensor("ab_rb", [128, T], BF16))
    mrb_b = p.enter_context(nc.sbuf_tensor("ab_mrb", [128, T], BF16))
    gbuf = p.enter_context(nc.sbuf_tensor("ab_g", [128, 512], F32))
    junk = p.enter_context(nc.sbuf_tensor("ab_junk", [128, 256], F32))
    psHeat = p.enter_context(nc.psum_tensor("ab_psHeat", [1, 512], F32))
    psSx = p.enter_context(nc.psum_tensor("ab_psSx", [1, 1024], F32))
    psSq = p.enter_context(nc.psum_tensor("ab_psSq", [1, 1024], F32))
    psMisc = p.enter_context(nc.psum_tensor("ab_psM", [128, 1024], F32))

    for dc in range(NCH):
        z, sq = zt[dc % 2], sqb[dc % 2]
        if dc >= 2:
            P.wait("scalar", "PE")
        P.do("scalar", lambda e, dc=dc, z=z: e.activation(z[:], xv(dc), AF.Identity))
        P.chain("scalar")
        P.do("scalar", lambda e, z=z, sq=sq: e.activation(sq[:], z[:], AF.Square))
        P.wait("tensor", "AC")
        for th in range(2):
            P.do("tensor", lambda e, dc=dc, th=th, z=z: e.matmul(
                psSx[0:1, th * 512:(th + 1) * 512], ones_col_b[:],
                z[:, th * 512:(th + 1) * 512], start=(dc == 0), stop=(dc == NCH - 1)), inc="PE")
            P.do("tensor", lambda e, dc=dc, th=th, sq=sq: e.matmul(
                psSq[0:1, th * 512:(th + 1) * 512], ones_col_b[:],
                sq[:, th * 512:(th + 1) * 512], start=(dc == 0), stop=(dc == NCH - 1)), inc="PE")
    P.wait("vector", "PE")
    rr = _ln_rows(nc, P, env, T, psSx[0:1, 0:T], psSq[0:1, 0:T], "mr")
    P.chain("vector")
    P.do("vector", lambda e: e.tensor_copy(rows_b[0:1, 0:T], rr["r"]))
    P.do("vector", lambda e: e.tensor_copy(rows_b[0:1, T:2 * T], rr["t3"]))
    P.wait("tensor", "DV")
    for th in range(2):
        P.do("tensor", lambda e, th=th: e.matmul(
            psMisc[:, th * 512:(th + 1) * 512], ones_row_b[:],
            rows_b[0:1, th * 512:(th + 1) * 512], start=True, stop=True), inc="PE")
    P.wait("vector", "PE")
    P.do("vector", lambda e: e.tensor_copy(rbf_b[:], psMisc[:, 0:T]), inc="DV")
    P.wait("tensor", "DV")
    for th in range(2):
        P.do("tensor", lambda e, th=th: e.matmul(
            psMisc[:, th * 512:(th + 1) * 512], ones_row_b[:],
            rows_b[0:1, T + th * 512:T + (th + 1) * 512], start=True, stop=True), inc="PE")
    P.wait("vector", "PE")
    P.do("vector", lambda e: e.tensor_copy(mrb_b[:], psMisc[:, 0:T]), inc="DV")

    def z_tile(lt, z):
        P.do("scalar", lambda e, lt=lt, z=z: e.activation(z[:], xv(lt), AF.Identity), inc="AC")
        P.wait("vector", "AC")
        P.do("vector", lambda e, z=z: e.tensor_tensor(z[:], z[:], rbf_b[:], ALU.mult))
        P.chain("vector")
        P.do("vector", lambda e, z=z: e.tensor_tensor(z[:], z[:], mrb_b[:], ALU.subtract))

    pairs_o = [(n, m) for n in range(4) for m in range(n + 1, 4)]
    for lt in range(NCH):
        z = zt[lt % 2]
        if lt >= 2:
            P.wait("scalar", "DV")
            P.wait("scalar", "AC")
        z_tile(lt, z)
        P.chain("vector")
        P.wait("tensor", "DV")
        for _h in range(3):
            P.do("tensor", lambda e: e.matmul(psHeat[0:1, 0:1], ones_col_b[:], ones_col_b[:],
                                              start=True, stop=True), inc="PE")
        P.wait("scalar", "DV")
        for n in range(4):
            P.chain("scalar")
            P.do("scalar", lambda e, lt=lt, n=n, z=z: e.activation(
                sqb[lt % 2][:, 0:256], z[:, n * 256:n * 256 + 256], AF.Square,
                accum_out=gbuf[:, lt * 16 + 5 * n:lt * 16 + 5 * n + 1]))
        for (n, m) in pairs_o:
            P.chain("vector")
            P.do("vector", lambda e, n=n, m=m, z=z: e.tensor_tensor(
                junk[:], z[:, n * 256:n * 256 + 256], z[:, m * 256:m * 256 + 256], ALU.mult))
            P.chain("vector")
            P.do("vector", lambda e, lt=lt, n=n, m=m: e.reduce_sum(
                gbuf[:, lt * 16 + n * 4 + m:lt * 16 + n * 4 + m + 1], junk[:], axis=AX.X))
    P.wait("vector", "AC")
    P.chain("vector")
    for lt in range(NCH):
        for (n, m) in pairs_o:
            P.do("vector", lambda e, lt=lt, n=n, m=m: e.tensor_copy(
                gbuf[:, lt * 16 + m * 4 + n:lt * 16 + m * 4 + n + 1],
                gbuf[:, lt * 16 + n * 4 + m:lt * 16 + n * 4 + m + 1]))
    P.chain("vector")
    P.do("vector", lambda e: e.tensor_scalar_mul(gbuf[:], gbuf[:], GSCALE), inc="DV")
    P.wait("sync", "DV")
    P.dma("sync", lambda e: e.dma_start(
        out=bass.AP(tensor=env["g_part"], offset=0, ap=[[16, 128], [16 * 128, 32], [1, 16]]),
        in_=gbuf[:].rearrange("p (a b) -> p a b", b=16)), "gpt")
    P.wslot("gpsimd", "gpt")
    P.do("gpsimd", lambda e: e.collective_compute(
        "AllReduce", ALU.add, replica_groups=[list(range(NCORE))],
        ins=[env["g_part"].ap()], outs=[env["g_full"].ap()]), inc="CC")
    P.wait("sync", "CC")
    P.dma("sync", lambda e: e.dma_start(
        out=gbuf[:].rearrange("p (a b) -> p a b", b=16),
        in_=bass.AP(tensor=env["g_full"], offset=0, ap=[[16, 128], [16 * 128, 32], [1, 16]])),
        "gld")
    g3 = gbuf[:].rearrange("p (a m) -> p a m", m=4)
    P.wslot("vector", "gld")
    P.do("vector", lambda e: e.reduce_max(junk[:, 0:128].unsqueeze(2), g3, axis=AX.X,
                                          negate=True))
    P.chain("vector")
    P.do("vector", lambda e: e.tensor_tensor(
        g3, g3, junk[:, 0:128].unsqueeze(2).broadcast_to([128, 128, 4]), ALU.add))
    P.wait("scalar", "DV")
    P.do("scalar", lambda e: e.activation(gbuf[:], gbuf[:], AF.Exp))
    P.wait("vector", "AC")
    P.do("vector", lambda e: e.reduce_sum(junk[:, 0:128].unsqueeze(2), g3, axis=AX.X))
    P.chain("vector")
    P.do("vector", lambda e: e.reciprocal(junk[:, 0:128], junk[:, 0:128]))
    P.chain("vector")
    P.do("vector", lambda e: e.tensor_tensor(
        g3, g3, junk[:, 0:128].unsqueeze(2).broadcast_to([128, 128, 4]), ALU.mult))
    P.chain("vector")
    P.do("vector", lambda e: e.tensor_scalar_add(
        gbuf[:].rearrange("p (a m) -> p a m", m=4)[:, :, 0:1].rearrange("p a b -> p (a b)") if False
        else bass.AP(tensor=gbuf[:].tensor, offset=gbuf[:].offset, ap=[[512, 128], [16, 32], [5, 4]]),
        bass.AP(tensor=gbuf[:].tensor, offset=gbuf[:].offset, ap=[[512, 128], [16, 32], [5, 4]]),
        1.0))

    msga = p.enter_context(nc.sbuf_tensor("ab_msg", [128, T], BF16))
    for lt in range(NCH):
        z = zt[lt % 2]
        if lt >= 2:
            P.wait("scalar", "DV")
        z_tile(lt, z)
        P.chain("vector")
        P.wait("tensor", "DV")
        for _h in range(3):
            P.do("tensor", lambda e: e.matmul(psHeat[0:1, 0:1], ones_col_b[:], ones_col_b[:],
                                              start=True, stop=True), inc="PE")
        for m in range(4):
            for n in range(4):
                if m == 0:
                    P.do("vector", lambda e, lt=lt, n=n, z=z: e.tensor_scalar(
                        msga[:, n * 256:n * 256 + 256], z[:, 0:256],
                        gbuf[:, lt * 16 + n * 4:lt * 16 + n * 4 + 1], None, ALU.mult))
                else:
                    P.chain("vector", back=3)
                    P.do("vector", lambda e, lt=lt, n=n, m=m, z=z: e.scalar_tensor_tensor(
                        msga[:, n * 256:n * 256 + 256], z[:, m * 256:m * 256 + 256],
                        gbuf[:, lt * 16 + n * 4 + m:lt * 16 + n * 4 + m + 1],
                        msga[:, n * 256:n * 256 + 256], ALU.mult, ALU.add))
        P.chain("vector")
        P.do("vector", lambda e, lt=lt: e.tensor_tensor(
            xv(lt), msga[:], xv(lt), ALU.add))
    p.close()


def emit_phase5(nc, P, env, y_tok):
    xv = env["xv"]
    idF = env["idF"]
    p = ExitStack()
    ostage = [p.enter_context(nc.sbuf_tensor(f"p5_os{i}", [128, D], F32)) for i in range(2)]
    psO = p.enter_context(nc.psum_tensor("p5_psO", [128, 2048], F32))
    for tc in range(NTT):
        st = ostage[tc % 2]
        if tc >= 2:
            P.wslot("vector", f"ys{tc % 2}")
        for g4 in range(NCH // 4):
            if tc * NCH // 4 + g4 >= 1:
                P.wait("tensor", "DV")
            for q in range(4):
                dc = g4 * 4 + q
                P.do("tensor", lambda e, tc=tc, dc=dc, q=q: e.transpose(
                    psO[:, q * 512:q * 512 + 128], xv(dc, tc * 128, 128), idF[:]), inc="PE")
            P.wait("vector", "PE")
            P.do("vector", lambda e, g4=g4, st=st: e.tensor_copy(
                st[:].rearrange("p (a c) -> p a c", c=128)[:, g4 * 4:(g4 + 1) * 4, :],
                psO[:].rearrange("p (s c) -> p s c", c=512)[:, :, 0:128]), inc="DV")
        P.wait("sync", "DV")
        P.dma("sync", lambda e, tc=tc, st=st: e.dma_start(
            out=y_tok[tc * 128:(tc + 1) * 128, :], in_=st[:]), f"ys{tc % 2}")
    P.wslot("sync", "ys0")
    P.wslot("sync", "ys1")
    p.close()


_CACHE = {}


def kernel(**inputs):
    f32 = np.float32
    x = np.ascontiguousarray(inputs["L0_attn_a_x"], dtype=f32)
    Wq = np.ascontiguousarray(inputs["L0_attn_a_Wq"], dtype=f32)
    bq = np.ascontiguousarray(inputs["L0_attn_a_bq"], dtype=f32)
    Wk = np.ascontiguousarray(inputs["L0_attn_a_Wk"], dtype=f32).reshape(D, 128)
    bk = np.ascontiguousarray(inputs["L0_attn_a_bk"], dtype=f32).reshape(128, 1)
    Wv = np.ascontiguousarray(inputs["L0_attn_a_Wv"], dtype=f32).reshape(D, 128)
    bv = np.ascontiguousarray(inputs["L0_attn_a_bv"], dtype=f32).reshape(128, 1)
    Wb = np.ascontiguousarray(inputs["L0_ffn_base_W"], dtype=f32)
    bb = np.ascontiguousarray(inputs["L0_ffn_base_b"], dtype=f32).reshape(1, D)
    rW = np.ascontiguousarray(inputs["L0_ffn_router_W"], dtype=f32)
    rb = np.ascontiguousarray(inputs["L0_ffn_router_b"], dtype=f32).reshape(1, 15)
    A = np.ascontiguousarray(inputs["L0_ffn_A"], dtype=f32).reshape(128, D)
    Bm = np.ascontiguousarray(inputs["L0_ffn_B"], dtype=f32)
    emap = np.ascontiguousarray(inputs["L0_ffn_expert_map"], dtype=np.int32).reshape(15, 1)
    Wq2 = Wq.reshape(D, 128 * 32)

    key = "prog"
    if key not in _CACHE:
        _CACHE[key] = build_program()
    nc = _CACHE[key]

    in_maps = []
    for c in range(NCORE):
        xs = np.ascontiguousarray(x[:, c * SLOC:(c + 1) * SLOC, :].reshape(T, D))
        in_maps.append(dict(
            x_tok=xs,
            w_slice=np.ascontiguousarray(Wb[c * OSL:(c + 1) * OSL]),
            wq_slice=np.ascontiguousarray(Wq2[c * OSL:(c + 1) * OSL]),
            wk_in=Wk, wv_in=Wv, bq_in=bq, bk_in=bk, bv_in=bv, bb_in=bb,
            rw_in=rW, rb_in=rb, a_in=A,
            b_in=np.ascontiguousarray(Bm[:, c * 512:(c + 1) * 512, :].reshape(D, 16)),
            emap_in=emap,
        ))
    res = run_bass_kernel_spmd(nc, in_maps, core_ids=list(range(NCORE)))
    out = np.empty((B, S, D), dtype=np.float32)
    for c in range(NCORE):
        out[:, c * SLOC:(c + 1) * SLOC, :] = res.results[c]["y_tok"].reshape(B, SLOC, D)
    return out


if __name__ == "__main__":
    pass
